# version 36
# baseline (speedup 1.0000x reference)
import numpy as np
import ml_dtypes
import concourse.bass as bass
import concourse.mybir as mybir
from concourse.bass_utils import run_bass_kernel_spmd
from concourse.alu_op_type import AluOpType as OP

F32 = mybir.dt.float32
BF16 = mybir.dt.bfloat16
AF = mybir.ActivationFunctionType
AX = mybir.AxisListType

T_FULL = 4096
C = 1024
H = 16
K = 64
TT = 128
NQ = 8
EPS = 1e-5

ENGINES = ("sync", "gpsimd", "tensor", "vector", "scalar")


class Prog:
    def __init__(self):
        self.ops = {e: [] for e in ENGINES}
        self.n = 0
        self.eng_of = {}

    def op(self, eng, fn, deps=(), dma=False):
        opid = self.n
        self.n += 1
        deps = [d for d in deps if d is not None]
        self.ops[eng].append((fn, deps, dma, opid))
        self.eng_of[opid] = (eng, dma)
        return opid

    def dma(self, eng, fn, deps=()):
        return self.op(eng, fn, deps, dma=True)

    def emit(self, nc, block, sems):
        val = {}
        for eng in ENGINES:
            cc = dc = 0
            for fn, deps, is_dma, opid in self.ops[eng]:
                if is_dma:
                    dc += 16
                    val[opid] = (sems[(eng, "d")], dc)
                else:
                    cc += 1
                    val[opid] = (sems[(eng, "c")], cc)

        def section(eng):
            def body(e):
                seen = {}
                dseen = 0
                for fn, deps, is_dma, opid in self.ops[eng]:
                    if is_dma:
                        sd, vd = val[opid]
                        if vd - 16 > 0 and dseen < vd - 16:
                            e.wait_ge(sd, vd - 16)
                            dseen = vd - 16
                            seen[sd.name] = max(seen.get(sd.name, -1), vd - 16)
                    for d in deps:
                        de, ddma = self.eng_of[d]
                        if de == eng == "tensor" and not ddma and not is_dma:
                            continue
                        if de == eng and ddma and is_dma:
                            continue
                        s, v = val[d]
                        if seen.get(s.name, -1) < v:
                            e.wait_ge(s, v)
                            seen[s.name] = v
                    inst = fn(e)
                    inst.then_inc(sems[(eng, "d" if is_dma else "c")], 16 if is_dma else 1)
            return body

        for eng in ENGINES:
            getattr(block, eng)(section(eng))


def build_nc(n_tiles=T_FULL // TT, debug=False):
    T = n_tiles * TT
    nc = bass.Bass("TRN2", target_bir_lowering=False)

    def din(name, shape, dt=F32):
        return nc.dram_tensor(name, shape, dt, kind="ExternalInput").ap()

    x_h = din("x", [T, C])
    w_h = {n: din("W" + n, [C, C]) for n in ["r", "k", "v", "g", "o", "fT"]}
    tmw1_h = din("tm_w1", [C, 160])
    w2aug_h = din("w2aug", [5 * 33, C])
    tdw1_h = din("td_w1", [C, 64])
    td2aug_h = din("td2aug", [65, C])
    tmaa_h = din("tmaa", [1, C])
    u_h = din("ufaaaa", [1, C])
    lnw_h = din("lnw", [1, C])
    lnb_h = din("lnb", [1, C])
    fb_h = din("fb", [1, C])
    lu_h = din("lora_up", [50, 4])
    ld_h = din("lora_down", [4, 100])
    da_h = din("down_aux", [100, C])
    uaT_h = din("up_auxT", [50, C])
    ident_h = din("ident", [128, 128])
    utri_h = din("utri", [128, 6 * 128])
    mask4_h = din("mask4", [128, 512], mybir.dt.uint8)
    ind2_h = din("ind2", [128, 2])

    out_h = nc.dram_tensor("out", [T, C], F32, kind="ExternalOutput").ap()
    dbg = {}
    if debug:
        for name, shape, dt in [
                ("d_xnfm", [128, NQ * TT], F32), ("d_sx", [128, NQ * TT], BF16),
                ("d_r", [128, NQ * TT], BF16), ("d_k", [128, NQ * TT], BF16),
                ("d_lw", [128, C], BF16), ("d_Er", [128, C], BF16),
                ("d_A", [128, H * 128], BF16), ("d_y", [128, C], F32),
                ("d_su", [128, H], F32), ("d_S", [128, H * K], F32),
                ("d_v", [128, C], BF16), ("d_yng", [128, NQ * TT], BF16),
                ("d_stm", [128, C], F32), ("d_weff", [128, NQ * C], BF16),
                ("d_g", [128, C], BF16), ("d_xi", [128, NQ * TT], BF16),
                ("d_Ktz", [128, H * 128], BF16), ("d_Khz", [128, H * 128], BF16)]:
            dbg[name] = nc.dram_tensor(name, shape, dt, kind="ExternalOutput").ap()

    p = Prog()
    dbg_toks = []

    def _dt(tok):
        dbg_toks.append(tok)
        return tok

    def sb(name, shape, dt):
        return nc.alloc_sbuf_tensor(name, shape, dt).ap()

    W_sb = {n: sb(f"W_{n}", [128, NQ * C], BF16) for n in ["r", "k", "v", "g", "o", "f"]}
    tmw1_sb = sb("tmw1_s", [128, NQ * 160], BF16)
    w2p = [sb(f"w2p{j}", [128, C], BF16) for j in range(3)]
    tdw1_sb = sb("tdw1_s", [128, NQ * 64], BF16)
    td2aug_sb = sb("td2aug_s", [65, C], BF16)
    tmaa_sb = sb("tmaa_s", [128, NQ], F32)
    u_sb = sb("ufa", [128, NQ], F32)
    lnwrow = sb("lnwrow_s", [1, C], BF16)
    lnbrow = sb("lnbrow_s", [1, C], BF16)
    fbrow = sb("fbrow", [1, C], BF16)
    lnw_bct = sb("lnw_bct", [128, C], BF16)
    lnb_bct = sb("lnb_bct", [128, C], BF16)
    ident = sb("ident_s", [128, 128], BF16)
    utri = sb("utri_s", [128, 6 * 128], BF16)
    mask4 = sb("mask4_s", [128, 512], mybir.dt.uint8)
    ind2 = sb("ind2_s", [128, 2], BF16)
    ind2u = sb("ind2u_s", [128, 2 * NQ], BF16)
    ones1 = sb("ones1", [1, 128], BF16)
    eps_sb = sb("eps_s", [128, 1], F32)

    x_tm = [sb("x_tm0", [128, C], F32), sb("x_tm1", [128, C], F32)]
    xn_fm = sb("xn_fm", [128, NQ * (TT + 1)], F32)
    sx_fm = sb("sx_fm", [128, NQ * TT], F32)
    xi_bf = sb("xi_bf", [128, NQ * TT], BF16)
    m_sb = [sb(f"mp{j}", [128, TT], BF16) for j in range(3)]
    tanhtd = sb("tanhtd", [65, TT], BF16)
    lw_tm = sb("lw_tm", [128, C], BF16)
    r_bf = sb("r_bf", [128, NQ * TT], BF16)
    k_bf = sb("k_bf", [128, NQ * TT], BF16)
    v_tm = sb("v_tm", [128, C], BF16)
    g_raw = sb("g_raw", [128, C], BF16)
    E1 = sb("E1_s", [128, 2 * C], BF16)
    E1f = E1[:].bitcast(F32)
    xi2 = E1[:, 0:NQ * TT]
    E2 = sb("E2_s", [128, C], F32)
    epl = sb("epl", [128, NQ], F32)
    cmf = sb("cmf", [128, NQ], F32)
    zc = sb("zc", [128, 1], F32)
    KtzA = sb("KtzA", [128, H * 128], BF16)
    KtzB = sb("KtzB", [128, H * 128], BF16)
    Ru = sb("Ru", [128, C], BF16)
    Rab = sb("Rab", [128, C], BF16)
    Khf = sb("Khf", [128, C], BF16)
    Khz = sb("Khz", [128, H * 128], BF16)
    A4 = sb("A4", [128, H * 128], BF16)
    su_sb = sb("su_s", [128, H], F32)
    Sz = [sb("SzA", [128, H * K], F32), sb("SzB", [128, H * K], F32)]
    S_bfz = sb("S_bfz", [128, H * K], BF16)
    y_tm = sb("y_tm", [128, C], F32)
    gn_sc = sb("gn_sc", [128, 4 * H], F32)
    gn_sc2 = sb("gn_sc2", [128, 3 * H], F32)
    ln_sc = sb("ln_sc", [128, 16], F32)
    yng_fm = sb("yng_fm", [128, NQ * TT], BF16)
    scr = sb("scr", [128, C], F32)
    scr2 = sb("scr2", [128, C], F32)
    sx_bf = sb("sx_bf", [128, NQ * TT], BF16)
    hi_tm = E1[:, C:2 * C]
    lo_tm = Rab[:]
    gl_b = Khf[:]
    glb_b = yng_fm[:]
    lnE = E2[:]
    rku = sx_bf[:]

    xnb = sb("xnb", [128, NQ * TT], BF16)
    gs_sb = g_raw
    dmix = E2[:].bitcast(BF16)[:, 0:NQ * TT]

    da_v = Sz[0][0:100, :]
    uaT_v = Sz[1][0:50, :]
    c1_v = A4[:].bitcast(F32)[0:4, :]
    c2_v = KtzA[:].bitcast(F32)[0:100, :]
    lu_v = Khz[:].bitcast(F32)[0:50, 0:4]
    ld_v = Khz[:].bitcast(F32)[0:4, 4:104]

    sbuf_used = nc.sbuf_base

    pb = [nc.alloc_psum_tensor(f"pb{i}", [128, 512], F32).ap() for i in range(8)]

    sems = {}
    sem_cms = []
    for e in ENGINES:
        for kind in ("c", "d"):
            cm = nc.semaphore(f"s_{e}_{kind}")
            sems[(e, kind)] = cm.__enter__()
            sem_cms.append(cm)

    blk = nc.Block()
    block = blk.__enter__()

    xnfm3 = xn_fm[:].rearrange("p (q c) -> p q c", q=NQ)
    sx3 = sx_fm[:].rearrange("p (q c) -> p q c", q=NQ)
    sxb3 = sx_bf[:].rearrange("p (q c) -> p q c", q=NQ)
    xnb3 = xnb[:].rearrange("p (q c) -> p q c", q=NQ)
    xi3 = xi_bf[:].rearrange("p (q c) -> p q c", q=NQ)
    xi23 = xi2.rearrange("p (q c) -> p q c", q=NQ)
    rb3 = r_bf[:].rearrange("p (q c) -> p q c", q=NQ)
    kb3 = k_bf[:].rearrange("p (q c) -> p q c", q=NQ)
    E1_3 = E1f.rearrange("p (q c) -> p q c", q=NQ)
    E2_3 = E2[:].rearrange("p (q c) -> p q c", q=NQ)
    KtzA4 = KtzA[:].rearrange("p (h c) -> p h c", h=H)
    KtzB4 = KtzB[:].rearrange("p (h c) -> p h c", h=H)
    Rab3 = Rab[:].rearrange("p (q c) -> p q c", q=NQ)
    rku3 = rku.rearrange("p (q c) -> p q c", q=NQ)
    yngfm3 = yng_fm[:].rearrange("p (q c) -> p q c", q=NQ)
    scr3 = scr[:].rearrange("p (q c) -> p q c", q=NQ)

    def wload(dst, src):
        return p.dma("gpsimd", lambda e, d=dst, s=src: e.dma_start(out=d, in_=s))

    t_ident = wload(ident[:], ident_h)
    t_utri = wload(utri[:], utri_h)
    t_mask4 = wload(mask4[:], mask4_h)
    t_ind2 = wload(ind2[:], ind2_h)
    t_zc = p.op("gpsimd", lambda e: e.memset(zc[:], 0.0))
    t_tmaa = p.dma("sync", lambda e: e.dma_start(out=tmaa_sb[:], in_=tmaa_h.rearrange("o (a p) -> p (o a)", p=128), allow_slow_non_contiguous=True))
    t_u = p.dma("sync", lambda e: e.dma_start(out=u_sb[:], in_=u_h.rearrange("o (a p) -> p (o a)", p=128), allow_slow_non_contiguous=True))
    t_iu = p.op("gpsimd", lambda e: e.tensor_tensor(
        out=ind2u[:].rearrange("p (q t) -> p q t", q=NQ),
        in0=u_sb[:].rearrange("p q -> p q ()").broadcast_to((128, NQ, 2)),
        in1=ind2[:].rearrange("p t -> p () t").broadcast_to((128, NQ, 2)),
        op=OP.mult), deps=[t_u, t_ind2])
    t_tmw1 = wload(tmw1_sb[:].rearrange("p (a c) -> p a c", a=NQ), tmw1_h.rearrange("(a p) c -> p a c", p=128))
    t_tdw1 = wload(tdw1_sb[:].rearrange("p (a c) -> p a c", a=NQ), tdw1_h.rearrange("(a p) c -> p a c", p=128))
    t_td2 = wload(td2aug_sb[:], td2aug_h)
    t_lnw = wload(lnwrow[:], lnw_h)
    t_lnb = wload(lnbrow[:], lnb_h)
    t_fb = wload(fbrow[:], fb_h)
    t_w2z = [p.op("gpsimd", lambda e, j=j: e.memset(w2p[j][:], 0.0)) for j in range(3)]
    t_w2 = []
    for i in range(5):
        j, base = i // 2, (i % 2) * 64
        t_w2.append(p.dma("gpsimd", lambda e, i=i, j=j, base=base: e.dma_start(
            out=w2p[j][base:base + 33, :], in_=w2aug_h[33 * i:33 * (i + 1), :]),
            deps=[t_w2z[j]]))
    t_ones = p.op("gpsimd", lambda e: e.memset(ones1[:], 1.0))
    t_eps = p.op("gpsimd", lambda e: e.memset(eps_sb[:], EPS))
    t_mz = [p.op("gpsimd", lambda e, j=j: e.memset(m_sb[j][:], 0.0)) for j in range(3)]
    t_mones = []
    for i in range(5):
        j, base = i // 2, (i % 2) * 64
        t_mones.append(p.op("gpsimd", lambda e, j=j, base=base: e.memset(m_sb[j][base + 32:base + 33, :], 1.0),
                            deps=[t_mz[j]]))
    t_tdones = p.op("gpsimd", lambda e: e.memset(tanhtd[64:65, :], 1.0))
    t_W = {}
    for n in ["r", "k", "v", "g", "o", "fT"]:
        key = "f" if n == "fT" else n
        t_W[key] = wload(W_sb[key][:].rearrange("p (a c) -> p a c", a=NQ),
                         w_h[n].rearrange("(a p) c -> p a c", p=128))
    t_lu = p.dma("sync", lambda e: e.dma_start(out=lu_v, in_=lu_h))
    t_ld = p.dma("sync", lambda e: e.dma_start(out=ld_v, in_=ld_h))
    t_da = p.dma("sync", lambda e: e.dma_start(out=da_v, in_=da_h))
    t_uaT = p.dma("sync", lambda e: e.dma_start(out=uaT_v, in_=uaT_h))

    mm = [None] * 4
    mm[0] = p.op("tensor", lambda e: e.matmul(pb[0][:], lhsT=ones1[:], rhs=lnwrow[:, 0:512], start=True, stop=True), deps=[t_ones, t_lnw])
    mm[1] = p.op("tensor", lambda e: e.matmul(pb[1][:], lhsT=ones1[:], rhs=lnwrow[:, 512:1024], start=True, stop=True))
    mm[2] = p.op("tensor", lambda e: e.matmul(pb[2][:], lhsT=ones1[:], rhs=lnbrow[:, 0:512], start=True, stop=True), deps=[t_lnb])
    mm[3] = p.op("tensor", lambda e: e.matmul(pb[3][:], lhsT=ones1[:], rhs=lnbrow[:, 512:1024], start=True, stop=True))
    ev = [None] * 4
    ev[0] = p.op("vector", lambda e: e.tensor_copy(lnw_bct[:, 0:512], pb[0][:]), deps=[mm[0]])
    ev[1] = p.op("vector", lambda e: e.tensor_copy(lnw_bct[:, 512:1024], pb[1][:]), deps=[mm[1]])
    ev[2] = p.op("vector", lambda e: e.tensor_copy(lnb_bct[:, 0:512], pb[2][:]), deps=[mm[2]])
    ev[3] = p.op("vector", lambda e: e.tensor_copy(lnb_bct[:, 512:1024], pb[3][:]), deps=[mm[3]])

    c1a = p.op("tensor", lambda e: e.matmul(pb[4][0:4, :], lhsT=lu_v, rhs=uaT_v[:, 0:512], start=True, stop=True), deps=[t_lu, t_uaT])
    c1b = p.op("tensor", lambda e: e.matmul(pb[5][0:4, :], lhsT=lu_v, rhs=uaT_v[:, 512:1024], start=True, stop=True))
    c1e = p.op("vector", lambda e: e.tensor_copy(c1_v[:, 0:512], pb[4][0:4, :]), deps=[c1a])
    c1f = p.op("vector", lambda e: e.tensor_copy(c1_v[:, 512:1024], pb[5][0:4, :]), deps=[c1b])
    c2a = p.op("tensor", lambda e: e.matmul(pb[6][0:100, :], lhsT=ld_v, rhs=c1_v[:, 0:512], start=True, stop=True), deps=[t_ld, c1e])
    c2b = p.op("tensor", lambda e: e.matmul(pb[7][0:100, :], lhsT=ld_v, rhs=c1_v[:, 512:1024], start=True, stop=True), deps=[c1f])
    c2e = p.op("vector", lambda e: e.tensor_copy(c2_v[:, 0:512], pb[6][0:100, :]), deps=[c2a])
    c2f = p.op("vector", lambda e: e.tensor_copy(c2_v[:, 512:1024], pb[7][0:100, :]), deps=[c2b])
    bank_free = {0: ev[0], 1: ev[1], 2: ev[2], 3: ev[3], 4: c1e, 5: c1f, 6: c2e, 7: c2f}
    weff_toks = []
    for q in range(NQ):
        b0 = (q % 2) * 2
        mm_a = p.op("tensor", lambda e, q=q, b0=b0: e.matmul(
            pb[b0][:], lhsT=da_v[:, q * 128:(q + 1) * 128], rhs=c2_v[:, 0:512], start=True, stop=True),
            deps=[t_da, c2e, c2f, bank_free[b0]])
        mm_b = p.op("tensor", lambda e, q=q, b0=b0: e.matmul(
            pb[b0 + 1][:], lhsT=da_v[:, q * 128:(q + 1) * 128], rhs=c2_v[:, 512:1024], start=True, stop=True),
            deps=[bank_free[b0 + 1]])
        we_a = p.op("vector", lambda e, q=q, b0=b0: e.tensor_tensor(
            out=W_sb["f"][:, q * C:q * C + 512], in0=pb[b0][:], in1=W_sb["f"][:, q * C:q * C + 512], op=OP.add),
            deps=[mm_a, t_W["f"]])
        we_b = p.op("vector", lambda e, q=q, b0=b0: e.tensor_tensor(
            out=W_sb["f"][:, q * C + 512:(q + 1) * C], in0=pb[b0 + 1][:], in1=W_sb["f"][:, q * C + 512:(q + 1) * C], op=OP.add),
            deps=[mm_b])
        bank_free[b0] = we_a
        bank_free[b0 + 1] = we_b
        weff_toks += [we_a, we_b]
    if debug:
        dbg_toks.append(p.dma("gpsimd", lambda e: e.dma_start(out=dbg["d_weff"], in_=W_sb["f"][:]), deps=weff_toks))

    lil_done = weff_toks
    t_zA4 = p.op("gpsimd", lambda e: e.memset(A4[:], 0.0), deps=[c2a, c2b])
    t_zKtzA = p.op("gpsimd", lambda e: e.memset(KtzA[:], 0.0), deps=lil_done)
    t_zKtzB = p.op("gpsimd", lambda e: e.memset(KtzB[:], 0.0), deps=lil_done)
    t_zKhz = p.op("gpsimd", lambda e: e.memset(Khz[:], 0.0), deps=lil_done)
    t_zS0 = p.op("gpsimd", lambda e: e.memset(Sz[0][:], 0.0), deps=lil_done)
    t_zS1 = p.op("gpsimd", lambda e: e.memset(Sz[1][:], 0.0), deps=lil_done)
    t_zSbf = p.op("gpsimd", lambda e: e.memset(S_bfz[:], 0.0))

    dep = {
        "xnfm_r": [], "sx_r": [], "xi_r": [], "lw_r": [], "rk_r": [], "vg_r": [],
        "hi_r": [], "lo_r": [], "scr_r": [], "scr2_r": [], "xnb_r": [], "stm_r": [],
        "y_r": [], "su_r": [], "A4_r": [t_zA4], "KtzA_r": [t_zKtzA], "KtzB_r": [t_zKtzB],
        "Ru_r": [], "Rab_r": [], "Khf_r": [], "Khz_r": [t_zKhz],
        "E1_r": [], "E2_r": [], "cmf_r": [], "epl_r": [], "gs_r": [], "gn_r": [],
        "yngfm_r": [], "m_r": [[] for _ in range(5)], "ttd_r": [],
        "out_dma": [None, None],
        "Sz_w": [t_zS0, t_zS1], "Sz_r": [[], []], "Sbf_ready": t_zSbf,
    }

    MAGIC = 0x5f3759df
    I32 = mybir.dt.int32

    def emit_rsqrt(dst, var_ap, scratch, deps, niter=4):
        n = var_ap.shape[-1]
        ve = scratch[:, 0:n]
        h_ = scratch[:, n:2 * n]
        t_ = scratch[:, 2 * n:3 * n]
        tok = p.op("vector", lambda e: e.tensor_single_scalar(out=ve, in_=var_ap, scalar=EPS, op=OP.add), deps=deps)
        tok = p.op("vector", lambda e: e.tensor_single_scalar(out=h_, in_=ve, scalar=-0.5, op=OP.mult), deps=[tok])
        vi = ve.bitcast(I32)
        di = dst.bitcast(I32)
        tok = p.op("vector", lambda e: e.tensor_single_scalar(out=di, in_=vi, scalar=1, op=OP.arith_shift_right), deps=[tok])
        tok = p.op("vector", lambda e: e.tensor_single_scalar(out=di, in_=di, scalar=-1, op=OP.bitwise_xor), deps=[tok])
        tok = p.op("vector", lambda e: e.tensor_single_scalar(out=di, in_=di, scalar=MAGIC + 1, op=OP.add), deps=[tok])
        for _ in range(niter):
            tok = p.op("vector", lambda e: e.tensor_tensor(out=t_, in0=dst, in1=dst, op=OP.mult), deps=[tok])
            tok = p.op("vector", lambda e: e.tensor_tensor(out=t_, in0=t_, in1=h_, op=OP.mult), deps=[tok])
            tok = p.op("vector", lambda e: e.scalar_tensor_tensor(out=dst, in0=t_, scalar=1.5, in1=dst, op0=OP.add, op1=OP.mult), deps=[tok])
        return tok

    out_tokens = []

    def bank_mm(bi, fn, deps=()):
        return p.op("tensor", fn, deps=list(deps) + [bank_free[bi]])

    GUARD_KEYS = ["xnfm_r", "sx_r", "xi_r", "lw_r", "rk_r", "vg_r", "hi_r", "lo_r",
                  "scr_r", "scr2_r", "xnb_r", "stm_r", "y_r", "su_r", "A4_r", "KtzA_r", "KtzB_r",
                  "Ru_r", "Rab_r", "Khf_r", "Khz_r", "E1_r", "E2_r", "cmf_r",
                  "epl_r", "gs_r", "gn_r", "yngfm_r", "ttd_r"]
    TS = {}

    def PF1(t):
        if debug and dbg_toks:
            for kk_ in GUARD_KEYS:
                dep[kk_] = dep[kk_] + dbg_toks
            dep["Sz_r"][0] = dep["Sz_r"][0] + dbg_toks
            dep["Sz_r"][1] = dep["Sz_r"][1] + dbg_toks
            dbg_toks.clear()
        S = TS.setdefault(t, {})
        xb = t % 2
        xt = x_tm[xb]

        t_x = p.dma("sync", lambda e, t=t, xt=xt: e.dma_start(
            out=xt[:], in_=x_h[t * TT:(t + 1) * TT, :]), deps=[dep["out_dma"][xb]])

        t_r1 = p.op("vector", lambda e, xt=xt: e.tensor_reduce(
            out=ln_sc[:, 0:1], in_=xt[:], axis=AX.X, op=OP.add), deps=[t_x])
        t_sq = p.op("scalar", lambda e, xt=xt: e.activation(
            out=lnE, in_=xt[:], func=AF.Square), deps=[t_x] + dep["E2_r"])
        dep["E2_r"] = []
        t_r2 = p.op("vector", lambda e: e.tensor_reduce(
            out=ln_sc[:, 1:2], in_=lnE, axis=AX.X, op=OP.add), deps=[t_sq])
        t_m = p.op("vector", lambda e: e.tensor_scalar_mul(ln_sc[:, 2:3], ln_sc[:, 0:1], 1.0 / C), deps=[t_r1])
        t_msq = p.op("vector", lambda e: e.tensor_tensor(
            out=ln_sc[:, 3:4], in0=ln_sc[:, 2:3], in1=ln_sc[:, 2:3], op=OP.mult), deps=[t_m])
        t_var = p.op("vector", lambda e: e.scalar_tensor_tensor(
            out=ln_sc[:, 4:5], in0=ln_sc[:, 1:2], scalar=1.0 / C, in1=ln_sc[:, 3:4],
            op0=OP.mult, op1=OP.subtract), deps=[t_r2, t_msq])
        aev_p = TS.get(t - 1, {}).get("a_ev", [])
        p.op("tensor", lambda e: e.matmul(
            pb[0][0:1, 0:1], lhsT=ident[:, 0:1], rhs=ident[:, 0:1], start=True, stop=True),
            deps=[t_r2] + aev_p[0:1])
        t_ln1 = p.op("scalar", lambda e: e.activation(
            out=ln_sc[:, 5:6], in_=ln_sc[:, 4:5], func=AF.Ln, bias=eps_sb[:]),
            deps=[t_var, t_eps])
        t_rstd = p.op("scalar", lambda e: e.activation(
            out=ln_sc[:, 6:7], in_=ln_sc[:, 5:6], func=AF.Exp, scale=-0.5), deps=[t_ln1])
        t_xn = p.op("vector", lambda e, xt=xt: e.tensor_scalar(
            out=lnE, in0=xt[:], scalar1=ln_sc[:, 2:3], scalar2=ln_sc[:, 6:7],
            op0=OP.subtract, op1=OP.mult), deps=[t_rstd, t_r2])
        p.op("tensor", lambda e: e.matmul(
            pb[0][0:1, 0:1], lhsT=ident[:, 0:1], rhs=ident[:, 0:1], start=True, stop=True),
            deps=[t_xn])
        t_hi = p.op("scalar", lambda e: e.activation(
            out=hi_tm, in_=lnE, func=AF.Copy), deps=[t_xn] + dep["hi_r"] + dep["E1_r"])
        t_lo = p.op("vector", lambda e: e.tensor_tensor(
            out=lo_tm, in0=lnE, in1=hi_tm, op=OP.subtract), deps=[t_hi] + dep["lo_r"] + dep["Rab_r"])
        dep["E2_r"] = [t_hi, t_lo]

        S["t_hi"] = t_hi
        S["t_lo"] = t_lo
        S["xt"] = xt
        S["xb"] = xb

    def PF2(t):
        S = TS[t]
        t_hi = S["t_hi"]
        t_lo = S["t_lo"]

        if t > 0:
            t_bnd = p.op("vector", lambda e: e.tensor_copy(xnfm3[:, :, 0:1], xnfm3[:, :, TT:TT + 1]),
                         deps=dep["xnfm_r"])
        else:
            t_bnd = p.op("vector", lambda e: e.memset(xnfm3[:, :, 0:1], 0.0), deps=dep["xnfm_r"])
        dep["xnfm_r"] = []
        mms = []
        for q in range(NQ):
            bi = q // 4
            bank_mm(bi, lambda e, q=q, bi=bi: e.matmul(
                pb[bi][:, (q % 4) * 128:(q % 4) * 128 + 128],
                lhsT=hi_tm[:, q * 128:(q + 1) * 128], rhs=ident[:], start=True, stop=False),
                deps=[t_hi, t_ident])
            mm2 = p.op("tensor", lambda e, q=q, bi=bi: e.matmul(
                pb[bi][:, (q % 4) * 128:(q % 4) * 128 + 128],
                lhsT=lo_tm[:, q * 128:(q + 1) * 128], rhs=ident[:], start=False, stop=True),
                deps=[t_lo])
            mms.append(mm2)
        dep["hi_r"] = [mms[-1]]
        dep["lo_r"] = [mms[-1]]
        dep["E1_r"].append(mms[-1])
        dep["Rab_r"].append(mms[-1])
        xn_ev = []
        for half in range(2):
            tok = p.op("scalar", lambda e, half=half: e.copy(
                xnfm3[:, 4 * half:4 * half + 4, 1:1 + TT],
                pb[half][:].rearrange("p (q c) -> p q c", q=4)),
                deps=[mms[4 * half + 3], t_bnd])
            bank_free[half] = tok
            xn_ev.append(tok)

        S["xn_ev"] = xn_ev

        t_xnbp = p.op("vector", lambda e: e.tensor_copy(xnb3, xnfm3[:, :, 1:TT + 1]),
                      deps=xn_ev + dep["xnb_r"])
        dep["xnb_r"] = []
        dep["xnfm_r"].append(t_xnbp)
        t_sx = p.op("vector", lambda e: e.tensor_tensor(
            out=sxb3, in0=xnfm3[:, :, 0:TT], in1=xnfm3[:, :, 1:TT + 1], op=OP.subtract),
            deps=xn_ev + dep["sx_r"])
        dep["sx_r"] = []
        dep["xnfm_r"].append(t_sx)
        p.op("tensor", lambda e: e.matmul(
            pb[0][0:1, 0:1], lhsT=ident[:, 0:1], rhs=ident[:, 0:1], start=True, stop=True),
            deps=[t_sx, xn_ev[0]])
        sxm = dmix
        sxm3 = sxm.rearrange("p (q c) -> p q c", q=NQ)
        t_sxm = []
        for q in range(NQ):
            tok = p.op("vector", lambda e, q=q: e.scalar_tensor_tensor(
                out=sxm3[:, q, :], in0=sxb3[:, q, :], scalar=tmaa_sb[:, q:q + 1],
                in1=sxb3[:, q, :], op0=OP.mult, op1=OP.bypass),
                deps=[t_sx, t_tmaa] + (dep["E2_r"] if q == 0 else []))
            t_sxm.append(tok)
            dep["sx_r"].append(tok)
        dep["E2_r"] = []

        mm_last = {}
        for part in range(2):
            cols = 128 if part == 0 else 32
            for q in range(NQ):
                bank_mm(4, lambda e, part=part, q=q, cols=cols: e.matmul(
                    pb[4][0:cols, part * 128:part * 128 + TT],
                    lhsT=tmw1_sb[:, q * 160 + part * 128:q * 160 + part * 128 + cols],
                    rhs=xnb[:, q * TT:(q + 1) * TT],
                    start=(q == 0), stop=False),
                    deps=[t_tmw1, t_xnbp])
            for q in range(NQ):
                tok = p.op("tensor", lambda e, part=part, q=q, cols=cols: e.matmul(
                    pb[4][0:cols, part * 128:part * 128 + TT],
                    lhsT=tmw1_sb[:, q * 160 + part * 128:q * 160 + part * 128 + cols],
                    rhs=sxm[:, q * TT:(q + 1) * TT],
                    start=False, stop=(q == NQ - 1)),
                    deps=[t_sxm[q]])
                dep["E2_r"].append(tok)
                if q == NQ - 1:
                    mm_last[part] = tok
        m_toks = []
        for i in range(5):
            part = 0 if i < 4 else 1
            rs = (i * 32) % 128
            j, base = i // 2, (i % 2) * 64
            tok = p.op("scalar", lambda e, part=part, rs=rs, j=j, base=base: e.activation(
                out=m_sb[j][base:base + 32, :], in_=pb[4][rs:rs + 32, part * 128:part * 128 + TT],
                func=AF.Tanh),
                deps=[mm_last[0], mm_last[1], t_mones[i]] + dep["m_r"][i])
            dep["m_r"][i] = []
            m_toks.append(tok)
        bank_free[4] = m_toks[4]

        S["m_toks"] = m_toks
        S["t_sx"] = t_sx
        S["t_xnbp"] = t_xnbp

    def PF3(t):
        S = TS[t]
        m_toks = S["m_toks"]
        t_sx = S["t_sx"]
        t_xnbp = S["t_xnbp"]
        xn_ev = S["xn_ev"]

        proj_ev = {"r": [], "k": [], "v": [], "g": []}
        lw_toks = []

        def emit_delta(i):
            jm, base = i // 2, (i % 2) * 64
            db0 = (i % 2) * 2
            dmm = []
            for q in range(NQ):
                bi = db0 + q // 4
                tok = bank_mm(bi, lambda e, q=q, bi=bi, jm=jm, base=base: e.matmul(
                    pb[bi][:, (q % 4) * 128:(q % 4) * 128 + 128],
                    lhsT=w2p[jm][base:base + 64, q * 128:(q + 1) * 128],
                    rhs=m_sb[jm][base:base + 64, :], start=True, stop=True),
                    deps=[t_w2[i], m_toks[i]])
                dmm.append(tok)
                dep["m_r"][i].append(tok)
            return dmm

        next_dmm = emit_delta(0)
        prev_xi = None
        for i in range(5):
            db0 = (i % 2) * 2
            dmm = next_dmm
            tmix = []
            for half in range(2):
                bi = db0 + half
                tok = p.op("vector", lambda e, bi=bi, half=half: e.tensor_tensor(
                    out=dmix[:, half * 512:(half + 1) * 512], in0=sx_bf[:, half * 512:(half + 1) * 512],
                    in1=pb[bi][:], op=OP.mult),
                    deps=[dmm[4 * half + 3], t_sx] + (dep["E2_r"] if i == 0 else [prev_xi]))
                bank_free[bi] = tok
                tmix.append(tok)
                dep["sx_r"].append(tok)
            if i == 0:
                dep["E2_r"] = []
            xcur = xi_bf[:] if (i % 2 == 0) else xi2
            xkey = "xi_r" if (i % 2 == 0) else "E1_r"
            xi_tok = p.op("vector", lambda e, xcur=xcur: e.tensor_tensor(
                out=xcur, in0=dmix[:], in1=xnb[:], op=OP.add),
                deps=tmix + [t_xnbp] + dep[xkey])
            dep["xnb_r"].append(xi_tok)
            prev_xi = xi_tok
            dep[xkey] = []
            if i < 4:
                next_dmm = emit_delta(i + 1)
            if i == 4:
                dep["E2_r"] = [xi_tok]
            if debug and t == 0 and i == 3:
                tok_dxi = p.dma("gpsimd", lambda e: e.dma_start(out=dbg["d_xi"], in_=xi2), deps=[xi_tok])
                dep["xi_r"].append(_dt(tok_dxi))

            if i == 0:
                last = None
                for q in range(NQ):
                    last = bank_mm(5, lambda e, q=q: e.matmul(
                        pb[5][0:64, 0:TT], lhsT=tdw1_sb[:, q * 64:(q + 1) * 64],
                        rhs=xi_bf[:, q * TT:(q + 1) * TT], start=(q == 0), stop=(q == NQ - 1)),
                        deps=[t_tdw1, xi_tok])
                    dep["xi_r"].append(last)
                t_tanh = p.op("scalar", lambda e: e.activation(
                    out=tanhtd[0:64, :], in_=pb[5][0:64, 0:TT], func=AF.Tanh),
                    deps=[last, t_tdones] + dep["ttd_r"])
                dep["ttd_r"] = []
                bank_free[5] = t_tanh
                wmm = []
                for half in range(2):
                    bi = 6 + half
                    tok = bank_mm(bi, lambda e, half=half, bi=bi: e.matmul(
                        pb[bi][:], lhsT=tanhtd[:],
                        rhs=td2aug_sb[:, half * 512:(half + 1) * 512], start=True, stop=True),
                        deps=[t_td2, t_tanh])
                    wmm.append(tok)
                    dep["ttd_r"].append(tok)
                for half in range(2):
                    tok_lw = p.op("scalar", lambda e, half=half: e.activation(
                        out=lw_tm[:, half * 512:(half + 1) * 512], in_=pb[6 + half][:], func=AF.Exp),
                        deps=[wmm[half]] + dep["lw_r"])
                    bank_free[6 + half] = tok_lw
                    lw_toks.append(tok_lw)
                dep["lw_r"] = []
            elif i in (1, 3):
                name = "k" if i == 1 else "r"
                dst = k_bf if i == 1 else r_bf
                mm_l = {}
                for qo in range(NQ):
                    bi = 4 + qo // 4
                    for qi in range(NQ):
                        tok = bank_mm(bi, lambda e, name=name, qo=qo, qi=qi, bi=bi: e.matmul(
                            pb[bi][:, (qo % 4) * 128:(qo % 4) * 128 + 128],
                            lhsT=W_sb[name][:, qi * C + qo * 128:qi * C + qo * 128 + 128],
                            rhs=xi2[:, qi * TT:(qi + 1) * TT],
                            start=(qi == 0), stop=(qi == NQ - 1)),
                            deps=[t_W[name], xi_tok])
                        dep["E1_r"].append(tok)
                        if qi == NQ - 1:
                            mm_l[qo] = tok
                for half in range(2):
                    bi = 4 + half
                    tok = p.op("scalar", lambda e, half=half, bi=bi, dst=dst: e.copy(
                        dst[:, half * 4 * TT:(half + 1) * 4 * TT], pb[bi][:]),
                        deps=[mm_l[4 * half + 0], mm_l[4 * half + 1], mm_l[4 * half + 2], mm_l[4 * half + 3]]
                        + (dep["rk_r"] if i == 1 and half == 0 else []))
                    bank_free[bi] = tok
                    proj_ev[name].append(tok)
                if i == 3:
                    dep["rk_r"] = []
            else:
                name = "v" if i == 2 else "g"
                dst = v_tm if i == 2 else g_raw
                mm_l = {}
                for half in range(2):
                    bi = 6 + half
                    for qi in range(NQ):
                        tok = bank_mm(bi, lambda e, name=name, half=half, qi=qi, bi=bi: e.matmul(
                            pb[bi][:],
                            lhsT=xi_bf[:, qi * TT:(qi + 1) * TT],
                            rhs=W_sb[name][:, qi * C + half * 512:qi * C + half * 512 + 512],
                            start=(qi == 0), stop=(qi == NQ - 1)),
                            deps=[t_W[name], xi_tok])
                        dep["xi_r"].append(tok)
                        if qi == NQ - 1:
                            mm_l[half] = tok
                for half in range(2):
                    bi = 6 + half
                    tok = p.op("scalar", lambda e, half=half, bi=bi, dst=dst: e.copy(
                        dst[:, half * 512:(half + 1) * 512], pb[bi][:]),
                        deps=[mm_l[half]] + (dep["vg_r"] if i == 2 and half == 0 else []))
                    bank_free[bi] = tok
                    proj_ev[name].append(tok)
                if i == 4:
                    dep["vg_r"] = []

        if debug and t == 0:
            dep["xnfm_r"].append(_dt(p.dma("sync", lambda e: e.dma_start(out=dbg["d_xnfm"], in_=xnfm3[:, :, 1:TT + 1]), deps=xn_ev)))
            dep["sx_r"].append(_dt(p.dma("sync", lambda e: e.dma_start(out=dbg["d_sx"], in_=sx_bf[:]), deps=[t_sx])))
            dep["rk_r"].append(_dt(p.dma("gpsimd", lambda e: e.dma_start(out=dbg["d_r"], in_=r_bf[:]), deps=proj_ev["r"])))
            dep["rk_r"].append(_dt(p.dma("gpsimd", lambda e: e.dma_start(out=dbg["d_k"], in_=k_bf[:]), deps=proj_ev["k"])))
            dep["vg_r"].append(_dt(p.dma("gpsimd", lambda e: e.dma_start(out=dbg["d_v"], in_=v_tm[:]), deps=proj_ev["v"])))
            dep["vg_r"].append(_dt(p.dma("gpsimd", lambda e: e.dma_start(out=dbg["d_g"], in_=g_raw[:]), deps=proj_ev["g"])))
            dep["lw_r"].append(_dt(p.dma("gpsimd", lambda e: e.dma_start(out=dbg["d_lw"], in_=lw_tm[:]), deps=lw_toks)))

        S["proj_ev"] = proj_ev
        S["lw_toks"] = lw_toks

    def PB1(t):
        S = TS[t]
        proj_ev = S["proj_ev"]
        lw_toks = S["lw_toks"]
        t_xnbp = S["t_xnbp"]

        wave1 = {}
        for q in range(NQ):
            m_ex = bank_mm(q // 4, lambda e, q=q: e.matmul(
                pb[q // 4][:, (q % 4) * 128:(q % 4) * 128 + 128],
                lhsT=lw_tm[:, q * 128:(q + 1) * 128],
                rhs=utri[:, 0:128], start=True, stop=True),
                deps=[lw_toks[q // 4], t_utri])
            m_ia = bank_mm(2 + q // 4, lambda e, q=q: e.matmul(
                pb[2 + q // 4][:, (q % 4) * 128:(q % 4) * 128 + 128],
                lhsT=lw_tm[:, q * 128:(q + 1) * 128],
                rhs=utri[:, 4 * 128:5 * 128], start=True, stop=True))
            m_ib = bank_mm(4 + q // 4, lambda e, q=q: e.matmul(
                pb[4 + q // 4][:, (q % 4) * 128:(q % 4) * 128 + 128],
                lhsT=lw_tm[:, q * 128:(q + 1) * 128],
                rhs=utri[:, 5 * 128:6 * 128], start=True, stop=True))
            wave1[q] = (m_ex, m_ia, m_ib)
            dep["lw_r"] += [m_ex, m_ia, m_ib]
        eRu_t = [p.op("scalar", lambda e, hf=hf: e.activation(
            out=E1f[:, hf * 512:(hf + 1) * 512], in_=pb[hf][:], func=AF.Exp, scale=-1.0),
            deps=[wave1[4 * hf + 3][0]] + dep["E1_r"]) for hf in range(2)]
        eA_t = [p.op("scalar", lambda e, hf=hf: e.activation(
            out=E2[:, hf * 512:(hf + 1) * 512], in_=pb[2 + hf][:], func=AF.Exp, scale=1.0),
            deps=[wave1[4 * hf + 3][1]] + dep["E2_r"]) for hf in range(2)]
        negf = [p.op("scalar", lambda e, q=q: e.activation(
            out=cmf[:, q:q + 1], in_=pb[4 + q // 4][:, (q % 4) * 128 + 127:(q % 4) * 128 + 128],
            func=AF.Copy, scale=-1.0), deps=[wave1[(q // 4) * 4 + 3][2]] + dep["cmf_r"]) for q in range(NQ)]
        Khf3 = Khf[:].rearrange("p (q c) -> p q c", q=NQ)
        ekhf = [p.op("scalar", lambda e, q=q: e.activation(
            out=Khf3[:, q, :], in_=pb[4 + q // 4][:, (q % 4) * 128:(q % 4) * 128 + 128],
            func=AF.Exp, bias=cmf[:, q:q + 1]),
            deps=[negf[q]] + (dep["Khf_r"] if q == 0 else [])) for q in range(NQ)]
        dep["E1_r"] = []
        dep["E2_r"] = []
        dep["cmf_r"] = ekhf[:]
        dep["Khf_r"] = []
        t_ru = p.op("gpsimd", lambda e: e.tensor_tensor(
            out=Ru[:].rearrange("p (q c) -> p q c", q=NQ), in0=rb3, in1=E1_3, op=OP.mult),
            deps=eRu_t + dep["Ru_r"] + proj_ev["r"])
        dep["Ru_r"] = []
        t_khf = p.op("gpsimd", lambda e: e.tensor_tensor(
            out=Khf3, in0=kb3, in1=Khf3, op=OP.mult),
            deps=ekhf + proj_ev["k"])
        ktzA_t = []
        for pr in range(2):
            ktzA_t.append(p.op("gpsimd", lambda e, pr=pr: e.tensor_tensor(
                out=KtzA4[pr * 64:pr * 64 + 64, pr::2, 0:64],
                in0=kb3[pr * 64:pr * 64 + 64, :, 0:64],
                in1=E2_3[pr * 64:pr * 64 + 64, :, 0:64], op=OP.mult),
                deps=eA_t + dep["KtzA_r"] + proj_ev["k"]))
        dep["KtzA_r"] = []
        eB_t = [p.op("scalar", lambda e, hf=hf: e.activation(
            out=E1f[:, hf * 512:(hf + 1) * 512], in_=pb[4 + hf][:], func=AF.Exp, scale=1.0),
            deps=[t_ru, wave1[4 * hf + 3][2]]) for hf in range(2)]
        bank_free[0] = eRu_t[0]
        bank_free[1] = eRu_t[1]
        bank_free[2] = eA_t[0]
        bank_free[3] = eA_t[1]
        bank_free[4] = eB_t[0]
        bank_free[5] = eB_t[1]
        ktzB_t = []
        for pr in range(2):
            ktzB_t.append(p.op("gpsimd", lambda e, pr=pr: e.tensor_tensor(
                out=KtzB4[pr * 64:pr * 64 + 64, pr::2, :],
                in0=kb3[pr * 64:pr * 64 + 64, :, :],
                in1=E1_3[pr * 64:pr * 64 + 64, :, :], op=OP.mult),
                deps=eB_t + dep["KtzB_r"] + proj_ev["k"]))
        dep["KtzB_r"] = []
        wave2 = {}
        for q in range(NQ):
            m_xa = bank_mm(q // 4, lambda e, q=q: e.matmul(
                pb[q // 4][:, (q % 4) * 128:(q % 4) * 128 + 128],
                lhsT=lw_tm[:, q * 128:(q + 1) * 128],
                rhs=utri[:, 2 * 128:3 * 128], start=True, stop=True))
            m_xb = bank_mm(2 + q // 4, lambda e, q=q: e.matmul(
                pb[2 + q // 4][:, (q % 4) * 128:(q % 4) * 128 + 128],
                lhsT=lw_tm[:, q * 128:(q + 1) * 128],
                rhs=utri[:, 3 * 128:4 * 128], start=True, stop=True))
            m_in = bank_mm(4 + q // 4, lambda e, q=q: e.matmul(
                pb[4 + q // 4][:, (q % 4) * 128:(q % 4) * 128 + 128],
                lhsT=lw_tm[:, q * 128:(q + 1) * 128],
                rhs=utri[:, 1 * 128:2 * 128], start=True, stop=True))
            wave2[q] = (m_xa, m_xb, m_in)
            dep["lw_r"] += [m_xa, m_xb, m_in]
        eRA_t = [p.op("scalar", lambda e, hf=hf: e.activation(
            out=E2[:, hf * 512:(hf + 1) * 512], in_=pb[hf][:], func=AF.Exp, scale=-1.0),
            deps=[wave2[4 * hf + 3][0]] + ktzA_t) for hf in range(2)]
        eRB_t = [p.op("scalar", lambda e, hf=hf: e.activation(
            out=E1f[:, hf * 512:(hf + 1) * 512], in_=pb[2 + hf][:], func=AF.Exp, scale=-1.0),
            deps=[wave2[4 * hf + 3][1], t_khf] + ktzB_t) for hf in range(2)]
        epl_toks = []
        for q in range(NQ):
            epl_toks.append(p.op("scalar", lambda e, q=q: e.activation(
                out=epl[:, q:q + 1], in_=pb[4 + q // 4][:, (q % 4) * 128 + 127:(q % 4) * 128 + 128],
                func=AF.Exp, scale=-1.0),
                deps=[wave2[(q // 4) * 4 + 3][2]] + dep["epl_r"]))
        dep["epl_r"] = []
        bank_free[0] = eRA_t[0]
        bank_free[1] = eRA_t[1]
        bank_free[2] = eRB_t[0]
        bank_free[3] = eRB_t[1]
        bank_free[4] = epl_toks[3]
        bank_free[5] = epl_toks[7]
        t_ra = p.op("gpsimd", lambda e: e.tensor_tensor(
            out=Rab3[:, :, 0:64], in0=rb3[:, :, 0:64], in1=E2_3[:, :, 0:64], op=OP.mult),
            deps=eRA_t + dep["Rab_r"] + proj_ev["r"])
        t_rb = p.op("gpsimd", lambda e: e.tensor_tensor(
            out=Rab3[:, :, 64:128], in0=rb3[:, :, 64:128], in1=E1_3[:, :, 64:128], op=OP.mult),
            deps=eRB_t + dep["Rab_r"] + proj_ev["r"])
        dep["Rab_r"] = []
        t_rku1 = p.op("gpsimd", lambda e: e.tensor_tensor(
            out=rku3, in0=rb3, in1=kb3, op=OP.mult),
            deps=proj_ev["r"] + proj_ev["k"] + dep["sx_r"])


        mm_f = {}
        for half in range(2):
            bi = 6 + half
            for qi in range(NQ):
                tok = bank_mm(bi, lambda e, half=half, qi=qi, bi=bi: e.matmul(
                    pb[bi][:],
                    lhsT=xnb[:, qi * TT:(qi + 1) * TT],
                    rhs=W_sb["f"][:, qi * C + half * 512:qi * C + half * 512 + 512],
                    start=(qi == 0), stop=False),
                    deps=[t_xnbp] + (weff_toks if t == 0 else []))
                dep["xnb_r"].append(tok)
            mm_f[half] = p.op("tensor", lambda e, half=half, bi=bi: e.matmul(
                pb[bi][:], lhsT=ones1[:], rhs=fbrow[:, half * 512:(half + 1) * 512],
                start=False, stop=True), deps=[t_fb, t_ones])
            bank_free[bi] = None

        kht_mm = {}
        for q in range(NQ):
            bi2 = 2 + q // 4
            kht_mm[q] = bank_mm(bi2, lambda e, q=q, bi2=bi2: e.matmul(
                pb[bi2][:, (q % 4) * 128:(q % 4) * 128 + 128],
                lhsT=Khf[:, q * 128:(q + 1) * 128], rhs=ident[:], start=True, stop=True),
                deps=[t_khf])
        khz_toks = {}
        for q in range(NQ):
            bi2 = 2 + q // 4
            for pr in range(2):
                h = 2 * q + pr
                khz_toks[h] = p.op("scalar", lambda e, q=q, bi2=bi2, pr=pr, h=h: e.activation(
                    out=Khz[:, h * 128 + pr * 64:h * 128 + pr * 64 + 64],
                    in_=pb[bi2][:, (q % 4) * 128 + pr * 64:(q % 4) * 128 + pr * 64 + 64],
                    func=AF.Copy),
                    deps=[kht_mm[(q // 4) * 4 + 3]] + dep["Khz_r"])
        dep["Khz_r"] = []
        bank_free[2] = khz_toks[7]
        bank_free[3] = khz_toks[15]

        a_mm = {}
        for h in range(H):
            bi = h // 4
            q = h // 2
            mA = bank_mm(bi, lambda e, h=h, bi=bi, q=q: e.matmul(
                pb[bi][:, (h % 4) * 128:(h % 4) * 128 + 64],
                lhsT=KtzA[:, h * 128:(h + 1) * 128],
                rhs=Rab[:, q * 128:q * 128 + 64],
                start=True, stop=True),
                deps=[ktzA_t[h % 2], t_ra])
            mB = bank_mm(bi, lambda e, h=h, bi=bi, q=q: e.matmul(
                pb[bi][:, (h % 4) * 128 + 64:(h % 4) * 128 + 128],
                lhsT=KtzB[:, h * 128:(h + 1) * 128],
                rhs=Rab[:, q * 128 + 64:q * 128 + 128],
                start=True, stop=True),
                deps=[ktzB_t[h % 2], t_rb])
            a_mm[h] = (mA, mB)
        a_ev = []
        for bi in range(4):
            tok = p.op("vector", lambda e, bi=bi: e.copy_predicated(
                out=A4[:, bi * 512:(bi + 1) * 512], mask=mask4[:], data=pb[bi][:]),
                deps=[a_mm[4 * bi + 3][1], t_mask4] + dep["A4_r"])
            bank_free[bi] = tok
            a_ev.append(tok)
        dep["A4_r"] = []

        su_mm = []
        for q in range(NQ):
            su_mm.append(bank_mm(0, lambda e, q=q: e.matmul(
                pb[0][:, 2 * q:2 * q + 2], lhsT=rku[:, q * 128:(q + 1) * 128],
                rhs=ind2u[:, 2 * q:2 * q + 2], start=True, stop=True),
                deps=[t_rku1, t_iu]))
        for tk in su_mm:
            dep["sx_r"].append(tk)
        t_su = p.op("vector", lambda e: e.tensor_copy(su_sb[:], pb[0][:, 0:H]), deps=su_mm + dep["su_r"])
        dep["su_r"] = []
        bank_free[0] = t_su


        if debug and t == 0:
            tok_der = p.dma("gpsimd", lambda e: e.dma_start(out=dbg["d_Er"], in_=Ru[:]), deps=[t_ru])
            dep["Ru_r"].append(_dt(tok_der))
            dep["A4_r"].append(_dt(p.dma("gpsimd", lambda e: e.dma_start(out=dbg["d_A"], in_=A4[:]), deps=a_ev)))
            dep["su_r"].append(_dt(p.dma("sync", lambda e: e.dma_start(out=dbg["d_su"], in_=su_sb[:]), deps=[t_su])))
            dep["KtzA_r"].append(_dt(p.dma("gpsimd", lambda e: e.dma_start(out=dbg["d_Ktz"], in_=KtzA[:]), deps=ktzA_t)))
            dep["Khz_r"].append(_dt(p.dma("gpsimd", lambda e: e.dma_start(out=dbg["d_Khz"], in_=Khz[:]), deps=list(khz_toks.values()))))

        dep["KtzA_r"] = dep["KtzA_r"] + [a_mm[h][0] for h in range(H)]
        dep["KtzB_r"] = [a_mm[h][1] for h in range(H)]
        dep["Rab_r"] = dep["Rab_r"] + [a_mm[h][0] for h in range(H)] + [a_mm[h][1] for h in range(H)]
        dep["Khf_r"] = list(kht_mm.values())
        dep["E1_r"] = dep["E1_r"] + [t_rb]
        dep["E2_r"] = [t_ra]
        dep["cmf_r"] = [t_khf]
        dep["rk_r"] = [t_ru, t_ra, t_rb, t_khf, t_rku1] + ktzA_t + ktzB_t
        S["t_su"] = t_su
        S["t_ru"] = t_ru
        S["t_khf"] = t_khf
        S["epl_toks"] = epl_toks
        S["khz_toks"] = khz_toks
        S["a_ev"] = a_ev
        S["a_mm"] = a_mm
        S["mm_f"] = mm_f

    def PB2a(t):
        S = TS[t]
        proj_ev = S["proj_ev"]
        t_su = S["t_su"]
        t_ru = S["t_ru"]
        t_khf = S["t_khf"]
        epl_toks = S["epl_toks"]
        khz_toks = S["khz_toks"]
        a_ev = S["a_ev"]
        a_mm = S["a_mm"]
        mm_f = S["mm_f"]
        par = t % 2
        Scur, Snxt = Sz[par], Sz[1 - par]

        t_vsu = p.op("gpsimd", lambda e: e.tensor_tensor(
            out=scr2[:].rearrange("p (h c) -> p h c", h=H),
            in0=v_tm[:].rearrange("p (h c) -> p h c", h=H),
            in1=su_sb[:].rearrange("p h -> p h ()").broadcast_to((128, H, K)),
            op=OP.mult),
            deps=[t_su] + proj_ev["v"] + dep["scr2_r"])
        dep["scr2_r"] = []
        t_sdec = p.op("gpsimd", lambda e, Scur=Scur, Snxt=Snxt: e.tensor_tensor(
            out=Snxt[:].rearrange("p (q c) -> p q c", q=NQ),
            in0=Scur[:].rearrange("p (q c) -> p q c", q=NQ),
            in1=epl[:].rearrange("p q -> p q ()").broadcast_to((128, NQ, 128)),
            op=OP.mult),
            deps=epl_toks + dep["Sz_r"][1 - par] + [dep["Sz_w"][1 - par], dep["Sz_w"][par]])
        dep["Sz_r"][1 - par] = []

        y_mm = {}
        s_mm = {}
        intra_mms = []
        for h in range(H):
            by = 4 + h // 8
            tok1 = bank_mm(by, lambda e, h=h, by=by: e.matmul(
                pb[by][:, (h % 8) * 64:(h % 8) * 64 + 64],
                lhsT=A4[:, h * 128:(h + 1) * 128],
                rhs=v_tm[:, h * K:h * K + K],
                start=True, stop=False), deps=[a_ev[h // 4]] + proj_ev["v"])
            intra_mms.append(tok1)
            y_mm[h] = p.op("tensor", lambda e, h=h, by=by: e.matmul(
                pb[by][:, (h % 8) * 64:(h % 8) * 64 + 64],
                lhsT=Ru[:, (h // 2) * 128:(h // 2) * 128 + 128],
                rhs=S_bfz[:, h * K:(h + 1) * K],
                start=False, stop=True), deps=[dep["Sbf_ready"], t_ru])
            bs = 2 + h // 8
            s_mm[h] = bank_mm(bs, lambda e, h=h, bs=bs: e.matmul(
                pb[bs][:, (h % 8) * 64:(h % 8) * 64 + 64],
                lhsT=Khz[:, h * 128:(h + 1) * 128],
                rhs=v_tm[:, h * K:h * K + K],
                start=True, stop=True), deps=[khz_toks[h]] + proj_ev["v"])
        yev = []
        sev = []
        for half in range(2):
            tok = p.op("vector", lambda e, half=half: e.tensor_tensor(
                out=y_tm[:, half * 512:half * 512 + 512],
                in0=scr2[:, half * 512:half * 512 + 512],
                in1=pb[4 + half][:], op=OP.add),
                deps=[y_mm[half * 8 + 7], t_vsu] + dep["y_r"])
            yev.append(tok)
            dep["scr2_r"].append(tok)
            tok2 = p.op("vector", lambda e, half=half, Snxt=Snxt: e.tensor_tensor(
                out=Snxt[:, half * 512:half * 512 + 512],
                in0=Snxt[:, half * 512:half * 512 + 512],
                in1=pb[2 + half][:], op=OP.add),
                deps=[s_mm[half * 8 + 7], t_sdec])
            sev.append(tok2)
        dep["y_r"] = []
        dep["Sz_r"][par] = [t_sdec]
        dep["Sz_r"][1 - par] = []
        dep["Sz_w"][1 - par] = sev[-1]
        bank_free[4] = yev[0]
        bank_free[5] = yev[1]
        bank_free[2] = sev[0]
        bank_free[3] = sev[1]
        t_sbf = p.op("vector", lambda e, Snxt=Snxt: e.tensor_copy(S_bfz[:], Snxt[:]),
                     deps=sev + list(y_mm.values()))
        dep["Sbf_ready"] = t_sbf
        dep["Sz_r"][1 - par].append(t_sbf)

        t_f2 = p.op("scalar", lambda e: e.activation(
            out=scr[:, 0:512], in_=pb[6][:], func=AF.Silu), deps=[mm_f[0]] + dep["scr_r"])
        t_f2b = p.op("scalar", lambda e: e.activation(
            out=scr[:, 512:1024], in_=pb[7][:], func=AF.Silu), deps=[mm_f[1]])
        bank_free[6] = t_f2
        bank_free[7] = t_f2b
        dep["scr_r"] = []

        dep["Ru_r"] = list(y_mm.values())
        dep["Khz_r"] = list(s_mm.values())
        dep["epl_r"] = [t_sdec, t_khf]
        dep["su_r"] = [t_vsu]
        dep["A4_r"] = dep["A4_r"] + intra_mms
        S["yev"] = yev
        S["sev"] = sev
        S["t_vsu"] = t_vsu
        S["t_f2"] = t_f2
        S["t_f2b"] = t_f2b

    def PB2b(t):
        S = TS[t]
        proj_ev = S["proj_ev"]
        yev = S["yev"]
        sev = S["sev"]
        t_vsu = S["t_vsu"]
        par = t % 2
        Snxt = Sz[1 - par]

        yg3 = y_tm[:].rearrange("p (h c) -> p h c", h=H)
        t_gm = p.op("vector", lambda e: e.tensor_reduce(
            out=gn_sc[:, 0:H], in_=yg3, axis=AX.X, op=OP.add), deps=yev + dep["gn_r"])
        dep["gn_r"] = []
        t_gsq = p.op("scalar", lambda e: e.activation(
            out=scr2[:], in_=y_tm[:], func=AF.Square), deps=yev + dep["scr2_r"])
        t_gr2 = p.op("vector", lambda e: e.tensor_reduce(
            out=gn_sc[:, H:2 * H], in_=scr2[:].rearrange("p (h c) -> p h c", h=H),
            axis=AX.X, op=OP.add), deps=[t_gsq])
        t_gmm = p.op("vector", lambda e: e.tensor_scalar_mul(gn_sc[:, 0:H], gn_sc[:, 0:H], 1.0 / K), deps=[t_gm])
        t_gm2 = p.op("vector", lambda e: e.tensor_tensor(
            out=gn_sc[:, 2 * H:3 * H], in0=gn_sc[:, 0:H], in1=gn_sc[:, 0:H], op=OP.mult), deps=[t_gmm])
        t_gvar = p.op("vector", lambda e: e.scalar_tensor_tensor(
            out=gn_sc[:, H:2 * H], in0=gn_sc[:, H:2 * H], scalar=1.0 / K,
            in1=gn_sc[:, 2 * H:3 * H], op0=OP.mult, op1=OP.subtract), deps=[t_gr2, t_gm2])
        mk_n = TS.get(t + 1, {}).get("m_toks")
        if mk_n is not None:
            p.op("tensor", lambda e: e.matmul(
                pb[4][0:1, 0:1], lhsT=ident[:, 0:1], rhs=ident[:, 0:1], start=True, stop=True),
                deps=[t_gm, mk_n[4]])
        t_gl1 = p.op("scalar", lambda e: e.activation(
            out=gn_sc2[:, 0:H], in_=gn_sc[:, H:2 * H], func=AF.Ln, bias=eps_sb[:]),
            deps=[t_gvar, t_eps])
        t_grs = p.op("scalar", lambda e: e.activation(
            out=gn_sc[:, 3 * H:4 * H], in_=gn_sc2[:, 0:H], func=AF.Exp, scale=-0.5), deps=[t_gl1])
        t_yc = p.op("gpsimd", lambda e: e.tensor_tensor(
            out=yg3, in0=yg3,
            in1=gn_sc[:, 0:H].rearrange("p h -> p h ()").broadcast_to((128, H, K)),
            op=OP.subtract), deps=[t_gmm, t_gsq])
        t_yn = p.op("gpsimd", lambda e: e.tensor_tensor(
            out=yg3, in0=yg3,
            in1=gn_sc[:, 3 * H:4 * H].rearrange("p h -> p h ()").broadcast_to((128, H, K)),
            op=OP.mult), deps=[t_grs, t_yc])
        if mk_n is not None:
            p.op("tensor", lambda e: e.matmul(
                pb[4][0:1, 0:1], lhsT=ident[:, 0:1], rhs=ident[:, 0:1], start=True, stop=True),
                deps=[t_grs, mk_n[4]])
        dep["scr2_r"] = [t_gr2]
        dep["y_r"] = [t_yn]
        dep["gn_r"] = [t_yc, t_yn]

        if debug and t == 1:
            dep["y_r"].append(_dt(p.dma("sync", lambda e: e.dma_start(out=dbg["d_y"], in_=y_tm[:]), deps=[t_yn])))
            dep["Sz_r"][1 - par].append(_dt(p.dma("sync", lambda e: e.dma_start(out=dbg["d_S"], in_=Snxt[:]), deps=[sev[-1]])))

        t_gs = p.op("scalar", lambda e: e.activation(
            out=gs_sb[:], in_=g_raw[:], func=AF.Silu),
            deps=proj_ev["g"] + dep["gs_r"])
        dep["gs_r"] = []
        t_gl = p.op("gpsimd", lambda e: e.tensor_tensor(
            out=gl_b, in0=gs_sb[:], in1=lnw_bct[:], op=OP.mult),
            deps=[t_gs, ev[0], ev[1]] + dep["Khf_r"])
        dep["Khf_r"] = []
        t_glb = p.op("gpsimd", lambda e: e.tensor_tensor(
            out=glb_b, in0=gs_sb[:], in1=lnb_bct[:], op=OP.mult),
            deps=[t_gs, ev[2], ev[3]] + dep["yngfm_r"])
        dep["vg_r"] = [t_gl, t_glb, t_vsu]
        t_y1 = p.op("gpsimd", lambda e: e.tensor_tensor(
            out=scr2[:], in0=y_tm[:], in1=gl_b, op=OP.mult),
            deps=[t_gl, t_yn] + dep["scr2_r"])
        t_y2 = p.op("gpsimd", lambda e: e.tensor_tensor(
            out=y_tm[:].bitcast(BF16)[:, 0:C],
            in0=scr2[:], in1=glb_b, op=OP.add),
            deps=[t_glb, t_y1] + dep["y_r"])
        dep["Khf_r"].append(t_y1)
        dep["scr2_r"] = [t_y2]
        dep["y_r"] = [t_y2]

        S["t_y2"] = t_y2

    def PB3(t):
        S = TS[t]
        proj_ev = S["proj_ev"]
        t_y2 = S["t_y2"]
        t_f2 = S["t_f2"]
        t_f2b = S["t_f2b"]
        xt = S["xt"]
        xb = S["xb"]

        mms2 = []
        for q in range(NQ):
            bi = q // 4
            mms2.append(bank_mm(bi, lambda e, q=q, bi=bi: e.matmul(
                pb[bi][:, (q % 4) * 128:(q % 4) * 128 + 128],
                lhsT=y_tm[:].bitcast(BF16)[:, q * 128:(q + 1) * 128],
                rhs=ident[:], start=True, stop=True),
                deps=[t_y2, t_ident]))
        yngfm_ev = []
        for half in range(2):
            tok = p.op("scalar", lambda e, half=half: e.copy(
                yngfm3[:, 4 * half:4 * half + 4, :],
                pb[half][:].rearrange("p (q c) -> p q c", q=4)),
                deps=[mms2[4 * half + 3]] + (dep["yngfm_r"] if half == 0 else []))
            bank_free[half] = tok
            yngfm_ev.append(tok)
        dep["yngfm_r"] = []
        dep["y_r"] = dep["y_r"] + mms2

        if debug and t == 0:
            dep["yngfm_r"].append(_dt(p.dma("gpsimd", lambda e: e.dma_start(out=dbg["d_yng"], in_=yng_fm[:]), deps=yngfm_ev)))

        mm_l = {}
        for half in range(2):
            bi = 4 + half
            for qi in range(NQ):
                tok = bank_mm(bi, lambda e, half=half, qi=qi, bi=bi: e.matmul(
                    pb[bi][:],
                    lhsT=yng_fm[:, qi * TT:(qi + 1) * TT],
                    rhs=W_sb["o"][:, qi * C + half * 512:qi * C + half * 512 + 512],
                    start=(qi == 0), stop=(qi == NQ - 1)),
                    deps=[t_W["o"], yngfm_ev[0], yngfm_ev[1]])
                dep["yngfm_r"].append(tok)
                if qi == NQ - 1:
                    mm_l[half] = tok
        stm_ev = []
        for half in range(2):
            bi = 4 + half
            tok = p.op("scalar", lambda e, half=half, bi=bi: e.copy(
                sx_fm[:, half * 512:(half + 1) * 512], pb[bi][:]),
                deps=[mm_l[half]] + (dep["stm_r"] if half == 0 else []))
            bank_free[bi] = tok
            stm_ev.append(tok)
        dep["stm_r"] = stm_ev[:]

        if debug and t == 0:
            dep["stm_r"].append(_dt(p.dma("sync", lambda e: e.dma_start(out=dbg["d_stm"], in_=sx_fm[:, 0:C]), deps=stm_ev)))

        S["stm_ev"] = stm_ev

    def PB3b(t):
        S = TS[t]
        stm_ev = S["stm_ev"]
        t_f2 = S["t_f2"]
        t_f2b = S["t_f2b"]
        xt = S["xt"]
        xb = S["xb"]
        t_f3 = p.op("vector", lambda e: e.tensor_tensor(
            out=scr2[:], in0=scr[:], in1=sx_fm[:, 0:C], op=OP.mult),
            deps=[t_f2, t_f2b, stm_ev[0], stm_ev[1]] + dep["scr2_r"])
        dep["stm_r"].append(t_f3)
        t_out = p.op("vector", lambda e, xt=xt: e.tensor_tensor(
            out=xt[:], in0=scr2[:], in1=xt[:], op=OP.add), deps=[t_f3])
        dep["scr_r"] = [t_f3]
        dep["scr2_r"] = [t_out]
        t_st = p.dma("scalar", lambda e, t=t, xt=xt: e.dma_start(
            out=out_h[t * TT:(t + 1) * TT, :], in_=xt[:]), deps=[t_out])
        dep["out_dma"][xb] = t_st
        out_tokens.append(t_st)

    PF1(0)
    PF2(0)
    PF3(0)
    for t in range(n_tiles):
        if t > 0:
            PB3b(t - 1)
        PB1(t)
        if t + 1 < n_tiles:
            PF1(t + 1)
        PB2a(t)
        if t + 1 < n_tiles:
            PF2(t + 1)
        PB2b(t)
        if t + 1 < n_tiles:
            PF3(t + 1)
        PB3(t)
    PB3b(n_tiles - 1)

    p.op("sync", lambda e: e.nop(), deps=out_tokens)
    p.emit(nc, block, sems)
    blk.__exit__(None, None, None)
    return nc, sbuf_used


def host_inputs(inp, b):
    f32 = np.float32
    w2 = np.asarray(inp["L0_attn_tm_w2"], f32)
    maa = np.asarray(inp["L0_attn_time_maa_wkvrg"], f32)
    w2aug = np.concatenate([np.concatenate([w2[i], maa[i:i + 1]], axis=0) for i in range(5)], axis=0)
    td2aug = np.concatenate([np.asarray(inp["L0_attn_td_w2"], f32),
                             np.asarray(inp["L0_attn_time_decay"], f32)[None, :]], axis=0)
    ident = np.eye(128, dtype=f32)
    jj = np.arange(128)
    uexc = (jj[:, None] < jj[None, :]).astype(f32)
    uinc = (jj[:, None] <= jj[None, :]).astype(f32)
    pre32 = (jj[:, None] < 32).astype(f32) * np.ones((1, 128), f32)
    pre96 = (jj[:, None] < 96).astype(f32) * np.ones((1, 128), f32)
    utri = np.concatenate([uexc, uinc, uexc - pre32, uexc - pre96,
                           uinc - pre32, uinc - pre96], axis=1)
    mask4 = np.tile(uexc, (1, 4)).astype(np.uint8)
    ind2 = np.zeros((128, 2), f32)
    ind2[0:64, 0] = 1.0
    ind2[64:128, 1] = 1.0
    return {
        "x": np.ascontiguousarray(np.asarray(inp["L0_attn_x"], f32)[b]),
        "Wr": np.ascontiguousarray(np.asarray(inp["L0_attn_Wr"], f32)),
        "Wk": np.ascontiguousarray(np.asarray(inp["L0_attn_Wk"], f32)),
        "Wv": np.ascontiguousarray(np.asarray(inp["L0_attn_Wv"], f32)),
        "Wg": np.ascontiguousarray(np.asarray(inp["L0_attn_Wg"], f32)),
        "Wo": np.ascontiguousarray(np.asarray(inp["L0_attn_Wo"], f32)),
        "WfT": np.ascontiguousarray(np.asarray(inp["L0_ffn_W"], f32).T),
        "tm_w1": np.ascontiguousarray(np.asarray(inp["L0_attn_tm_w1"], f32)),
        "w2aug": np.ascontiguousarray(w2aug),
        "td_w1": np.ascontiguousarray(np.asarray(inp["L0_attn_td_w1"], f32)),
        "td2aug": np.ascontiguousarray(td2aug),
        "tmaa": np.ascontiguousarray(np.asarray(inp["L0_attn_time_maa_x"], f32)[None, :]),
        "ufaaaa": np.ascontiguousarray(np.asarray(inp["L0_attn_time_faaaa"], f32).reshape(1, -1)),
        "lnw": np.ascontiguousarray(np.asarray(inp["L0_attn_ln_w"], f32)[None, :]),
        "lnb": np.ascontiguousarray(np.asarray(inp["L0_attn_ln_b"], f32)[None, :]),
        "fb": np.ascontiguousarray(np.asarray(inp["L0_ffn_b"], f32)[None, :]),
        "lora_up": np.ascontiguousarray(np.asarray(inp["L0_ffn_lora_up"], f32)),
        "lora_down": np.ascontiguousarray(np.asarray(inp["L0_ffn_lora_down"], f32)),
        "down_aux": np.ascontiguousarray(np.asarray(inp["L0_ffn_down_aux"], f32)),
        "up_auxT": np.ascontiguousarray(np.asarray(inp["L0_ffn_up_aux"], f32).T),
        "ident": ident, "utri": utri, "mask4": mask4, "ind2": ind2,
    }


def kernel(**inputs):
    nc, _ = build_nc(n_tiles=T_FULL // TT, debug=False)
    in_maps = [host_inputs(inputs, b) for b in range(8)]
    res = run_bass_kernel_spmd(nc, in_maps, core_ids=list(range(8)))
    out = np.stack([np.asarray(r["out"]) for r in res.results], axis=0)
    return out.astype(np.float32)


# version 37
# speedup vs baseline: 1.0365x; 1.0365x over previous
import numpy as np
import ml_dtypes
import concourse.bass as bass
import concourse.mybir as mybir
from concourse.bass_utils import run_bass_kernel_spmd
from concourse.alu_op_type import AluOpType as OP

F32 = mybir.dt.float32
BF16 = mybir.dt.bfloat16
AF = mybir.ActivationFunctionType
AX = mybir.AxisListType

T_FULL = 4096
C = 1024
H = 16
K = 64
TT = 128
NQ = 8
EPS = 1e-5

ENGINES = ("sync", "gpsimd", "tensor", "vector", "scalar")


class Prog:
    def __init__(self):
        self.ops = {e: [] for e in ENGINES}
        self.n = 0
        self.eng_of = {}

    def op(self, eng, fn, deps=(), dma=False):
        opid = self.n
        self.n += 1
        deps = [d for d in deps if d is not None]
        self.ops[eng].append((fn, deps, dma, opid))
        self.eng_of[opid] = (eng, dma)
        return opid

    def dma(self, eng, fn, deps=()):
        return self.op(eng, fn, deps, dma=True)

    def emit(self, nc, block, sems):
        val = {}
        for eng in ENGINES:
            cc = dc = 0
            for fn, deps, is_dma, opid in self.ops[eng]:
                if is_dma:
                    dc += 16
                    val[opid] = (sems[(eng, "d")], dc)
                else:
                    cc += 1
                    val[opid] = (sems[(eng, "c")], cc)

        def section(eng):
            def body(e):
                seen = {}
                dseen = 0
                for fn, deps, is_dma, opid in self.ops[eng]:
                    if is_dma:
                        sd, vd = val[opid]
                        if vd - 16 > 0 and dseen < vd - 16:
                            e.wait_ge(sd, vd - 16)
                            dseen = vd - 16
                            seen[sd.name] = max(seen.get(sd.name, -1), vd - 16)
                    for d in deps:
                        de, ddma = self.eng_of[d]
                        if de == eng == "tensor" and not ddma and not is_dma:
                            continue
                        if de == eng and ddma and is_dma:
                            continue
                        s, v = val[d]
                        if seen.get(s.name, -1) < v:
                            e.wait_ge(s, v)
                            seen[s.name] = v
                    inst = fn(e)
                    inst.then_inc(sems[(eng, "d" if is_dma else "c")], 16 if is_dma else 1)
            return body

        for eng in ENGINES:
            getattr(block, eng)(section(eng))


def build_nc(n_tiles=T_FULL // TT, debug=False):
    T = n_tiles * TT
    nc = bass.Bass("TRN2", target_bir_lowering=False)

    def din(name, shape, dt=F32):
        return nc.dram_tensor(name, shape, dt, kind="ExternalInput").ap()

    x_h = din("x", [T, C])
    w_h = {n: din("W" + n, [C, C]) for n in ["r", "k", "v", "g", "o", "fT"]}
    tmw1_h = din("tm_w1", [C, 160])
    w2aug_h = din("w2aug", [5 * 33, C])
    tdw1_h = din("td_w1", [C, 64])
    td2aug_h = din("td2aug", [65, C])
    tmaa_h = din("tmaa", [1, C])
    u_h = din("ufaaaa", [1, C])
    lnw_h = din("lnw", [1, C])
    lnb_h = din("lnb", [1, C])
    fb_h = din("fb", [1, C])
    lu_h = din("lora_up", [50, 4])
    ld_h = din("lora_down", [4, 100])
    da_h = din("down_aux", [100, C])
    uaT_h = din("up_auxT", [50, C])
    ident_h = din("ident", [128, 128])
    utri_h = din("utri", [128, 6 * 128])
    mask4_h = din("mask4", [128, 512], mybir.dt.uint8)
    ind2_h = din("ind2", [128, 2])

    out_h = nc.dram_tensor("out", [T, C], F32, kind="ExternalOutput").ap()
    dbg = {}
    if debug:
        for name, shape, dt in [
                ("d_xnfm", [128, NQ * TT], F32), ("d_sx", [128, NQ * TT], BF16),
                ("d_r", [128, NQ * TT], BF16), ("d_k", [128, NQ * TT], BF16),
                ("d_lw", [128, C], BF16), ("d_Er", [128, C], BF16),
                ("d_A", [128, H * 128], BF16), ("d_y", [128, C], F32),
                ("d_su", [128, H], F32), ("d_S", [128, H * K], F32),
                ("d_v", [128, C], BF16), ("d_yng", [128, NQ * TT], BF16),
                ("d_stm", [128, C], F32), ("d_weff", [128, NQ * C], BF16),
                ("d_g", [128, C], BF16), ("d_xi", [128, NQ * TT], BF16),
                ("d_Ktz", [128, H * 128], BF16), ("d_Khz", [128, H * 128], BF16)]:
            dbg[name] = nc.dram_tensor(name, shape, dt, kind="ExternalOutput").ap()

    p = Prog()
    dbg_toks = []

    def _dt(tok):
        dbg_toks.append(tok)
        return tok

    def sb(name, shape, dt):
        return nc.alloc_sbuf_tensor(name, shape, dt).ap()

    W_sb = {n: sb(f"W_{n}", [128, NQ * C], BF16) for n in ["r", "k", "v", "g", "o", "f"]}
    tmw1_sb = sb("tmw1_s", [128, NQ * 160], BF16)
    w2p = [sb(f"w2p{j}", [128, C], BF16) for j in range(3)]
    tdw1_sb = sb("tdw1_s", [128, NQ * 64], BF16)
    td2aug_sb = sb("td2aug_s", [65, C], BF16)
    tmaa_sb = sb("tmaa_s", [128, NQ], F32)
    u_sb = sb("ufa", [128, NQ], F32)
    lnwrow = sb("lnwrow_s", [1, C], BF16)
    lnbrow = sb("lnbrow_s", [1, C], BF16)
    fbrow = sb("fbrow", [1, C], BF16)
    lnw_bct = sb("lnw_bct", [128, C], BF16)
    lnb_bct = sb("lnb_bct", [128, C], BF16)
    ident = sb("ident_s", [128, 128], BF16)
    utri = sb("utri_s", [128, 6 * 128], BF16)
    mask4 = sb("mask4_s", [128, 512], mybir.dt.uint8)
    ind2 = sb("ind2_s", [128, 2], BF16)
    ones1 = sb("ones1", [1, 128], BF16)
    eps_sb = sb("eps_s", [128, 1], F32)

    x_tm = [sb("x_tm0", [128, C], F32), sb("x_tm1", [128, C], F32)]
    xn_fm = sb("xn_fm", [128, NQ * (TT + 1)], F32)
    sx_fm = sb("sx_fm", [128, NQ * TT], F32)
    xi_bf = sb("xi_bf", [128, NQ * TT], BF16)
    m_sb = [sb(f"mp{j}", [128, TT], BF16) for j in range(3)]
    tanhtd = sb("tanhtd", [65, TT], BF16)
    lw_tm = sb("lw_tm", [128, C], BF16)
    r_bf = sb("r_bf", [128, NQ * TT], BF16)
    k_bf = sb("k_bf", [128, NQ * TT], BF16)
    v_tm = sb("v_tm", [128, C], BF16)
    g_raw = sb("g_raw", [128, C], BF16)
    E1 = sb("E1_s", [128, 2 * C], BF16)
    E1f = E1[:].bitcast(F32)
    xi2 = E1[:, 0:NQ * TT]
    E2 = sb("E2_s", [128, C], F32)
    epl = sb("epl", [128, NQ], F32)
    cmf = sb("cmf", [128, NQ], F32)
    zc = sb("zc", [128, 1], F32)
    KtzA = sb("KtzA", [128, H * 128], BF16)
    KtzB = sb("KtzB", [128, H * 128], BF16)
    Ru = sb("Ru", [128, C], BF16)
    Rab = sb("Rab", [128, C], BF16)
    Khf = sb("Khf", [128, C], BF16)
    Khz = sb("Khz", [128, H * 128], BF16)
    A4 = sb("A4", [128, H * 128], BF16)
    su_sb = sb("su_s", [128, H], F32)
    Sz = [sb("SzA", [128, H * K], F32), sb("SzB", [128, H * K], F32)]
    S_bfz = sb("S_bfz", [128, H * K], BF16)
    y_tm = sb("y_tm", [128, C], F32)
    gn_sc = sb("gn_sc", [128, 4 * H], F32)
    gn_sc2 = sb("gn_sc2", [128, 3 * H], F32)
    ln_sc = sb("ln_sc", [128, 16], F32)
    yng_fm = sb("yng_fm", [128, NQ * TT], BF16)
    scr = sb("scr", [128, C], F32)
    scr2 = sb("scr2", [128, C], F32)
    sx_bf = sb("sx_bf", [128, NQ * TT], BF16)
    hi_tm = E1[:, C:2 * C]
    lo_tm = Rab[:]
    gl_b = Khf[:]
    glb_b = yng_fm[:]
    lnE = E2[:]
    rku = sx_bf[:]

    xnb = sb("xnb", [128, NQ * TT], BF16)
    gs_sb = g_raw
    dmix = E2[:].bitcast(BF16)[:, 0:NQ * TT]

    da_v = Sz[0][0:100, :]
    uaT_v = Sz[1][0:50, :]
    c1_v = A4[:].bitcast(F32)[0:4, :]
    c2_v = KtzA[:].bitcast(F32)[0:100, :]
    lu_v = Khz[:].bitcast(F32)[0:50, 0:4]
    ld_v = Khz[:].bitcast(F32)[0:4, 4:104]

    sbuf_used = nc.sbuf_base

    pb = [nc.alloc_psum_tensor(f"pb{i}", [128, 512], F32).ap() for i in range(8)]

    sems = {}
    sem_cms = []
    for e in ENGINES:
        for kind in ("c", "d"):
            cm = nc.semaphore(f"s_{e}_{kind}")
            sems[(e, kind)] = cm.__enter__()
            sem_cms.append(cm)

    blk = nc.Block()
    block = blk.__enter__()

    xnfm3 = xn_fm[:].rearrange("p (q c) -> p q c", q=NQ)
    sx3 = sx_fm[:].rearrange("p (q c) -> p q c", q=NQ)
    sxb3 = sx_bf[:].rearrange("p (q c) -> p q c", q=NQ)
    xnb3 = xnb[:].rearrange("p (q c) -> p q c", q=NQ)
    xi3 = xi_bf[:].rearrange("p (q c) -> p q c", q=NQ)
    xi23 = xi2.rearrange("p (q c) -> p q c", q=NQ)
    rb3 = r_bf[:].rearrange("p (q c) -> p q c", q=NQ)
    kb3 = k_bf[:].rearrange("p (q c) -> p q c", q=NQ)
    E1_3 = E1f.rearrange("p (q c) -> p q c", q=NQ)
    E2_3 = E2[:].rearrange("p (q c) -> p q c", q=NQ)
    KtzA4 = KtzA[:].rearrange("p (h c) -> p h c", h=H)
    KtzB4 = KtzB[:].rearrange("p (h c) -> p h c", h=H)
    Rab3 = Rab[:].rearrange("p (q c) -> p q c", q=NQ)
    rku3 = rku.rearrange("p (q c) -> p q c", q=NQ)
    yngfm3 = yng_fm[:].rearrange("p (q c) -> p q c", q=NQ)
    scr3 = scr[:].rearrange("p (q c) -> p q c", q=NQ)

    def wload(dst, src):
        return p.dma("gpsimd", lambda e, d=dst, s=src: e.dma_start(out=d, in_=s))

    t_ident = wload(ident[:], ident_h)
    t_utri = wload(utri[:], utri_h)
    t_mask4 = wload(mask4[:], mask4_h)
    t_ind2 = wload(ind2[:], ind2_h)
    t_zc = p.op("gpsimd", lambda e: e.memset(zc[:], 0.0))
    t_tmaa = p.dma("sync", lambda e: e.dma_start(out=tmaa_sb[:], in_=tmaa_h.rearrange("o (a p) -> p (o a)", p=128), allow_slow_non_contiguous=True))
    t_u = p.dma("sync", lambda e: e.dma_start(out=u_sb[:], in_=u_h.rearrange("o (a p) -> p (o a)", p=128), allow_slow_non_contiguous=True))
    t_tmw1 = wload(tmw1_sb[:].rearrange("p (a c) -> p a c", a=NQ), tmw1_h.rearrange("(a p) c -> p a c", p=128))
    t_tdw1 = wload(tdw1_sb[:].rearrange("p (a c) -> p a c", a=NQ), tdw1_h.rearrange("(a p) c -> p a c", p=128))
    t_td2 = wload(td2aug_sb[:], td2aug_h)
    t_lnw = wload(lnwrow[:], lnw_h)
    t_lnb = wload(lnbrow[:], lnb_h)
    t_fb = wload(fbrow[:], fb_h)
    t_w2z = [p.op("gpsimd", lambda e, j=j: e.memset(w2p[j][:], 0.0)) for j in range(3)]
    t_w2 = []
    for i in range(5):
        j, base = i // 2, (i % 2) * 64
        t_w2.append(p.dma("gpsimd", lambda e, i=i, j=j, base=base: e.dma_start(
            out=w2p[j][base:base + 33, :], in_=w2aug_h[33 * i:33 * (i + 1), :]),
            deps=[t_w2z[j]]))
    t_ones = p.op("gpsimd", lambda e: e.memset(ones1[:], 1.0))
    t_eps = p.op("gpsimd", lambda e: e.memset(eps_sb[:], EPS))
    t_mz = [p.op("gpsimd", lambda e, j=j: e.memset(m_sb[j][:], 0.0)) for j in range(3)]
    t_mones = []
    for i in range(5):
        j, base = i // 2, (i % 2) * 64
        t_mones.append(p.op("gpsimd", lambda e, j=j, base=base: e.memset(m_sb[j][base + 32:base + 33, :], 1.0),
                            deps=[t_mz[j]]))
    t_tdones = p.op("gpsimd", lambda e: e.memset(tanhtd[64:65, :], 1.0))
    t_W = {}
    for n in ["r", "k", "v", "g", "o", "fT"]:
        key = "f" if n == "fT" else n
        t_W[key] = wload(W_sb[key][:].rearrange("p (a c) -> p a c", a=NQ),
                         w_h[n].rearrange("(a p) c -> p a c", p=128))
    t_lu = p.dma("sync", lambda e: e.dma_start(out=lu_v, in_=lu_h))
    t_ld = p.dma("sync", lambda e: e.dma_start(out=ld_v, in_=ld_h))
    t_da = p.dma("sync", lambda e: e.dma_start(out=da_v, in_=da_h))
    t_uaT = p.dma("sync", lambda e: e.dma_start(out=uaT_v, in_=uaT_h))

    mm = [None] * 4
    mm[0] = p.op("tensor", lambda e: e.matmul(pb[0][:], lhsT=ones1[:], rhs=lnwrow[:, 0:512], start=True, stop=True), deps=[t_ones, t_lnw])
    mm[1] = p.op("tensor", lambda e: e.matmul(pb[1][:], lhsT=ones1[:], rhs=lnwrow[:, 512:1024], start=True, stop=True))
    mm[2] = p.op("tensor", lambda e: e.matmul(pb[2][:], lhsT=ones1[:], rhs=lnbrow[:, 0:512], start=True, stop=True), deps=[t_lnb])
    mm[3] = p.op("tensor", lambda e: e.matmul(pb[3][:], lhsT=ones1[:], rhs=lnbrow[:, 512:1024], start=True, stop=True))
    ev = [None] * 4
    ev[0] = p.op("vector", lambda e: e.tensor_copy(lnw_bct[:, 0:512], pb[0][:]), deps=[mm[0]])
    ev[1] = p.op("vector", lambda e: e.tensor_copy(lnw_bct[:, 512:1024], pb[1][:]), deps=[mm[1]])
    ev[2] = p.op("vector", lambda e: e.tensor_copy(lnb_bct[:, 0:512], pb[2][:]), deps=[mm[2]])
    ev[3] = p.op("vector", lambda e: e.tensor_copy(lnb_bct[:, 512:1024], pb[3][:]), deps=[mm[3]])

    c1a = p.op("tensor", lambda e: e.matmul(pb[4][0:4, :], lhsT=lu_v, rhs=uaT_v[:, 0:512], start=True, stop=True), deps=[t_lu, t_uaT])
    c1b = p.op("tensor", lambda e: e.matmul(pb[5][0:4, :], lhsT=lu_v, rhs=uaT_v[:, 512:1024], start=True, stop=True))
    c1e = p.op("vector", lambda e: e.tensor_copy(c1_v[:, 0:512], pb[4][0:4, :]), deps=[c1a])
    c1f = p.op("vector", lambda e: e.tensor_copy(c1_v[:, 512:1024], pb[5][0:4, :]), deps=[c1b])
    c2a = p.op("tensor", lambda e: e.matmul(pb[6][0:100, :], lhsT=ld_v, rhs=c1_v[:, 0:512], start=True, stop=True), deps=[t_ld, c1e])
    c2b = p.op("tensor", lambda e: e.matmul(pb[7][0:100, :], lhsT=ld_v, rhs=c1_v[:, 512:1024], start=True, stop=True), deps=[c1f])
    c2e = p.op("vector", lambda e: e.tensor_copy(c2_v[:, 0:512], pb[6][0:100, :]), deps=[c2a])
    c2f = p.op("vector", lambda e: e.tensor_copy(c2_v[:, 512:1024], pb[7][0:100, :]), deps=[c2b])
    bank_free = {0: ev[0], 1: ev[1], 2: ev[2], 3: ev[3], 4: c1e, 5: c1f, 6: c2e, 7: c2f}
    weff_toks = []
    for q in range(NQ):
        b0 = (q % 2) * 2
        mm_a = p.op("tensor", lambda e, q=q, b0=b0: e.matmul(
            pb[b0][:], lhsT=da_v[:, q * 128:(q + 1) * 128], rhs=c2_v[:, 0:512], start=True, stop=True),
            deps=[t_da, c2e, c2f, bank_free[b0]])
        mm_b = p.op("tensor", lambda e, q=q, b0=b0: e.matmul(
            pb[b0 + 1][:], lhsT=da_v[:, q * 128:(q + 1) * 128], rhs=c2_v[:, 512:1024], start=True, stop=True),
            deps=[bank_free[b0 + 1]])
        we_a = p.op("vector", lambda e, q=q, b0=b0: e.tensor_tensor(
            out=W_sb["f"][:, q * C:q * C + 512], in0=pb[b0][:], in1=W_sb["f"][:, q * C:q * C + 512], op=OP.add),
            deps=[mm_a, t_W["f"]])
        we_b = p.op("vector", lambda e, q=q, b0=b0: e.tensor_tensor(
            out=W_sb["f"][:, q * C + 512:(q + 1) * C], in0=pb[b0 + 1][:], in1=W_sb["f"][:, q * C + 512:(q + 1) * C], op=OP.add),
            deps=[mm_b])
        bank_free[b0] = we_a
        bank_free[b0 + 1] = we_b
        weff_toks += [we_a, we_b]
    if debug:
        dbg_toks.append(p.dma("gpsimd", lambda e: e.dma_start(out=dbg["d_weff"], in_=W_sb["f"][:]), deps=weff_toks))

    lil_done = weff_toks
    t_zA4 = p.op("gpsimd", lambda e: e.memset(A4[:], 0.0), deps=[c2a, c2b])
    t_zKtzA = p.op("gpsimd", lambda e: e.memset(KtzA[:], 0.0), deps=lil_done)
    t_zKtzB = p.op("gpsimd", lambda e: e.memset(KtzB[:], 0.0), deps=lil_done)
    t_zKhz = p.op("gpsimd", lambda e: e.memset(Khz[:], 0.0), deps=lil_done)
    t_zS0 = p.op("gpsimd", lambda e: e.memset(Sz[0][:], 0.0), deps=lil_done)
    t_zS1 = p.op("gpsimd", lambda e: e.memset(Sz[1][:], 0.0), deps=lil_done)
    t_zSbf = p.op("gpsimd", lambda e: e.memset(S_bfz[:], 0.0))

    dep = {
        "xnfm_r": [], "sx_r": [], "xi_r": [], "lw_r": [], "rk_r": [], "vg_r": [],
        "hi_r": [], "lo_r": [], "scr_r": [], "scr2_r": [], "xnb_r": [], "stm_r": [],
        "y_r": [], "su_r": [], "A4_r": [t_zA4], "KtzA_r": [t_zKtzA], "KtzB_r": [t_zKtzB],
        "Ru_r": [], "Rab_r": [], "Khf_r": [], "Khz_r": [t_zKhz],
        "E1_r": [], "E2_r": [], "cmf_r": [], "epl_r": [], "gs_r": [], "gn_r": [],
        "yngfm_r": [], "m_r": [[] for _ in range(5)], "ttd_r": [],
        "out_dma": [None, None],
        "Sz_w": [t_zS0, t_zS1], "Sz_r": [[], []], "Sbf_ready": t_zSbf,
    }

    MAGIC = 0x5f3759df
    I32 = mybir.dt.int32

    def emit_rsqrt(dst, var_ap, scratch, deps, niter=4):
        n = var_ap.shape[-1]
        ve = scratch[:, 0:n]
        h_ = scratch[:, n:2 * n]
        t_ = scratch[:, 2 * n:3 * n]
        tok = p.op("vector", lambda e: e.tensor_single_scalar(out=ve, in_=var_ap, scalar=EPS, op=OP.add), deps=deps)
        tok = p.op("vector", lambda e: e.tensor_single_scalar(out=h_, in_=ve, scalar=-0.5, op=OP.mult), deps=[tok])
        vi = ve.bitcast(I32)
        di = dst.bitcast(I32)
        tok = p.op("vector", lambda e: e.tensor_single_scalar(out=di, in_=vi, scalar=1, op=OP.arith_shift_right), deps=[tok])
        tok = p.op("vector", lambda e: e.tensor_single_scalar(out=di, in_=di, scalar=-1, op=OP.bitwise_xor), deps=[tok])
        tok = p.op("vector", lambda e: e.tensor_single_scalar(out=di, in_=di, scalar=MAGIC + 1, op=OP.add), deps=[tok])
        for _ in range(niter):
            tok = p.op("vector", lambda e: e.tensor_tensor(out=t_, in0=dst, in1=dst, op=OP.mult), deps=[tok])
            tok = p.op("vector", lambda e: e.tensor_tensor(out=t_, in0=t_, in1=h_, op=OP.mult), deps=[tok])
            tok = p.op("vector", lambda e: e.scalar_tensor_tensor(out=dst, in0=t_, scalar=1.5, in1=dst, op0=OP.add, op1=OP.mult), deps=[tok])
        return tok

    out_tokens = []

    def bank_mm(bi, fn, deps=()):
        return p.op("tensor", fn, deps=list(deps) + [bank_free[bi]])

    GUARD_KEYS = ["xnfm_r", "sx_r", "xi_r", "lw_r", "rk_r", "vg_r", "hi_r", "lo_r",
                  "scr_r", "scr2_r", "xnb_r", "stm_r", "y_r", "su_r", "A4_r", "KtzA_r", "KtzB_r",
                  "Ru_r", "Rab_r", "Khf_r", "Khz_r", "E1_r", "E2_r", "cmf_r",
                  "epl_r", "gs_r", "gn_r", "yngfm_r", "ttd_r"]
    TS = {}

    def PF1(t):
        if debug and dbg_toks:
            for kk_ in GUARD_KEYS:
                dep[kk_] = dep[kk_] + dbg_toks
            dep["Sz_r"][0] = dep["Sz_r"][0] + dbg_toks
            dep["Sz_r"][1] = dep["Sz_r"][1] + dbg_toks
            dbg_toks.clear()
        S = TS.setdefault(t, {})
        xb = t % 2
        xt = x_tm[xb]

        t_x = p.dma("sync", lambda e, t=t, xt=xt: e.dma_start(
            out=xt[:], in_=x_h[t * TT:(t + 1) * TT, :]), deps=[dep["out_dma"][xb]])

        t_r1 = p.op("vector", lambda e, xt=xt: e.tensor_reduce(
            out=ln_sc[:, 0:1], in_=xt[:], axis=AX.X, op=OP.add), deps=[t_x])
        t_sq = p.op("scalar", lambda e, xt=xt: e.activation(
            out=lnE, in_=xt[:], func=AF.Square), deps=[t_x] + dep["E2_r"])
        dep["E2_r"] = []
        t_r2 = p.op("vector", lambda e: e.tensor_reduce(
            out=ln_sc[:, 1:2], in_=lnE, axis=AX.X, op=OP.add), deps=[t_sq])
        t_m = p.op("vector", lambda e: e.tensor_scalar_mul(ln_sc[:, 2:3], ln_sc[:, 0:1], 1.0 / C), deps=[t_r1])
        t_msq = p.op("vector", lambda e: e.tensor_tensor(
            out=ln_sc[:, 3:4], in0=ln_sc[:, 2:3], in1=ln_sc[:, 2:3], op=OP.mult), deps=[t_m])
        t_var = p.op("vector", lambda e: e.scalar_tensor_tensor(
            out=ln_sc[:, 4:5], in0=ln_sc[:, 1:2], scalar=1.0 / C, in1=ln_sc[:, 3:4],
            op0=OP.mult, op1=OP.subtract), deps=[t_r2, t_msq])
        t_ln1 = p.op("scalar", lambda e: e.activation(
            out=ln_sc[:, 5:6], in_=ln_sc[:, 4:5], func=AF.Ln, bias=eps_sb[:]),
            deps=[t_var, t_eps])
        t_rstd = p.op("scalar", lambda e: e.activation(
            out=ln_sc[:, 6:7], in_=ln_sc[:, 5:6], func=AF.Exp, scale=-0.5), deps=[t_ln1])
        t_xn = p.op("vector", lambda e, xt=xt: e.tensor_scalar(
            out=lnE, in0=xt[:], scalar1=ln_sc[:, 2:3], scalar2=ln_sc[:, 6:7],
            op0=OP.subtract, op1=OP.mult), deps=[t_rstd, t_r2])
        t_hi = p.op("scalar", lambda e: e.activation(
            out=hi_tm, in_=lnE, func=AF.Copy), deps=[t_xn] + dep["hi_r"] + dep["E1_r"])
        t_lo = p.op("vector", lambda e: e.tensor_tensor(
            out=lo_tm, in0=lnE, in1=hi_tm, op=OP.subtract), deps=[t_hi] + dep["lo_r"] + dep["Rab_r"])
        dep["E2_r"] = [t_hi, t_lo]

        S["t_hi"] = t_hi
        S["t_lo"] = t_lo
        S["xt"] = xt
        S["xb"] = xb

    def PF2(t):
        S = TS[t]
        t_hi = S["t_hi"]
        t_lo = S["t_lo"]

        if t > 0:
            t_bnd = p.op("vector", lambda e: e.tensor_copy(xnfm3[:, :, 0:1], xnfm3[:, :, TT:TT + 1]),
                         deps=dep["xnfm_r"])
        else:
            t_bnd = p.op("vector", lambda e: e.memset(xnfm3[:, :, 0:1], 0.0), deps=dep["xnfm_r"])
        dep["xnfm_r"] = []
        mms = []
        for q in range(NQ):
            bi = q // 4
            bank_mm(bi, lambda e, q=q, bi=bi: e.matmul(
                pb[bi][:, (q % 4) * 128:(q % 4) * 128 + 128],
                lhsT=hi_tm[:, q * 128:(q + 1) * 128], rhs=ident[:], start=True, stop=False),
                deps=[t_hi, t_ident])
            mm2 = p.op("tensor", lambda e, q=q, bi=bi: e.matmul(
                pb[bi][:, (q % 4) * 128:(q % 4) * 128 + 128],
                lhsT=lo_tm[:, q * 128:(q + 1) * 128], rhs=ident[:], start=False, stop=True),
                deps=[t_lo])
            mms.append(mm2)
        dep["hi_r"] = [mms[-1]]
        dep["lo_r"] = [mms[-1]]
        dep["E1_r"].append(mms[-1])
        dep["Rab_r"].append(mms[-1])
        xn_ev = []
        for half in range(2):
            tok = p.op("scalar", lambda e, half=half: e.copy(
                xnfm3[:, 4 * half:4 * half + 4, 1:1 + TT],
                pb[half][:].rearrange("p (q c) -> p q c", q=4)),
                deps=[mms[4 * half + 3], t_bnd])
            bank_free[half] = tok
            xn_ev.append(tok)

        S["xn_ev"] = xn_ev

        t_xnbp = p.op("vector", lambda e: e.tensor_copy(xnb3, xnfm3[:, :, 1:TT + 1]),
                      deps=xn_ev + dep["xnb_r"])
        dep["xnb_r"] = []
        dep["xnfm_r"].append(t_xnbp)
        t_sx = p.op("vector", lambda e: e.tensor_tensor(
            out=sxb3, in0=xnfm3[:, :, 0:TT], in1=xnfm3[:, :, 1:TT + 1], op=OP.subtract),
            deps=xn_ev + dep["sx_r"])
        dep["sx_r"] = []
        dep["xnfm_r"].append(t_sx)
        t_xxx = []
        for q in range(NQ):
            tok = p.op("vector", lambda e, q=q: e.scalar_tensor_tensor(
                out=xi3[:, q, :], in0=sxb3[:, q, :], scalar=tmaa_sb[:, q:q + 1],
                in1=xnb3[:, q, :], op0=OP.mult, op1=OP.add),
                deps=[t_sx, t_tmaa, t_xnbp] + dep["xi_r"])
            t_xxx.append(tok)
            dep["xnb_r"].append(tok)
            dep["sx_r"].append(tok)
        dep["xi_r"] = []

        mm_last = {}
        for part in range(2):
            cols = 128 if part == 0 else 32
            for q in range(NQ):
                tok = bank_mm(4, lambda e, part=part, q=q, cols=cols: e.matmul(
                    pb[4][0:cols, part * 128:part * 128 + TT],
                    lhsT=tmw1_sb[:, q * 160 + part * 128:q * 160 + part * 128 + cols],
                    rhs=xi_bf[:, q * TT:(q + 1) * TT],
                    start=(q == 0), stop=(q == NQ - 1)),
                    deps=[t_tmw1, t_xxx[q]])
                if q == NQ - 1:
                    mm_last[part] = tok
        m_toks = []
        for i in range(5):
            part = 0 if i < 4 else 1
            rs = (i * 32) % 128
            j, base = i // 2, (i % 2) * 64
            tok = p.op("scalar", lambda e, part=part, rs=rs, j=j, base=base: e.activation(
                out=m_sb[j][base:base + 32, :], in_=pb[4][rs:rs + 32, part * 128:part * 128 + TT],
                func=AF.Tanh),
                deps=[mm_last[0], mm_last[1], t_mones[i]] + dep["m_r"][i])
            dep["m_r"][i] = []
            m_toks.append(tok)
        bank_free[4] = m_toks[4]

        S["m_toks"] = m_toks
        S["t_sx"] = t_sx
        S["t_xnbp"] = t_xnbp

    def PF3(t):
        S = TS[t]
        m_toks = S["m_toks"]
        t_sx = S["t_sx"]
        t_xnbp = S["t_xnbp"]
        xn_ev = S["xn_ev"]

        proj_ev = {"r": [], "k": [], "v": [], "g": []}
        lw_toks = []

        def emit_delta(i):
            jm, base = i // 2, (i % 2) * 64
            db0 = (i % 2) * 2
            dmm = []
            for q in range(NQ):
                bi = db0 + q // 4
                tok = bank_mm(bi, lambda e, q=q, bi=bi, jm=jm, base=base: e.matmul(
                    pb[bi][:, (q % 4) * 128:(q % 4) * 128 + 128],
                    lhsT=w2p[jm][base:base + 64, q * 128:(q + 1) * 128],
                    rhs=m_sb[jm][base:base + 64, :], start=True, stop=True),
                    deps=[t_w2[i], m_toks[i]])
                dmm.append(tok)
                dep["m_r"][i].append(tok)
            return dmm

        next_dmm = emit_delta(0)
        prev_xi = None
        for i in range(5):
            db0 = (i % 2) * 2
            dmm = next_dmm
            tmix = []
            for half in range(2):
                bi = db0 + half
                tok = p.op("vector", lambda e, bi=bi, half=half: e.tensor_tensor(
                    out=dmix[:, half * 512:(half + 1) * 512], in0=sx_bf[:, half * 512:(half + 1) * 512],
                    in1=pb[bi][:], op=OP.mult),
                    deps=[dmm[4 * half + 3], t_sx] + (dep["E2_r"] if i == 0 else [prev_xi]))
                bank_free[bi] = tok
                tmix.append(tok)
                dep["sx_r"].append(tok)
            if i == 0:
                dep["E2_r"] = []
            xcur = xi_bf[:] if (i % 2 == 0) else xi2
            xkey = "xi_r" if (i % 2 == 0) else "E1_r"
            xi_tok = p.op("vector", lambda e, xcur=xcur: e.tensor_tensor(
                out=xcur, in0=dmix[:], in1=xnb[:], op=OP.add),
                deps=tmix + [t_xnbp] + dep[xkey])
            dep["xnb_r"].append(xi_tok)
            prev_xi = xi_tok
            dep[xkey] = []
            if i < 4:
                next_dmm = emit_delta(i + 1)
            if i == 4:
                dep["E2_r"] = [xi_tok]
            if debug and t == 0 and i == 3:
                tok_dxi = p.dma("gpsimd", lambda e: e.dma_start(out=dbg["d_xi"], in_=xi_bf[:]), deps=[xi_tok])
                dep["xi_r"].append(_dt(tok_dxi))

            if i == 0:
                last = None
                for q in range(NQ):
                    last = bank_mm(5, lambda e, q=q: e.matmul(
                        pb[5][0:64, 0:TT], lhsT=tdw1_sb[:, q * 64:(q + 1) * 64],
                        rhs=xi_bf[:, q * TT:(q + 1) * TT], start=(q == 0), stop=(q == NQ - 1)),
                        deps=[t_tdw1, xi_tok])
                    dep["xi_r"].append(last)
                t_tanh = p.op("scalar", lambda e: e.activation(
                    out=tanhtd[0:64, :], in_=pb[5][0:64, 0:TT], func=AF.Tanh),
                    deps=[last, t_tdones] + dep["ttd_r"])
                dep["ttd_r"] = []
                bank_free[5] = t_tanh
                wmm = []
                for half in range(2):
                    bi = 6 + half
                    tok = bank_mm(bi, lambda e, half=half, bi=bi: e.matmul(
                        pb[bi][:], lhsT=tanhtd[:],
                        rhs=td2aug_sb[:, half * 512:(half + 1) * 512], start=True, stop=True),
                        deps=[t_td2, t_tanh])
                    wmm.append(tok)
                    dep["ttd_r"].append(tok)
                for half in range(2):
                    tok_lw = p.op("scalar", lambda e, half=half: e.activation(
                        out=lw_tm[:, half * 512:(half + 1) * 512], in_=pb[6 + half][:], func=AF.Exp),
                        deps=[wmm[half]] + dep["lw_r"])
                    bank_free[6 + half] = tok_lw
                    lw_toks.append(tok_lw)
                dep["lw_r"] = []
            elif i in (1, 3):
                name = "k" if i == 1 else "r"
                dst = k_bf if i == 1 else r_bf
                mm_l = {}
                for qo in range(NQ):
                    bi = 4 + qo // 4
                    for qi in range(NQ):
                        tok = bank_mm(bi, lambda e, name=name, qo=qo, qi=qi, bi=bi: e.matmul(
                            pb[bi][:, (qo % 4) * 128:(qo % 4) * 128 + 128],
                            lhsT=W_sb[name][:, qi * C + qo * 128:qi * C + qo * 128 + 128],
                            rhs=xi2[:, qi * TT:(qi + 1) * TT],
                            start=(qi == 0), stop=(qi == NQ - 1)),
                            deps=[t_W[name], xi_tok])
                        dep["E1_r"].append(tok)
                        if qi == NQ - 1:
                            mm_l[qo] = tok
                for half in range(2):
                    bi = 4 + half
                    tok = p.op("scalar", lambda e, half=half, bi=bi, dst=dst: e.copy(
                        dst[:, half * 4 * TT:(half + 1) * 4 * TT], pb[bi][:]),
                        deps=[mm_l[4 * half + 0], mm_l[4 * half + 1], mm_l[4 * half + 2], mm_l[4 * half + 3]]
                        + (dep["rk_r"] if i == 1 and half == 0 else []))
                    bank_free[bi] = tok
                    proj_ev[name].append(tok)
                if i == 3:
                    dep["rk_r"] = []
            else:
                name = "v" if i == 2 else "g"
                dst = v_tm if i == 2 else g_raw
                mm_l = {}
                for half in range(2):
                    bi = 6 + half
                    for qi in range(NQ):
                        tok = bank_mm(bi, lambda e, name=name, half=half, qi=qi, bi=bi: e.matmul(
                            pb[bi][:],
                            lhsT=xi_bf[:, qi * TT:(qi + 1) * TT],
                            rhs=W_sb[name][:, qi * C + half * 512:qi * C + half * 512 + 512],
                            start=(qi == 0), stop=(qi == NQ - 1)),
                            deps=[t_W[name], xi_tok])
                        dep["xi_r"].append(tok)
                        if qi == NQ - 1:
                            mm_l[half] = tok
                for half in range(2):
                    bi = 6 + half
                    tok = p.op("scalar", lambda e, half=half, bi=bi, dst=dst: e.copy(
                        dst[:, half * 512:(half + 1) * 512], pb[bi][:]),
                        deps=[mm_l[half]] + (dep["vg_r"] if i == 2 and half == 0 else []))
                    bank_free[bi] = tok
                    proj_ev[name].append(tok)
                if i == 4:
                    dep["vg_r"] = []

        if debug and t == 0:
            dep["xnfm_r"].append(_dt(p.dma("sync", lambda e: e.dma_start(out=dbg["d_xnfm"], in_=xnfm3[:, :, 1:TT + 1]), deps=xn_ev)))
            dep["sx_r"].append(_dt(p.dma("sync", lambda e: e.dma_start(out=dbg["d_sx"], in_=sx_bf[:]), deps=[t_sx])))
            dep["rk_r"].append(_dt(p.dma("gpsimd", lambda e: e.dma_start(out=dbg["d_r"], in_=r_bf[:]), deps=proj_ev["r"])))
            dep["rk_r"].append(_dt(p.dma("gpsimd", lambda e: e.dma_start(out=dbg["d_k"], in_=k_bf[:]), deps=proj_ev["k"])))
            dep["vg_r"].append(_dt(p.dma("gpsimd", lambda e: e.dma_start(out=dbg["d_v"], in_=v_tm[:]), deps=proj_ev["v"])))
            dep["vg_r"].append(_dt(p.dma("gpsimd", lambda e: e.dma_start(out=dbg["d_g"], in_=g_raw[:]), deps=proj_ev["g"])))
            dep["lw_r"].append(_dt(p.dma("gpsimd", lambda e: e.dma_start(out=dbg["d_lw"], in_=lw_tm[:]), deps=lw_toks)))

        S["proj_ev"] = proj_ev
        S["lw_toks"] = lw_toks

    def PB1(t):
        S = TS[t]
        proj_ev = S["proj_ev"]
        lw_toks = S["lw_toks"]
        t_xnbp = S["t_xnbp"]

        wave1 = {}
        for q in range(NQ):
            m_ex = bank_mm(q // 4, lambda e, q=q: e.matmul(
                pb[q // 4][:, (q % 4) * 128:(q % 4) * 128 + 128],
                lhsT=lw_tm[:, q * 128:(q + 1) * 128],
                rhs=utri[:, 0:128], start=True, stop=True),
                deps=[lw_toks[q // 4], t_utri])
            m_ia = bank_mm(2 + q // 4, lambda e, q=q: e.matmul(
                pb[2 + q // 4][:, (q % 4) * 128:(q % 4) * 128 + 128],
                lhsT=lw_tm[:, q * 128:(q + 1) * 128],
                rhs=utri[:, 4 * 128:5 * 128], start=True, stop=True))
            m_ib = bank_mm(4 + q // 4, lambda e, q=q: e.matmul(
                pb[4 + q // 4][:, (q % 4) * 128:(q % 4) * 128 + 128],
                lhsT=lw_tm[:, q * 128:(q + 1) * 128],
                rhs=utri[:, 5 * 128:6 * 128], start=True, stop=True))
            wave1[q] = (m_ex, m_ia, m_ib)
            dep["lw_r"] += [m_ex, m_ia, m_ib]
        eRu_t = [p.op("scalar", lambda e, hf=hf: e.activation(
            out=E1f[:, hf * 512:(hf + 1) * 512], in_=pb[hf][:], func=AF.Exp, scale=-1.0),
            deps=[wave1[4 * hf + 3][0]] + dep["E1_r"]) for hf in range(2)]
        eA_t = [p.op("scalar", lambda e, hf=hf: e.activation(
            out=E2[:, hf * 512:(hf + 1) * 512], in_=pb[2 + hf][:], func=AF.Exp, scale=1.0),
            deps=[wave1[4 * hf + 3][1]] + dep["E2_r"]) for hf in range(2)]
        negf = [p.op("scalar", lambda e, q=q: e.activation(
            out=cmf[:, q:q + 1], in_=pb[4 + q // 4][:, (q % 4) * 128 + 127:(q % 4) * 128 + 128],
            func=AF.Copy, scale=-1.0), deps=[wave1[(q // 4) * 4 + 3][2]] + dep["cmf_r"]) for q in range(NQ)]
        Khf3 = Khf[:].rearrange("p (q c) -> p q c", q=NQ)
        ekhf = [p.op("scalar", lambda e, q=q: e.activation(
            out=Khf3[:, q, :], in_=pb[4 + q // 4][:, (q % 4) * 128:(q % 4) * 128 + 128],
            func=AF.Exp, bias=cmf[:, q:q + 1]),
            deps=[negf[q]] + (dep["Khf_r"] if q == 0 else [])) for q in range(NQ)]
        dep["E1_r"] = []
        dep["E2_r"] = []
        dep["cmf_r"] = ekhf[:]
        dep["Khf_r"] = []
        t_ru = p.op("gpsimd", lambda e: e.tensor_tensor(
            out=Ru[:].rearrange("p (q c) -> p q c", q=NQ), in0=rb3, in1=E1_3, op=OP.mult),
            deps=eRu_t + dep["Ru_r"] + proj_ev["r"])
        dep["Ru_r"] = []
        t_khf = p.op("gpsimd", lambda e: e.tensor_tensor(
            out=Khf3, in0=kb3, in1=Khf3, op=OP.mult),
            deps=ekhf + proj_ev["k"])
        ktzA_t = []
        for pr in range(2):
            ktzA_t.append(p.op("gpsimd", lambda e, pr=pr: e.tensor_tensor(
                out=KtzA4[pr * 64:pr * 64 + 64, pr::2, 0:64],
                in0=kb3[pr * 64:pr * 64 + 64, :, 0:64],
                in1=E2_3[pr * 64:pr * 64 + 64, :, 0:64], op=OP.mult),
                deps=eA_t + dep["KtzA_r"] + proj_ev["k"]))
        dep["KtzA_r"] = []
        eB_t = [p.op("scalar", lambda e, hf=hf: e.activation(
            out=E1f[:, hf * 512:(hf + 1) * 512], in_=pb[4 + hf][:], func=AF.Exp, scale=1.0),
            deps=[t_ru, wave1[4 * hf + 3][2]]) for hf in range(2)]
        bank_free[0] = eRu_t[0]
        bank_free[1] = eRu_t[1]
        bank_free[2] = eA_t[0]
        bank_free[3] = eA_t[1]
        bank_free[4] = eB_t[0]
        bank_free[5] = eB_t[1]
        ktzB_t = []
        for pr in range(2):
            ktzB_t.append(p.op("gpsimd", lambda e, pr=pr: e.tensor_tensor(
                out=KtzB4[pr * 64:pr * 64 + 64, pr::2, :],
                in0=kb3[pr * 64:pr * 64 + 64, :, :],
                in1=E1_3[pr * 64:pr * 64 + 64, :, :], op=OP.mult),
                deps=eB_t + dep["KtzB_r"] + proj_ev["k"]))
        dep["KtzB_r"] = []
        wave2 = {}
        for q in range(NQ):
            m_xa = bank_mm(q // 4, lambda e, q=q: e.matmul(
                pb[q // 4][:, (q % 4) * 128:(q % 4) * 128 + 128],
                lhsT=lw_tm[:, q * 128:(q + 1) * 128],
                rhs=utri[:, 2 * 128:3 * 128], start=True, stop=True))
            m_xb = bank_mm(2 + q // 4, lambda e, q=q: e.matmul(
                pb[2 + q // 4][:, (q % 4) * 128:(q % 4) * 128 + 128],
                lhsT=lw_tm[:, q * 128:(q + 1) * 128],
                rhs=utri[:, 3 * 128:4 * 128], start=True, stop=True))
            m_in = bank_mm(4 + q // 4, lambda e, q=q: e.matmul(
                pb[4 + q // 4][:, (q % 4) * 128:(q % 4) * 128 + 128],
                lhsT=lw_tm[:, q * 128:(q + 1) * 128],
                rhs=utri[:, 1 * 128:2 * 128], start=True, stop=True))
            wave2[q] = (m_xa, m_xb, m_in)
            dep["lw_r"] += [m_xa, m_xb, m_in]
        eRA_t = [p.op("scalar", lambda e, hf=hf: e.activation(
            out=E2_3[:, 4 * hf:4 * hf + 4, 0:64],
            in_=pb[hf][:].rearrange("p (qq c) -> p qq c", qq=4)[:, :, 0:64],
            func=AF.Exp, scale=-1.0),
            deps=[wave2[4 * hf + 3][0]] + ktzA_t) for hf in range(2)]
        eRB_t = [p.op("scalar", lambda e, hf=hf: e.activation(
            out=E1_3[:, 4 * hf:4 * hf + 4, 64:128],
            in_=pb[2 + hf][:].rearrange("p (qq c) -> p qq c", qq=4)[:, :, 64:128],
            func=AF.Exp, scale=-1.0),
            deps=[wave2[4 * hf + 3][1], t_khf] + ktzB_t) for hf in range(2)]
        epl_toks = []
        for q in range(NQ):
            epl_toks.append(p.op("scalar", lambda e, q=q: e.activation(
                out=epl[:, q:q + 1], in_=pb[4 + q // 4][:, (q % 4) * 128 + 127:(q % 4) * 128 + 128],
                func=AF.Exp, scale=-1.0),
                deps=[wave2[(q // 4) * 4 + 3][2]] + dep["epl_r"]))
        dep["epl_r"] = []
        bank_free[0] = eRA_t[0]
        bank_free[1] = eRA_t[1]
        bank_free[2] = eRB_t[0]
        bank_free[3] = eRB_t[1]
        bank_free[4] = epl_toks[3]
        bank_free[5] = epl_toks[7]
        t_ra = p.op("gpsimd", lambda e: e.tensor_tensor(
            out=Rab3[:, :, 0:64], in0=rb3[:, :, 0:64], in1=E2_3[:, :, 0:64], op=OP.mult),
            deps=eRA_t + dep["Rab_r"] + proj_ev["r"])
        t_rb = p.op("gpsimd", lambda e: e.tensor_tensor(
            out=Rab3[:, :, 64:128], in0=rb3[:, :, 64:128], in1=E1_3[:, :, 64:128], op=OP.mult),
            deps=eRB_t + dep["Rab_r"] + proj_ev["r"])
        dep["Rab_r"] = []
        t_rku1 = p.op("gpsimd", lambda e: e.tensor_tensor(
            out=rku3, in0=rb3, in1=kb3, op=OP.mult),
            deps=proj_ev["r"] + proj_ev["k"] + dep["sx_r"])
        t_rku2 = p.op("gpsimd", lambda e: e.tensor_tensor(
            out=rku3, in0=rku3,
            in1=u_sb[:].rearrange("p q -> p q ()").broadcast_to((128, NQ, 128)), op=OP.mult),
            deps=[t_rku1, t_u])

        mm_f = {}
        for half in range(2):
            bi = 6 + half
            for qi in range(NQ):
                tok = bank_mm(bi, lambda e, half=half, qi=qi, bi=bi: e.matmul(
                    pb[bi][:],
                    lhsT=xnb[:, qi * TT:(qi + 1) * TT],
                    rhs=W_sb["f"][:, qi * C + half * 512:qi * C + half * 512 + 512],
                    start=(qi == 0), stop=False),
                    deps=[t_xnbp] + (weff_toks if t == 0 else []))
                dep["xnb_r"].append(tok)
            mm_f[half] = p.op("tensor", lambda e, half=half, bi=bi: e.matmul(
                pb[bi][:], lhsT=ones1[:], rhs=fbrow[:, half * 512:(half + 1) * 512],
                start=False, stop=True), deps=[t_fb, t_ones])
            bank_free[bi] = None

        kht_mm = {}
        for q in range(NQ):
            bi2 = 2 + q // 4
            kht_mm[q] = bank_mm(bi2, lambda e, q=q, bi2=bi2: e.matmul(
                pb[bi2][:, (q % 4) * 128:(q % 4) * 128 + 128],
                lhsT=Khf[:, q * 128:(q + 1) * 128], rhs=ident[:], start=True, stop=True),
                deps=[t_khf])
        khz_toks = {}
        for q in range(NQ):
            bi2 = 2 + q // 4
            for pr in range(2):
                h = 2 * q + pr
                khz_toks[h] = p.op("scalar", lambda e, q=q, bi2=bi2, pr=pr, h=h: e.activation(
                    out=Khz[:, h * 128 + pr * 64:h * 128 + pr * 64 + 64],
                    in_=pb[bi2][:, (q % 4) * 128 + pr * 64:(q % 4) * 128 + pr * 64 + 64],
                    func=AF.Copy),
                    deps=[kht_mm[(q // 4) * 4 + 3]] + dep["Khz_r"])
        dep["Khz_r"] = []
        bank_free[2] = khz_toks[7]
        bank_free[3] = khz_toks[15]

        a_mm = {}
        for h in range(H):
            bi = h // 4
            q = h // 2
            mA = bank_mm(bi, lambda e, h=h, bi=bi, q=q: e.matmul(
                pb[bi][:, (h % 4) * 128:(h % 4) * 128 + 64],
                lhsT=KtzA[:, h * 128:(h + 1) * 128],
                rhs=Rab[:, q * 128:q * 128 + 64],
                start=True, stop=True),
                deps=[ktzA_t[h % 2], t_ra])
            mB = bank_mm(bi, lambda e, h=h, bi=bi, q=q: e.matmul(
                pb[bi][:, (h % 4) * 128 + 64:(h % 4) * 128 + 128],
                lhsT=KtzB[:, h * 128:(h + 1) * 128],
                rhs=Rab[:, q * 128 + 64:q * 128 + 128],
                start=True, stop=True),
                deps=[ktzB_t[h % 2], t_rb])
            a_mm[h] = (mA, mB)
        a_ev = []
        for bi in range(4):
            tok = p.op("vector", lambda e, bi=bi: e.copy_predicated(
                out=A4[:, bi * 512:(bi + 1) * 512], mask=mask4[:], data=pb[bi][:]),
                deps=[a_mm[4 * bi + 3][1], t_mask4] + dep["A4_r"])
            bank_free[bi] = tok
            a_ev.append(tok)
        dep["A4_r"] = []

        su_mm = []
        for q in range(NQ):
            su_mm.append(bank_mm(0, lambda e, q=q: e.matmul(
                pb[0][:, 2 * q:2 * q + 2], lhsT=rku[:, q * 128:(q + 1) * 128],
                rhs=ind2[:], start=True, stop=True),
                deps=[t_rku2, t_ind2]))
        for tk in su_mm:
            dep["sx_r"].append(tk)
        t_su = p.op("vector", lambda e: e.tensor_copy(su_sb[:], pb[0][:, 0:H]), deps=su_mm + dep["su_r"])
        dep["su_r"] = []
        bank_free[0] = t_su


        if debug and t == 0:
            tok_der = p.dma("gpsimd", lambda e: e.dma_start(out=dbg["d_Er"], in_=Ru[:]), deps=[t_ru])
            dep["Ru_r"].append(_dt(tok_der))
            dep["A4_r"].append(_dt(p.dma("gpsimd", lambda e: e.dma_start(out=dbg["d_A"], in_=A4[:]), deps=a_ev)))
            dep["su_r"].append(_dt(p.dma("sync", lambda e: e.dma_start(out=dbg["d_su"], in_=su_sb[:]), deps=[t_su])))
            dep["KtzA_r"].append(_dt(p.dma("gpsimd", lambda e: e.dma_start(out=dbg["d_Ktz"], in_=KtzA[:]), deps=ktzA_t)))
            dep["Khz_r"].append(_dt(p.dma("gpsimd", lambda e: e.dma_start(out=dbg["d_Khz"], in_=Khz[:]), deps=list(khz_toks.values()))))

        dep["KtzA_r"] = dep["KtzA_r"] + [a_mm[h][0] for h in range(H)]
        dep["KtzB_r"] = [a_mm[h][1] for h in range(H)]
        dep["Rab_r"] = dep["Rab_r"] + [a_mm[h][0] for h in range(H)] + [a_mm[h][1] for h in range(H)]
        dep["Khf_r"] = list(kht_mm.values())
        dep["E1_r"] = dep["E1_r"] + [t_rb]
        dep["E2_r"] = [t_ra]
        dep["cmf_r"] = [t_khf]
        dep["rk_r"] = [t_ru, t_ra, t_rb, t_khf, t_rku1] + ktzA_t + ktzB_t
        S["t_su"] = t_su
        S["t_ru"] = t_ru
        S["t_khf"] = t_khf
        S["epl_toks"] = epl_toks
        S["khz_toks"] = khz_toks
        S["a_ev"] = a_ev
        S["a_mm"] = a_mm
        S["mm_f"] = mm_f

    def PB2a(t):
        S = TS[t]
        proj_ev = S["proj_ev"]
        t_su = S["t_su"]
        t_ru = S["t_ru"]
        t_khf = S["t_khf"]
        epl_toks = S["epl_toks"]
        khz_toks = S["khz_toks"]
        a_ev = S["a_ev"]
        a_mm = S["a_mm"]
        mm_f = S["mm_f"]
        par = t % 2
        Scur, Snxt = Sz[par], Sz[1 - par]

        t_vsu = p.op("gpsimd", lambda e: e.tensor_tensor(
            out=scr2[:].rearrange("p (h c) -> p h c", h=H),
            in0=v_tm[:].rearrange("p (h c) -> p h c", h=H),
            in1=su_sb[:].rearrange("p h -> p h ()").broadcast_to((128, H, K)),
            op=OP.mult),
            deps=[t_su] + proj_ev["v"] + dep["scr2_r"])
        dep["scr2_r"] = []
        t_sdec = p.op("gpsimd", lambda e, Scur=Scur, Snxt=Snxt: e.tensor_tensor(
            out=Snxt[:].rearrange("p (q c) -> p q c", q=NQ),
            in0=Scur[:].rearrange("p (q c) -> p q c", q=NQ),
            in1=epl[:].rearrange("p q -> p q ()").broadcast_to((128, NQ, 128)),
            op=OP.mult),
            deps=epl_toks + dep["Sz_r"][1 - par] + [dep["Sz_w"][1 - par], dep["Sz_w"][par]])
        dep["Sz_r"][1 - par] = []

        y_mm = {}
        s_mm = {}
        intra_mms = []
        for h in range(H):
            by = 4 + h // 8
            tok1 = bank_mm(by, lambda e, h=h, by=by: e.matmul(
                pb[by][:, (h % 8) * 64:(h % 8) * 64 + 64],
                lhsT=A4[:, h * 128:(h + 1) * 128],
                rhs=v_tm[:, h * K:h * K + K],
                start=True, stop=False), deps=[a_ev[h // 4]] + proj_ev["v"])
            intra_mms.append(tok1)
            y_mm[h] = p.op("tensor", lambda e, h=h, by=by: e.matmul(
                pb[by][:, (h % 8) * 64:(h % 8) * 64 + 64],
                lhsT=Ru[:, (h // 2) * 128:(h // 2) * 128 + 128],
                rhs=S_bfz[:, h * K:(h + 1) * K],
                start=False, stop=True), deps=[dep["Sbf_ready"], t_ru])
            bs = 2 + h // 8
            s_mm[h] = bank_mm(bs, lambda e, h=h, bs=bs: e.matmul(
                pb[bs][:, (h % 8) * 64:(h % 8) * 64 + 64],
                lhsT=Khz[:, h * 128:(h + 1) * 128],
                rhs=v_tm[:, h * K:h * K + K],
                start=True, stop=True), deps=[khz_toks[h]] + proj_ev["v"])
        yev = []
        sev = []
        for half in range(2):
            tok = p.op("vector", lambda e, half=half: e.tensor_tensor(
                out=y_tm[:, half * 512:half * 512 + 512],
                in0=scr2[:, half * 512:half * 512 + 512],
                in1=pb[4 + half][:], op=OP.add),
                deps=[y_mm[half * 8 + 7], t_vsu] + dep["y_r"])
            yev.append(tok)
            dep["scr2_r"].append(tok)
            tok2 = p.op("vector", lambda e, half=half, Snxt=Snxt: e.tensor_tensor(
                out=Snxt[:, half * 512:half * 512 + 512],
                in0=Snxt[:, half * 512:half * 512 + 512],
                in1=pb[2 + half][:], op=OP.add),
                deps=[s_mm[half * 8 + 7], t_sdec])
            sev.append(tok2)
        dep["y_r"] = []
        dep["Sz_r"][par] = [t_sdec]
        dep["Sz_r"][1 - par] = []
        dep["Sz_w"][1 - par] = sev[-1]
        bank_free[4] = yev[0]
        bank_free[5] = yev[1]
        bank_free[2] = sev[0]
        bank_free[3] = sev[1]
        t_sbf = p.op("vector", lambda e, Snxt=Snxt: e.tensor_copy(S_bfz[:], Snxt[:]),
                     deps=sev + list(y_mm.values()))
        dep["Sbf_ready"] = t_sbf
        dep["Sz_r"][1 - par].append(t_sbf)

        t_f2 = p.op("scalar", lambda e: e.activation(
            out=scr[:, 0:512], in_=pb[6][:], func=AF.Silu), deps=[mm_f[0]] + dep["scr_r"])
        t_f2b = p.op("scalar", lambda e: e.activation(
            out=scr[:, 512:1024], in_=pb[7][:], func=AF.Silu), deps=[mm_f[1]])
        bank_free[6] = t_f2
        bank_free[7] = t_f2b
        dep["scr_r"] = []

        dep["Ru_r"] = list(y_mm.values())
        dep["Khz_r"] = list(s_mm.values())
        dep["epl_r"] = [t_sdec, t_khf]
        dep["su_r"] = [t_vsu]
        dep["A4_r"] = dep["A4_r"] + intra_mms
        S["yev"] = yev
        S["sev"] = sev
        S["t_vsu"] = t_vsu
        S["t_f2"] = t_f2
        S["t_f2b"] = t_f2b

    def PB2b(t):
        S = TS[t]
        proj_ev = S["proj_ev"]
        yev = S["yev"]
        sev = S["sev"]
        t_vsu = S["t_vsu"]
        par = t % 2
        Snxt = Sz[1 - par]

        yg3 = y_tm[:].rearrange("p (h c) -> p h c", h=H)
        t_gm = p.op("vector", lambda e: e.tensor_reduce(
            out=gn_sc[:, 0:H], in_=yg3, axis=AX.X, op=OP.add), deps=yev + dep["gn_r"])
        dep["gn_r"] = []
        t_gsq = p.op("scalar", lambda e: e.activation(
            out=scr2[:], in_=y_tm[:], func=AF.Square), deps=yev + dep["scr2_r"])
        t_gr2 = p.op("vector", lambda e: e.tensor_reduce(
            out=gn_sc[:, H:2 * H], in_=scr2[:].rearrange("p (h c) -> p h c", h=H),
            axis=AX.X, op=OP.add), deps=[t_gsq])
        t_gmm = p.op("vector", lambda e: e.tensor_scalar_mul(gn_sc[:, 0:H], gn_sc[:, 0:H], 1.0 / K), deps=[t_gm])
        t_gm2 = p.op("vector", lambda e: e.tensor_tensor(
            out=gn_sc[:, 2 * H:3 * H], in0=gn_sc[:, 0:H], in1=gn_sc[:, 0:H], op=OP.mult), deps=[t_gmm])
        t_gvar = p.op("vector", lambda e: e.scalar_tensor_tensor(
            out=gn_sc[:, H:2 * H], in0=gn_sc[:, H:2 * H], scalar=1.0 / K,
            in1=gn_sc[:, 2 * H:3 * H], op0=OP.mult, op1=OP.subtract), deps=[t_gr2, t_gm2])
        t_gl1 = p.op("scalar", lambda e: e.activation(
            out=gn_sc2[:, 0:H], in_=gn_sc[:, H:2 * H], func=AF.Ln, bias=eps_sb[:]),
            deps=[t_gvar, t_eps])
        t_grs = p.op("scalar", lambda e: e.activation(
            out=gn_sc[:, 3 * H:4 * H], in_=gn_sc2[:, 0:H], func=AF.Exp, scale=-0.5), deps=[t_gl1])
        t_yc = p.op("gpsimd", lambda e: e.tensor_tensor(
            out=yg3, in0=yg3,
            in1=gn_sc[:, 0:H].rearrange("p h -> p h ()").broadcast_to((128, H, K)),
            op=OP.subtract), deps=[t_gmm, t_gsq])
        t_yn = p.op("gpsimd", lambda e: e.tensor_tensor(
            out=yg3, in0=yg3,
            in1=gn_sc[:, 3 * H:4 * H].rearrange("p h -> p h ()").broadcast_to((128, H, K)),
            op=OP.mult), deps=[t_grs, t_yc])
        dep["scr2_r"] = [t_gr2]
        dep["y_r"] = [t_yn]
        dep["gn_r"] = [t_yc, t_yn]

        if debug and t == 1:
            dep["y_r"].append(_dt(p.dma("sync", lambda e: e.dma_start(out=dbg["d_y"], in_=y_tm[:]), deps=[t_yn])))
            dep["Sz_r"][1 - par].append(_dt(p.dma("sync", lambda e: e.dma_start(out=dbg["d_S"], in_=Snxt[:]), deps=[sev[-1]])))

        t_gs = p.op("scalar", lambda e: e.activation(
            out=gs_sb[:], in_=g_raw[:], func=AF.Silu),
            deps=proj_ev["g"] + dep["gs_r"])
        dep["gs_r"] = []
        t_gl = p.op("gpsimd", lambda e: e.tensor_tensor(
            out=gl_b, in0=gs_sb[:], in1=lnw_bct[:], op=OP.mult),
            deps=[t_gs, ev[0], ev[1]] + dep["Khf_r"])
        dep["Khf_r"] = []
        t_glb = p.op("gpsimd", lambda e: e.tensor_tensor(
            out=glb_b, in0=gs_sb[:], in1=lnb_bct[:], op=OP.mult),
            deps=[t_gs, ev[2], ev[3]] + dep["yngfm_r"])
        dep["vg_r"] = [t_gl, t_glb, t_vsu]
        t_y1 = p.op("gpsimd", lambda e: e.tensor_tensor(
            out=scr2[:], in0=y_tm[:], in1=gl_b, op=OP.mult),
            deps=[t_gl, t_yn] + dep["scr2_r"])
        t_y2 = p.op("gpsimd", lambda e: e.tensor_tensor(
            out=y_tm[:].bitcast(BF16)[:, 0:C],
            in0=scr2[:], in1=glb_b, op=OP.add),
            deps=[t_glb, t_y1] + dep["y_r"])
        dep["Khf_r"].append(t_y1)
        dep["scr2_r"] = [t_y2]
        dep["y_r"] = [t_y2]

        S["t_y2"] = t_y2

    def PB3(t):
        S = TS[t]
        proj_ev = S["proj_ev"]
        t_y2 = S["t_y2"]
        t_f2 = S["t_f2"]
        t_f2b = S["t_f2b"]
        xt = S["xt"]
        xb = S["xb"]

        mms2 = []
        for q in range(NQ):
            bi = q // 4
            mms2.append(bank_mm(bi, lambda e, q=q, bi=bi: e.matmul(
                pb[bi][:, (q % 4) * 128:(q % 4) * 128 + 128],
                lhsT=y_tm[:].bitcast(BF16)[:, q * 128:(q + 1) * 128],
                rhs=ident[:], start=True, stop=True),
                deps=[t_y2, t_ident]))
        yngfm_ev = []
        for half in range(2):
            tok = p.op("scalar", lambda e, half=half: e.copy(
                yngfm3[:, 4 * half:4 * half + 4, :],
                pb[half][:].rearrange("p (q c) -> p q c", q=4)),
                deps=[mms2[4 * half + 3]] + (dep["yngfm_r"] if half == 0 else []))
            bank_free[half] = tok
            yngfm_ev.append(tok)
        dep["yngfm_r"] = []
        dep["y_r"] = dep["y_r"] + mms2

        if debug and t == 0:
            dep["yngfm_r"].append(_dt(p.dma("gpsimd", lambda e: e.dma_start(out=dbg["d_yng"], in_=yng_fm[:]), deps=yngfm_ev)))

        mm_l = {}
        for half in range(2):
            bi = 4 + half
            for qi in range(NQ):
                tok = bank_mm(bi, lambda e, half=half, qi=qi, bi=bi: e.matmul(
                    pb[bi][:],
                    lhsT=yng_fm[:, qi * TT:(qi + 1) * TT],
                    rhs=W_sb["o"][:, qi * C + half * 512:qi * C + half * 512 + 512],
                    start=(qi == 0), stop=(qi == NQ - 1)),
                    deps=[t_W["o"], yngfm_ev[0], yngfm_ev[1]])
                dep["yngfm_r"].append(tok)
                if qi == NQ - 1:
                    mm_l[half] = tok
        stm_ev = []
        for half in range(2):
            bi = 4 + half
            tok = p.op("scalar", lambda e, half=half, bi=bi: e.copy(
                sx_fm[:, half * 512:(half + 1) * 512], pb[bi][:]),
                deps=[mm_l[half]] + (dep["stm_r"] if half == 0 else []))
            bank_free[bi] = tok
            stm_ev.append(tok)
        dep["stm_r"] = stm_ev[:]

        if debug and t == 0:
            dep["stm_r"].append(_dt(p.dma("sync", lambda e: e.dma_start(out=dbg["d_stm"], in_=sx_fm[:, 0:C]), deps=stm_ev)))

        t_f3 = p.op("vector", lambda e: e.tensor_tensor(
            out=scr2[:], in0=scr[:], in1=sx_fm[:, 0:C], op=OP.mult),
            deps=[t_f2, t_f2b, stm_ev[0], stm_ev[1]] + dep["scr2_r"])
        dep["stm_r"].append(t_f3)
        t_out = p.op("vector", lambda e, xt=xt: e.tensor_tensor(
            out=xt[:], in0=scr2[:], in1=xt[:], op=OP.add), deps=[t_f3])
        dep["scr_r"] = [t_f3]
        dep["scr2_r"] = [t_out]
        t_st = p.dma("scalar", lambda e, t=t, xt=xt: e.dma_start(
            out=out_h[t * TT:(t + 1) * TT, :], in_=xt[:]), deps=[t_out])
        dep["out_dma"][xb] = t_st
        out_tokens.append(t_st)

    PF1(0)
    PF2(0)
    PF3(0)
    for t in range(n_tiles):
        PB1(t)
        if t + 1 < n_tiles:
            PF1(t + 1)
        PB2a(t)
        if t + 1 < n_tiles:
            PF2(t + 1)
        PB2b(t)
        if t + 1 < n_tiles:
            PF3(t + 1)
        PB3(t)

    p.op("sync", lambda e: e.nop(), deps=out_tokens)
    p.emit(nc, block, sems)
    blk.__exit__(None, None, None)
    return nc, sbuf_used


def host_inputs(inp, b):
    f32 = np.float32
    w2 = np.asarray(inp["L0_attn_tm_w2"], f32)
    maa = np.asarray(inp["L0_attn_time_maa_wkvrg"], f32)
    w2aug = np.concatenate([np.concatenate([w2[i], maa[i:i + 1]], axis=0) for i in range(5)], axis=0)
    td2aug = np.concatenate([np.asarray(inp["L0_attn_td_w2"], f32),
                             np.asarray(inp["L0_attn_time_decay"], f32)[None, :]], axis=0)
    ident = np.eye(128, dtype=f32)
    jj = np.arange(128)
    uexc = (jj[:, None] < jj[None, :]).astype(f32)
    uinc = (jj[:, None] <= jj[None, :]).astype(f32)
    pre32 = (jj[:, None] < 32).astype(f32) * np.ones((1, 128), f32)
    pre96 = (jj[:, None] < 96).astype(f32) * np.ones((1, 128), f32)
    utri = np.concatenate([uexc, uinc, uexc - pre32, uexc - pre96,
                           uinc - pre32, uinc - pre96], axis=1)
    mask4 = np.tile(uexc, (1, 4)).astype(np.uint8)
    ind2 = np.zeros((128, 2), f32)
    ind2[0:64, 0] = 1.0
    ind2[64:128, 1] = 1.0
    return {
        "x": np.ascontiguousarray(np.asarray(inp["L0_attn_x"], f32)[b]),
        "Wr": np.ascontiguousarray(np.asarray(inp["L0_attn_Wr"], f32)),
        "Wk": np.ascontiguousarray(np.asarray(inp["L0_attn_Wk"], f32)),
        "Wv": np.ascontiguousarray(np.asarray(inp["L0_attn_Wv"], f32)),
        "Wg": np.ascontiguousarray(np.asarray(inp["L0_attn_Wg"], f32)),
        "Wo": np.ascontiguousarray(np.asarray(inp["L0_attn_Wo"], f32)),
        "WfT": np.ascontiguousarray(np.asarray(inp["L0_ffn_W"], f32).T),
        "tm_w1": np.ascontiguousarray(np.asarray(inp["L0_attn_tm_w1"], f32)),
        "w2aug": np.ascontiguousarray(w2aug),
        "td_w1": np.ascontiguousarray(np.asarray(inp["L0_attn_td_w1"], f32)),
        "td2aug": np.ascontiguousarray(td2aug),
        "tmaa": np.ascontiguousarray(np.asarray(inp["L0_attn_time_maa_x"], f32)[None, :]),
        "ufaaaa": np.ascontiguousarray(np.asarray(inp["L0_attn_time_faaaa"], f32).reshape(1, -1)),
        "lnw": np.ascontiguousarray(np.asarray(inp["L0_attn_ln_w"], f32)[None, :]),
        "lnb": np.ascontiguousarray(np.asarray(inp["L0_attn_ln_b"], f32)[None, :]),
        "fb": np.ascontiguousarray(np.asarray(inp["L0_ffn_b"], f32)[None, :]),
        "lora_up": np.ascontiguousarray(np.asarray(inp["L0_ffn_lora_up"], f32)),
        "lora_down": np.ascontiguousarray(np.asarray(inp["L0_ffn_lora_down"], f32)),
        "down_aux": np.ascontiguousarray(np.asarray(inp["L0_ffn_down_aux"], f32)),
        "up_auxT": np.ascontiguousarray(np.asarray(inp["L0_ffn_up_aux"], f32).T),
        "ident": ident, "utri": utri, "mask4": mask4, "ind2": ind2,
    }


def kernel(**inputs):
    nc, _ = build_nc(n_tiles=T_FULL // TT, debug=False)
    in_maps = [host_inputs(inputs, b) for b in range(8)]
    res = run_bass_kernel_spmd(nc, in_maps, core_ids=list(range(8)))
    out = np.stack([np.asarray(r["out"]) for r in res.results], axis=0)
    return out.astype(np.float32)


# version 38
# speedup vs baseline: 1.0385x; 1.0019x over previous
import numpy as np
import ml_dtypes
import concourse.bass as bass
import concourse.mybir as mybir
from concourse.bass_utils import run_bass_kernel_spmd
from concourse.alu_op_type import AluOpType as OP

F32 = mybir.dt.float32
BF16 = mybir.dt.bfloat16
AF = mybir.ActivationFunctionType
AX = mybir.AxisListType

T_FULL = 4096
C = 1024
H = 16
K = 64
TT = 128
NQ = 8
EPS = 1e-5

ENGINES = ("sync", "gpsimd", "tensor", "vector", "scalar")


class Prog:
    def __init__(self):
        self.ops = {e: [] for e in ENGINES}
        self.n = 0
        self.eng_of = {}

    def op(self, eng, fn, deps=(), dma=False):
        opid = self.n
        self.n += 1
        deps = [d for d in deps if d is not None]
        self.ops[eng].append((fn, deps, dma, opid))
        self.eng_of[opid] = (eng, dma)
        return opid

    def dma(self, eng, fn, deps=()):
        return self.op(eng, fn, deps, dma=True)

    def emit(self, nc, block, sems):
        val = {}
        for eng in ENGINES:
            cc = dc = 0
            for fn, deps, is_dma, opid in self.ops[eng]:
                if is_dma:
                    dc += 16
                    val[opid] = (sems[(eng, "d")], dc)
                else:
                    cc += 1
                    val[opid] = (sems[(eng, "c")], cc)

        def section(eng):
            def body(e):
                seen = {}
                dseen = 0
                for fn, deps, is_dma, opid in self.ops[eng]:
                    if is_dma:
                        sd, vd = val[opid]
                        if vd - 16 > 0 and dseen < vd - 16:
                            e.wait_ge(sd, vd - 16)
                            dseen = vd - 16
                            seen[sd.name] = max(seen.get(sd.name, -1), vd - 16)
                    for d in deps:
                        de, ddma = self.eng_of[d]
                        if de == eng == "tensor" and not ddma and not is_dma:
                            continue
                        if de == eng and ddma and is_dma:
                            continue
                        s, v = val[d]
                        if seen.get(s.name, -1) < v:
                            e.wait_ge(s, v)
                            seen[s.name] = v
                    inst = fn(e)
                    inst.then_inc(sems[(eng, "d" if is_dma else "c")], 16 if is_dma else 1)
            return body

        for eng in ENGINES:
            getattr(block, eng)(section(eng))


def build_nc(n_tiles=T_FULL // TT, debug=False):
    T = n_tiles * TT
    nc = bass.Bass("TRN2", target_bir_lowering=False)

    def din(name, shape, dt=F32):
        return nc.dram_tensor(name, shape, dt, kind="ExternalInput").ap()

    x_h = din("x", [T, C])
    w_h = {n: din("W" + n, [C, C]) for n in ["r", "k", "v", "g", "o", "fT"]}
    tmw1_h = din("tm_w1", [C, 160])
    w2aug_h = din("w2aug", [5 * 33, C])
    tdw1_h = din("td_w1", [C, 64])
    td2aug_h = din("td2aug", [65, C])
    tmaa_h = din("tmaa", [1, C])
    u_h = din("ufaaaa", [1, C])
    lnw_h = din("lnw", [1, C])
    lnb_h = din("lnb", [1, C])
    fb_h = din("fb", [1, C])
    lu_h = din("lora_up", [50, 4])
    ld_h = din("lora_down", [4, 100])
    da_h = din("down_aux", [100, C])
    uaT_h = din("up_auxT", [50, C])
    ident_h = din("ident", [128, 128])
    utri_h = din("utri", [128, 6 * 128])
    mask4_h = din("mask4", [128, 512], mybir.dt.uint8)
    ind2_h = din("ind2", [128, 2])

    out_h = nc.dram_tensor("out", [T, C], F32, kind="ExternalOutput").ap()
    dbg = {}
    if debug:
        for name, shape, dt in [
                ("d_xnfm", [128, NQ * TT], F32), ("d_sx", [128, NQ * TT], BF16),
                ("d_r", [128, NQ * TT], BF16), ("d_k", [128, NQ * TT], BF16),
                ("d_lw", [128, C], BF16), ("d_Er", [128, C], BF16),
                ("d_A", [128, H * 128], BF16), ("d_y", [128, C], F32),
                ("d_su", [128, H], F32), ("d_S", [128, H * K], F32),
                ("d_v", [128, C], BF16), ("d_yng", [128, NQ * TT], BF16),
                ("d_stm", [128, C], F32), ("d_weff", [128, NQ * C], BF16),
                ("d_g", [128, C], BF16), ("d_xi", [128, NQ * TT], BF16),
                ("d_Ktz", [128, H * 128], BF16), ("d_Khz", [128, H * 128], BF16)]:
            dbg[name] = nc.dram_tensor(name, shape, dt, kind="ExternalOutput").ap()

    p = Prog()
    dbg_toks = []

    def _dt(tok):
        dbg_toks.append(tok)
        return tok

    def sb(name, shape, dt):
        return nc.alloc_sbuf_tensor(name, shape, dt).ap()

    W_sb = {n: sb(f"W_{n}", [128, NQ * C], BF16) for n in ["r", "k", "v", "g", "o", "f"]}
    tmw1_sb = sb("tmw1_s", [128, NQ * 160], BF16)
    w2p = [sb(f"w2p{j}", [128, C], BF16) for j in range(3)]
    tdw1_sb = sb("tdw1_s", [128, NQ * 64], BF16)
    td2aug_sb = sb("td2aug_s", [65, C], BF16)
    tmaa_sb = sb("tmaa_s", [128, NQ], F32)
    u_sb = sb("ufa", [128, NQ], F32)
    lnwrow = sb("lnwrow_s", [1, C], BF16)
    lnbrow = sb("lnbrow_s", [1, C], BF16)
    fbrow = sb("fbrow", [1, C], BF16)
    lnw_bct = sb("lnw_bct", [128, C], BF16)
    lnb_bct = sb("lnb_bct", [128, C], BF16)
    ident = sb("ident_s", [128, 128], BF16)
    utri = sb("utri_s", [128, 6 * 128], BF16)
    mask4 = sb("mask4_s", [128, 512], mybir.dt.uint8)
    ind2 = sb("ind2_s", [128, 2], BF16)
    ones1 = sb("ones1", [1, 128], BF16)
    eps_sb = sb("eps_s", [128, 1], F32)

    x_tm = [sb("x_tm0", [128, C], F32), sb("x_tm1", [128, C], F32)]
    xn_fm = sb("xn_fm", [128, NQ * (TT + 1)], F32)
    sx_fm = sb("sx_fm", [128, NQ * TT], F32)
    xi_bf = sb("xi_bf", [128, NQ * TT], BF16)
    m_sb = [sb(f"mp{j}", [128, TT], BF16) for j in range(3)]
    tanhtd = sb("tanhtd", [65, TT], BF16)
    lw_tm = sb("lw_tm", [128, C], BF16)
    r_bf = sb("r_bf", [128, NQ * TT], BF16)
    k_bf = sb("k_bf", [128, NQ * TT], BF16)
    v_tm = sb("v_tm", [128, C], BF16)
    g_raw = sb("g_raw", [128, C], BF16)
    E1 = sb("E1_s", [128, 2 * C], BF16)
    E1f = E1[:].bitcast(F32)
    xi2 = E1[:, 0:NQ * TT]
    E2 = sb("E2_s", [128, C], F32)
    epl = sb("epl", [128, NQ], F32)
    cmf = sb("cmf", [128, NQ], F32)
    zc = sb("zc", [128, 1], F32)
    KtzA = sb("KtzA", [128, H * 128], BF16)
    KtzB = sb("KtzB", [128, H * 128], BF16)
    Ru = sb("Ru", [128, C], BF16)
    Rab = sb("Rab", [128, C], BF16)
    Khf = sb("Khf", [128, C], BF16)
    Khz = sb("Khz", [128, H * 128], BF16)
    A4 = sb("A4", [128, H * 128], BF16)
    su_sb = sb("su_s", [128, H], F32)
    Sz = [sb("SzA", [128, H * K], F32), sb("SzB", [128, H * K], F32)]
    S_bfz = sb("S_bfz", [128, H * K], BF16)
    y_tm = sb("y_tm", [128, C], F32)
    gn_sc = sb("gn_sc", [128, 4 * H], F32)
    gn_sc2 = sb("gn_sc2", [128, 3 * H], F32)
    ln_sc = sb("ln_sc", [128, 16], F32)
    yng_fm = sb("yng_fm", [128, NQ * TT], BF16)
    scr = sb("scr", [128, C], F32)
    scr2 = sb("scr2", [128, C], F32)
    sx_bf = sb("sx_bf", [128, NQ * TT], BF16)
    hi_tm = E1[:, C:2 * C]
    lo_tm = Rab[:]
    gl_b = Khf[:]
    glb_b = yng_fm[:]
    lnE = E2[:]
    rku = sx_bf[:]

    xnb = sb("xnb", [128, NQ * TT], BF16)
    gs_sb = g_raw
    dmix = E2[:].bitcast(BF16)[:, 0:NQ * TT]

    da_v = Sz[0][0:100, :]
    uaT_v = Sz[1][0:50, :]
    c1_v = A4[:].bitcast(F32)[0:4, :]
    c2_v = KtzA[:].bitcast(F32)[0:100, :]
    lu_v = Khz[:].bitcast(F32)[0:50, 0:4]
    ld_v = Khz[:].bitcast(F32)[0:4, 4:104]

    sbuf_used = nc.sbuf_base

    pb = [nc.alloc_psum_tensor(f"pb{i}", [128, 512], F32).ap() for i in range(8)]

    sems = {}
    sem_cms = []
    for e in ENGINES:
        for kind in ("c", "d"):
            cm = nc.semaphore(f"s_{e}_{kind}")
            sems[(e, kind)] = cm.__enter__()
            sem_cms.append(cm)

    blk = nc.Block()
    block = blk.__enter__()

    xnfm3 = xn_fm[:].rearrange("p (q c) -> p q c", q=NQ)
    sx3 = sx_fm[:].rearrange("p (q c) -> p q c", q=NQ)
    sxb3 = sx_bf[:].rearrange("p (q c) -> p q c", q=NQ)
    xnb3 = xnb[:].rearrange("p (q c) -> p q c", q=NQ)
    xi3 = xi_bf[:].rearrange("p (q c) -> p q c", q=NQ)
    xi23 = xi2.rearrange("p (q c) -> p q c", q=NQ)
    rb3 = r_bf[:].rearrange("p (q c) -> p q c", q=NQ)
    kb3 = k_bf[:].rearrange("p (q c) -> p q c", q=NQ)
    E1_3 = E1f.rearrange("p (q c) -> p q c", q=NQ)
    E2_3 = E2[:].rearrange("p (q c) -> p q c", q=NQ)
    KtzA4 = KtzA[:].rearrange("p (h c) -> p h c", h=H)
    KtzB4 = KtzB[:].rearrange("p (h c) -> p h c", h=H)
    Rab3 = Rab[:].rearrange("p (q c) -> p q c", q=NQ)
    rku3 = rku.rearrange("p (q c) -> p q c", q=NQ)
    yngfm3 = yng_fm[:].rearrange("p (q c) -> p q c", q=NQ)
    scr3 = scr[:].rearrange("p (q c) -> p q c", q=NQ)

    def wload(dst, src):
        return p.dma("gpsimd", lambda e, d=dst, s=src: e.dma_start(out=d, in_=s))

    t_ident = wload(ident[:], ident_h)
    t_utri = wload(utri[:], utri_h)
    t_mask4 = wload(mask4[:], mask4_h)
    t_ind2 = wload(ind2[:], ind2_h)
    t_zc = p.op("gpsimd", lambda e: e.memset(zc[:], 0.0))
    t_tmaa = p.dma("sync", lambda e: e.dma_start(out=tmaa_sb[:], in_=tmaa_h.rearrange("o (a p) -> p (o a)", p=128), allow_slow_non_contiguous=True))
    t_u = p.dma("sync", lambda e: e.dma_start(out=u_sb[:], in_=u_h.rearrange("o (a p) -> p (o a)", p=128), allow_slow_non_contiguous=True))
    t_tmw1 = wload(tmw1_sb[:].rearrange("p (a c) -> p a c", a=NQ), tmw1_h.rearrange("(a p) c -> p a c", p=128))
    t_tdw1 = wload(tdw1_sb[:].rearrange("p (a c) -> p a c", a=NQ), tdw1_h.rearrange("(a p) c -> p a c", p=128))
    t_td2 = wload(td2aug_sb[:], td2aug_h)
    t_lnw = wload(lnwrow[:], lnw_h)
    t_lnb = wload(lnbrow[:], lnb_h)
    t_fb = wload(fbrow[:], fb_h)
    t_w2z = [p.op("gpsimd", lambda e, j=j: e.memset(w2p[j][:], 0.0)) for j in range(3)]
    t_w2 = []
    for i in range(5):
        j, base = i // 2, (i % 2) * 64
        t_w2.append(p.dma("gpsimd", lambda e, i=i, j=j, base=base: e.dma_start(
            out=w2p[j][base:base + 33, :], in_=w2aug_h[33 * i:33 * (i + 1), :]),
            deps=[t_w2z[j]]))
    t_ones = p.op("gpsimd", lambda e: e.memset(ones1[:], 1.0))
    t_eps = p.op("gpsimd", lambda e: e.memset(eps_sb[:], EPS))
    t_mz = [p.op("gpsimd", lambda e, j=j: e.memset(m_sb[j][:], 0.0)) for j in range(3)]
    t_mones = []
    for i in range(5):
        j, base = i // 2, (i % 2) * 64
        t_mones.append(p.op("gpsimd", lambda e, j=j, base=base: e.memset(m_sb[j][base + 32:base + 33, :], 1.0),
                            deps=[t_mz[j]]))
    t_tdones = p.op("gpsimd", lambda e: e.memset(tanhtd[64:65, :], 1.0))
    t_W = {}
    for n in ["r", "k", "v", "g", "o", "fT"]:
        key = "f" if n == "fT" else n
        t_W[key] = wload(W_sb[key][:].rearrange("p (a c) -> p a c", a=NQ),
                         w_h[n].rearrange("(a p) c -> p a c", p=128))
    t_lu = p.dma("sync", lambda e: e.dma_start(out=lu_v, in_=lu_h))
    t_ld = p.dma("sync", lambda e: e.dma_start(out=ld_v, in_=ld_h))
    t_da = p.dma("sync", lambda e: e.dma_start(out=da_v, in_=da_h))
    t_uaT = p.dma("sync", lambda e: e.dma_start(out=uaT_v, in_=uaT_h))

    mm = [None] * 4
    mm[0] = p.op("tensor", lambda e: e.matmul(pb[0][:], lhsT=ones1[:], rhs=lnwrow[:, 0:512], start=True, stop=True), deps=[t_ones, t_lnw])
    mm[1] = p.op("tensor", lambda e: e.matmul(pb[1][:], lhsT=ones1[:], rhs=lnwrow[:, 512:1024], start=True, stop=True))
    mm[2] = p.op("tensor", lambda e: e.matmul(pb[2][:], lhsT=ones1[:], rhs=lnbrow[:, 0:512], start=True, stop=True), deps=[t_lnb])
    mm[3] = p.op("tensor", lambda e: e.matmul(pb[3][:], lhsT=ones1[:], rhs=lnbrow[:, 512:1024], start=True, stop=True))
    ev = [None] * 4
    ev[0] = p.op("vector", lambda e: e.tensor_copy(lnw_bct[:, 0:512], pb[0][:]), deps=[mm[0]])
    ev[1] = p.op("vector", lambda e: e.tensor_copy(lnw_bct[:, 512:1024], pb[1][:]), deps=[mm[1]])
    ev[2] = p.op("vector", lambda e: e.tensor_copy(lnb_bct[:, 0:512], pb[2][:]), deps=[mm[2]])
    ev[3] = p.op("vector", lambda e: e.tensor_copy(lnb_bct[:, 512:1024], pb[3][:]), deps=[mm[3]])

    c1a = p.op("tensor", lambda e: e.matmul(pb[4][0:4, :], lhsT=lu_v, rhs=uaT_v[:, 0:512], start=True, stop=True), deps=[t_lu, t_uaT])
    c1b = p.op("tensor", lambda e: e.matmul(pb[5][0:4, :], lhsT=lu_v, rhs=uaT_v[:, 512:1024], start=True, stop=True))
    c1e = p.op("vector", lambda e: e.tensor_copy(c1_v[:, 0:512], pb[4][0:4, :]), deps=[c1a])
    c1f = p.op("vector", lambda e: e.tensor_copy(c1_v[:, 512:1024], pb[5][0:4, :]), deps=[c1b])
    c2a = p.op("tensor", lambda e: e.matmul(pb[6][0:100, :], lhsT=ld_v, rhs=c1_v[:, 0:512], start=True, stop=True), deps=[t_ld, c1e])
    c2b = p.op("tensor", lambda e: e.matmul(pb[7][0:100, :], lhsT=ld_v, rhs=c1_v[:, 512:1024], start=True, stop=True), deps=[c1f])
    c2e = p.op("vector", lambda e: e.tensor_copy(c2_v[:, 0:512], pb[6][0:100, :]), deps=[c2a])
    c2f = p.op("vector", lambda e: e.tensor_copy(c2_v[:, 512:1024], pb[7][0:100, :]), deps=[c2b])
    bank_free = {0: ev[0], 1: ev[1], 2: ev[2], 3: ev[3], 4: c1e, 5: c1f, 6: c2e, 7: c2f}
    weff_toks = []
    for q in range(NQ):
        b0 = (q % 2) * 2
        mm_a = p.op("tensor", lambda e, q=q, b0=b0: e.matmul(
            pb[b0][:], lhsT=da_v[:, q * 128:(q + 1) * 128], rhs=c2_v[:, 0:512], start=True, stop=True),
            deps=[t_da, c2e, c2f, bank_free[b0]])
        mm_b = p.op("tensor", lambda e, q=q, b0=b0: e.matmul(
            pb[b0 + 1][:], lhsT=da_v[:, q * 128:(q + 1) * 128], rhs=c2_v[:, 512:1024], start=True, stop=True),
            deps=[bank_free[b0 + 1]])
        we_a = p.op("vector", lambda e, q=q, b0=b0: e.tensor_tensor(
            out=W_sb["f"][:, q * C:q * C + 512], in0=pb[b0][:], in1=W_sb["f"][:, q * C:q * C + 512], op=OP.add),
            deps=[mm_a, t_W["f"]])
        we_b = p.op("vector", lambda e, q=q, b0=b0: e.tensor_tensor(
            out=W_sb["f"][:, q * C + 512:(q + 1) * C], in0=pb[b0 + 1][:], in1=W_sb["f"][:, q * C + 512:(q + 1) * C], op=OP.add),
            deps=[mm_b])
        bank_free[b0] = we_a
        bank_free[b0 + 1] = we_b
        weff_toks += [we_a, we_b]
    if debug:
        dbg_toks.append(p.dma("gpsimd", lambda e: e.dma_start(out=dbg["d_weff"], in_=W_sb["f"][:]), deps=weff_toks))

    lil_done = weff_toks
    t_zA4 = p.op("gpsimd", lambda e: e.memset(A4[:], 0.0), deps=[c2a, c2b])
    t_zKtzA = p.op("gpsimd", lambda e: e.memset(KtzA[:], 0.0), deps=lil_done)
    t_zKtzB = p.op("gpsimd", lambda e: e.memset(KtzB[:], 0.0), deps=lil_done)
    t_zKhz = p.op("gpsimd", lambda e: e.memset(Khz[:], 0.0), deps=lil_done)
    t_zS0 = p.op("gpsimd", lambda e: e.memset(Sz[0][:], 0.0), deps=lil_done)
    t_zS1 = p.op("gpsimd", lambda e: e.memset(Sz[1][:], 0.0), deps=lil_done)
    t_zSbf = p.op("gpsimd", lambda e: e.memset(S_bfz[:], 0.0))

    dep = {
        "xnfm_r": [], "sx_r": [], "xi_r": [], "lw_r": [], "rk_r": [], "vg_r": [],
        "hi_r": [], "lo_r": [], "scr_r": [], "scr2_r": [], "xnb_r": [], "stm_r": [],
        "y_r": [], "su_r": [], "A4_r": [t_zA4], "KtzA_r": [t_zKtzA], "KtzB_r": [t_zKtzB],
        "Ru_r": [], "Rab_r": [], "Khf_r": [], "Khz_r": [t_zKhz],
        "E1_r": [], "E2_r": [], "cmf_r": [], "epl_r": [], "gs_r": [], "gn_r": [],
        "yngfm_r": [], "m_r": [[] for _ in range(5)], "ttd_r": [],
        "out_dma": [None, None],
        "Sz_w": [t_zS0, t_zS1], "Sz_r": [[], []], "Sbf_ready": t_zSbf,
    }

    MAGIC = 0x5f3759df
    I32 = mybir.dt.int32

    def emit_rsqrt(dst, var_ap, scratch, deps, niter=4):
        n = var_ap.shape[-1]
        ve = scratch[:, 0:n]
        h_ = scratch[:, n:2 * n]
        t_ = scratch[:, 2 * n:3 * n]
        tok = p.op("vector", lambda e: e.tensor_single_scalar(out=ve, in_=var_ap, scalar=EPS, op=OP.add), deps=deps)
        tok = p.op("vector", lambda e: e.tensor_single_scalar(out=h_, in_=ve, scalar=-0.5, op=OP.mult), deps=[tok])
        vi = ve.bitcast(I32)
        di = dst.bitcast(I32)
        tok = p.op("vector", lambda e: e.tensor_single_scalar(out=di, in_=vi, scalar=1, op=OP.arith_shift_right), deps=[tok])
        tok = p.op("vector", lambda e: e.tensor_single_scalar(out=di, in_=di, scalar=-1, op=OP.bitwise_xor), deps=[tok])
        tok = p.op("vector", lambda e: e.tensor_single_scalar(out=di, in_=di, scalar=MAGIC + 1, op=OP.add), deps=[tok])
        for _ in range(niter):
            tok = p.op("vector", lambda e: e.tensor_tensor(out=t_, in0=dst, in1=dst, op=OP.mult), deps=[tok])
            tok = p.op("vector", lambda e: e.tensor_tensor(out=t_, in0=t_, in1=h_, op=OP.mult), deps=[tok])
            tok = p.op("vector", lambda e: e.scalar_tensor_tensor(out=dst, in0=t_, scalar=1.5, in1=dst, op0=OP.add, op1=OP.mult), deps=[tok])
        return tok

    out_tokens = []

    def bank_mm(bi, fn, deps=()):
        return p.op("tensor", fn, deps=list(deps) + [bank_free[bi]])

    GUARD_KEYS = ["xnfm_r", "sx_r", "xi_r", "lw_r", "rk_r", "vg_r", "hi_r", "lo_r",
                  "scr_r", "scr2_r", "xnb_r", "stm_r", "y_r", "su_r", "A4_r", "KtzA_r", "KtzB_r",
                  "Ru_r", "Rab_r", "Khf_r", "Khz_r", "E1_r", "E2_r", "cmf_r",
                  "epl_r", "gs_r", "gn_r", "yngfm_r", "ttd_r"]
    TS = {}

    def PF1(t):
        if debug and dbg_toks:
            for kk_ in GUARD_KEYS:
                dep[kk_] = dep[kk_] + dbg_toks
            dep["Sz_r"][0] = dep["Sz_r"][0] + dbg_toks
            dep["Sz_r"][1] = dep["Sz_r"][1] + dbg_toks
            dbg_toks.clear()
        S = TS.setdefault(t, {})
        xb = t % 2
        xt = x_tm[xb]

        t_x = p.dma("sync", lambda e, t=t, xt=xt: e.dma_start(
            out=xt[:], in_=x_h[t * TT:(t + 1) * TT, :]), deps=[dep["out_dma"][xb]])

        t_r1 = p.op("vector", lambda e, xt=xt: e.tensor_reduce(
            out=ln_sc[:, 0:1], in_=xt[:], axis=AX.X, op=OP.add), deps=[t_x])
        t_sq = p.op("scalar", lambda e, xt=xt: e.activation(
            out=lnE, in_=xt[:], func=AF.Square), deps=[t_x] + dep["E2_r"])
        dep["E2_r"] = []
        t_r2 = p.op("vector", lambda e: e.tensor_reduce(
            out=ln_sc[:, 1:2], in_=lnE, axis=AX.X, op=OP.add), deps=[t_sq])
        t_m = p.op("vector", lambda e: e.tensor_scalar_mul(ln_sc[:, 2:3], ln_sc[:, 0:1], 1.0 / C), deps=[t_r1])
        t_msq = p.op("vector", lambda e: e.tensor_tensor(
            out=ln_sc[:, 3:4], in0=ln_sc[:, 2:3], in1=ln_sc[:, 2:3], op=OP.mult), deps=[t_m])
        t_var = p.op("vector", lambda e: e.scalar_tensor_tensor(
            out=ln_sc[:, 4:5], in0=ln_sc[:, 1:2], scalar=1.0 / C, in1=ln_sc[:, 3:4],
            op0=OP.mult, op1=OP.subtract), deps=[t_r2, t_msq])
        t_ln1 = p.op("scalar", lambda e: e.activation(
            out=ln_sc[:, 5:6], in_=ln_sc[:, 4:5], func=AF.Ln, bias=eps_sb[:]),
            deps=[t_var, t_eps])
        t_rstd = p.op("scalar", lambda e: e.activation(
            out=ln_sc[:, 6:7], in_=ln_sc[:, 5:6], func=AF.Exp, scale=-0.5), deps=[t_ln1])
        t_xn = p.op("vector", lambda e, xt=xt: e.tensor_scalar(
            out=lnE, in0=xt[:], scalar1=ln_sc[:, 2:3], scalar2=ln_sc[:, 6:7],
            op0=OP.subtract, op1=OP.mult), deps=[t_rstd, t_r2])
        t_hi = p.op("scalar", lambda e: e.activation(
            out=hi_tm, in_=lnE, func=AF.Copy), deps=[t_xn] + dep["hi_r"] + dep["E1_r"])
        t_lo = p.op("vector", lambda e: e.tensor_tensor(
            out=lo_tm, in0=lnE, in1=hi_tm, op=OP.subtract), deps=[t_hi] + dep["lo_r"] + dep["Rab_r"])
        dep["E2_r"] = [t_hi, t_lo]

        S["t_hi"] = t_hi
        S["t_lo"] = t_lo
        S["xt"] = xt
        S["xb"] = xb

    def PF2(t):
        S = TS[t]
        t_hi = S["t_hi"]
        t_lo = S["t_lo"]

        if t > 0:
            t_bnd = p.op("vector", lambda e: e.tensor_copy(xnfm3[:, :, 0:1], xnfm3[:, :, TT:TT + 1]),
                         deps=dep["xnfm_r"])
        else:
            t_bnd = p.op("vector", lambda e: e.memset(xnfm3[:, :, 0:1], 0.0), deps=dep["xnfm_r"])
        dep["xnfm_r"] = []
        mms = []
        for q in range(NQ):
            bi = q // 4
            bank_mm(bi, lambda e, q=q, bi=bi: e.matmul(
                pb[bi][:, (q % 4) * 128:(q % 4) * 128 + 128],
                lhsT=hi_tm[:, q * 128:(q + 1) * 128], rhs=ident[:], start=True, stop=False),
                deps=[t_hi, t_ident])
            mm2 = p.op("tensor", lambda e, q=q, bi=bi: e.matmul(
                pb[bi][:, (q % 4) * 128:(q % 4) * 128 + 128],
                lhsT=lo_tm[:, q * 128:(q + 1) * 128], rhs=ident[:], start=False, stop=True),
                deps=[t_lo])
            mms.append(mm2)
        dep["hi_r"] = [mms[-1]]
        dep["lo_r"] = [mms[-1]]
        dep["E1_r"].append(mms[-1])
        dep["Rab_r"].append(mms[-1])
        xn_ev = []
        for half in range(2):
            tok = p.op("scalar", lambda e, half=half: e.copy(
                xnfm3[:, 4 * half:4 * half + 4, 1:1 + TT],
                pb[half][:].rearrange("p (q c) -> p q c", q=4)),
                deps=[mms[4 * half + 3], t_bnd])
            bank_free[half] = tok
            xn_ev.append(tok)

        S["xn_ev"] = xn_ev

        t_xnbp = p.op("vector", lambda e: e.tensor_copy(xnb3, xnfm3[:, :, 1:TT + 1]),
                      deps=xn_ev + dep["xnb_r"])
        dep["xnb_r"] = []
        dep["xnfm_r"].append(t_xnbp)
        t_sx = p.op("vector", lambda e: e.tensor_tensor(
            out=sxb3, in0=xnfm3[:, :, 0:TT], in1=xnfm3[:, :, 1:TT + 1], op=OP.subtract),
            deps=xn_ev + dep["sx_r"])
        dep["sx_r"] = []
        dep["xnfm_r"].append(t_sx)
        t_xxx = []
        for q in range(NQ):
            tok = p.op("vector", lambda e, q=q: e.scalar_tensor_tensor(
                out=xi3[:, q, :], in0=sxb3[:, q, :], scalar=tmaa_sb[:, q:q + 1],
                in1=xnb3[:, q, :], op0=OP.mult, op1=OP.add),
                deps=[t_sx, t_tmaa, t_xnbp] + dep["xi_r"])
            t_xxx.append(tok)
            dep["xnb_r"].append(tok)
            dep["sx_r"].append(tok)
        dep["xi_r"] = []

        mm_last = {}
        for part in range(2):
            cols = 128 if part == 0 else 32
            for q in range(NQ):
                tok = bank_mm(4, lambda e, part=part, q=q, cols=cols: e.matmul(
                    pb[4][0:cols, part * 128:part * 128 + TT],
                    lhsT=tmw1_sb[:, q * 160 + part * 128:q * 160 + part * 128 + cols],
                    rhs=xi_bf[:, q * TT:(q + 1) * TT],
                    start=(q == 0), stop=(q == NQ - 1)),
                    deps=[t_tmw1, t_xxx[q]])
                if q == NQ - 1:
                    mm_last[part] = tok
        m_toks = []
        for i in range(5):
            part = 0 if i < 4 else 1
            rs = (i * 32) % 128
            j, base = i // 2, (i % 2) * 64
            tok = p.op("scalar", lambda e, part=part, rs=rs, j=j, base=base: e.activation(
                out=m_sb[j][base:base + 32, :], in_=pb[4][rs:rs + 32, part * 128:part * 128 + TT],
                func=AF.Tanh),
                deps=[mm_last[0], mm_last[1], t_mones[i]] + dep["m_r"][i])
            dep["m_r"][i] = []
            m_toks.append(tok)
        bank_free[4] = m_toks[4]

        S["m_toks"] = m_toks
        S["t_sx"] = t_sx
        S["t_xnbp"] = t_xnbp

    def PF3(t):
        S = TS[t]
        m_toks = S["m_toks"]
        t_sx = S["t_sx"]
        t_xnbp = S["t_xnbp"]
        xn_ev = S["xn_ev"]

        proj_ev = {"r": [], "k": [], "v": [], "g": []}
        lw_toks = []

        def emit_delta(i):
            jm, base = i // 2, (i % 2) * 64
            db0 = (i % 2) * 2
            dmm = []
            for q in range(NQ):
                bi = db0 + q // 4
                tok = bank_mm(bi, lambda e, q=q, bi=bi, jm=jm, base=base: e.matmul(
                    pb[bi][:, (q % 4) * 128:(q % 4) * 128 + 128],
                    lhsT=w2p[jm][base:base + 64, q * 128:(q + 1) * 128],
                    rhs=m_sb[jm][base:base + 64, :], start=True, stop=True),
                    deps=[t_w2[i], m_toks[i]])
                dmm.append(tok)
                dep["m_r"][i].append(tok)
            return dmm

        next_dmm = emit_delta(0)
        prev_xi = None
        for i in range(5):
            db0 = (i % 2) * 2
            dmm = next_dmm
            tmix = []
            for half in range(2):
                bi = db0 + half
                tok = p.op("vector", lambda e, bi=bi, half=half: e.tensor_tensor(
                    out=dmix[:, half * 512:(half + 1) * 512], in0=sx_bf[:, half * 512:(half + 1) * 512],
                    in1=pb[bi][:], op=OP.mult),
                    deps=[dmm[4 * half + 3], t_sx] + (dep["E2_r"] if i == 0 else [prev_xi]))
                bank_free[bi] = tok
                tmix.append(tok)
                dep["sx_r"].append(tok)
            if i == 0:
                dep["E2_r"] = []
            xcur = xi_bf[:] if (i % 2 == 0) else xi2
            xkey = "xi_r" if (i % 2 == 0) else "E1_r"
            xi_tok = p.op("vector", lambda e, xcur=xcur: e.tensor_tensor(
                out=xcur, in0=dmix[:], in1=xnb[:], op=OP.add),
                deps=tmix + [t_xnbp] + dep[xkey])
            dep["xnb_r"].append(xi_tok)
            prev_xi = xi_tok
            dep[xkey] = []
            if i < 4:
                next_dmm = emit_delta(i + 1)
            if i == 4:
                dep["E2_r"] = [xi_tok]
            if debug and t == 0 and i == 3:
                tok_dxi = p.dma("gpsimd", lambda e: e.dma_start(out=dbg["d_xi"], in_=xi_bf[:]), deps=[xi_tok])
                dep["xi_r"].append(_dt(tok_dxi))

            if i == 0:
                last = None
                for q in range(NQ):
                    last = bank_mm(5, lambda e, q=q: e.matmul(
                        pb[5][0:64, 0:TT], lhsT=tdw1_sb[:, q * 64:(q + 1) * 64],
                        rhs=xi_bf[:, q * TT:(q + 1) * TT], start=(q == 0), stop=(q == NQ - 1)),
                        deps=[t_tdw1, xi_tok])
                    dep["xi_r"].append(last)
                t_tanh = p.op("scalar", lambda e: e.activation(
                    out=tanhtd[0:64, :], in_=pb[5][0:64, 0:TT], func=AF.Tanh),
                    deps=[last, t_tdones] + dep["ttd_r"])
                dep["ttd_r"] = []
                bank_free[5] = t_tanh
                wmm = []
                for half in range(2):
                    bi = 6 + half
                    tok = bank_mm(bi, lambda e, half=half, bi=bi: e.matmul(
                        pb[bi][:], lhsT=tanhtd[:],
                        rhs=td2aug_sb[:, half * 512:(half + 1) * 512], start=True, stop=True),
                        deps=[t_td2, t_tanh])
                    wmm.append(tok)
                    dep["ttd_r"].append(tok)
                for half in range(2):
                    tok_lw = p.op("scalar", lambda e, half=half: e.activation(
                        out=lw_tm[:, half * 512:(half + 1) * 512], in_=pb[6 + half][:], func=AF.Exp),
                        deps=[wmm[half]] + dep["lw_r"])
                    bank_free[6 + half] = tok_lw
                    lw_toks.append(tok_lw)
                dep["lw_r"] = []
            elif i in (1, 3):
                name = "k" if i == 1 else "r"
                dst = k_bf if i == 1 else r_bf
                mm_l = {}
                for qo in range(NQ):
                    bi = 4 + qo // 4
                    for qi in range(NQ):
                        tok = bank_mm(bi, lambda e, name=name, qo=qo, qi=qi, bi=bi: e.matmul(
                            pb[bi][:, (qo % 4) * 128:(qo % 4) * 128 + 128],
                            lhsT=W_sb[name][:, qi * C + qo * 128:qi * C + qo * 128 + 128],
                            rhs=xi2[:, qi * TT:(qi + 1) * TT],
                            start=(qi == 0), stop=(qi == NQ - 1)),
                            deps=[t_W[name], xi_tok])
                        dep["E1_r"].append(tok)
                        if qi == NQ - 1:
                            mm_l[qo] = tok
                for half in range(2):
                    bi = 4 + half
                    tok = p.op("scalar", lambda e, half=half, bi=bi, dst=dst: e.copy(
                        dst[:, half * 4 * TT:(half + 1) * 4 * TT], pb[bi][:]),
                        deps=[mm_l[4 * half + 0], mm_l[4 * half + 1], mm_l[4 * half + 2], mm_l[4 * half + 3]]
                        + (dep["rk_r"] if i == 1 and half == 0 else []))
                    bank_free[bi] = tok
                    proj_ev[name].append(tok)
                if i == 3:
                    dep["rk_r"] = []
            else:
                name = "v" if i == 2 else "g"
                dst = v_tm if i == 2 else g_raw
                mm_l = {}
                for half in range(2):
                    bi = 6 + half
                    for qi in range(NQ):
                        tok = bank_mm(bi, lambda e, name=name, half=half, qi=qi, bi=bi: e.matmul(
                            pb[bi][:],
                            lhsT=xi_bf[:, qi * TT:(qi + 1) * TT],
                            rhs=W_sb[name][:, qi * C + half * 512:qi * C + half * 512 + 512],
                            start=(qi == 0), stop=(qi == NQ - 1)),
                            deps=[t_W[name], xi_tok])
                        dep["xi_r"].append(tok)
                        if qi == NQ - 1:
                            mm_l[half] = tok
                for half in range(2):
                    bi = 6 + half
                    tok = p.op("scalar", lambda e, half=half, bi=bi, dst=dst: e.copy(
                        dst[:, half * 512:(half + 1) * 512], pb[bi][:]),
                        deps=[mm_l[half]] + (dep["vg_r"] if i == 2 and half == 0 else []))
                    bank_free[bi] = tok
                    proj_ev[name].append(tok)
                if i == 4:
                    dep["vg_r"] = []

        if debug and t == 0:
            dep["xnfm_r"].append(_dt(p.dma("sync", lambda e: e.dma_start(out=dbg["d_xnfm"], in_=xnfm3[:, :, 1:TT + 1]), deps=xn_ev)))
            dep["sx_r"].append(_dt(p.dma("sync", lambda e: e.dma_start(out=dbg["d_sx"], in_=sx_bf[:]), deps=[t_sx])))
            dep["rk_r"].append(_dt(p.dma("gpsimd", lambda e: e.dma_start(out=dbg["d_r"], in_=r_bf[:]), deps=proj_ev["r"])))
            dep["rk_r"].append(_dt(p.dma("gpsimd", lambda e: e.dma_start(out=dbg["d_k"], in_=k_bf[:]), deps=proj_ev["k"])))
            dep["vg_r"].append(_dt(p.dma("gpsimd", lambda e: e.dma_start(out=dbg["d_v"], in_=v_tm[:]), deps=proj_ev["v"])))
            dep["vg_r"].append(_dt(p.dma("gpsimd", lambda e: e.dma_start(out=dbg["d_g"], in_=g_raw[:]), deps=proj_ev["g"])))
            dep["lw_r"].append(_dt(p.dma("gpsimd", lambda e: e.dma_start(out=dbg["d_lw"], in_=lw_tm[:]), deps=lw_toks)))

        S["proj_ev"] = proj_ev
        S["lw_toks"] = lw_toks

    def PB1(t):
        S = TS[t]
        proj_ev = S["proj_ev"]
        lw_toks = S["lw_toks"]
        t_xnbp = S["t_xnbp"]

        wave1 = {}
        for q in range(NQ):
            m_ex = bank_mm(q // 4, lambda e, q=q: e.matmul(
                pb[q // 4][:, (q % 4) * 128:(q % 4) * 128 + 128],
                lhsT=lw_tm[:, q * 128:(q + 1) * 128],
                rhs=utri[:, 0:128], start=True, stop=True),
                deps=[lw_toks[q // 4], t_utri])
            m_ia = bank_mm(2 + q // 4, lambda e, q=q: e.matmul(
                pb[2 + q // 4][:, (q % 4) * 128:(q % 4) * 128 + 128],
                lhsT=lw_tm[:, q * 128:(q + 1) * 128],
                rhs=utri[:, 4 * 128:5 * 128], start=True, stop=True))
            m_ib = bank_mm(4 + q // 4, lambda e, q=q: e.matmul(
                pb[4 + q // 4][:, (q % 4) * 128:(q % 4) * 128 + 128],
                lhsT=lw_tm[:, q * 128:(q + 1) * 128],
                rhs=utri[:, 5 * 128:6 * 128], start=True, stop=True))
            wave1[q] = (m_ex, m_ia, m_ib)
            dep["lw_r"] += [m_ex, m_ia, m_ib]
        eRu_t = [p.op("scalar", lambda e, hf=hf: e.activation(
            out=E1f[:, hf * 512:(hf + 1) * 512], in_=pb[hf][:], func=AF.Exp, scale=-1.0),
            deps=[wave1[4 * hf + 3][0]] + dep["E1_r"]) for hf in range(2)]
        eA_t = [p.op("scalar", lambda e, hf=hf: e.activation(
            out=E2_3[:, 4 * hf:4 * hf + 4, 0:64],
            in_=pb[2 + hf][:].rearrange("p (qq c) -> p qq c", qq=4)[:, :, 0:64],
            func=AF.Exp, scale=1.0),
            deps=[wave1[4 * hf + 3][1]] + dep["E2_r"]) for hf in range(2)]
        negf = [p.op("scalar", lambda e, q=q: e.activation(
            out=cmf[:, q:q + 1], in_=pb[4 + q // 4][:, (q % 4) * 128 + 127:(q % 4) * 128 + 128],
            func=AF.Copy, scale=-1.0), deps=[wave1[(q // 4) * 4 + 3][2]] + dep["cmf_r"]) for q in range(NQ)]
        Khf3 = Khf[:].rearrange("p (q c) -> p q c", q=NQ)
        ekhf = [p.op("scalar", lambda e, q=q: e.activation(
            out=Khf3[:, q, :], in_=pb[4 + q // 4][:, (q % 4) * 128:(q % 4) * 128 + 128],
            func=AF.Exp, bias=cmf[:, q:q + 1]),
            deps=[negf[q]] + (dep["Khf_r"] if q == 0 else [])) for q in range(NQ)]
        dep["E1_r"] = []
        dep["E2_r"] = []
        dep["cmf_r"] = ekhf[:]
        dep["Khf_r"] = []
        t_ru = p.op("gpsimd", lambda e: e.tensor_tensor(
            out=Ru[:].rearrange("p (q c) -> p q c", q=NQ), in0=rb3, in1=E1_3, op=OP.mult),
            deps=eRu_t + dep["Ru_r"] + proj_ev["r"])
        dep["Ru_r"] = []
        t_khf = p.op("gpsimd", lambda e: e.tensor_tensor(
            out=Khf3, in0=kb3, in1=Khf3, op=OP.mult),
            deps=ekhf + proj_ev["k"])
        ktzA_t = []
        for pr in range(2):
            ktzA_t.append(p.op("gpsimd", lambda e, pr=pr: e.tensor_tensor(
                out=KtzA4[pr * 64:pr * 64 + 64, pr::2, 0:64],
                in0=kb3[pr * 64:pr * 64 + 64, :, 0:64],
                in1=E2_3[pr * 64:pr * 64 + 64, :, 0:64], op=OP.mult),
                deps=eA_t + dep["KtzA_r"] + proj_ev["k"]))
        dep["KtzA_r"] = []
        eB_t = [p.op("scalar", lambda e, hf=hf: e.activation(
            out=E1f[:, hf * 512:(hf + 1) * 512], in_=pb[4 + hf][:], func=AF.Exp, scale=1.0),
            deps=[t_ru, wave1[4 * hf + 3][2]]) for hf in range(2)]
        bank_free[0] = eRu_t[0]
        bank_free[1] = eRu_t[1]
        bank_free[2] = eA_t[0]
        bank_free[3] = eA_t[1]
        bank_free[4] = eB_t[0]
        bank_free[5] = eB_t[1]
        ktzB_t = []
        for pr in range(2):
            ktzB_t.append(p.op("gpsimd", lambda e, pr=pr: e.tensor_tensor(
                out=KtzB4[pr * 64:pr * 64 + 64, pr::2, :],
                in0=kb3[pr * 64:pr * 64 + 64, :, :],
                in1=E1_3[pr * 64:pr * 64 + 64, :, :], op=OP.mult),
                deps=eB_t + dep["KtzB_r"] + proj_ev["k"]))
        dep["KtzB_r"] = []
        wave2 = {}
        for q in range(NQ):
            m_xa = bank_mm(q // 4, lambda e, q=q: e.matmul(
                pb[q // 4][:, (q % 4) * 128:(q % 4) * 128 + 128],
                lhsT=lw_tm[:, q * 128:(q + 1) * 128],
                rhs=utri[:, 2 * 128:3 * 128], start=True, stop=True))
            m_xb = bank_mm(2 + q // 4, lambda e, q=q: e.matmul(
                pb[2 + q // 4][:, (q % 4) * 128:(q % 4) * 128 + 128],
                lhsT=lw_tm[:, q * 128:(q + 1) * 128],
                rhs=utri[:, 3 * 128:4 * 128], start=True, stop=True))
            m_in = bank_mm(4 + q // 4, lambda e, q=q: e.matmul(
                pb[4 + q // 4][:, (q % 4) * 128:(q % 4) * 128 + 128],
                lhsT=lw_tm[:, q * 128:(q + 1) * 128],
                rhs=utri[:, 1 * 128:2 * 128], start=True, stop=True))
            wave2[q] = (m_xa, m_xb, m_in)
            dep["lw_r"] += [m_xa, m_xb, m_in]
        eRA_t = [p.op("scalar", lambda e, hf=hf: e.activation(
            out=E2_3[:, 4 * hf:4 * hf + 4, 0:64],
            in_=pb[hf][:].rearrange("p (qq c) -> p qq c", qq=4)[:, :, 0:64],
            func=AF.Exp, scale=-1.0),
            deps=[wave2[4 * hf + 3][0]] + ktzA_t) for hf in range(2)]
        eRB_t = [p.op("scalar", lambda e, hf=hf: e.activation(
            out=E1_3[:, 4 * hf:4 * hf + 4, 64:128],
            in_=pb[2 + hf][:].rearrange("p (qq c) -> p qq c", qq=4)[:, :, 64:128],
            func=AF.Exp, scale=-1.0),
            deps=[wave2[4 * hf + 3][1], t_khf] + ktzB_t) for hf in range(2)]
        epl_toks = []
        for q in range(NQ):
            epl_toks.append(p.op("scalar", lambda e, q=q: e.activation(
                out=epl[:, q:q + 1], in_=pb[4 + q // 4][:, (q % 4) * 128 + 127:(q % 4) * 128 + 128],
                func=AF.Exp, scale=-1.0),
                deps=[wave2[(q // 4) * 4 + 3][2]] + dep["epl_r"]))
        dep["epl_r"] = []
        bank_free[0] = eRA_t[0]
        bank_free[1] = eRA_t[1]
        bank_free[2] = eRB_t[0]
        bank_free[3] = eRB_t[1]
        bank_free[4] = epl_toks[3]
        bank_free[5] = epl_toks[7]
        t_ra = p.op("gpsimd", lambda e: e.tensor_tensor(
            out=Rab3[:, :, 0:64], in0=rb3[:, :, 0:64], in1=E2_3[:, :, 0:64], op=OP.mult),
            deps=eRA_t + dep["Rab_r"] + proj_ev["r"])
        t_rb = p.op("gpsimd", lambda e: e.tensor_tensor(
            out=Rab3[:, :, 64:128], in0=rb3[:, :, 64:128], in1=E1_3[:, :, 64:128], op=OP.mult),
            deps=eRB_t + dep["Rab_r"] + proj_ev["r"])
        dep["Rab_r"] = []
        t_rku1 = p.op("gpsimd", lambda e: e.tensor_tensor(
            out=rku3, in0=rb3, in1=kb3, op=OP.mult),
            deps=proj_ev["r"] + proj_ev["k"] + dep["sx_r"])
        t_rku2 = p.op("gpsimd", lambda e: e.tensor_tensor(
            out=rku3, in0=rku3,
            in1=u_sb[:].rearrange("p q -> p q ()").broadcast_to((128, NQ, 128)), op=OP.mult),
            deps=[t_rku1, t_u])

        mm_f = {}
        for half in range(2):
            bi = 6 + half
            for qi in range(NQ):
                tok = bank_mm(bi, lambda e, half=half, qi=qi, bi=bi: e.matmul(
                    pb[bi][:],
                    lhsT=xnb[:, qi * TT:(qi + 1) * TT],
                    rhs=W_sb["f"][:, qi * C + half * 512:qi * C + half * 512 + 512],
                    start=(qi == 0), stop=False),
                    deps=[t_xnbp] + (weff_toks if t == 0 else []))
                dep["xnb_r"].append(tok)
            mm_f[half] = p.op("tensor", lambda e, half=half, bi=bi: e.matmul(
                pb[bi][:], lhsT=ones1[:], rhs=fbrow[:, half * 512:(half + 1) * 512],
                start=False, stop=True), deps=[t_fb, t_ones])
            bank_free[bi] = None

        kht_mm = {}
        for q in range(NQ):
            bi2 = 2 + q // 4
            kht_mm[q] = bank_mm(bi2, lambda e, q=q, bi2=bi2: e.matmul(
                pb[bi2][:, (q % 4) * 128:(q % 4) * 128 + 128],
                lhsT=Khf[:, q * 128:(q + 1) * 128], rhs=ident[:], start=True, stop=True),
                deps=[t_khf])
        khz_toks = {}
        for q in range(NQ):
            bi2 = 2 + q // 4
            for pr in range(2):
                h = 2 * q + pr
                khz_toks[h] = p.op("scalar", lambda e, q=q, bi2=bi2, pr=pr, h=h: e.activation(
                    out=Khz[:, h * 128 + pr * 64:h * 128 + pr * 64 + 64],
                    in_=pb[bi2][:, (q % 4) * 128 + pr * 64:(q % 4) * 128 + pr * 64 + 64],
                    func=AF.Copy),
                    deps=[kht_mm[(q // 4) * 4 + 3]] + dep["Khz_r"])
        dep["Khz_r"] = []
        bank_free[2] = khz_toks[7]
        bank_free[3] = khz_toks[15]

        a_mm = {}
        for h in range(H):
            bi = h // 4
            q = h // 2
            mA = bank_mm(bi, lambda e, h=h, bi=bi, q=q: e.matmul(
                pb[bi][:, (h % 4) * 128:(h % 4) * 128 + 64],
                lhsT=KtzA[:, h * 128:(h + 1) * 128],
                rhs=Rab[:, q * 128:q * 128 + 64],
                start=True, stop=True),
                deps=[ktzA_t[h % 2], t_ra])
            mB = bank_mm(bi, lambda e, h=h, bi=bi, q=q: e.matmul(
                pb[bi][:, (h % 4) * 128 + 64:(h % 4) * 128 + 128],
                lhsT=KtzB[:, h * 128:(h + 1) * 128],
                rhs=Rab[:, q * 128 + 64:q * 128 + 128],
                start=True, stop=True),
                deps=[ktzB_t[h % 2], t_rb])
            a_mm[h] = (mA, mB)
        a_ev = []
        for bi in range(4):
            tok = p.op("vector", lambda e, bi=bi: e.copy_predicated(
                out=A4[:, bi * 512:(bi + 1) * 512], mask=mask4[:], data=pb[bi][:]),
                deps=[a_mm[4 * bi + 3][1], t_mask4] + dep["A4_r"])
            bank_free[bi] = tok
            a_ev.append(tok)
        dep["A4_r"] = []

        su_mm = []
        for q in range(NQ):
            su_mm.append(bank_mm(0, lambda e, q=q: e.matmul(
                pb[0][:, 2 * q:2 * q + 2], lhsT=rku[:, q * 128:(q + 1) * 128],
                rhs=ind2[:], start=True, stop=True),
                deps=[t_rku2, t_ind2]))
        for tk in su_mm:
            dep["sx_r"].append(tk)
        t_su = p.op("vector", lambda e: e.tensor_copy(su_sb[:], pb[0][:, 0:H]), deps=su_mm + dep["su_r"])
        dep["su_r"] = []
        bank_free[0] = t_su


        if debug and t == 0:
            tok_der = p.dma("gpsimd", lambda e: e.dma_start(out=dbg["d_Er"], in_=Ru[:]), deps=[t_ru])
            dep["Ru_r"].append(_dt(tok_der))
            dep["A4_r"].append(_dt(p.dma("gpsimd", lambda e: e.dma_start(out=dbg["d_A"], in_=A4[:]), deps=a_ev)))
            dep["su_r"].append(_dt(p.dma("sync", lambda e: e.dma_start(out=dbg["d_su"], in_=su_sb[:]), deps=[t_su])))
            dep["KtzA_r"].append(_dt(p.dma("gpsimd", lambda e: e.dma_start(out=dbg["d_Ktz"], in_=KtzA[:]), deps=ktzA_t)))
            dep["Khz_r"].append(_dt(p.dma("gpsimd", lambda e: e.dma_start(out=dbg["d_Khz"], in_=Khz[:]), deps=list(khz_toks.values()))))

        dep["KtzA_r"] = dep["KtzA_r"] + [a_mm[h][0] for h in range(H)]
        dep["KtzB_r"] = [a_mm[h][1] for h in range(H)]
        dep["Rab_r"] = dep["Rab_r"] + [a_mm[h][0] for h in range(H)] + [a_mm[h][1] for h in range(H)]
        dep["Khf_r"] = list(kht_mm.values())
        dep["E1_r"] = dep["E1_r"] + [t_rb]
        dep["E2_r"] = [t_ra]
        dep["cmf_r"] = [t_khf]
        dep["rk_r"] = [t_ru, t_ra, t_rb, t_khf, t_rku1] + ktzA_t + ktzB_t
        S["t_su"] = t_su
        S["t_ru"] = t_ru
        S["t_khf"] = t_khf
        S["epl_toks"] = epl_toks
        S["khz_toks"] = khz_toks
        S["a_ev"] = a_ev
        S["a_mm"] = a_mm
        S["mm_f"] = mm_f

    def PB2a(t):
        S = TS[t]
        proj_ev = S["proj_ev"]
        t_su = S["t_su"]
        t_ru = S["t_ru"]
        t_khf = S["t_khf"]
        epl_toks = S["epl_toks"]
        khz_toks = S["khz_toks"]
        a_ev = S["a_ev"]
        a_mm = S["a_mm"]
        mm_f = S["mm_f"]
        par = t % 2
        Scur, Snxt = Sz[par], Sz[1 - par]

        t_vsu = p.op("gpsimd", lambda e: e.tensor_tensor(
            out=scr2[:].rearrange("p (h c) -> p h c", h=H),
            in0=v_tm[:].rearrange("p (h c) -> p h c", h=H),
            in1=su_sb[:].rearrange("p h -> p h ()").broadcast_to((128, H, K)),
            op=OP.mult),
            deps=[t_su] + proj_ev["v"] + dep["scr2_r"])
        dep["scr2_r"] = []
        t_sdec = p.op("gpsimd", lambda e, Scur=Scur, Snxt=Snxt: e.tensor_tensor(
            out=Snxt[:].rearrange("p (q c) -> p q c", q=NQ),
            in0=Scur[:].rearrange("p (q c) -> p q c", q=NQ),
            in1=epl[:].rearrange("p q -> p q ()").broadcast_to((128, NQ, 128)),
            op=OP.mult),
            deps=epl_toks + dep["Sz_r"][1 - par] + [dep["Sz_w"][1 - par], dep["Sz_w"][par]])
        dep["Sz_r"][1 - par] = []

        y_mm = {}
        s_mm = {}
        intra_mms = []
        for h in range(H):
            by = 4 + h // 8
            tok1 = bank_mm(by, lambda e, h=h, by=by: e.matmul(
                pb[by][:, (h % 8) * 64:(h % 8) * 64 + 64],
                lhsT=A4[:, h * 128:(h + 1) * 128],
                rhs=v_tm[:, h * K:h * K + K],
                start=True, stop=False), deps=[a_ev[h // 4]] + proj_ev["v"])
            intra_mms.append(tok1)
            y_mm[h] = p.op("tensor", lambda e, h=h, by=by: e.matmul(
                pb[by][:, (h % 8) * 64:(h % 8) * 64 + 64],
                lhsT=Ru[:, (h // 2) * 128:(h // 2) * 128 + 128],
                rhs=S_bfz[:, h * K:(h + 1) * K],
                start=False, stop=True), deps=[dep["Sbf_ready"], t_ru])
            bs = 2 + h // 8
            s_mm[h] = bank_mm(bs, lambda e, h=h, bs=bs: e.matmul(
                pb[bs][:, (h % 8) * 64:(h % 8) * 64 + 64],
                lhsT=Khz[:, h * 128:(h + 1) * 128],
                rhs=v_tm[:, h * K:h * K + K],
                start=True, stop=True), deps=[khz_toks[h]] + proj_ev["v"])
        yev = []
        sev = []
        for half in range(2):
            tok = p.op("vector", lambda e, half=half: e.tensor_tensor(
                out=y_tm[:, half * 512:half * 512 + 512],
                in0=scr2[:, half * 512:half * 512 + 512],
                in1=pb[4 + half][:], op=OP.add),
                deps=[y_mm[half * 8 + 7], t_vsu] + dep["y_r"])
            yev.append(tok)
            dep["scr2_r"].append(tok)
            tok2 = p.op("vector", lambda e, half=half, Snxt=Snxt: e.tensor_tensor(
                out=Snxt[:, half * 512:half * 512 + 512],
                in0=Snxt[:, half * 512:half * 512 + 512],
                in1=pb[2 + half][:], op=OP.add),
                deps=[s_mm[half * 8 + 7], t_sdec])
            sev.append(tok2)
        dep["y_r"] = []
        dep["Sz_r"][par] = [t_sdec]
        dep["Sz_r"][1 - par] = []
        dep["Sz_w"][1 - par] = sev[-1]
        bank_free[4] = yev[0]
        bank_free[5] = yev[1]
        bank_free[2] = sev[0]
        bank_free[3] = sev[1]
        t_sbf = p.op("vector", lambda e, Snxt=Snxt: e.tensor_copy(S_bfz[:], Snxt[:]),
                     deps=sev + list(y_mm.values()))
        dep["Sbf_ready"] = t_sbf
        dep["Sz_r"][1 - par].append(t_sbf)

        t_f2 = p.op("scalar", lambda e: e.activation(
            out=scr[:, 0:512], in_=pb[6][:], func=AF.Silu), deps=[mm_f[0]] + dep["scr_r"])
        t_f2b = p.op("scalar", lambda e: e.activation(
            out=scr[:, 512:1024], in_=pb[7][:], func=AF.Silu), deps=[mm_f[1]])
        bank_free[6] = t_f2
        bank_free[7] = t_f2b
        dep["scr_r"] = []

        dep["Ru_r"] = list(y_mm.values())
        dep["Khz_r"] = list(s_mm.values())
        dep["epl_r"] = [t_sdec, t_khf]
        dep["su_r"] = [t_vsu]
        dep["A4_r"] = dep["A4_r"] + intra_mms
        S["yev"] = yev
        S["sev"] = sev
        S["t_vsu"] = t_vsu
        S["t_f2"] = t_f2
        S["t_f2b"] = t_f2b

    def PB2b(t):
        S = TS[t]
        proj_ev = S["proj_ev"]
        yev = S["yev"]
        sev = S["sev"]
        t_vsu = S["t_vsu"]
        par = t % 2
        Snxt = Sz[1 - par]

        yg3 = y_tm[:].rearrange("p (h c) -> p h c", h=H)
        t_gm = p.op("vector", lambda e: e.tensor_reduce(
            out=gn_sc[:, 0:H], in_=yg3, axis=AX.X, op=OP.add), deps=yev + dep["gn_r"])
        dep["gn_r"] = []
        t_gsq = p.op("scalar", lambda e: e.activation(
            out=scr2[:], in_=y_tm[:], func=AF.Square), deps=yev + dep["scr2_r"])
        t_gr2 = p.op("vector", lambda e: e.tensor_reduce(
            out=gn_sc[:, H:2 * H], in_=scr2[:].rearrange("p (h c) -> p h c", h=H),
            axis=AX.X, op=OP.add), deps=[t_gsq])
        t_gmm = p.op("vector", lambda e: e.tensor_scalar_mul(gn_sc[:, 0:H], gn_sc[:, 0:H], 1.0 / K), deps=[t_gm])
        t_gm2 = p.op("vector", lambda e: e.tensor_tensor(
            out=gn_sc[:, 2 * H:3 * H], in0=gn_sc[:, 0:H], in1=gn_sc[:, 0:H], op=OP.mult), deps=[t_gmm])
        t_gvar = p.op("vector", lambda e: e.scalar_tensor_tensor(
            out=gn_sc[:, H:2 * H], in0=gn_sc[:, H:2 * H], scalar=1.0 / K,
            in1=gn_sc[:, 2 * H:3 * H], op0=OP.mult, op1=OP.subtract), deps=[t_gr2, t_gm2])
        t_gl1 = p.op("scalar", lambda e: e.activation(
            out=gn_sc2[:, 0:H], in_=gn_sc[:, H:2 * H], func=AF.Ln, bias=eps_sb[:]),
            deps=[t_gvar, t_eps])
        t_grs = p.op("scalar", lambda e: e.activation(
            out=gn_sc[:, 3 * H:4 * H], in_=gn_sc2[:, 0:H], func=AF.Exp, scale=-0.5), deps=[t_gl1])
        t_yc = p.op("gpsimd", lambda e: e.tensor_tensor(
            out=yg3, in0=yg3,
            in1=gn_sc[:, 0:H].rearrange("p h -> p h ()").broadcast_to((128, H, K)),
            op=OP.subtract), deps=[t_gmm, t_gsq])
        t_yn = p.op("gpsimd", lambda e: e.tensor_tensor(
            out=yg3, in0=yg3,
            in1=gn_sc[:, 3 * H:4 * H].rearrange("p h -> p h ()").broadcast_to((128, H, K)),
            op=OP.mult), deps=[t_grs, t_yc])
        dep["scr2_r"] = [t_gr2]
        dep["y_r"] = [t_yn]
        dep["gn_r"] = [t_yc, t_yn]

        if debug and t == 1:
            dep["y_r"].append(_dt(p.dma("sync", lambda e: e.dma_start(out=dbg["d_y"], in_=y_tm[:]), deps=[t_yn])))
            dep["Sz_r"][1 - par].append(_dt(p.dma("sync", lambda e: e.dma_start(out=dbg["d_S"], in_=Snxt[:]), deps=[sev[-1]])))

        t_gs = p.op("scalar", lambda e: e.activation(
            out=gs_sb[:], in_=g_raw[:], func=AF.Silu),
            deps=proj_ev["g"] + dep["gs_r"])
        dep["gs_r"] = []
        t_gl = p.op("gpsimd", lambda e: e.tensor_tensor(
            out=gl_b, in0=gs_sb[:], in1=lnw_bct[:], op=OP.mult),
            deps=[t_gs, ev[0], ev[1]] + dep["Khf_r"])
        dep["Khf_r"] = []
        t_glb = p.op("gpsimd", lambda e: e.tensor_tensor(
            out=glb_b, in0=gs_sb[:], in1=lnb_bct[:], op=OP.mult),
            deps=[t_gs, ev[2], ev[3]] + dep["yngfm_r"])
        dep["vg_r"] = [t_gl, t_glb, t_vsu]
        t_y1 = p.op("gpsimd", lambda e: e.tensor_tensor(
            out=scr2[:], in0=y_tm[:], in1=gl_b, op=OP.mult),
            deps=[t_gl, t_yn] + dep["scr2_r"])
        t_y2 = p.op("gpsimd", lambda e: e.tensor_tensor(
            out=y_tm[:].bitcast(BF16)[:, 0:C],
            in0=scr2[:], in1=glb_b, op=OP.add),
            deps=[t_glb, t_y1] + dep["y_r"])
        dep["Khf_r"].append(t_y1)
        dep["scr2_r"] = [t_y2]
        dep["y_r"] = [t_y2]

        S["t_y2"] = t_y2

    def PB3(t):
        S = TS[t]
        proj_ev = S["proj_ev"]
        t_y2 = S["t_y2"]
        t_f2 = S["t_f2"]
        t_f2b = S["t_f2b"]
        xt = S["xt"]
        xb = S["xb"]

        mms2 = []
        for q in range(NQ):
            bi = q // 4
            mms2.append(bank_mm(bi, lambda e, q=q, bi=bi: e.matmul(
                pb[bi][:, (q % 4) * 128:(q % 4) * 128 + 128],
                lhsT=y_tm[:].bitcast(BF16)[:, q * 128:(q + 1) * 128],
                rhs=ident[:], start=True, stop=True),
                deps=[t_y2, t_ident]))
        yngfm_ev = []
        for half in range(2):
            tok = p.op("scalar", lambda e, half=half: e.copy(
                yngfm3[:, 4 * half:4 * half + 4, :],
                pb[half][:].rearrange("p (q c) -> p q c", q=4)),
                deps=[mms2[4 * half + 3]] + (dep["yngfm_r"] if half == 0 else []))
            bank_free[half] = tok
            yngfm_ev.append(tok)
        dep["yngfm_r"] = []
        dep["y_r"] = dep["y_r"] + mms2

        if debug and t == 0:
            dep["yngfm_r"].append(_dt(p.dma("gpsimd", lambda e: e.dma_start(out=dbg["d_yng"], in_=yng_fm[:]), deps=yngfm_ev)))

        mm_l = {}
        for half in range(2):
            bi = 4 + half
            for qi in range(NQ):
                tok = bank_mm(bi, lambda e, half=half, qi=qi, bi=bi: e.matmul(
                    pb[bi][:],
                    lhsT=yng_fm[:, qi * TT:(qi + 1) * TT],
                    rhs=W_sb["o"][:, qi * C + half * 512:qi * C + half * 512 + 512],
                    start=(qi == 0), stop=(qi == NQ - 1)),
                    deps=[t_W["o"], yngfm_ev[0], yngfm_ev[1]])
                dep["yngfm_r"].append(tok)
                if qi == NQ - 1:
                    mm_l[half] = tok
        stm_ev = []
        for half in range(2):
            bi = 4 + half
            tok = p.op("scalar", lambda e, half=half, bi=bi: e.copy(
                sx_fm[:, half * 512:(half + 1) * 512], pb[bi][:]),
                deps=[mm_l[half]] + (dep["stm_r"] if half == 0 else []))
            bank_free[bi] = tok
            stm_ev.append(tok)
        dep["stm_r"] = stm_ev[:]

        if debug and t == 0:
            dep["stm_r"].append(_dt(p.dma("sync", lambda e: e.dma_start(out=dbg["d_stm"], in_=sx_fm[:, 0:C]), deps=stm_ev)))

        t_f3 = p.op("vector", lambda e: e.tensor_tensor(
            out=scr2[:], in0=scr[:], in1=sx_fm[:, 0:C], op=OP.mult),
            deps=[t_f2, t_f2b, stm_ev[0], stm_ev[1]] + dep["scr2_r"])
        dep["stm_r"].append(t_f3)
        t_out = p.op("vector", lambda e, xt=xt: e.tensor_tensor(
            out=xt[:], in0=scr2[:], in1=xt[:], op=OP.add), deps=[t_f3])
        dep["scr_r"] = [t_f3]
        dep["scr2_r"] = [t_out]
        t_st = p.dma("scalar", lambda e, t=t, xt=xt: e.dma_start(
            out=out_h[t * TT:(t + 1) * TT, :], in_=xt[:]), deps=[t_out])
        dep["out_dma"][xb] = t_st
        out_tokens.append(t_st)

    PF1(0)
    PF2(0)
    PF3(0)
    for t in range(n_tiles):
        PB1(t)
        if t + 1 < n_tiles:
            PF1(t + 1)
        PB2a(t)
        if t + 1 < n_tiles:
            PF2(t + 1)
        PB2b(t)
        if t + 1 < n_tiles:
            PF3(t + 1)
        PB3(t)

    p.op("sync", lambda e: e.nop(), deps=out_tokens)
    p.emit(nc, block, sems)
    blk.__exit__(None, None, None)
    return nc, sbuf_used


def host_inputs(inp, b):
    f32 = np.float32
    w2 = np.asarray(inp["L0_attn_tm_w2"], f32)
    maa = np.asarray(inp["L0_attn_time_maa_wkvrg"], f32)
    w2aug = np.concatenate([np.concatenate([w2[i], maa[i:i + 1]], axis=0) for i in range(5)], axis=0)
    td2aug = np.concatenate([np.asarray(inp["L0_attn_td_w2"], f32),
                             np.asarray(inp["L0_attn_time_decay"], f32)[None, :]], axis=0)
    ident = np.eye(128, dtype=f32)
    jj = np.arange(128)
    uexc = (jj[:, None] < jj[None, :]).astype(f32)
    uinc = (jj[:, None] <= jj[None, :]).astype(f32)
    pre32 = (jj[:, None] < 32).astype(f32) * np.ones((1, 128), f32)
    pre96 = (jj[:, None] < 96).astype(f32) * np.ones((1, 128), f32)
    utri = np.concatenate([uexc, uinc, uexc - pre32, uexc - pre96,
                           uinc - pre32, uinc - pre96], axis=1)
    mask4 = np.tile(uexc, (1, 4)).astype(np.uint8)
    ind2 = np.zeros((128, 2), f32)
    ind2[0:64, 0] = 1.0
    ind2[64:128, 1] = 1.0
    return {
        "x": np.ascontiguousarray(np.asarray(inp["L0_attn_x"], f32)[b]),
        "Wr": np.ascontiguousarray(np.asarray(inp["L0_attn_Wr"], f32)),
        "Wk": np.ascontiguousarray(np.asarray(inp["L0_attn_Wk"], f32)),
        "Wv": np.ascontiguousarray(np.asarray(inp["L0_attn_Wv"], f32)),
        "Wg": np.ascontiguousarray(np.asarray(inp["L0_attn_Wg"], f32)),
        "Wo": np.ascontiguousarray(np.asarray(inp["L0_attn_Wo"], f32)),
        "WfT": np.ascontiguousarray(np.asarray(inp["L0_ffn_W"], f32).T),
        "tm_w1": np.ascontiguousarray(np.asarray(inp["L0_attn_tm_w1"], f32)),
        "w2aug": np.ascontiguousarray(w2aug),
        "td_w1": np.ascontiguousarray(np.asarray(inp["L0_attn_td_w1"], f32)),
        "td2aug": np.ascontiguousarray(td2aug),
        "tmaa": np.ascontiguousarray(np.asarray(inp["L0_attn_time_maa_x"], f32)[None, :]),
        "ufaaaa": np.ascontiguousarray(np.asarray(inp["L0_attn_time_faaaa"], f32).reshape(1, -1)),
        "lnw": np.ascontiguousarray(np.asarray(inp["L0_attn_ln_w"], f32)[None, :]),
        "lnb": np.ascontiguousarray(np.asarray(inp["L0_attn_ln_b"], f32)[None, :]),
        "fb": np.ascontiguousarray(np.asarray(inp["L0_ffn_b"], f32)[None, :]),
        "lora_up": np.ascontiguousarray(np.asarray(inp["L0_ffn_lora_up"], f32)),
        "lora_down": np.ascontiguousarray(np.asarray(inp["L0_ffn_lora_down"], f32)),
        "down_aux": np.ascontiguousarray(np.asarray(inp["L0_ffn_down_aux"], f32)),
        "up_auxT": np.ascontiguousarray(np.asarray(inp["L0_ffn_up_aux"], f32).T),
        "ident": ident, "utri": utri, "mask4": mask4, "ind2": ind2,
    }


def kernel(**inputs):
    nc, _ = build_nc(n_tiles=T_FULL // TT, debug=False)
    in_maps = [host_inputs(inputs, b) for b in range(8)]
    res = run_bass_kernel_spmd(nc, in_maps, core_ids=list(range(8)))
    out = np.stack([np.asarray(r["out"]) for r in res.results], axis=0)
    return out.astype(np.float32)


# version 39
# speedup vs baseline: 1.0402x; 1.0016x over previous
import numpy as np
import ml_dtypes
import concourse.bass as bass
import concourse.mybir as mybir
from concourse.bass_utils import run_bass_kernel_spmd
from concourse.alu_op_type import AluOpType as OP

F32 = mybir.dt.float32
BF16 = mybir.dt.bfloat16
AF = mybir.ActivationFunctionType
AX = mybir.AxisListType

T_FULL = 4096
C = 1024
H = 16
K = 64
TT = 128
NQ = 8
EPS = 1e-5

ENGINES = ("sync", "gpsimd", "tensor", "vector", "scalar")


class Prog:
    def __init__(self):
        self.ops = {e: [] for e in ENGINES}
        self.n = 0
        self.eng_of = {}

    def op(self, eng, fn, deps=(), dma=False):
        opid = self.n
        self.n += 1
        deps = [d for d in deps if d is not None]
        self.ops[eng].append((fn, deps, dma, opid))
        self.eng_of[opid] = (eng, dma)
        return opid

    def dma(self, eng, fn, deps=()):
        return self.op(eng, fn, deps, dma=True)

    def emit(self, nc, block, sems):
        val = {}
        for eng in ENGINES:
            cc = dc = 0
            for fn, deps, is_dma, opid in self.ops[eng]:
                if is_dma:
                    dc += 16
                    val[opid] = (sems[(eng, "d")], dc)
                else:
                    cc += 1
                    val[opid] = (sems[(eng, "c")], cc)

        def section(eng):
            def body(e):
                seen = {}
                dseen = 0
                for fn, deps, is_dma, opid in self.ops[eng]:
                    if is_dma:
                        sd, vd = val[opid]
                        if vd - 16 > 0 and dseen < vd - 16:
                            e.wait_ge(sd, vd - 16)
                            dseen = vd - 16
                            seen[sd.name] = max(seen.get(sd.name, -1), vd - 16)
                    for d in deps:
                        de, ddma = self.eng_of[d]
                        if de == eng == "tensor" and not ddma and not is_dma:
                            continue
                        if de == eng and ddma and is_dma:
                            continue
                        s, v = val[d]
                        if seen.get(s.name, -1) < v:
                            e.wait_ge(s, v)
                            seen[s.name] = v
                    inst = fn(e)
                    inst.then_inc(sems[(eng, "d" if is_dma else "c")], 16 if is_dma else 1)
            return body

        for eng in ENGINES:
            getattr(block, eng)(section(eng))


def build_nc(n_tiles=T_FULL // TT, debug=False):
    T = n_tiles * TT
    nc = bass.Bass("TRN2", target_bir_lowering=False)

    def din(name, shape, dt=F32):
        return nc.dram_tensor(name, shape, dt, kind="ExternalInput").ap()

    x_h = din("x", [T, C])
    w_h = {n: din("W" + n, [C, C]) for n in ["r", "k", "v", "g", "o", "fT"]}
    tmw1_h = din("tm_w1", [C, 160])
    w2aug_h = din("w2aug", [5 * 33, C])
    tdw1_h = din("td_w1", [C, 64])
    td2aug_h = din("td2aug", [65, C])
    tmaa_h = din("tmaa", [1, C])
    u_h = din("ufaaaa", [1, C])
    lnw_h = din("lnw", [1, C])
    lnb_h = din("lnb", [1, C])
    fb_h = din("fb", [1, C])
    lu_h = din("lora_up", [50, 4])
    ld_h = din("lora_down", [4, 100])
    da_h = din("down_aux", [100, C])
    uaT_h = din("up_auxT", [50, C])
    ident_h = din("ident", [128, 128])
    utri_h = din("utri", [128, 6 * 128])
    mask4_h = din("mask4", [128, 512], mybir.dt.uint8)
    ind2_h = din("ind2", [128, 2])

    out_h = nc.dram_tensor("out", [T, C], F32, kind="ExternalOutput").ap()
    dbg = {}
    if debug:
        for name, shape, dt in [
                ("d_xnfm", [128, NQ * TT], F32), ("d_sx", [128, NQ * TT], BF16),
                ("d_r", [128, NQ * TT], BF16), ("d_k", [128, NQ * TT], BF16),
                ("d_lw", [128, C], BF16), ("d_Er", [128, C], BF16),
                ("d_A", [128, H * 128], BF16), ("d_y", [128, C], F32),
                ("d_su", [128, H], F32), ("d_S", [128, H * K], F32),
                ("d_v", [128, C], BF16), ("d_yng", [128, NQ * TT], BF16),
                ("d_stm", [128, C], F32), ("d_weff", [128, NQ * C], BF16),
                ("d_g", [128, C], BF16), ("d_xi", [128, NQ * TT], BF16),
                ("d_Ktz", [128, H * 128], BF16), ("d_Khz", [128, H * 128], BF16)]:
            dbg[name] = nc.dram_tensor(name, shape, dt, kind="ExternalOutput").ap()

    p = Prog()
    dbg_toks = []

    def _dt(tok):
        dbg_toks.append(tok)
        return tok

    def sb(name, shape, dt):
        return nc.alloc_sbuf_tensor(name, shape, dt).ap()

    W_sb = {n: sb(f"W_{n}", [128, NQ * C], BF16) for n in ["r", "k", "v", "g", "o", "f"]}
    tmw1_sb = sb("tmw1_s", [128, NQ * 160], BF16)
    w2p = [sb(f"w2p{j}", [128, C], BF16) for j in range(3)]
    tdw1_sb = sb("tdw1_s", [128, NQ * 64], BF16)
    td2aug_sb = sb("td2aug_s", [65, C], BF16)
    tmaa_sb = sb("tmaa_s", [128, NQ], F32)
    u_sb = sb("ufa", [128, NQ], F32)
    lnwrow = sb("lnwrow_s", [1, C], BF16)
    lnbrow = sb("lnbrow_s", [1, C], BF16)
    fbrow = sb("fbrow", [1, C], BF16)
    lnw_bct = sb("lnw_bct", [128, C], BF16)
    lnb_bct = sb("lnb_bct", [128, C], BF16)
    ident = sb("ident_s", [128, 128], BF16)
    utri = sb("utri_s", [128, 6 * 128], BF16)
    mask4 = sb("mask4_s", [128, 512], mybir.dt.uint8)
    ind2 = sb("ind2_s", [128, 2], BF16)
    ind2u = sb("ind2u_s", [128, 2 * NQ], BF16)
    ones1 = sb("ones1", [1, 128], BF16)
    eps_sb = sb("eps_s", [128, 1], F32)

    x_tm = [sb("x_tm0", [128, C], F32), sb("x_tm1", [128, C], F32)]
    xn_fm = sb("xn_fm", [128, NQ * (TT + 1)], F32)
    sx_fm = sb("sx_fm", [128, NQ * TT], F32)
    xi_bf = sb("xi_bf", [128, NQ * TT], BF16)
    m_sb = [sb(f"mp{j}", [128, TT], BF16) for j in range(3)]
    tanhtd = sb("tanhtd", [65, TT], BF16)
    lw_tm = sb("lw_tm", [128, C], BF16)
    r_bf = sb("r_bf", [128, NQ * TT], BF16)
    k_bf = sb("k_bf", [128, NQ * TT], BF16)
    v_tm = sb("v_tm", [128, C], BF16)
    g_raw = sb("g_raw", [128, C], BF16)
    E1 = sb("E1_s", [128, 2 * C], BF16)
    E1f = E1[:].bitcast(F32)
    xi2 = E1[:, 0:NQ * TT]
    E2 = sb("E2_s", [128, C], F32)
    epl = sb("epl", [128, NQ], F32)
    cmf = sb("cmf", [128, NQ], F32)
    zc = sb("zc", [128, 1], F32)
    KtzA = sb("KtzA", [128, H * 128], BF16)
    KtzB = sb("KtzB", [128, H * 128], BF16)
    Ru = sb("Ru", [128, C], BF16)
    Rab = sb("Rab", [128, C], BF16)
    Khf = sb("Khf", [128, C], BF16)
    Khz = sb("Khz", [128, H * 128], BF16)
    A4 = sb("A4", [128, H * 128], BF16)
    su_sb = sb("su_s", [128, H], F32)
    Sz = [sb("SzA", [128, H * K], F32), sb("SzB", [128, H * K], F32)]
    S_bfz = sb("S_bfz", [128, H * K], BF16)
    y_tm = sb("y_tm", [128, C], F32)
    gn_sc = sb("gn_sc", [128, 4 * H], F32)
    gn_sc2 = sb("gn_sc2", [128, 3 * H], F32)
    ln_sc = sb("ln_sc", [128, 16], F32)
    yng_fm = sb("yng_fm", [128, NQ * TT], BF16)
    scr = sb("scr", [128, C], F32)
    scr2 = sb("scr2", [128, C], F32)
    sx_bf = sb("sx_bf", [128, NQ * TT], BF16)
    hi_tm = E1[:, C:2 * C]
    lo_tm = Rab[:]
    gl_b = Khf[:]
    glb_b = yng_fm[:]
    lnE = E2[:]
    rku = sx_bf[:]

    xnb = sb("xnb", [128, NQ * TT], BF16)
    gs_sb = g_raw
    dmix = E2[:].bitcast(BF16)[:, 0:NQ * TT]

    da_v = Sz[0][0:100, :]
    uaT_v = Sz[1][0:50, :]
    c1_v = A4[:].bitcast(F32)[0:4, :]
    c2_v = KtzA[:].bitcast(F32)[0:100, :]
    lu_v = Khz[:].bitcast(F32)[0:50, 0:4]
    ld_v = Khz[:].bitcast(F32)[0:4, 4:104]

    sbuf_used = nc.sbuf_base

    pb = [nc.alloc_psum_tensor(f"pb{i}", [128, 512], F32).ap() for i in range(8)]

    sems = {}
    sem_cms = []
    for e in ENGINES:
        for kind in ("c", "d"):
            cm = nc.semaphore(f"s_{e}_{kind}")
            sems[(e, kind)] = cm.__enter__()
            sem_cms.append(cm)

    blk = nc.Block()
    block = blk.__enter__()

    xnfm3 = xn_fm[:].rearrange("p (q c) -> p q c", q=NQ)
    sx3 = sx_fm[:].rearrange("p (q c) -> p q c", q=NQ)
    sxb3 = sx_bf[:].rearrange("p (q c) -> p q c", q=NQ)
    xnb3 = xnb[:].rearrange("p (q c) -> p q c", q=NQ)
    xi3 = xi_bf[:].rearrange("p (q c) -> p q c", q=NQ)
    xi23 = xi2.rearrange("p (q c) -> p q c", q=NQ)
    rb3 = r_bf[:].rearrange("p (q c) -> p q c", q=NQ)
    kb3 = k_bf[:].rearrange("p (q c) -> p q c", q=NQ)
    E1_3 = E1f.rearrange("p (q c) -> p q c", q=NQ)
    E2_3 = E2[:].rearrange("p (q c) -> p q c", q=NQ)
    KtzA4 = KtzA[:].rearrange("p (h c) -> p h c", h=H)
    KtzB4 = KtzB[:].rearrange("p (h c) -> p h c", h=H)
    Rab3 = Rab[:].rearrange("p (q c) -> p q c", q=NQ)
    rku3 = rku.rearrange("p (q c) -> p q c", q=NQ)
    yngfm3 = yng_fm[:].rearrange("p (q c) -> p q c", q=NQ)
    scr3 = scr[:].rearrange("p (q c) -> p q c", q=NQ)

    def wload(dst, src):
        return p.dma("gpsimd", lambda e, d=dst, s=src: e.dma_start(out=d, in_=s))

    t_ident = wload(ident[:], ident_h)
    t_utri = wload(utri[:], utri_h)
    t_mask4 = wload(mask4[:], mask4_h)
    t_ind2 = wload(ind2[:], ind2_h)
    t_zc = p.op("gpsimd", lambda e: e.memset(zc[:], 0.0))
    t_tmaa = p.dma("sync", lambda e: e.dma_start(out=tmaa_sb[:], in_=tmaa_h.rearrange("o (a p) -> p (o a)", p=128), allow_slow_non_contiguous=True))
    t_u = p.dma("sync", lambda e: e.dma_start(out=u_sb[:], in_=u_h.rearrange("o (a p) -> p (o a)", p=128), allow_slow_non_contiguous=True))
    t_iu = p.op("gpsimd", lambda e: e.tensor_tensor(
        out=ind2u[:].rearrange("p (q t) -> p q t", q=NQ),
        in0=u_sb[:].rearrange("p q -> p q ()").broadcast_to((128, NQ, 2)),
        in1=ind2[:].rearrange("p t -> p () t").broadcast_to((128, NQ, 2)),
        op=OP.mult), deps=[t_u, t_ind2])
    t_tmw1 = wload(tmw1_sb[:].rearrange("p (a c) -> p a c", a=NQ), tmw1_h.rearrange("(a p) c -> p a c", p=128))
    t_tdw1 = wload(tdw1_sb[:].rearrange("p (a c) -> p a c", a=NQ), tdw1_h.rearrange("(a p) c -> p a c", p=128))
    t_td2 = wload(td2aug_sb[:], td2aug_h)
    t_lnw = wload(lnwrow[:], lnw_h)
    t_lnb = wload(lnbrow[:], lnb_h)
    t_fb = wload(fbrow[:], fb_h)
    t_w2z = [p.op("gpsimd", lambda e, j=j: e.memset(w2p[j][:], 0.0)) for j in range(3)]
    t_w2 = []
    for i in range(5):
        j, base = i // 2, (i % 2) * 64
        t_w2.append(p.dma("gpsimd", lambda e, i=i, j=j, base=base: e.dma_start(
            out=w2p[j][base:base + 33, :], in_=w2aug_h[33 * i:33 * (i + 1), :]),
            deps=[t_w2z[j]]))
    t_ones = p.op("gpsimd", lambda e: e.memset(ones1[:], 1.0))
    t_eps = p.op("gpsimd", lambda e: e.memset(eps_sb[:], EPS))
    t_mz = [p.op("gpsimd", lambda e, j=j: e.memset(m_sb[j][:], 0.0)) for j in range(3)]
    t_mones = []
    for i in range(5):
        j, base = i // 2, (i % 2) * 64
        t_mones.append(p.op("gpsimd", lambda e, j=j, base=base: e.memset(m_sb[j][base + 32:base + 33, :], 1.0),
                            deps=[t_mz[j]]))
    t_tdones = p.op("gpsimd", lambda e: e.memset(tanhtd[64:65, :], 1.0))
    t_W = {}
    for n in ["r", "k", "v", "g", "o", "fT"]:
        key = "f" if n == "fT" else n
        t_W[key] = wload(W_sb[key][:].rearrange("p (a c) -> p a c", a=NQ),
                         w_h[n].rearrange("(a p) c -> p a c", p=128))
    t_lu = p.dma("sync", lambda e: e.dma_start(out=lu_v, in_=lu_h))
    t_ld = p.dma("sync", lambda e: e.dma_start(out=ld_v, in_=ld_h))
    t_da = p.dma("sync", lambda e: e.dma_start(out=da_v, in_=da_h))
    t_uaT = p.dma("sync", lambda e: e.dma_start(out=uaT_v, in_=uaT_h))

    mm = [None] * 4
    mm[0] = p.op("tensor", lambda e: e.matmul(pb[0][:], lhsT=ones1[:], rhs=lnwrow[:, 0:512], start=True, stop=True), deps=[t_ones, t_lnw])
    mm[1] = p.op("tensor", lambda e: e.matmul(pb[1][:], lhsT=ones1[:], rhs=lnwrow[:, 512:1024], start=True, stop=True))
    mm[2] = p.op("tensor", lambda e: e.matmul(pb[2][:], lhsT=ones1[:], rhs=lnbrow[:, 0:512], start=True, stop=True), deps=[t_lnb])
    mm[3] = p.op("tensor", lambda e: e.matmul(pb[3][:], lhsT=ones1[:], rhs=lnbrow[:, 512:1024], start=True, stop=True))
    ev = [None] * 4
    ev[0] = p.op("vector", lambda e: e.tensor_copy(lnw_bct[:, 0:512], pb[0][:]), deps=[mm[0]])
    ev[1] = p.op("vector", lambda e: e.tensor_copy(lnw_bct[:, 512:1024], pb[1][:]), deps=[mm[1]])
    ev[2] = p.op("vector", lambda e: e.tensor_copy(lnb_bct[:, 0:512], pb[2][:]), deps=[mm[2]])
    ev[3] = p.op("vector", lambda e: e.tensor_copy(lnb_bct[:, 512:1024], pb[3][:]), deps=[mm[3]])

    c1a = p.op("tensor", lambda e: e.matmul(pb[4][0:4, :], lhsT=lu_v, rhs=uaT_v[:, 0:512], start=True, stop=True), deps=[t_lu, t_uaT])
    c1b = p.op("tensor", lambda e: e.matmul(pb[5][0:4, :], lhsT=lu_v, rhs=uaT_v[:, 512:1024], start=True, stop=True))
    c1e = p.op("vector", lambda e: e.tensor_copy(c1_v[:, 0:512], pb[4][0:4, :]), deps=[c1a])
    c1f = p.op("vector", lambda e: e.tensor_copy(c1_v[:, 512:1024], pb[5][0:4, :]), deps=[c1b])
    c2a = p.op("tensor", lambda e: e.matmul(pb[6][0:100, :], lhsT=ld_v, rhs=c1_v[:, 0:512], start=True, stop=True), deps=[t_ld, c1e])
    c2b = p.op("tensor", lambda e: e.matmul(pb[7][0:100, :], lhsT=ld_v, rhs=c1_v[:, 512:1024], start=True, stop=True), deps=[c1f])
    c2e = p.op("vector", lambda e: e.tensor_copy(c2_v[:, 0:512], pb[6][0:100, :]), deps=[c2a])
    c2f = p.op("vector", lambda e: e.tensor_copy(c2_v[:, 512:1024], pb[7][0:100, :]), deps=[c2b])
    bank_free = {0: ev[0], 1: ev[1], 2: ev[2], 3: ev[3], 4: c1e, 5: c1f, 6: c2e, 7: c2f}
    weff_toks = []
    for q in range(NQ):
        b0 = (q % 2) * 2
        mm_a = p.op("tensor", lambda e, q=q, b0=b0: e.matmul(
            pb[b0][:], lhsT=da_v[:, q * 128:(q + 1) * 128], rhs=c2_v[:, 0:512], start=True, stop=True),
            deps=[t_da, c2e, c2f, bank_free[b0]])
        mm_b = p.op("tensor", lambda e, q=q, b0=b0: e.matmul(
            pb[b0 + 1][:], lhsT=da_v[:, q * 128:(q + 1) * 128], rhs=c2_v[:, 512:1024], start=True, stop=True),
            deps=[bank_free[b0 + 1]])
        we_a = p.op("vector", lambda e, q=q, b0=b0: e.tensor_tensor(
            out=W_sb["f"][:, q * C:q * C + 512], in0=pb[b0][:], in1=W_sb["f"][:, q * C:q * C + 512], op=OP.add),
            deps=[mm_a, t_W["f"]])
        we_b = p.op("vector", lambda e, q=q, b0=b0: e.tensor_tensor(
            out=W_sb["f"][:, q * C + 512:(q + 1) * C], in0=pb[b0 + 1][:], in1=W_sb["f"][:, q * C + 512:(q + 1) * C], op=OP.add),
            deps=[mm_b])
        bank_free[b0] = we_a
        bank_free[b0 + 1] = we_b
        weff_toks += [we_a, we_b]
    if debug:
        dbg_toks.append(p.dma("gpsimd", lambda e: e.dma_start(out=dbg["d_weff"], in_=W_sb["f"][:]), deps=weff_toks))

    lil_done = weff_toks
    t_zA4 = p.op("gpsimd", lambda e: e.memset(A4[:], 0.0), deps=[c2a, c2b])
    t_zKtzA = p.op("gpsimd", lambda e: e.memset(KtzA[:], 0.0), deps=lil_done)
    t_zKtzB = p.op("gpsimd", lambda e: e.memset(KtzB[:], 0.0), deps=lil_done)
    t_zKhz = p.op("gpsimd", lambda e: e.memset(Khz[:], 0.0), deps=lil_done)
    t_zS0 = p.op("gpsimd", lambda e: e.memset(Sz[0][:], 0.0), deps=lil_done)
    t_zS1 = p.op("gpsimd", lambda e: e.memset(Sz[1][:], 0.0), deps=lil_done)
    t_zSbf = p.op("gpsimd", lambda e: e.memset(S_bfz[:], 0.0))

    dep = {
        "xnfm_r": [], "sx_r": [], "xi_r": [], "lw_r": [], "rk_r": [], "vg_r": [],
        "hi_r": [], "lo_r": [], "scr_r": [], "scr2_r": [], "xnb_r": [], "stm_r": [],
        "y_r": [], "su_r": [], "A4_r": [t_zA4], "KtzA_r": [t_zKtzA], "KtzB_r": [t_zKtzB],
        "Ru_r": [], "Rab_r": [], "Khf_r": [], "Khz_r": [t_zKhz],
        "E1_r": [], "E2_r": [], "cmf_r": [], "epl_r": [], "gs_r": [], "gn_r": [],
        "yngfm_r": [], "m_r": [[] for _ in range(5)], "ttd_r": [],
        "out_dma": [None, None],
        "Sz_w": [t_zS0, t_zS1], "Sz_r": [[], []], "Sbf_ready": t_zSbf,
    }

    MAGIC = 0x5f3759df
    I32 = mybir.dt.int32

    def emit_rsqrt(dst, var_ap, scratch, deps, niter=4):
        n = var_ap.shape[-1]
        ve = scratch[:, 0:n]
        h_ = scratch[:, n:2 * n]
        t_ = scratch[:, 2 * n:3 * n]
        tok = p.op("vector", lambda e: e.tensor_single_scalar(out=ve, in_=var_ap, scalar=EPS, op=OP.add), deps=deps)
        tok = p.op("vector", lambda e: e.tensor_single_scalar(out=h_, in_=ve, scalar=-0.5, op=OP.mult), deps=[tok])
        vi = ve.bitcast(I32)
        di = dst.bitcast(I32)
        tok = p.op("vector", lambda e: e.tensor_single_scalar(out=di, in_=vi, scalar=1, op=OP.arith_shift_right), deps=[tok])
        tok = p.op("vector", lambda e: e.tensor_single_scalar(out=di, in_=di, scalar=-1, op=OP.bitwise_xor), deps=[tok])
        tok = p.op("vector", lambda e: e.tensor_single_scalar(out=di, in_=di, scalar=MAGIC + 1, op=OP.add), deps=[tok])
        for _ in range(niter):
            tok = p.op("vector", lambda e: e.tensor_tensor(out=t_, in0=dst, in1=dst, op=OP.mult), deps=[tok])
            tok = p.op("vector", lambda e: e.tensor_tensor(out=t_, in0=t_, in1=h_, op=OP.mult), deps=[tok])
            tok = p.op("vector", lambda e: e.scalar_tensor_tensor(out=dst, in0=t_, scalar=1.5, in1=dst, op0=OP.add, op1=OP.mult), deps=[tok])
        return tok

    out_tokens = []

    def bank_mm(bi, fn, deps=()):
        return p.op("tensor", fn, deps=list(deps) + [bank_free[bi]])

    GUARD_KEYS = ["xnfm_r", "sx_r", "xi_r", "lw_r", "rk_r", "vg_r", "hi_r", "lo_r",
                  "scr_r", "scr2_r", "xnb_r", "stm_r", "y_r", "su_r", "A4_r", "KtzA_r", "KtzB_r",
                  "Ru_r", "Rab_r", "Khf_r", "Khz_r", "E1_r", "E2_r", "cmf_r",
                  "epl_r", "gs_r", "gn_r", "yngfm_r", "ttd_r"]
    TS = {}

    def PF1(t):
        if debug and dbg_toks:
            for kk_ in GUARD_KEYS:
                dep[kk_] = dep[kk_] + dbg_toks
            dep["Sz_r"][0] = dep["Sz_r"][0] + dbg_toks
            dep["Sz_r"][1] = dep["Sz_r"][1] + dbg_toks
            dbg_toks.clear()
        S = TS.setdefault(t, {})
        xb = t % 2
        xt = x_tm[xb]

        t_x = p.dma("sync", lambda e, t=t, xt=xt: e.dma_start(
            out=xt[:], in_=x_h[t * TT:(t + 1) * TT, :]), deps=[dep["out_dma"][xb]])

        t_r1 = p.op("vector", lambda e, xt=xt: e.tensor_reduce(
            out=ln_sc[:, 0:1], in_=xt[:], axis=AX.X, op=OP.add), deps=[t_x])
        t_sq = p.op("scalar", lambda e, xt=xt: e.activation(
            out=lnE, in_=xt[:], func=AF.Square), deps=[t_x] + dep["E2_r"])
        dep["E2_r"] = []
        t_r2 = p.op("vector", lambda e: e.tensor_reduce(
            out=ln_sc[:, 1:2], in_=lnE, axis=AX.X, op=OP.add), deps=[t_sq])
        t_m = p.op("vector", lambda e: e.tensor_scalar_mul(ln_sc[:, 2:3], ln_sc[:, 0:1], 1.0 / C), deps=[t_r1])
        t_msq = p.op("vector", lambda e: e.tensor_scalar(
            out=ln_sc[:, 3:4], in0=ln_sc[:, 2:3], scalar1=ln_sc[:, 2:3], scalar2=1.0,
            op0=OP.mult, op1=OP.mult), deps=[t_m])
        t_var = p.op("vector", lambda e: e.scalar_tensor_tensor(
            out=ln_sc[:, 4:5], in0=ln_sc[:, 1:2], scalar=1.0 / C, in1=ln_sc[:, 3:4],
            op0=OP.mult, op1=OP.subtract), deps=[t_r2, t_msq])
        t_ln1 = p.op("scalar", lambda e: e.activation(
            out=ln_sc[:, 5:6], in_=ln_sc[:, 4:5], func=AF.Ln, bias=eps_sb[:]),
            deps=[t_var, t_eps])
        t_rstd = p.op("scalar", lambda e: e.activation(
            out=ln_sc[:, 6:7], in_=ln_sc[:, 5:6], func=AF.Exp, scale=-0.5), deps=[t_ln1])
        t_xn = p.op("vector", lambda e, xt=xt: e.tensor_scalar(
            out=lnE, in0=xt[:], scalar1=ln_sc[:, 2:3], scalar2=ln_sc[:, 6:7],
            op0=OP.subtract, op1=OP.mult), deps=[t_rstd, t_r2])
        t_hi = p.op("scalar", lambda e: e.activation(
            out=hi_tm, in_=lnE, func=AF.Copy), deps=[t_xn] + dep["hi_r"] + dep["E1_r"])
        t_lo = p.op("vector", lambda e: e.tensor_tensor(
            out=lo_tm, in0=lnE, in1=hi_tm, op=OP.subtract), deps=[t_hi] + dep["lo_r"] + dep["Rab_r"])
        dep["E2_r"] = [t_hi, t_lo]

        S["t_hi"] = t_hi
        S["t_lo"] = t_lo
        S["xt"] = xt
        S["xb"] = xb

    def PF2(t):
        S = TS[t]
        t_hi = S["t_hi"]
        t_lo = S["t_lo"]

        if t > 0:
            t_bnd = p.op("vector", lambda e: e.tensor_copy(xnfm3[:, :, 0:1], xnfm3[:, :, TT:TT + 1]),
                         deps=dep["xnfm_r"])
        else:
            t_bnd = p.op("vector", lambda e: e.memset(xnfm3[:, :, 0:1], 0.0), deps=dep["xnfm_r"])
        dep["xnfm_r"] = []
        mms = []
        for q in range(NQ):
            bi = q // 4
            bank_mm(bi, lambda e, q=q, bi=bi: e.matmul(
                pb[bi][:, (q % 4) * 128:(q % 4) * 128 + 128],
                lhsT=hi_tm[:, q * 128:(q + 1) * 128], rhs=ident[:], start=True, stop=False),
                deps=[t_hi, t_ident])
            mm2 = p.op("tensor", lambda e, q=q, bi=bi: e.matmul(
                pb[bi][:, (q % 4) * 128:(q % 4) * 128 + 128],
                lhsT=lo_tm[:, q * 128:(q + 1) * 128], rhs=ident[:], start=False, stop=True),
                deps=[t_lo])
            mms.append(mm2)
        dep["hi_r"] = [mms[-1]]
        dep["lo_r"] = [mms[-1]]
        dep["E1_r"].append(mms[-1])
        dep["Rab_r"].append(mms[-1])
        xn_ev = []
        for half in range(2):
            tok = p.op("scalar", lambda e, half=half: e.copy(
                xnfm3[:, 4 * half:4 * half + 4, 1:1 + TT],
                pb[half][:].rearrange("p (q c) -> p q c", q=4)),
                deps=[mms[4 * half + 3], t_bnd])
            bank_free[half] = tok
            xn_ev.append(tok)

        S["xn_ev"] = xn_ev

        t_xnbp = p.op("vector", lambda e: e.tensor_copy(xnb3, xnfm3[:, :, 1:TT + 1]),
                      deps=xn_ev + dep["xnb_r"])
        dep["xnb_r"] = []
        dep["xnfm_r"].append(t_xnbp)
        t_sx = p.op("vector", lambda e: e.tensor_tensor(
            out=sxb3, in0=xnfm3[:, :, 0:TT], in1=xnfm3[:, :, 1:TT + 1], op=OP.subtract),
            deps=xn_ev + dep["sx_r"])
        dep["sx_r"] = []
        dep["xnfm_r"].append(t_sx)
        t_xxx = []
        for q in range(NQ):
            tok = p.op("vector", lambda e, q=q: e.scalar_tensor_tensor(
                out=xi3[:, q, :], in0=sxb3[:, q, :], scalar=tmaa_sb[:, q:q + 1],
                in1=xnb3[:, q, :], op0=OP.mult, op1=OP.add),
                deps=[t_sx, t_tmaa, t_xnbp] + dep["xi_r"])
            t_xxx.append(tok)
            dep["xnb_r"].append(tok)
            dep["sx_r"].append(tok)
        dep["xi_r"] = []

        mm_last = {}
        for part in range(2):
            cols = 128 if part == 0 else 32
            for q in range(NQ):
                tok = bank_mm(4, lambda e, part=part, q=q, cols=cols: e.matmul(
                    pb[4][0:cols, part * 128:part * 128 + TT],
                    lhsT=tmw1_sb[:, q * 160 + part * 128:q * 160 + part * 128 + cols],
                    rhs=xi_bf[:, q * TT:(q + 1) * TT],
                    start=(q == 0), stop=(q == NQ - 1)),
                    deps=[t_tmw1, t_xxx[q]])
                if q == NQ - 1:
                    mm_last[part] = tok
        m_toks = []
        for i in range(5):
            part = 0 if i < 4 else 1
            rs = (i * 32) % 128
            j, base = i // 2, (i % 2) * 64
            tok = p.op("scalar", lambda e, part=part, rs=rs, j=j, base=base: e.activation(
                out=m_sb[j][base:base + 32, :], in_=pb[4][rs:rs + 32, part * 128:part * 128 + TT],
                func=AF.Tanh),
                deps=[mm_last[0], mm_last[1], t_mones[i]] + dep["m_r"][i])
            dep["m_r"][i] = []
            m_toks.append(tok)
        bank_free[4] = m_toks[4]

        S["m_toks"] = m_toks
        S["t_sx"] = t_sx
        S["t_xnbp"] = t_xnbp

    def PF3(t):
        S = TS[t]
        m_toks = S["m_toks"]
        t_sx = S["t_sx"]
        t_xnbp = S["t_xnbp"]
        xn_ev = S["xn_ev"]

        proj_ev = {"r": [], "k": [], "v": [], "g": []}
        lw_toks = []

        def emit_delta(i):
            jm, base = i // 2, (i % 2) * 64
            db0 = (i % 2) * 2
            dmm = []
            for q in range(NQ):
                bi = db0 + q // 4
                tok = bank_mm(bi, lambda e, q=q, bi=bi, jm=jm, base=base: e.matmul(
                    pb[bi][:, (q % 4) * 128:(q % 4) * 128 + 128],
                    lhsT=w2p[jm][base:base + 64, q * 128:(q + 1) * 128],
                    rhs=m_sb[jm][base:base + 64, :], start=True, stop=True),
                    deps=[t_w2[i], m_toks[i]])
                dmm.append(tok)
                dep["m_r"][i].append(tok)
            return dmm

        next_dmm = emit_delta(0)
        prev_xi = None
        for i in range(5):
            db0 = (i % 2) * 2
            dmm = next_dmm
            tmix = []
            for half in range(2):
                bi = db0 + half
                tok = p.op("vector", lambda e, bi=bi, half=half: e.tensor_tensor(
                    out=dmix[:, half * 512:(half + 1) * 512], in0=sx_bf[:, half * 512:(half + 1) * 512],
                    in1=pb[bi][:], op=OP.mult),
                    deps=[dmm[4 * half + 3], t_sx] + (dep["E2_r"] if i == 0 else [prev_xi]))
                bank_free[bi] = tok
                tmix.append(tok)
                dep["sx_r"].append(tok)
            if i == 0:
                dep["E2_r"] = []
            xcur = xi_bf[:] if (i % 2 == 0) else xi2
            xkey = "xi_r" if (i % 2 == 0) else "E1_r"
            xi_tok = p.op("vector", lambda e, xcur=xcur: e.tensor_tensor(
                out=xcur, in0=dmix[:], in1=xnb[:], op=OP.add),
                deps=tmix + [t_xnbp] + dep[xkey])
            dep["xnb_r"].append(xi_tok)
            prev_xi = xi_tok
            dep[xkey] = []
            if i < 4:
                next_dmm = emit_delta(i + 1)
            if i == 4:
                dep["E2_r"] = [xi_tok]
            if debug and t == 0 and i == 3:
                tok_dxi = p.dma("gpsimd", lambda e: e.dma_start(out=dbg["d_xi"], in_=xi_bf[:]), deps=[xi_tok])
                dep["xi_r"].append(_dt(tok_dxi))

            if i == 0:
                last = None
                for q in range(NQ):
                    last = bank_mm(5, lambda e, q=q: e.matmul(
                        pb[5][0:64, 0:TT], lhsT=tdw1_sb[:, q * 64:(q + 1) * 64],
                        rhs=xi_bf[:, q * TT:(q + 1) * TT], start=(q == 0), stop=(q == NQ - 1)),
                        deps=[t_tdw1, xi_tok])
                    dep["xi_r"].append(last)
                t_tanh = p.op("scalar", lambda e: e.activation(
                    out=tanhtd[0:64, :], in_=pb[5][0:64, 0:TT], func=AF.Tanh),
                    deps=[last, t_tdones] + dep["ttd_r"])
                dep["ttd_r"] = []
                bank_free[5] = t_tanh
                wmm = []
                for half in range(2):
                    bi = 6 + half
                    tok = bank_mm(bi, lambda e, half=half, bi=bi: e.matmul(
                        pb[bi][:], lhsT=tanhtd[:],
                        rhs=td2aug_sb[:, half * 512:(half + 1) * 512], start=True, stop=True),
                        deps=[t_td2, t_tanh])
                    wmm.append(tok)
                    dep["ttd_r"].append(tok)
                for half in range(2):
                    tok_lw = p.op("scalar", lambda e, half=half: e.activation(
                        out=lw_tm[:, half * 512:(half + 1) * 512], in_=pb[6 + half][:], func=AF.Exp),
                        deps=[wmm[half]] + dep["lw_r"])
                    bank_free[6 + half] = tok_lw
                    lw_toks.append(tok_lw)
                dep["lw_r"] = []
            elif i in (1, 3):
                name = "k" if i == 1 else "r"
                dst = k_bf if i == 1 else r_bf
                mm_l = {}
                for qo in range(NQ):
                    bi = 4 + qo // 4
                    for qi in range(NQ):
                        tok = bank_mm(bi, lambda e, name=name, qo=qo, qi=qi, bi=bi: e.matmul(
                            pb[bi][:, (qo % 4) * 128:(qo % 4) * 128 + 128],
                            lhsT=W_sb[name][:, qi * C + qo * 128:qi * C + qo * 128 + 128],
                            rhs=xi2[:, qi * TT:(qi + 1) * TT],
                            start=(qi == 0), stop=(qi == NQ - 1)),
                            deps=[t_W[name], xi_tok])
                        dep["E1_r"].append(tok)
                        if qi == NQ - 1:
                            mm_l[qo] = tok
                for half in range(2):
                    bi = 4 + half
                    tok = p.op("scalar", lambda e, half=half, bi=bi, dst=dst: e.copy(
                        dst[:, half * 4 * TT:(half + 1) * 4 * TT], pb[bi][:]),
                        deps=[mm_l[4 * half + 0], mm_l[4 * half + 1], mm_l[4 * half + 2], mm_l[4 * half + 3]]
                        + (dep["rk_r"] if i == 1 and half == 0 else []))
                    bank_free[bi] = tok
                    proj_ev[name].append(tok)
                if i == 3:
                    dep["rk_r"] = []
            else:
                name = "v" if i == 2 else "g"
                dst = v_tm if i == 2 else g_raw
                mm_l = {}
                for half in range(2):
                    bi = 6 + half
                    for qi in range(NQ):
                        tok = bank_mm(bi, lambda e, name=name, half=half, qi=qi, bi=bi: e.matmul(
                            pb[bi][:],
                            lhsT=xi_bf[:, qi * TT:(qi + 1) * TT],
                            rhs=W_sb[name][:, qi * C + half * 512:qi * C + half * 512 + 512],
                            start=(qi == 0), stop=(qi == NQ - 1)),
                            deps=[t_W[name], xi_tok])
                        dep["xi_r"].append(tok)
                        if qi == NQ - 1:
                            mm_l[half] = tok
                for half in range(2):
                    bi = 6 + half
                    tok = p.op("scalar", lambda e, half=half, bi=bi, dst=dst: e.copy(
                        dst[:, half * 512:(half + 1) * 512], pb[bi][:]),
                        deps=[mm_l[half]] + (dep["vg_r"] if i == 2 and half == 0 else []))
                    bank_free[bi] = tok
                    proj_ev[name].append(tok)
                if i == 4:
                    dep["vg_r"] = []

        if debug and t == 0:
            dep["xnfm_r"].append(_dt(p.dma("sync", lambda e: e.dma_start(out=dbg["d_xnfm"], in_=xnfm3[:, :, 1:TT + 1]), deps=xn_ev)))
            dep["sx_r"].append(_dt(p.dma("sync", lambda e: e.dma_start(out=dbg["d_sx"], in_=sx_bf[:]), deps=[t_sx])))
            dep["rk_r"].append(_dt(p.dma("gpsimd", lambda e: e.dma_start(out=dbg["d_r"], in_=r_bf[:]), deps=proj_ev["r"])))
            dep["rk_r"].append(_dt(p.dma("gpsimd", lambda e: e.dma_start(out=dbg["d_k"], in_=k_bf[:]), deps=proj_ev["k"])))
            dep["vg_r"].append(_dt(p.dma("gpsimd", lambda e: e.dma_start(out=dbg["d_v"], in_=v_tm[:]), deps=proj_ev["v"])))
            dep["vg_r"].append(_dt(p.dma("gpsimd", lambda e: e.dma_start(out=dbg["d_g"], in_=g_raw[:]), deps=proj_ev["g"])))
            dep["lw_r"].append(_dt(p.dma("gpsimd", lambda e: e.dma_start(out=dbg["d_lw"], in_=lw_tm[:]), deps=lw_toks)))

        S["proj_ev"] = proj_ev
        S["lw_toks"] = lw_toks

    def PB1(t):
        S = TS[t]
        proj_ev = S["proj_ev"]
        lw_toks = S["lw_toks"]
        t_xnbp = S["t_xnbp"]

        wave1 = {}
        for q in range(NQ):
            m_ex = bank_mm(q // 4, lambda e, q=q: e.matmul(
                pb[q // 4][:, (q % 4) * 128:(q % 4) * 128 + 128],
                lhsT=lw_tm[:, q * 128:(q + 1) * 128],
                rhs=utri[:, 0:128], start=True, stop=True),
                deps=[lw_toks[q // 4], t_utri])
            m_ia = bank_mm(2 + q // 4, lambda e, q=q: e.matmul(
                pb[2 + q // 4][:, (q % 4) * 128:(q % 4) * 128 + 128],
                lhsT=lw_tm[:, q * 128:(q + 1) * 128],
                rhs=utri[:, 4 * 128:5 * 128], start=True, stop=True))
            m_ib = bank_mm(4 + q // 4, lambda e, q=q: e.matmul(
                pb[4 + q // 4][:, (q % 4) * 128:(q % 4) * 128 + 128],
                lhsT=lw_tm[:, q * 128:(q + 1) * 128],
                rhs=utri[:, 5 * 128:6 * 128], start=True, stop=True))
            wave1[q] = (m_ex, m_ia, m_ib)
            dep["lw_r"] += [m_ex, m_ia, m_ib]
        eRu_t = [p.op("scalar", lambda e, hf=hf: e.activation(
            out=E1f[:, hf * 512:(hf + 1) * 512], in_=pb[hf][:], func=AF.Exp, scale=-1.0),
            deps=[wave1[4 * hf + 3][0]] + dep["E1_r"]) for hf in range(2)]
        eA_t = [p.op("scalar", lambda e, hf=hf: e.activation(
            out=E2_3[:, 4 * hf:4 * hf + 4, 0:64],
            in_=pb[2 + hf][:].rearrange("p (qq c) -> p qq c", qq=4)[:, :, 0:64],
            func=AF.Exp, scale=1.0),
            deps=[wave1[4 * hf + 3][1]] + dep["E2_r"]) for hf in range(2)]
        negf = [p.op("scalar", lambda e, q=q: e.activation(
            out=cmf[:, q:q + 1], in_=pb[4 + q // 4][:, (q % 4) * 128 + 127:(q % 4) * 128 + 128],
            func=AF.Copy, scale=-1.0), deps=[wave1[(q // 4) * 4 + 3][2]] + dep["cmf_r"]) for q in range(NQ)]
        Khf3 = Khf[:].rearrange("p (q c) -> p q c", q=NQ)
        ekhf = [p.op("scalar", lambda e, q=q: e.activation(
            out=Khf3[:, q, :], in_=pb[4 + q // 4][:, (q % 4) * 128:(q % 4) * 128 + 128],
            func=AF.Exp, bias=cmf[:, q:q + 1]),
            deps=[negf[q]] + (dep["Khf_r"] if q == 0 else [])) for q in range(NQ)]
        dep["E1_r"] = []
        dep["E2_r"] = []
        dep["cmf_r"] = ekhf[:]
        dep["Khf_r"] = []
        t_ru = p.op("gpsimd", lambda e: e.tensor_tensor(
            out=Ru[:].rearrange("p (q c) -> p q c", q=NQ), in0=rb3, in1=E1_3, op=OP.mult),
            deps=eRu_t + dep["Ru_r"] + proj_ev["r"])
        dep["Ru_r"] = []
        t_khf = p.op("gpsimd", lambda e: e.tensor_tensor(
            out=Khf3, in0=kb3, in1=Khf3, op=OP.mult),
            deps=ekhf + proj_ev["k"])
        ktzA_t = []
        for pr in range(2):
            ktzA_t.append(p.op("gpsimd", lambda e, pr=pr: e.tensor_tensor(
                out=KtzA4[pr * 64:pr * 64 + 64, pr::2, 0:64],
                in0=kb3[pr * 64:pr * 64 + 64, :, 0:64],
                in1=E2_3[pr * 64:pr * 64 + 64, :, 0:64], op=OP.mult),
                deps=eA_t + dep["KtzA_r"] + proj_ev["k"]))
        dep["KtzA_r"] = []
        eB_t = [p.op("scalar", lambda e, hf=hf: e.activation(
            out=E1f[:, hf * 512:(hf + 1) * 512], in_=pb[4 + hf][:], func=AF.Exp, scale=1.0),
            deps=[t_ru, wave1[4 * hf + 3][2]]) for hf in range(2)]
        bank_free[0] = eRu_t[0]
        bank_free[1] = eRu_t[1]
        bank_free[2] = eA_t[0]
        bank_free[3] = eA_t[1]
        bank_free[4] = eB_t[0]
        bank_free[5] = eB_t[1]
        ktzB_t = []
        for pr in range(2):
            ktzB_t.append(p.op("gpsimd", lambda e, pr=pr: e.tensor_tensor(
                out=KtzB4[pr * 64:pr * 64 + 64, pr::2, :],
                in0=kb3[pr * 64:pr * 64 + 64, :, :],
                in1=E1_3[pr * 64:pr * 64 + 64, :, :], op=OP.mult),
                deps=eB_t + dep["KtzB_r"] + proj_ev["k"]))
        dep["KtzB_r"] = []
        wave2 = {}
        for q in range(NQ):
            m_xa = bank_mm(q // 4, lambda e, q=q: e.matmul(
                pb[q // 4][:, (q % 4) * 128:(q % 4) * 128 + 128],
                lhsT=lw_tm[:, q * 128:(q + 1) * 128],
                rhs=utri[:, 2 * 128:3 * 128], start=True, stop=True))
            m_xb = bank_mm(2 + q // 4, lambda e, q=q: e.matmul(
                pb[2 + q // 4][:, (q % 4) * 128:(q % 4) * 128 + 128],
                lhsT=lw_tm[:, q * 128:(q + 1) * 128],
                rhs=utri[:, 3 * 128:4 * 128], start=True, stop=True))
            m_in = bank_mm(4 + q // 4, lambda e, q=q: e.matmul(
                pb[4 + q // 4][:, (q % 4) * 128:(q % 4) * 128 + 128],
                lhsT=lw_tm[:, q * 128:(q + 1) * 128],
                rhs=utri[:, 1 * 128:2 * 128], start=True, stop=True))
            wave2[q] = (m_xa, m_xb, m_in)
            dep["lw_r"] += [m_xa, m_xb, m_in]
        eRA_t = [p.op("scalar", lambda e, hf=hf: e.activation(
            out=E2_3[:, 4 * hf:4 * hf + 4, 0:64],
            in_=pb[hf][:].rearrange("p (qq c) -> p qq c", qq=4)[:, :, 0:64],
            func=AF.Exp, scale=-1.0),
            deps=[wave2[4 * hf + 3][0]] + ktzA_t) for hf in range(2)]
        eRB_t = [p.op("scalar", lambda e, hf=hf: e.activation(
            out=E1_3[:, 4 * hf:4 * hf + 4, 64:128],
            in_=pb[2 + hf][:].rearrange("p (qq c) -> p qq c", qq=4)[:, :, 64:128],
            func=AF.Exp, scale=-1.0),
            deps=[wave2[4 * hf + 3][1], t_khf] + ktzB_t) for hf in range(2)]
        epl_toks = []
        for q in range(NQ):
            epl_toks.append(p.op("scalar", lambda e, q=q: e.activation(
                out=epl[:, q:q + 1], in_=pb[4 + q // 4][:, (q % 4) * 128 + 127:(q % 4) * 128 + 128],
                func=AF.Exp, scale=-1.0),
                deps=[wave2[(q // 4) * 4 + 3][2]] + dep["epl_r"]))
        dep["epl_r"] = []
        bank_free[0] = eRA_t[0]
        bank_free[1] = eRA_t[1]
        bank_free[2] = eRB_t[0]
        bank_free[3] = eRB_t[1]
        bank_free[4] = epl_toks[3]
        bank_free[5] = epl_toks[7]
        t_ra = p.op("gpsimd", lambda e: e.tensor_tensor(
            out=Rab3[:, :, 0:64], in0=rb3[:, :, 0:64], in1=E2_3[:, :, 0:64], op=OP.mult),
            deps=eRA_t + dep["Rab_r"] + proj_ev["r"])
        t_rb = p.op("gpsimd", lambda e: e.tensor_tensor(
            out=Rab3[:, :, 64:128], in0=rb3[:, :, 64:128], in1=E1_3[:, :, 64:128], op=OP.mult),
            deps=eRB_t + dep["Rab_r"] + proj_ev["r"])
        dep["Rab_r"] = []
        t_rku1 = p.op("gpsimd", lambda e: e.tensor_tensor(
            out=rku3, in0=rb3, in1=kb3, op=OP.mult),
            deps=proj_ev["r"] + proj_ev["k"] + dep["sx_r"])


        mm_f = {}
        for half in range(2):
            bi = 6 + half
            for qi in range(NQ):
                tok = bank_mm(bi, lambda e, half=half, qi=qi, bi=bi: e.matmul(
                    pb[bi][:],
                    lhsT=xnb[:, qi * TT:(qi + 1) * TT],
                    rhs=W_sb["f"][:, qi * C + half * 512:qi * C + half * 512 + 512],
                    start=(qi == 0), stop=False),
                    deps=[t_xnbp] + (weff_toks if t == 0 else []))
                dep["xnb_r"].append(tok)
            mm_f[half] = p.op("tensor", lambda e, half=half, bi=bi: e.matmul(
                pb[bi][:], lhsT=ones1[:], rhs=fbrow[:, half * 512:(half + 1) * 512],
                start=False, stop=True), deps=[t_fb, t_ones])
            bank_free[bi] = None

        kht_mm = {}
        for q in range(NQ):
            bi2 = 2 + q // 4
            kht_mm[q] = bank_mm(bi2, lambda e, q=q, bi2=bi2: e.matmul(
                pb[bi2][:, (q % 4) * 128:(q % 4) * 128 + 128],
                lhsT=Khf[:, q * 128:(q + 1) * 128], rhs=ident[:], start=True, stop=True),
                deps=[t_khf])
        khz_toks = {}
        for q in range(NQ):
            bi2 = 2 + q // 4
            for pr in range(2):
                h = 2 * q + pr
                khz_toks[h] = p.op("scalar", lambda e, q=q, bi2=bi2, pr=pr, h=h: e.activation(
                    out=Khz[:, h * 128 + pr * 64:h * 128 + pr * 64 + 64],
                    in_=pb[bi2][:, (q % 4) * 128 + pr * 64:(q % 4) * 128 + pr * 64 + 64],
                    func=AF.Copy),
                    deps=[kht_mm[(q // 4) * 4 + 3]] + dep["Khz_r"])
        dep["Khz_r"] = []
        bank_free[2] = khz_toks[7]
        bank_free[3] = khz_toks[15]

        a_mm = {}
        for h in range(H):
            bi = h // 4
            q = h // 2
            mA = bank_mm(bi, lambda e, h=h, bi=bi, q=q: e.matmul(
                pb[bi][:, (h % 4) * 128:(h % 4) * 128 + 64],
                lhsT=KtzA[:, h * 128:(h + 1) * 128],
                rhs=Rab[:, q * 128:q * 128 + 64],
                start=True, stop=True),
                deps=[ktzA_t[h % 2], t_ra])
            mB = bank_mm(bi, lambda e, h=h, bi=bi, q=q: e.matmul(
                pb[bi][:, (h % 4) * 128 + 64:(h % 4) * 128 + 128],
                lhsT=KtzB[:, h * 128:(h + 1) * 128],
                rhs=Rab[:, q * 128 + 64:q * 128 + 128],
                start=True, stop=True),
                deps=[ktzB_t[h % 2], t_rb])
            a_mm[h] = (mA, mB)
        a_ev = []
        for bi in range(4):
            tok = p.op("vector", lambda e, bi=bi: e.copy_predicated(
                out=A4[:, bi * 512:(bi + 1) * 512], mask=mask4[:], data=pb[bi][:]),
                deps=[a_mm[4 * bi + 3][1], t_mask4] + dep["A4_r"])
            bank_free[bi] = tok
            a_ev.append(tok)
        dep["A4_r"] = []

        su_mm = []
        for q in range(NQ):
            su_mm.append(bank_mm(0, lambda e, q=q: e.matmul(
                pb[0][:, 2 * q:2 * q + 2], lhsT=rku[:, q * 128:(q + 1) * 128],
                rhs=ind2u[:, 2 * q:2 * q + 2], start=True, stop=True),
                deps=[t_rku1, t_iu]))
        for tk in su_mm:
            dep["sx_r"].append(tk)
        t_su = p.op("vector", lambda e: e.tensor_copy(su_sb[:], pb[0][:, 0:H]), deps=su_mm + dep["su_r"])
        dep["su_r"] = []
        bank_free[0] = t_su


        if debug and t == 0:
            tok_der = p.dma("gpsimd", lambda e: e.dma_start(out=dbg["d_Er"], in_=Ru[:]), deps=[t_ru])
            dep["Ru_r"].append(_dt(tok_der))
            dep["A4_r"].append(_dt(p.dma("gpsimd", lambda e: e.dma_start(out=dbg["d_A"], in_=A4[:]), deps=a_ev)))
            dep["su_r"].append(_dt(p.dma("sync", lambda e: e.dma_start(out=dbg["d_su"], in_=su_sb[:]), deps=[t_su])))
            dep["KtzA_r"].append(_dt(p.dma("gpsimd", lambda e: e.dma_start(out=dbg["d_Ktz"], in_=KtzA[:]), deps=ktzA_t)))
            dep["Khz_r"].append(_dt(p.dma("gpsimd", lambda e: e.dma_start(out=dbg["d_Khz"], in_=Khz[:]), deps=list(khz_toks.values()))))

        dep["KtzA_r"] = dep["KtzA_r"] + [a_mm[h][0] for h in range(H)]
        dep["KtzB_r"] = [a_mm[h][1] for h in range(H)]
        dep["Rab_r"] = dep["Rab_r"] + [a_mm[h][0] for h in range(H)] + [a_mm[h][1] for h in range(H)]
        dep["Khf_r"] = list(kht_mm.values())
        dep["E1_r"] = dep["E1_r"] + [t_rb]
        dep["E2_r"] = [t_ra]
        dep["cmf_r"] = [t_khf]
        dep["rk_r"] = [t_ru, t_ra, t_rb, t_khf, t_rku1] + ktzA_t + ktzB_t
        S["t_su"] = t_su
        S["t_ru"] = t_ru
        S["t_khf"] = t_khf
        S["epl_toks"] = epl_toks
        S["khz_toks"] = khz_toks
        S["a_ev"] = a_ev
        S["a_mm"] = a_mm
        S["mm_f"] = mm_f

    def PB2a(t):
        S = TS[t]
        proj_ev = S["proj_ev"]
        t_su = S["t_su"]
        t_ru = S["t_ru"]
        t_khf = S["t_khf"]
        epl_toks = S["epl_toks"]
        khz_toks = S["khz_toks"]
        a_ev = S["a_ev"]
        a_mm = S["a_mm"]
        mm_f = S["mm_f"]
        par = t % 2
        Scur, Snxt = Sz[par], Sz[1 - par]

        t_vsu = p.op("gpsimd", lambda e: e.tensor_tensor(
            out=scr2[:].rearrange("p (h c) -> p h c", h=H),
            in0=v_tm[:].rearrange("p (h c) -> p h c", h=H),
            in1=su_sb[:].rearrange("p h -> p h ()").broadcast_to((128, H, K)),
            op=OP.mult),
            deps=[t_su] + proj_ev["v"] + dep["scr2_r"])
        dep["scr2_r"] = []
        t_sdec = p.op("gpsimd", lambda e, Scur=Scur, Snxt=Snxt: e.tensor_tensor(
            out=Snxt[:].rearrange("p (q c) -> p q c", q=NQ),
            in0=Scur[:].rearrange("p (q c) -> p q c", q=NQ),
            in1=epl[:].rearrange("p q -> p q ()").broadcast_to((128, NQ, 128)),
            op=OP.mult),
            deps=epl_toks + dep["Sz_r"][1 - par] + [dep["Sz_w"][1 - par], dep["Sz_w"][par]])
        dep["Sz_r"][1 - par] = []

        y_mm = {}
        s_mm = {}
        intra_mms = []
        for h in range(H):
            by = 4 + h // 8
            tok1 = bank_mm(by, lambda e, h=h, by=by: e.matmul(
                pb[by][:, (h % 8) * 64:(h % 8) * 64 + 64],
                lhsT=A4[:, h * 128:(h + 1) * 128],
                rhs=v_tm[:, h * K:h * K + K],
                start=True, stop=False), deps=[a_ev[h // 4]] + proj_ev["v"])
            intra_mms.append(tok1)
            y_mm[h] = p.op("tensor", lambda e, h=h, by=by: e.matmul(
                pb[by][:, (h % 8) * 64:(h % 8) * 64 + 64],
                lhsT=Ru[:, (h // 2) * 128:(h // 2) * 128 + 128],
                rhs=S_bfz[:, h * K:(h + 1) * K],
                start=False, stop=True), deps=[dep["Sbf_ready"], t_ru])
            bs = 2 + h // 8
            s_mm[h] = bank_mm(bs, lambda e, h=h, bs=bs: e.matmul(
                pb[bs][:, (h % 8) * 64:(h % 8) * 64 + 64],
                lhsT=Khz[:, h * 128:(h + 1) * 128],
                rhs=v_tm[:, h * K:h * K + K],
                start=True, stop=True), deps=[khz_toks[h]] + proj_ev["v"])
        yev = []
        sev = []
        for half in range(2):
            tok = p.op("vector", lambda e, half=half: e.tensor_tensor(
                out=y_tm[:, half * 512:half * 512 + 512],
                in0=scr2[:, half * 512:half * 512 + 512],
                in1=pb[4 + half][:], op=OP.add),
                deps=[y_mm[half * 8 + 7], t_vsu] + dep["y_r"])
            yev.append(tok)
            dep["scr2_r"].append(tok)
            tok2 = p.op("vector", lambda e, half=half, Snxt=Snxt: e.tensor_tensor(
                out=Snxt[:, half * 512:half * 512 + 512],
                in0=Snxt[:, half * 512:half * 512 + 512],
                in1=pb[2 + half][:], op=OP.add),
                deps=[s_mm[half * 8 + 7], t_sdec])
            sev.append(tok2)
        dep["y_r"] = []
        dep["Sz_r"][par] = [t_sdec]
        dep["Sz_r"][1 - par] = []
        dep["Sz_w"][1 - par] = sev[-1]
        bank_free[4] = yev[0]
        bank_free[5] = yev[1]
        bank_free[2] = sev[0]
        bank_free[3] = sev[1]
        t_sbf = p.op("vector", lambda e, Snxt=Snxt: e.tensor_copy(S_bfz[:], Snxt[:]),
                     deps=sev + list(y_mm.values()))
        dep["Sbf_ready"] = t_sbf
        dep["Sz_r"][1 - par].append(t_sbf)

        t_f2 = p.op("scalar", lambda e: e.activation(
            out=scr[:, 0:512], in_=pb[6][:], func=AF.Silu), deps=[mm_f[0]] + dep["scr_r"])
        t_f2b = p.op("scalar", lambda e: e.activation(
            out=scr[:, 512:1024], in_=pb[7][:], func=AF.Silu), deps=[mm_f[1]])
        bank_free[6] = t_f2
        bank_free[7] = t_f2b
        dep["scr_r"] = []

        dep["Ru_r"] = list(y_mm.values())
        dep["Khz_r"] = list(s_mm.values())
        dep["epl_r"] = [t_sdec, t_khf]
        dep["su_r"] = [t_vsu]
        dep["A4_r"] = dep["A4_r"] + intra_mms
        S["yev"] = yev
        S["sev"] = sev
        S["t_vsu"] = t_vsu
        S["t_f2"] = t_f2
        S["t_f2b"] = t_f2b

    def PB2b(t):
        S = TS[t]
        proj_ev = S["proj_ev"]
        yev = S["yev"]
        sev = S["sev"]
        t_vsu = S["t_vsu"]
        par = t % 2
        Snxt = Sz[1 - par]

        yg3 = y_tm[:].rearrange("p (h c) -> p h c", h=H)
        t_gm = p.op("vector", lambda e: e.tensor_reduce(
            out=gn_sc[:, 0:H], in_=yg3, axis=AX.X, op=OP.add), deps=yev + dep["gn_r"])
        dep["gn_r"] = []
        t_gsq = p.op("scalar", lambda e: e.activation(
            out=scr2[:], in_=y_tm[:], func=AF.Square), deps=yev + dep["scr2_r"])
        t_gr2 = p.op("vector", lambda e: e.tensor_reduce(
            out=gn_sc[:, H:2 * H], in_=scr2[:].rearrange("p (h c) -> p h c", h=H),
            axis=AX.X, op=OP.add), deps=[t_gsq])
        t_gmm = p.op("vector", lambda e: e.tensor_scalar_mul(gn_sc[:, 0:H], gn_sc[:, 0:H], 1.0 / K), deps=[t_gm])
        t_gm2 = p.op("vector", lambda e: e.tensor_tensor(
            out=gn_sc[:, 2 * H:3 * H], in0=gn_sc[:, 0:H], in1=gn_sc[:, 0:H], op=OP.mult), deps=[t_gmm])
        t_gvar = p.op("vector", lambda e: e.scalar_tensor_tensor(
            out=gn_sc[:, H:2 * H], in0=gn_sc[:, H:2 * H], scalar=1.0 / K,
            in1=gn_sc[:, 2 * H:3 * H], op0=OP.mult, op1=OP.subtract), deps=[t_gr2, t_gm2])
        t_gl1 = p.op("scalar", lambda e: e.activation(
            out=gn_sc2[:, 0:H], in_=gn_sc[:, H:2 * H], func=AF.Ln, bias=eps_sb[:]),
            deps=[t_gvar, t_eps])
        t_grs = p.op("scalar", lambda e: e.activation(
            out=gn_sc[:, 3 * H:4 * H], in_=gn_sc2[:, 0:H], func=AF.Exp, scale=-0.5), deps=[t_gl1])
        t_yc = p.op("gpsimd", lambda e: e.tensor_tensor(
            out=yg3, in0=yg3,
            in1=gn_sc[:, 0:H].rearrange("p h -> p h ()").broadcast_to((128, H, K)),
            op=OP.subtract), deps=[t_gmm, t_gsq])
        t_yn = p.op("gpsimd", lambda e: e.tensor_tensor(
            out=yg3, in0=yg3,
            in1=gn_sc[:, 3 * H:4 * H].rearrange("p h -> p h ()").broadcast_to((128, H, K)),
            op=OP.mult), deps=[t_grs, t_yc])
        dep["scr2_r"] = [t_gr2]
        dep["y_r"] = [t_yn]
        dep["gn_r"] = [t_yc, t_yn]

        if debug and t == 1:
            dep["y_r"].append(_dt(p.dma("sync", lambda e: e.dma_start(out=dbg["d_y"], in_=y_tm[:]), deps=[t_yn])))
            dep["Sz_r"][1 - par].append(_dt(p.dma("sync", lambda e: e.dma_start(out=dbg["d_S"], in_=Snxt[:]), deps=[sev[-1]])))

        t_gs = p.op("scalar", lambda e: e.activation(
            out=gs_sb[:], in_=g_raw[:], func=AF.Silu),
            deps=proj_ev["g"] + dep["gs_r"])
        dep["gs_r"] = []
        t_gl = p.op("gpsimd", lambda e: e.tensor_tensor(
            out=gl_b, in0=gs_sb[:], in1=lnw_bct[:], op=OP.mult),
            deps=[t_gs, ev[0], ev[1]] + dep["Khf_r"])
        dep["Khf_r"] = []
        t_glb = p.op("gpsimd", lambda e: e.tensor_tensor(
            out=glb_b, in0=gs_sb[:], in1=lnb_bct[:], op=OP.mult),
            deps=[t_gs, ev[2], ev[3]] + dep["yngfm_r"])
        dep["vg_r"] = [t_gl, t_glb, t_vsu]
        t_y1 = p.op("gpsimd", lambda e: e.tensor_tensor(
            out=scr2[:], in0=y_tm[:], in1=gl_b, op=OP.mult),
            deps=[t_gl, t_yn] + dep["scr2_r"])
        t_y2 = p.op("gpsimd", lambda e: e.tensor_tensor(
            out=y_tm[:].bitcast(BF16)[:, 0:C],
            in0=scr2[:], in1=glb_b, op=OP.add),
            deps=[t_glb, t_y1] + dep["y_r"])
        dep["Khf_r"].append(t_y1)
        dep["scr2_r"] = [t_y2]
        dep["y_r"] = [t_y2]

        S["t_y2"] = t_y2

    def PB3(t):
        S = TS[t]
        proj_ev = S["proj_ev"]
        t_y2 = S["t_y2"]
        t_f2 = S["t_f2"]
        t_f2b = S["t_f2b"]
        xt = S["xt"]
        xb = S["xb"]

        mms2 = []
        for q in range(NQ):
            bi = q // 4
            mms2.append(bank_mm(bi, lambda e, q=q, bi=bi: e.matmul(
                pb[bi][:, (q % 4) * 128:(q % 4) * 128 + 128],
                lhsT=y_tm[:].bitcast(BF16)[:, q * 128:(q + 1) * 128],
                rhs=ident[:], start=True, stop=True),
                deps=[t_y2, t_ident]))
        yngfm_ev = []
        for half in range(2):
            tok = p.op("scalar", lambda e, half=half: e.copy(
                yngfm3[:, 4 * half:4 * half + 4, :],
                pb[half][:].rearrange("p (q c) -> p q c", q=4)),
                deps=[mms2[4 * half + 3]] + (dep["yngfm_r"] if half == 0 else []))
            bank_free[half] = tok
            yngfm_ev.append(tok)
        dep["yngfm_r"] = []
        dep["y_r"] = dep["y_r"] + mms2

        if debug and t == 0:
            dep["yngfm_r"].append(_dt(p.dma("gpsimd", lambda e: e.dma_start(out=dbg["d_yng"], in_=yng_fm[:]), deps=yngfm_ev)))

        mm_l = {}
        for half in range(2):
            bi = 4 + half
            for qi in range(NQ):
                tok = bank_mm(bi, lambda e, half=half, qi=qi, bi=bi: e.matmul(
                    pb[bi][:],
                    lhsT=yng_fm[:, qi * TT:(qi + 1) * TT],
                    rhs=W_sb["o"][:, qi * C + half * 512:qi * C + half * 512 + 512],
                    start=(qi == 0), stop=(qi == NQ - 1)),
                    deps=[t_W["o"], yngfm_ev[0], yngfm_ev[1]])
                dep["yngfm_r"].append(tok)
                if qi == NQ - 1:
                    mm_l[half] = tok
        stm_ev = []
        for half in range(2):
            bi = 4 + half
            tok = p.op("scalar", lambda e, half=half, bi=bi: e.copy(
                sx_fm[:, half * 512:(half + 1) * 512], pb[bi][:]),
                deps=[mm_l[half]] + (dep["stm_r"] if half == 0 else []))
            bank_free[bi] = tok
            stm_ev.append(tok)
        dep["stm_r"] = stm_ev[:]

        if debug and t == 0:
            dep["stm_r"].append(_dt(p.dma("sync", lambda e: e.dma_start(out=dbg["d_stm"], in_=sx_fm[:, 0:C]), deps=stm_ev)))

        t_f3 = p.op("vector", lambda e: e.tensor_tensor(
            out=scr2[:], in0=scr[:], in1=sx_fm[:, 0:C], op=OP.mult),
            deps=[t_f2, t_f2b, stm_ev[0], stm_ev[1]] + dep["scr2_r"])
        dep["stm_r"].append(t_f3)
        t_out = p.op("vector", lambda e, xt=xt: e.tensor_tensor(
            out=xt[:], in0=scr2[:], in1=xt[:], op=OP.add), deps=[t_f3])
        dep["scr_r"] = [t_f3]
        dep["scr2_r"] = [t_out]
        t_st = p.dma("scalar", lambda e, t=t, xt=xt: e.dma_start(
            out=out_h[t * TT:(t + 1) * TT, :], in_=xt[:]), deps=[t_out])
        dep["out_dma"][xb] = t_st
        out_tokens.append(t_st)

    PF1(0)
    PF2(0)
    PF3(0)
    for t in range(n_tiles):
        PB1(t)
        if t + 1 < n_tiles:
            PF1(t + 1)
        PB2a(t)
        if t + 1 < n_tiles:
            PF2(t + 1)
        PB2b(t)
        if t + 1 < n_tiles:
            PF3(t + 1)
        PB3(t)

    p.op("sync", lambda e: e.nop(), deps=out_tokens)
    p.emit(nc, block, sems)
    blk.__exit__(None, None, None)
    return nc, sbuf_used


def host_inputs(inp, b):
    f32 = np.float32
    w2 = np.asarray(inp["L0_attn_tm_w2"], f32)
    maa = np.asarray(inp["L0_attn_time_maa_wkvrg"], f32)
    w2aug = np.concatenate([np.concatenate([w2[i], maa[i:i + 1]], axis=0) for i in range(5)], axis=0)
    td2aug = np.concatenate([np.asarray(inp["L0_attn_td_w2"], f32),
                             np.asarray(inp["L0_attn_time_decay"], f32)[None, :]], axis=0)
    ident = np.eye(128, dtype=f32)
    jj = np.arange(128)
    uexc = (jj[:, None] < jj[None, :]).astype(f32)
    uinc = (jj[:, None] <= jj[None, :]).astype(f32)
    pre32 = (jj[:, None] < 32).astype(f32) * np.ones((1, 128), f32)
    pre96 = (jj[:, None] < 96).astype(f32) * np.ones((1, 128), f32)
    utri = np.concatenate([uexc, uinc, uexc - pre32, uexc - pre96,
                           uinc - pre32, uinc - pre96], axis=1)
    mask4 = np.tile(uexc, (1, 4)).astype(np.uint8)
    ind2 = np.zeros((128, 2), f32)
    ind2[0:64, 0] = 1.0
    ind2[64:128, 1] = 1.0
    return {
        "x": np.ascontiguousarray(np.asarray(inp["L0_attn_x"], f32)[b]),
        "Wr": np.ascontiguousarray(np.asarray(inp["L0_attn_Wr"], f32)),
        "Wk": np.ascontiguousarray(np.asarray(inp["L0_attn_Wk"], f32)),
        "Wv": np.ascontiguousarray(np.asarray(inp["L0_attn_Wv"], f32)),
        "Wg": np.ascontiguousarray(np.asarray(inp["L0_attn_Wg"], f32)),
        "Wo": np.ascontiguousarray(np.asarray(inp["L0_attn_Wo"], f32)),
        "WfT": np.ascontiguousarray(np.asarray(inp["L0_ffn_W"], f32).T),
        "tm_w1": np.ascontiguousarray(np.asarray(inp["L0_attn_tm_w1"], f32)),
        "w2aug": np.ascontiguousarray(w2aug),
        "td_w1": np.ascontiguousarray(np.asarray(inp["L0_attn_td_w1"], f32)),
        "td2aug": np.ascontiguousarray(td2aug),
        "tmaa": np.ascontiguousarray(np.asarray(inp["L0_attn_time_maa_x"], f32)[None, :]),
        "ufaaaa": np.ascontiguousarray(np.asarray(inp["L0_attn_time_faaaa"], f32).reshape(1, -1)),
        "lnw": np.ascontiguousarray(np.asarray(inp["L0_attn_ln_w"], f32)[None, :]),
        "lnb": np.ascontiguousarray(np.asarray(inp["L0_attn_ln_b"], f32)[None, :]),
        "fb": np.ascontiguousarray(np.asarray(inp["L0_ffn_b"], f32)[None, :]),
        "lora_up": np.ascontiguousarray(np.asarray(inp["L0_ffn_lora_up"], f32)),
        "lora_down": np.ascontiguousarray(np.asarray(inp["L0_ffn_lora_down"], f32)),
        "down_aux": np.ascontiguousarray(np.asarray(inp["L0_ffn_down_aux"], f32)),
        "up_auxT": np.ascontiguousarray(np.asarray(inp["L0_ffn_up_aux"], f32).T),
        "ident": ident, "utri": utri, "mask4": mask4, "ind2": ind2,
    }


def kernel(**inputs):
    nc, _ = build_nc(n_tiles=T_FULL // TT, debug=False)
    in_maps = [host_inputs(inputs, b) for b in range(8)]
    res = run_bass_kernel_spmd(nc, in_maps, core_ids=list(range(8)))
    out = np.stack([np.asarray(r["out"]) for r in res.results], axis=0)
    return out.astype(np.float32)


# version 40
# speedup vs baseline: 1.0612x; 1.0202x over previous
import numpy as np
import ml_dtypes
import concourse.bass as bass
import concourse.mybir as mybir
from concourse.bass_utils import run_bass_kernel_spmd
from concourse.alu_op_type import AluOpType as OP

F32 = mybir.dt.float32
BF16 = mybir.dt.bfloat16
AF = mybir.ActivationFunctionType
AX = mybir.AxisListType

T_FULL = 4096
C = 1024
H = 16
K = 64
TT = 128
NQ = 8
EPS = 1e-5

ENGINES = ("sync", "gpsimd", "tensor", "vector", "scalar")


class Prog:
    def __init__(self):
        self.ops = {e: [] for e in ENGINES}
        self.n = 0
        self.eng_of = {}

    def op(self, eng, fn, deps=(), dma=False):
        opid = self.n
        self.n += 1
        deps = [d for d in deps if d is not None]
        self.ops[eng].append((fn, deps, dma, opid))
        self.eng_of[opid] = (eng, dma)
        return opid

    def dma(self, eng, fn, deps=()):
        return self.op(eng, fn, deps, dma=True)

    def emit(self, nc, block, sems):
        val = {}
        for eng in ENGINES:
            cc = dc = 0
            for fn, deps, is_dma, opid in self.ops[eng]:
                if is_dma:
                    dc += 16
                    val[opid] = (sems[(eng, "d")], dc)
                else:
                    cc += 1
                    val[opid] = (sems[(eng, "c")], cc)

        def section(eng):
            def body(e):
                seen = {}
                dseen = 0
                for fn, deps, is_dma, opid in self.ops[eng]:
                    if is_dma:
                        sd, vd = val[opid]
                        if vd - 16 > 0 and dseen < vd - 16:
                            e.wait_ge(sd, vd - 16)
                            dseen = vd - 16
                            seen[sd.name] = max(seen.get(sd.name, -1), vd - 16)
                    for d in deps:
                        de, ddma = self.eng_of[d]
                        if de == eng == "tensor" and not ddma and not is_dma:
                            continue
                        if de == eng and ddma and is_dma:
                            continue
                        s, v = val[d]
                        if seen.get(s.name, -1) < v:
                            e.wait_ge(s, v)
                            seen[s.name] = v
                    inst = fn(e)
                    inst.then_inc(sems[(eng, "d" if is_dma else "c")], 16 if is_dma else 1)
            return body

        for eng in ENGINES:
            getattr(block, eng)(section(eng))


def build_nc(n_tiles=T_FULL // TT, debug=False):
    T = n_tiles * TT
    nc = bass.Bass("TRN2", target_bir_lowering=False)

    def din(name, shape, dt=F32):
        return nc.dram_tensor(name, shape, dt, kind="ExternalInput").ap()

    x_h = din("x", [T, C])
    w_h = {n: din("W" + n, [C, C]) for n in ["r", "k", "v", "g", "o", "fT"]}
    tmw1_h = din("tm_w1", [C, 160])
    w2aug_h = din("w2aug", [5 * 33, C])
    tdw1_h = din("td_w1", [C, 64])
    td2aug_h = din("td2aug", [65, C])
    tmaa_h = din("tmaa", [1, C])
    u_h = din("ufaaaa", [1, C])
    lnw_h = din("lnw", [1, C])
    lnb_h = din("lnb", [1, C])
    fb_h = din("fb", [1, C])
    lu_h = din("lora_up", [50, 4])
    ld_h = din("lora_down", [4, 100])
    da_h = din("down_aux", [100, C])
    uaT_h = din("up_auxT", [50, C])
    ident_h = din("ident", [128, 128])
    utri_h = din("utri", [128, 6 * 128])
    mask4_h = din("mask4", [128, 512], mybir.dt.uint8)
    ind2_h = din("ind2", [128, 2])

    out_h = nc.dram_tensor("out", [T, C], F32, kind="ExternalOutput").ap()
    dbg = {}
    if debug:
        for name, shape, dt in [
                ("d_xnfm", [128, NQ * TT], F32), ("d_sx", [128, NQ * TT], BF16),
                ("d_r", [128, NQ * TT], BF16), ("d_k", [128, NQ * TT], BF16),
                ("d_lw", [128, C], BF16), ("d_Er", [128, C], BF16),
                ("d_A", [128, H * 128], BF16), ("d_y", [128, C], F32),
                ("d_su", [128, H], F32), ("d_S", [128, H * K], F32),
                ("d_v", [128, C], BF16), ("d_yng", [128, NQ * TT], BF16),
                ("d_stm", [128, C], F32), ("d_weff", [128, NQ * C], BF16),
                ("d_g", [128, C], BF16), ("d_xi", [128, NQ * TT], BF16),
                ("d_Ktz", [128, H * 128], BF16), ("d_Khz", [128, H * 128], BF16)]:
            dbg[name] = nc.dram_tensor(name, shape, dt, kind="ExternalOutput").ap()

    p = Prog()
    dbg_toks = []

    def _dt(tok):
        dbg_toks.append(tok)
        return tok

    def sb(name, shape, dt):
        return nc.alloc_sbuf_tensor(name, shape, dt).ap()

    W_sb = {n: sb(f"W_{n}", [128, NQ * C], BF16) for n in ["r", "k", "v", "g", "o", "f"]}
    tmw1_sb = sb("tmw1_s", [128, NQ * 160], BF16)
    w2p = [sb(f"w2p{j}", [128, C], BF16) for j in range(3)]
    tdw1_sb = sb("tdw1_s", [128, NQ * 64], BF16)
    td2aug_sb = sb("td2aug_s", [65, C], BF16)
    tmaa_sb = sb("tmaa_s", [128, NQ], F32)
    u_sb = sb("ufa", [128, NQ], F32)
    lnwrow = sb("lnwrow_s", [1, C], BF16)
    lnbrow = sb("lnbrow_s", [1, C], BF16)
    fbrow = sb("fbrow", [1, C], BF16)
    lnw_bct = sb("lnw_bct", [128, C], BF16)
    lnb_bct = sb("lnb_bct", [128, C], BF16)
    ident = sb("ident_s", [128, 128], BF16)
    utri = sb("utri_s", [128, 6 * 128], BF16)
    mask4 = sb("mask4_s", [128, 512], mybir.dt.uint8)
    ind2 = sb("ind2_s", [128, 2], BF16)
    ind2u = sb("ind2u_s", [128, 2 * NQ], BF16)
    ones1 = sb("ones1", [1, 128], BF16)
    eps_sb = sb("eps_s", [128, 1], F32)

    x_tm = [sb("x_tm0", [128, C], F32), sb("x_tm1", [128, C], F32)]
    xn_fm = sb("xn_fm", [128, NQ * (TT + 1)], F32)
    sx_fm = sb("sx_fm", [128, NQ * TT], F32)
    xi_bf = sb("xi_bf", [128, NQ * TT], BF16)
    m_sb = [sb(f"mp{j}", [128, TT], BF16) for j in range(3)]
    tanhtd = sb("tanhtd", [65, TT], BF16)
    lw_tm = sb("lw_tm", [128, C], BF16)
    r_bf = sb("r_bf", [128, NQ * TT], BF16)
    k_bf = sb("k_bf", [128, NQ * TT], BF16)
    v_tm = sb("v_tm", [128, C], BF16)
    g_raw = sb("g_raw", [128, C], BF16)
    E1 = sb("E1_s", [128, 2 * C], BF16)
    E1f = E1[:].bitcast(F32)
    xi2 = E1[:, 0:NQ * TT]
    E2 = sb("E2_s", [128, C], F32)
    epl = sb("epl", [128, NQ], F32)
    cmf = sb("cmf", [128, NQ], F32)
    zc = sb("zc", [128, 1], F32)
    KtzA = sb("KtzA", [128, H * 128], BF16)
    KtzB = sb("KtzB", [128, H * 128], BF16)
    Ru = sb("Ru", [128, C], BF16)
    Rab = sb("Rab", [128, C], BF16)
    Khf = sb("Khf", [128, C], BF16)
    Khz = sb("Khz", [128, H * 128], BF16)
    A4 = sb("A4", [128, H * 128], BF16)
    su_sb = sb("su_s", [128, H], F32)
    Sz = [sb("SzA", [128, H * K], F32), sb("SzB", [128, H * K], F32)]
    S_bfz = sb("S_bfz", [128, H * K], BF16)
    y_tm = sb("y_tm", [128, C], F32)
    gn_sc = sb("gn_sc", [128, 4 * H], F32)
    gn_sc2 = sb("gn_sc2", [128, 3 * H], F32)
    ln_sc = sb("ln_sc", [128, 16], F32)
    yng_fm = sb("yng_fm", [128, NQ * TT], BF16)
    scr = sb("scr", [128, C], F32)
    scr2 = sb("scr2", [128, C], F32)
    sx_bf = sb("sx_bf", [128, NQ * TT], BF16)
    hi_tm = E1[:, C:2 * C]
    lo_tm = Rab[:]
    gl_b = Khf[:]
    glb_b = yng_fm[:]
    lnE = E2[:]
    rku = sx_bf[:]

    xnb = sb("xnb", [128, NQ * TT], BF16)
    gs_sb = g_raw
    dmix = E2[:].bitcast(BF16)[:, 0:NQ * TT]

    da_v = Sz[0][0:100, :]
    uaT_v = Sz[1][0:50, :]
    c1_v = A4[:].bitcast(F32)[0:4, :]
    c2_v = KtzA[:].bitcast(F32)[0:100, :]
    lu_v = Khz[:].bitcast(F32)[0:50, 0:4]
    ld_v = Khz[:].bitcast(F32)[0:4, 4:104]

    sbuf_used = nc.sbuf_base

    pb = [nc.alloc_psum_tensor(f"pb{i}", [128, 512], F32).ap() for i in range(8)]

    sems = {}
    sem_cms = []
    for e in ENGINES:
        for kind in ("c", "d"):
            cm = nc.semaphore(f"s_{e}_{kind}")
            sems[(e, kind)] = cm.__enter__()
            sem_cms.append(cm)

    blk = nc.Block()
    block = blk.__enter__()

    xnfm3 = xn_fm[:].rearrange("p (q c) -> p q c", q=NQ)
    sx3 = sx_fm[:].rearrange("p (q c) -> p q c", q=NQ)
    sxb3 = sx_bf[:].rearrange("p (q c) -> p q c", q=NQ)
    xnb3 = xnb[:].rearrange("p (q c) -> p q c", q=NQ)
    xi3 = xi_bf[:].rearrange("p (q c) -> p q c", q=NQ)
    xi23 = xi2.rearrange("p (q c) -> p q c", q=NQ)
    rb3 = r_bf[:].rearrange("p (q c) -> p q c", q=NQ)
    kb3 = k_bf[:].rearrange("p (q c) -> p q c", q=NQ)
    E1_3 = E1f.rearrange("p (q c) -> p q c", q=NQ)
    E2_3 = E2[:].rearrange("p (q c) -> p q c", q=NQ)
    KtzA4 = KtzA[:].rearrange("p (h c) -> p h c", h=H)
    KtzB4 = KtzB[:].rearrange("p (h c) -> p h c", h=H)
    Rab3 = Rab[:].rearrange("p (q c) -> p q c", q=NQ)
    rku3 = rku.rearrange("p (q c) -> p q c", q=NQ)
    yngfm3 = yng_fm[:].rearrange("p (q c) -> p q c", q=NQ)
    scr3 = scr[:].rearrange("p (q c) -> p q c", q=NQ)

    def wload(dst, src, eng="gpsimd"):
        return p.dma(eng, lambda e, d=dst, s=src: e.dma_start(out=d, in_=s))

    t_ident = wload(ident[:], ident_h)
    t_utri = wload(utri[:], utri_h)
    t_mask4 = wload(mask4[:], mask4_h)
    t_ind2 = wload(ind2[:], ind2_h)
    t_zc = p.op("gpsimd", lambda e: e.memset(zc[:], 0.0))
    t_tmaa = p.dma("sync", lambda e: e.dma_start(out=tmaa_sb[:], in_=tmaa_h.rearrange("o (a p) -> p (o a)", p=128), allow_slow_non_contiguous=True))
    t_u = p.dma("sync", lambda e: e.dma_start(out=u_sb[:], in_=u_h.rearrange("o (a p) -> p (o a)", p=128), allow_slow_non_contiguous=True))
    t_iu = p.op("gpsimd", lambda e: e.tensor_tensor(
        out=ind2u[:].rearrange("p (q t) -> p q t", q=NQ),
        in0=u_sb[:].rearrange("p q -> p q ()").broadcast_to((128, NQ, 2)),
        in1=ind2[:].rearrange("p t -> p () t").broadcast_to((128, NQ, 2)),
        op=OP.mult), deps=[t_u, t_ind2])
    t_tmw1 = wload(tmw1_sb[:].rearrange("p (a c) -> p a c", a=NQ), tmw1_h.rearrange("(a p) c -> p a c", p=128))
    t_tdw1 = wload(tdw1_sb[:].rearrange("p (a c) -> p a c", a=NQ), tdw1_h.rearrange("(a p) c -> p a c", p=128))
    t_td2 = wload(td2aug_sb[:], td2aug_h)
    t_lnw = wload(lnwrow[:], lnw_h)
    t_lnb = wload(lnbrow[:], lnb_h)
    t_fb = wload(fbrow[:], fb_h)
    t_w2z = [p.op("gpsimd", lambda e, j=j: e.memset(w2p[j][:], 0.0)) for j in range(3)]
    t_w2 = []
    for i in range(5):
        j, base = i // 2, (i % 2) * 64
        t_w2.append(p.dma("gpsimd", lambda e, i=i, j=j, base=base: e.dma_start(
            out=w2p[j][base:base + 33, :], in_=w2aug_h[33 * i:33 * (i + 1), :]),
            deps=[t_w2z[j]]))
    t_ones = p.op("gpsimd", lambda e: e.memset(ones1[:], 1.0))
    t_eps = p.op("gpsimd", lambda e: e.memset(eps_sb[:], EPS))
    t_mz = [p.op("gpsimd", lambda e, j=j: e.memset(m_sb[j][:], 0.0)) for j in range(3)]
    t_mones = []
    for i in range(5):
        j, base = i // 2, (i % 2) * 64
        t_mones.append(p.op("gpsimd", lambda e, j=j, base=base: e.memset(m_sb[j][base + 32:base + 33, :], 1.0),
                            deps=[t_mz[j]]))
    t_tdones = p.op("gpsimd", lambda e: e.memset(tanhtd[64:65, :], 1.0))
    t_W = {}
    for n in ["fT", "k", "r", "v", "g", "o"]:
        key = "f" if n == "fT" else n
        t_W[key] = wload(W_sb[key][:].rearrange("p (a c) -> p a c", a=NQ),
                         w_h[n].rearrange("(a p) c -> p a c", p=128))
    t_lu = p.dma("sync", lambda e: e.dma_start(out=lu_v, in_=lu_h))
    t_ld = p.dma("sync", lambda e: e.dma_start(out=ld_v, in_=ld_h))
    t_da = p.dma("sync", lambda e: e.dma_start(out=da_v, in_=da_h))
    t_uaT = p.dma("sync", lambda e: e.dma_start(out=uaT_v, in_=uaT_h))

    mm = [None] * 4
    mm[0] = p.op("tensor", lambda e: e.matmul(pb[0][:], lhsT=ones1[:], rhs=lnwrow[:, 0:512], start=True, stop=True), deps=[t_ones, t_lnw])
    mm[1] = p.op("tensor", lambda e: e.matmul(pb[1][:], lhsT=ones1[:], rhs=lnwrow[:, 512:1024], start=True, stop=True))
    mm[2] = p.op("tensor", lambda e: e.matmul(pb[2][:], lhsT=ones1[:], rhs=lnbrow[:, 0:512], start=True, stop=True), deps=[t_lnb])
    mm[3] = p.op("tensor", lambda e: e.matmul(pb[3][:], lhsT=ones1[:], rhs=lnbrow[:, 512:1024], start=True, stop=True))
    ev = [None] * 4
    ev[0] = p.op("vector", lambda e: e.tensor_copy(lnw_bct[:, 0:512], pb[0][:]), deps=[mm[0]])
    ev[1] = p.op("vector", lambda e: e.tensor_copy(lnw_bct[:, 512:1024], pb[1][:]), deps=[mm[1]])
    ev[2] = p.op("vector", lambda e: e.tensor_copy(lnb_bct[:, 0:512], pb[2][:]), deps=[mm[2]])
    ev[3] = p.op("vector", lambda e: e.tensor_copy(lnb_bct[:, 512:1024], pb[3][:]), deps=[mm[3]])

    c1a = p.op("tensor", lambda e: e.matmul(pb[4][0:4, :], lhsT=lu_v, rhs=uaT_v[:, 0:512], start=True, stop=True), deps=[t_lu, t_uaT])
    c1b = p.op("tensor", lambda e: e.matmul(pb[5][0:4, :], lhsT=lu_v, rhs=uaT_v[:, 512:1024], start=True, stop=True))
    c1e = p.op("vector", lambda e: e.tensor_copy(c1_v[:, 0:512], pb[4][0:4, :]), deps=[c1a])
    c1f = p.op("vector", lambda e: e.tensor_copy(c1_v[:, 512:1024], pb[5][0:4, :]), deps=[c1b])
    c2a = p.op("tensor", lambda e: e.matmul(pb[6][0:100, :], lhsT=ld_v, rhs=c1_v[:, 0:512], start=True, stop=True), deps=[t_ld, c1e])
    c2b = p.op("tensor", lambda e: e.matmul(pb[7][0:100, :], lhsT=ld_v, rhs=c1_v[:, 512:1024], start=True, stop=True), deps=[c1f])
    c2e = p.op("vector", lambda e: e.tensor_copy(c2_v[:, 0:512], pb[6][0:100, :]), deps=[c2a])
    c2f = p.op("vector", lambda e: e.tensor_copy(c2_v[:, 512:1024], pb[7][0:100, :]), deps=[c2b])
    bank_free = {0: ev[0], 1: ev[1], 2: ev[2], 3: ev[3], 4: c1e, 5: c1f, 6: c2e, 7: c2f}
    weff_toks = []
    for q in range(NQ):
        b0 = (q % 2) * 2
        mm_a = p.op("tensor", lambda e, q=q, b0=b0: e.matmul(
            pb[b0][:], lhsT=da_v[:, q * 128:(q + 1) * 128], rhs=c2_v[:, 0:512], start=True, stop=True),
            deps=[t_da, c2e, c2f, bank_free[b0]])
        mm_b = p.op("tensor", lambda e, q=q, b0=b0: e.matmul(
            pb[b0 + 1][:], lhsT=da_v[:, q * 128:(q + 1) * 128], rhs=c2_v[:, 512:1024], start=True, stop=True),
            deps=[bank_free[b0 + 1]])
        we_a = p.op("vector", lambda e, q=q, b0=b0: e.tensor_tensor(
            out=W_sb["f"][:, q * C:q * C + 512], in0=pb[b0][:], in1=W_sb["f"][:, q * C:q * C + 512], op=OP.add),
            deps=[mm_a, t_W["f"]])
        we_b = p.op("vector", lambda e, q=q, b0=b0: e.tensor_tensor(
            out=W_sb["f"][:, q * C + 512:(q + 1) * C], in0=pb[b0 + 1][:], in1=W_sb["f"][:, q * C + 512:(q + 1) * C], op=OP.add),
            deps=[mm_b])
        bank_free[b0] = we_a
        bank_free[b0 + 1] = we_b
        weff_toks += [we_a, we_b]
    if debug:
        dbg_toks.append(p.dma("gpsimd", lambda e: e.dma_start(out=dbg["d_weff"], in_=W_sb["f"][:]), deps=weff_toks))

    lil_done = weff_toks
    t_zA4 = p.op("gpsimd", lambda e: e.memset(A4[:], 0.0), deps=[c2a, c2b])
    t_zKtzA = p.op("gpsimd", lambda e: e.memset(KtzA[:], 0.0), deps=lil_done)
    t_zKtzB = p.op("gpsimd", lambda e: e.memset(KtzB[:], 0.0), deps=lil_done)
    t_zKhz = p.op("gpsimd", lambda e: e.memset(Khz[:], 0.0), deps=lil_done)
    t_zS0 = p.op("gpsimd", lambda e: e.memset(Sz[0][:], 0.0), deps=lil_done)
    t_zS1 = p.op("gpsimd", lambda e: e.memset(Sz[1][:], 0.0), deps=lil_done)
    t_zSbf = p.op("gpsimd", lambda e: e.memset(S_bfz[:], 0.0))

    dep = {
        "xnfm_r": [], "sx_r": [], "xi_r": [], "lw_r": [], "rk_r": [], "vg_r": [],
        "hi_r": [], "lo_r": [], "scr_r": [], "scr2_r": [], "xnb_r": [], "stm_r": [],
        "y_r": [], "su_r": [], "A4_r": [t_zA4], "KtzA_r": [t_zKtzA], "KtzB_r": [t_zKtzB],
        "Ru_r": [], "Rab_r": [], "Khf_r": [], "Khz_r": [t_zKhz],
        "E1_r": [], "E2_r": [], "cmf_r": [], "epl_r": [], "gs_r": [], "gn_r": [],
        "yngfm_r": [], "m_r": [[] for _ in range(5)], "ttd_r": [],
        "out_dma": [None, None],
        "Sz_w": [t_zS0, t_zS1], "Sz_r": [[], []], "Sbf_ready": t_zSbf,
    }

    MAGIC = 0x5f3759df
    I32 = mybir.dt.int32

    def emit_rsqrt(dst, var_ap, scratch, deps, niter=4):
        n = var_ap.shape[-1]
        ve = scratch[:, 0:n]
        h_ = scratch[:, n:2 * n]
        t_ = scratch[:, 2 * n:3 * n]
        tok = p.op("vector", lambda e: e.tensor_single_scalar(out=ve, in_=var_ap, scalar=EPS, op=OP.add), deps=deps)
        tok = p.op("vector", lambda e: e.tensor_single_scalar(out=h_, in_=ve, scalar=-0.5, op=OP.mult), deps=[tok])
        vi = ve.bitcast(I32)
        di = dst.bitcast(I32)
        tok = p.op("vector", lambda e: e.tensor_single_scalar(out=di, in_=vi, scalar=1, op=OP.arith_shift_right), deps=[tok])
        tok = p.op("vector", lambda e: e.tensor_single_scalar(out=di, in_=di, scalar=-1, op=OP.bitwise_xor), deps=[tok])
        tok = p.op("vector", lambda e: e.tensor_single_scalar(out=di, in_=di, scalar=MAGIC + 1, op=OP.add), deps=[tok])
        for _ in range(niter):
            tok = p.op("vector", lambda e: e.tensor_tensor(out=t_, in0=dst, in1=dst, op=OP.mult), deps=[tok])
            tok = p.op("vector", lambda e: e.tensor_tensor(out=t_, in0=t_, in1=h_, op=OP.mult), deps=[tok])
            tok = p.op("vector", lambda e: e.scalar_tensor_tensor(out=dst, in0=t_, scalar=1.5, in1=dst, op0=OP.add, op1=OP.mult), deps=[tok])
        return tok

    out_tokens = []

    def bank_mm(bi, fn, deps=()):
        return p.op("tensor", fn, deps=list(deps) + [bank_free[bi]])

    GUARD_KEYS = ["xnfm_r", "sx_r", "xi_r", "lw_r", "rk_r", "vg_r", "hi_r", "lo_r",
                  "scr_r", "scr2_r", "xnb_r", "stm_r", "y_r", "su_r", "A4_r", "KtzA_r", "KtzB_r",
                  "Ru_r", "Rab_r", "Khf_r", "Khz_r", "E1_r", "E2_r", "cmf_r",
                  "epl_r", "gs_r", "gn_r", "yngfm_r", "ttd_r"]
    TS = {}

    def PF1(t):
        if debug and dbg_toks:
            for kk_ in GUARD_KEYS:
                dep[kk_] = dep[kk_] + dbg_toks
            dep["Sz_r"][0] = dep["Sz_r"][0] + dbg_toks
            dep["Sz_r"][1] = dep["Sz_r"][1] + dbg_toks
            dbg_toks.clear()
        S = TS.setdefault(t, {})
        xb = t % 2
        xt = x_tm[xb]

        t_x = p.dma("sync", lambda e, t=t, xt=xt: e.dma_start(
            out=xt[:], in_=x_h[t * TT:(t + 1) * TT, :]), deps=[dep["out_dma"][xb]])

        t_r1 = p.op("vector", lambda e, xt=xt: e.tensor_reduce(
            out=ln_sc[:, 0:1], in_=xt[:], axis=AX.X, op=OP.add), deps=[t_x])
        t_sq = p.op("scalar", lambda e, xt=xt: e.activation(
            out=lnE, in_=xt[:], func=AF.Square), deps=[t_x] + dep["E2_r"])
        dep["E2_r"] = []
        t_r2 = p.op("vector", lambda e: e.tensor_reduce(
            out=ln_sc[:, 1:2], in_=lnE, axis=AX.X, op=OP.add), deps=[t_sq])
        t_m = p.op("vector", lambda e: e.tensor_scalar_mul(ln_sc[:, 2:3], ln_sc[:, 0:1], 1.0 / C), deps=[t_r1])
        t_msq = p.op("vector", lambda e: e.tensor_scalar(
            out=ln_sc[:, 3:4], in0=ln_sc[:, 2:3], scalar1=ln_sc[:, 2:3], scalar2=1.0,
            op0=OP.mult, op1=OP.mult), deps=[t_m])
        t_var = p.op("vector", lambda e: e.scalar_tensor_tensor(
            out=ln_sc[:, 4:5], in0=ln_sc[:, 1:2], scalar=1.0 / C, in1=ln_sc[:, 3:4],
            op0=OP.mult, op1=OP.subtract), deps=[t_r2, t_msq])
        t_ln1 = p.op("scalar", lambda e: e.activation(
            out=ln_sc[:, 5:6], in_=ln_sc[:, 4:5], func=AF.Ln, bias=eps_sb[:]),
            deps=[t_var, t_eps])
        t_rstd = p.op("scalar", lambda e: e.activation(
            out=ln_sc[:, 6:7], in_=ln_sc[:, 5:6], func=AF.Exp, scale=-0.5), deps=[t_ln1])
        t_xn = p.op("vector", lambda e, xt=xt: e.tensor_scalar(
            out=lnE, in0=xt[:], scalar1=ln_sc[:, 2:3], scalar2=ln_sc[:, 6:7],
            op0=OP.subtract, op1=OP.mult), deps=[t_rstd, t_r2])
        t_hi = p.op("scalar", lambda e: e.activation(
            out=hi_tm, in_=lnE, func=AF.Copy), deps=[t_xn] + dep["hi_r"] + dep["E1_r"])
        t_lo = p.op("vector", lambda e: e.tensor_tensor(
            out=lo_tm, in0=lnE, in1=hi_tm, op=OP.subtract), deps=[t_hi] + dep["lo_r"] + dep["Rab_r"])
        dep["E2_r"] = [t_hi, t_lo]

        S["t_hi"] = t_hi
        S["t_lo"] = t_lo
        S["xt"] = xt
        S["xb"] = xb

    def PF2(t):
        S = TS[t]
        t_hi = S["t_hi"]
        t_lo = S["t_lo"]

        if t > 0:
            t_bnd = p.op("vector", lambda e: e.tensor_copy(xnfm3[:, :, 0:1], xnfm3[:, :, TT:TT + 1]),
                         deps=dep["xnfm_r"])
        else:
            t_bnd = p.op("vector", lambda e: e.memset(xnfm3[:, :, 0:1], 0.0), deps=dep["xnfm_r"])
        dep["xnfm_r"] = []
        mms = []
        for q in range(NQ):
            bi = q // 4
            bank_mm(bi, lambda e, q=q, bi=bi: e.matmul(
                pb[bi][:, (q % 4) * 128:(q % 4) * 128 + 128],
                lhsT=hi_tm[:, q * 128:(q + 1) * 128], rhs=ident[:], start=True, stop=False),
                deps=[t_hi, t_ident])
            mm2 = p.op("tensor", lambda e, q=q, bi=bi: e.matmul(
                pb[bi][:, (q % 4) * 128:(q % 4) * 128 + 128],
                lhsT=lo_tm[:, q * 128:(q + 1) * 128], rhs=ident[:], start=False, stop=True),
                deps=[t_lo])
            mms.append(mm2)
        dep["hi_r"] = [mms[-1]]
        dep["lo_r"] = [mms[-1]]
        dep["E1_r"].append(mms[-1])
        dep["Rab_r"].append(mms[-1])
        xn_ev = []
        for half in range(2):
            tok = p.op("scalar", lambda e, half=half: e.copy(
                xnfm3[:, 4 * half:4 * half + 4, 1:1 + TT],
                pb[half][:].rearrange("p (q c) -> p q c", q=4)),
                deps=[mms[4 * half + 3], t_bnd])
            bank_free[half] = tok
            xn_ev.append(tok)

        S["xn_ev"] = xn_ev

        t_xnbp = p.op("vector", lambda e: e.tensor_copy(xnb3, xnfm3[:, :, 1:TT + 1]),
                      deps=xn_ev + dep["xnb_r"])
        dep["xnb_r"] = []
        dep["xnfm_r"].append(t_xnbp)
        t_sx = p.op("vector", lambda e: e.tensor_tensor(
            out=sxb3, in0=xnfm3[:, :, 0:TT], in1=xnfm3[:, :, 1:TT + 1], op=OP.subtract),
            deps=xn_ev + dep["sx_r"])
        dep["sx_r"] = []
        dep["xnfm_r"].append(t_sx)
        t_xxx = []
        for q in range(NQ):
            tok = p.op("vector", lambda e, q=q: e.scalar_tensor_tensor(
                out=xi3[:, q, :], in0=sxb3[:, q, :], scalar=tmaa_sb[:, q:q + 1],
                in1=xnb3[:, q, :], op0=OP.mult, op1=OP.add),
                deps=[t_sx, t_tmaa, t_xnbp] + dep["xi_r"])
            t_xxx.append(tok)
            dep["xnb_r"].append(tok)
            dep["sx_r"].append(tok)
        dep["xi_r"] = []

        mm_last = {}
        for part in range(2):
            cols = 128 if part == 0 else 32
            for q in range(NQ):
                tok = bank_mm(4, lambda e, part=part, q=q, cols=cols: e.matmul(
                    pb[4][0:cols, part * 128:part * 128 + TT],
                    lhsT=tmw1_sb[:, q * 160 + part * 128:q * 160 + part * 128 + cols],
                    rhs=xi_bf[:, q * TT:(q + 1) * TT],
                    start=(q == 0), stop=(q == NQ - 1)),
                    deps=[t_tmw1, t_xxx[q]])
                if q == NQ - 1:
                    mm_last[part] = tok
        m_toks = []
        for i in range(5):
            part = 0 if i < 4 else 1
            rs = (i * 32) % 128
            j, base = i // 2, (i % 2) * 64
            tok = p.op("scalar", lambda e, part=part, rs=rs, j=j, base=base: e.activation(
                out=m_sb[j][base:base + 32, :], in_=pb[4][rs:rs + 32, part * 128:part * 128 + TT],
                func=AF.Tanh),
                deps=[mm_last[0], mm_last[1], t_mones[i]] + dep["m_r"][i])
            dep["m_r"][i] = []
            m_toks.append(tok)
        bank_free[4] = m_toks[4]

        S["m_toks"] = m_toks
        S["t_sx"] = t_sx
        S["t_xnbp"] = t_xnbp

    def PF3(t):
        S = TS[t]
        m_toks = S["m_toks"]
        t_sx = S["t_sx"]
        t_xnbp = S["t_xnbp"]
        xn_ev = S["xn_ev"]

        proj_ev = {"r": [], "k": [], "v": [], "g": []}
        lw_toks = []

        def emit_delta(i):
            jm, base = i // 2, (i % 2) * 64
            db0 = (i % 2) * 2
            dmm = []
            for q in range(NQ):
                bi = db0 + q // 4
                tok = bank_mm(bi, lambda e, q=q, bi=bi, jm=jm, base=base: e.matmul(
                    pb[bi][:, (q % 4) * 128:(q % 4) * 128 + 128],
                    lhsT=w2p[jm][base:base + 64, q * 128:(q + 1) * 128],
                    rhs=m_sb[jm][base:base + 64, :], start=True, stop=True),
                    deps=[t_w2[i], m_toks[i]])
                dmm.append(tok)
                dep["m_r"][i].append(tok)
            return dmm

        next_dmm = emit_delta(0)
        prev_xi = None
        for i in range(5):
            db0 = (i % 2) * 2
            dmm = next_dmm
            tmix = []
            for half in range(2):
                bi = db0 + half
                tok = p.op("vector", lambda e, bi=bi, half=half: e.tensor_tensor(
                    out=dmix[:, half * 512:(half + 1) * 512], in0=sx_bf[:, half * 512:(half + 1) * 512],
                    in1=pb[bi][:], op=OP.mult),
                    deps=[dmm[4 * half + 3], t_sx] + (dep["E2_r"] if i == 0 else [prev_xi]))
                bank_free[bi] = tok
                tmix.append(tok)
                dep["sx_r"].append(tok)
            if i == 0:
                dep["E2_r"] = []
            xcur = xi_bf[:] if (i % 2 == 0) else xi2
            xkey = "xi_r" if (i % 2 == 0) else "E1_r"
            xi_tok = p.op("vector", lambda e, xcur=xcur: e.tensor_tensor(
                out=xcur, in0=dmix[:], in1=xnb[:], op=OP.add),
                deps=tmix + [t_xnbp] + dep[xkey])
            dep["xnb_r"].append(xi_tok)
            prev_xi = xi_tok
            dep[xkey] = []
            if i < 4:
                next_dmm = emit_delta(i + 1)
            if i == 4:
                dep["E2_r"] = [xi_tok]
            if debug and t == 0 and i == 3:
                tok_dxi = p.dma("gpsimd", lambda e: e.dma_start(out=dbg["d_xi"], in_=xi_bf[:]), deps=[xi_tok])
                dep["xi_r"].append(_dt(tok_dxi))

            if i == 0:
                last = None
                for q in range(NQ):
                    last = bank_mm(5, lambda e, q=q: e.matmul(
                        pb[5][0:64, 0:TT], lhsT=tdw1_sb[:, q * 64:(q + 1) * 64],
                        rhs=xi_bf[:, q * TT:(q + 1) * TT], start=(q == 0), stop=(q == NQ - 1)),
                        deps=[t_tdw1, xi_tok])
                    dep["xi_r"].append(last)
                t_tanh = p.op("scalar", lambda e: e.activation(
                    out=tanhtd[0:64, :], in_=pb[5][0:64, 0:TT], func=AF.Tanh),
                    deps=[last, t_tdones] + dep["ttd_r"])
                dep["ttd_r"] = []
                bank_free[5] = t_tanh
                wmm = []
                for half in range(2):
                    bi = 6 + half
                    tok = bank_mm(bi, lambda e, half=half, bi=bi: e.matmul(
                        pb[bi][:], lhsT=tanhtd[:],
                        rhs=td2aug_sb[:, half * 512:(half + 1) * 512], start=True, stop=True),
                        deps=[t_td2, t_tanh])
                    wmm.append(tok)
                    dep["ttd_r"].append(tok)
                for half in range(2):
                    tok_lw = p.op("scalar", lambda e, half=half: e.activation(
                        out=lw_tm[:, half * 512:(half + 1) * 512], in_=pb[6 + half][:], func=AF.Exp),
                        deps=[wmm[half]] + dep["lw_r"])
                    bank_free[6 + half] = tok_lw
                    lw_toks.append(tok_lw)
                dep["lw_r"] = []
            elif i in (1, 3):
                name = "k" if i == 1 else "r"
                dst = k_bf if i == 1 else r_bf
                mm_l = {}
                for qo in range(NQ):
                    bi = 4 + qo // 4
                    for qi in range(NQ):
                        tok = bank_mm(bi, lambda e, name=name, qo=qo, qi=qi, bi=bi: e.matmul(
                            pb[bi][:, (qo % 4) * 128:(qo % 4) * 128 + 128],
                            lhsT=W_sb[name][:, qi * C + qo * 128:qi * C + qo * 128 + 128],
                            rhs=xi2[:, qi * TT:(qi + 1) * TT],
                            start=(qi == 0), stop=(qi == NQ - 1)),
                            deps=[t_W[name], xi_tok])
                        dep["E1_r"].append(tok)
                        if qi == NQ - 1:
                            mm_l[qo] = tok
                for half in range(2):
                    bi = 4 + half
                    tok = p.op("scalar", lambda e, half=half, bi=bi, dst=dst: e.copy(
                        dst[:, half * 4 * TT:(half + 1) * 4 * TT], pb[bi][:]),
                        deps=[mm_l[4 * half + 0], mm_l[4 * half + 1], mm_l[4 * half + 2], mm_l[4 * half + 3]]
                        + (dep["rk_r"] if i == 1 and half == 0 else []))
                    bank_free[bi] = tok
                    proj_ev[name].append(tok)
                if i == 3:
                    dep["rk_r"] = []
            else:
                name = "v" if i == 2 else "g"
                dst = v_tm if i == 2 else g_raw
                mm_l = {}
                for half in range(2):
                    bi = 6 + half
                    for qi in range(NQ):
                        tok = bank_mm(bi, lambda e, name=name, half=half, qi=qi, bi=bi: e.matmul(
                            pb[bi][:],
                            lhsT=xi_bf[:, qi * TT:(qi + 1) * TT],
                            rhs=W_sb[name][:, qi * C + half * 512:qi * C + half * 512 + 512],
                            start=(qi == 0), stop=(qi == NQ - 1)),
                            deps=[t_W[name], xi_tok])
                        dep["xi_r"].append(tok)
                        if qi == NQ - 1:
                            mm_l[half] = tok
                for half in range(2):
                    bi = 6 + half
                    tok = p.op("scalar", lambda e, half=half, bi=bi, dst=dst: e.copy(
                        dst[:, half * 512:(half + 1) * 512], pb[bi][:]),
                        deps=[mm_l[half]] + (dep["vg_r"] if i == 2 and half == 0 else []))
                    bank_free[bi] = tok
                    proj_ev[name].append(tok)
                if i == 4:
                    dep["vg_r"] = []

        if debug and t == 0:
            dep["xnfm_r"].append(_dt(p.dma("sync", lambda e: e.dma_start(out=dbg["d_xnfm"], in_=xnfm3[:, :, 1:TT + 1]), deps=xn_ev)))
            dep["sx_r"].append(_dt(p.dma("sync", lambda e: e.dma_start(out=dbg["d_sx"], in_=sx_bf[:]), deps=[t_sx])))
            dep["rk_r"].append(_dt(p.dma("gpsimd", lambda e: e.dma_start(out=dbg["d_r"], in_=r_bf[:]), deps=proj_ev["r"])))
            dep["rk_r"].append(_dt(p.dma("gpsimd", lambda e: e.dma_start(out=dbg["d_k"], in_=k_bf[:]), deps=proj_ev["k"])))
            dep["vg_r"].append(_dt(p.dma("gpsimd", lambda e: e.dma_start(out=dbg["d_v"], in_=v_tm[:]), deps=proj_ev["v"])))
            dep["vg_r"].append(_dt(p.dma("gpsimd", lambda e: e.dma_start(out=dbg["d_g"], in_=g_raw[:]), deps=proj_ev["g"])))
            dep["lw_r"].append(_dt(p.dma("gpsimd", lambda e: e.dma_start(out=dbg["d_lw"], in_=lw_tm[:]), deps=lw_toks)))

        S["proj_ev"] = proj_ev
        S["lw_toks"] = lw_toks

    def PB1(t):
        S = TS[t]
        proj_ev = S["proj_ev"]
        lw_toks = S["lw_toks"]
        t_xnbp = S["t_xnbp"]

        wave1 = {}
        for q in range(NQ):
            m_ex = bank_mm(q // 4, lambda e, q=q: e.matmul(
                pb[q // 4][:, (q % 4) * 128:(q % 4) * 128 + 128],
                lhsT=lw_tm[:, q * 128:(q + 1) * 128],
                rhs=utri[:, 0:128], start=True, stop=True),
                deps=[lw_toks[q // 4], t_utri])
            m_ia = bank_mm(2 + q // 4, lambda e, q=q: e.matmul(
                pb[2 + q // 4][:, (q % 4) * 128:(q % 4) * 128 + 128],
                lhsT=lw_tm[:, q * 128:(q + 1) * 128],
                rhs=utri[:, 4 * 128:5 * 128], start=True, stop=True))
            m_ib = bank_mm(4 + q // 4, lambda e, q=q: e.matmul(
                pb[4 + q // 4][:, (q % 4) * 128:(q % 4) * 128 + 128],
                lhsT=lw_tm[:, q * 128:(q + 1) * 128],
                rhs=utri[:, 5 * 128:6 * 128], start=True, stop=True))
            wave1[q] = (m_ex, m_ia, m_ib)
            dep["lw_r"] += [m_ex, m_ia, m_ib]
        eRu_t = [p.op("scalar", lambda e, hf=hf: e.activation(
            out=E1f[:, hf * 512:(hf + 1) * 512], in_=pb[hf][:], func=AF.Exp, scale=-1.0),
            deps=[wave1[4 * hf + 3][0]] + dep["E1_r"]) for hf in range(2)]
        eA_t = [p.op("scalar", lambda e, hf=hf: e.activation(
            out=E2_3[:, 4 * hf:4 * hf + 4, 0:64],
            in_=pb[2 + hf][:].rearrange("p (qq c) -> p qq c", qq=4)[:, :, 0:64],
            func=AF.Exp, scale=1.0),
            deps=[wave1[4 * hf + 3][1]] + dep["E2_r"]) for hf in range(2)]
        negf = [p.op("scalar", lambda e, q=q: e.activation(
            out=cmf[:, q:q + 1], in_=pb[4 + q // 4][:, (q % 4) * 128 + 127:(q % 4) * 128 + 128],
            func=AF.Copy, scale=-1.0), deps=[wave1[(q // 4) * 4 + 3][2]] + dep["cmf_r"]) for q in range(NQ)]
        Khf3 = Khf[:].rearrange("p (q c) -> p q c", q=NQ)
        ekhf = [p.op("scalar", lambda e, q=q: e.activation(
            out=Khf3[:, q, :], in_=pb[4 + q // 4][:, (q % 4) * 128:(q % 4) * 128 + 128],
            func=AF.Exp, bias=cmf[:, q:q + 1]),
            deps=[negf[q]] + (dep["Khf_r"] if q == 0 else [])) for q in range(NQ)]
        dep["E1_r"] = []
        dep["E2_r"] = []
        dep["cmf_r"] = ekhf[:]
        dep["Khf_r"] = []
        t_ru = p.op("gpsimd", lambda e: e.tensor_tensor(
            out=Ru[:].rearrange("p (q c) -> p q c", q=NQ), in0=rb3, in1=E1_3, op=OP.mult),
            deps=eRu_t + dep["Ru_r"] + proj_ev["r"])
        dep["Ru_r"] = []
        t_khf = p.op("gpsimd", lambda e: e.tensor_tensor(
            out=Khf3, in0=kb3, in1=Khf3, op=OP.mult),
            deps=ekhf + proj_ev["k"])
        ktzA_t = []
        for pr in range(2):
            ktzA_t.append(p.op("gpsimd", lambda e, pr=pr: e.tensor_tensor(
                out=KtzA4[pr * 64:pr * 64 + 64, pr::2, 0:64],
                in0=kb3[pr * 64:pr * 64 + 64, :, 0:64],
                in1=E2_3[pr * 64:pr * 64 + 64, :, 0:64], op=OP.mult),
                deps=eA_t + dep["KtzA_r"] + proj_ev["k"]))
        dep["KtzA_r"] = []
        eB_t = [p.op("scalar", lambda e, hf=hf: e.activation(
            out=E1f[:, hf * 512:(hf + 1) * 512], in_=pb[4 + hf][:], func=AF.Exp, scale=1.0),
            deps=[t_ru, wave1[4 * hf + 3][2]]) for hf in range(2)]
        bank_free[0] = eRu_t[0]
        bank_free[1] = eRu_t[1]
        bank_free[2] = eA_t[0]
        bank_free[3] = eA_t[1]
        bank_free[4] = eB_t[0]
        bank_free[5] = eB_t[1]
        ktzB_t = []
        for pr in range(2):
            ktzB_t.append(p.op("gpsimd", lambda e, pr=pr: e.tensor_tensor(
                out=KtzB4[pr * 64:pr * 64 + 64, pr::2, :],
                in0=kb3[pr * 64:pr * 64 + 64, :, :],
                in1=E1_3[pr * 64:pr * 64 + 64, :, :], op=OP.mult),
                deps=eB_t + dep["KtzB_r"] + proj_ev["k"]))
        dep["KtzB_r"] = []
        wave2 = {}
        for q in range(NQ):
            m_xa = bank_mm(q // 4, lambda e, q=q: e.matmul(
                pb[q // 4][:, (q % 4) * 128:(q % 4) * 128 + 128],
                lhsT=lw_tm[:, q * 128:(q + 1) * 128],
                rhs=utri[:, 2 * 128:3 * 128], start=True, stop=True))
            m_xb = bank_mm(2 + q // 4, lambda e, q=q: e.matmul(
                pb[2 + q // 4][:, (q % 4) * 128:(q % 4) * 128 + 128],
                lhsT=lw_tm[:, q * 128:(q + 1) * 128],
                rhs=utri[:, 3 * 128:4 * 128], start=True, stop=True))
            m_in = bank_mm(4 + q // 4, lambda e, q=q: e.matmul(
                pb[4 + q // 4][:, (q % 4) * 128:(q % 4) * 128 + 128],
                lhsT=lw_tm[:, q * 128:(q + 1) * 128],
                rhs=utri[:, 1 * 128:2 * 128], start=True, stop=True))
            wave2[q] = (m_xa, m_xb, m_in)
            dep["lw_r"] += [m_xa, m_xb, m_in]
        eRA_t = [p.op("scalar", lambda e, hf=hf: e.activation(
            out=E2_3[:, 4 * hf:4 * hf + 4, 0:64],
            in_=pb[hf][:].rearrange("p (qq c) -> p qq c", qq=4)[:, :, 0:64],
            func=AF.Exp, scale=-1.0),
            deps=[wave2[4 * hf + 3][0]] + ktzA_t) for hf in range(2)]
        eRB_t = [p.op("scalar", lambda e, hf=hf: e.activation(
            out=E1_3[:, 4 * hf:4 * hf + 4, 64:128],
            in_=pb[2 + hf][:].rearrange("p (qq c) -> p qq c", qq=4)[:, :, 64:128],
            func=AF.Exp, scale=-1.0),
            deps=[wave2[4 * hf + 3][1], t_khf] + ktzB_t) for hf in range(2)]
        epl_toks = []
        for q in range(NQ):
            epl_toks.append(p.op("scalar", lambda e, q=q: e.activation(
                out=epl[:, q:q + 1], in_=pb[4 + q // 4][:, (q % 4) * 128 + 127:(q % 4) * 128 + 128],
                func=AF.Exp, scale=-1.0),
                deps=[wave2[(q // 4) * 4 + 3][2]] + dep["epl_r"]))
        dep["epl_r"] = []
        bank_free[0] = eRA_t[0]
        bank_free[1] = eRA_t[1]
        bank_free[2] = eRB_t[0]
        bank_free[3] = eRB_t[1]
        bank_free[4] = epl_toks[3]
        bank_free[5] = epl_toks[7]
        t_ra = p.op("gpsimd", lambda e: e.tensor_tensor(
            out=Rab3[:, :, 0:64], in0=rb3[:, :, 0:64], in1=E2_3[:, :, 0:64], op=OP.mult),
            deps=eRA_t + dep["Rab_r"] + proj_ev["r"])
        t_rb = p.op("gpsimd", lambda e: e.tensor_tensor(
            out=Rab3[:, :, 64:128], in0=rb3[:, :, 64:128], in1=E1_3[:, :, 64:128], op=OP.mult),
            deps=eRB_t + dep["Rab_r"] + proj_ev["r"])
        dep["Rab_r"] = []
        t_rku1 = p.op("gpsimd", lambda e: e.tensor_tensor(
            out=rku3, in0=rb3, in1=kb3, op=OP.mult),
            deps=proj_ev["r"] + proj_ev["k"] + dep["sx_r"])


        mm_f = {}
        for half in range(2):
            bi = 6 + half
            for qi in range(NQ):
                tok = bank_mm(bi, lambda e, half=half, qi=qi, bi=bi: e.matmul(
                    pb[bi][:],
                    lhsT=xnb[:, qi * TT:(qi + 1) * TT],
                    rhs=W_sb["f"][:, qi * C + half * 512:qi * C + half * 512 + 512],
                    start=(qi == 0), stop=False),
                    deps=[t_xnbp] + (weff_toks if t == 0 else []))
                dep["xnb_r"].append(tok)
            mm_f[half] = p.op("tensor", lambda e, half=half, bi=bi: e.matmul(
                pb[bi][:], lhsT=ones1[:], rhs=fbrow[:, half * 512:(half + 1) * 512],
                start=False, stop=True), deps=[t_fb, t_ones])
            bank_free[bi] = None

        kht_mm = {}
        for q in range(NQ):
            bi2 = 2 + q // 4
            kht_mm[q] = bank_mm(bi2, lambda e, q=q, bi2=bi2: e.matmul(
                pb[bi2][:, (q % 4) * 128:(q % 4) * 128 + 128],
                lhsT=Khf[:, q * 128:(q + 1) * 128], rhs=ident[:], start=True, stop=True),
                deps=[t_khf])
        khz_toks = {}
        for q in range(NQ):
            bi2 = 2 + q // 4
            for pr in range(2):
                h = 2 * q + pr
                khz_toks[h] = p.op("scalar", lambda e, q=q, bi2=bi2, pr=pr, h=h: e.activation(
                    out=Khz[:, h * 128 + pr * 64:h * 128 + pr * 64 + 64],
                    in_=pb[bi2][:, (q % 4) * 128 + pr * 64:(q % 4) * 128 + pr * 64 + 64],
                    func=AF.Copy),
                    deps=[kht_mm[(q // 4) * 4 + 3]] + dep["Khz_r"])
        dep["Khz_r"] = []
        bank_free[2] = khz_toks[7]
        bank_free[3] = khz_toks[15]

        a_mm = {}
        for h in range(H):
            bi = h // 4
            q = h // 2
            mA = bank_mm(bi, lambda e, h=h, bi=bi, q=q: e.matmul(
                pb[bi][:, (h % 4) * 128:(h % 4) * 128 + 64],
                lhsT=KtzA[:, h * 128:(h + 1) * 128],
                rhs=Rab[:, q * 128:q * 128 + 64],
                start=True, stop=True),
                deps=[ktzA_t[h % 2], t_ra])
            mB = bank_mm(bi, lambda e, h=h, bi=bi, q=q: e.matmul(
                pb[bi][:, (h % 4) * 128 + 64:(h % 4) * 128 + 128],
                lhsT=KtzB[:, h * 128:(h + 1) * 128],
                rhs=Rab[:, q * 128 + 64:q * 128 + 128],
                start=True, stop=True),
                deps=[ktzB_t[h % 2], t_rb])
            a_mm[h] = (mA, mB)
        a_ev = []
        for bi in range(4):
            tok = p.op("vector", lambda e, bi=bi: e.copy_predicated(
                out=A4[:, bi * 512:(bi + 1) * 512], mask=mask4[:], data=pb[bi][:]),
                deps=[a_mm[4 * bi + 3][1], t_mask4] + dep["A4_r"])
            bank_free[bi] = tok
            a_ev.append(tok)
        dep["A4_r"] = []

        su_mm = []
        for q in range(NQ):
            su_mm.append(bank_mm(0, lambda e, q=q: e.matmul(
                pb[0][:, 2 * q:2 * q + 2], lhsT=rku[:, q * 128:(q + 1) * 128],
                rhs=ind2u[:, 2 * q:2 * q + 2], start=True, stop=True),
                deps=[t_rku1, t_iu]))
        for tk in su_mm:
            dep["sx_r"].append(tk)
        t_su = p.op("vector", lambda e: e.tensor_copy(su_sb[:], pb[0][:, 0:H]), deps=su_mm + dep["su_r"])
        dep["su_r"] = []
        bank_free[0] = t_su


        if debug and t == 0:
            tok_der = p.dma("gpsimd", lambda e: e.dma_start(out=dbg["d_Er"], in_=Ru[:]), deps=[t_ru])
            dep["Ru_r"].append(_dt(tok_der))
            dep["A4_r"].append(_dt(p.dma("gpsimd", lambda e: e.dma_start(out=dbg["d_A"], in_=A4[:]), deps=a_ev)))
            dep["su_r"].append(_dt(p.dma("sync", lambda e: e.dma_start(out=dbg["d_su"], in_=su_sb[:]), deps=[t_su])))
            dep["KtzA_r"].append(_dt(p.dma("gpsimd", lambda e: e.dma_start(out=dbg["d_Ktz"], in_=KtzA[:]), deps=ktzA_t)))
            dep["Khz_r"].append(_dt(p.dma("gpsimd", lambda e: e.dma_start(out=dbg["d_Khz"], in_=Khz[:]), deps=list(khz_toks.values()))))

        dep["KtzA_r"] = dep["KtzA_r"] + [a_mm[h][0] for h in range(H)]
        dep["KtzB_r"] = [a_mm[h][1] for h in range(H)]
        dep["Rab_r"] = dep["Rab_r"] + [a_mm[h][0] for h in range(H)] + [a_mm[h][1] for h in range(H)]
        dep["Khf_r"] = list(kht_mm.values())
        dep["E1_r"] = dep["E1_r"] + [t_rb]
        dep["E2_r"] = [t_ra]
        dep["cmf_r"] = [t_khf]
        dep["rk_r"] = [t_ru, t_ra, t_rb, t_khf, t_rku1] + ktzA_t + ktzB_t
        S["t_su"] = t_su
        S["t_ru"] = t_ru
        S["t_khf"] = t_khf
        S["epl_toks"] = epl_toks
        S["khz_toks"] = khz_toks
        S["a_ev"] = a_ev
        S["a_mm"] = a_mm
        S["mm_f"] = mm_f

    def PB2a(t):
        S = TS[t]
        proj_ev = S["proj_ev"]
        t_su = S["t_su"]
        t_ru = S["t_ru"]
        t_khf = S["t_khf"]
        epl_toks = S["epl_toks"]
        khz_toks = S["khz_toks"]
        a_ev = S["a_ev"]
        a_mm = S["a_mm"]
        mm_f = S["mm_f"]
        par = t % 2
        Scur, Snxt = Sz[par], Sz[1 - par]

        t_vsu = p.op("gpsimd", lambda e: e.tensor_tensor(
            out=scr2[:].rearrange("p (h c) -> p h c", h=H),
            in0=v_tm[:].rearrange("p (h c) -> p h c", h=H),
            in1=su_sb[:].rearrange("p h -> p h ()").broadcast_to((128, H, K)),
            op=OP.mult),
            deps=[t_su] + proj_ev["v"] + dep["scr2_r"])
        dep["scr2_r"] = []
        t_sdec = p.op("gpsimd", lambda e, Scur=Scur, Snxt=Snxt: e.tensor_tensor(
            out=Snxt[:].rearrange("p (q c) -> p q c", q=NQ),
            in0=Scur[:].rearrange("p (q c) -> p q c", q=NQ),
            in1=epl[:].rearrange("p q -> p q ()").broadcast_to((128, NQ, 128)),
            op=OP.mult),
            deps=epl_toks + dep["Sz_r"][1 - par] + [dep["Sz_w"][1 - par], dep["Sz_w"][par]])
        dep["Sz_r"][1 - par] = []

        y_mm = {}
        s_mm = {}
        intra_mms = []
        for h in range(H):
            by = 4 + h // 8
            tok1 = bank_mm(by, lambda e, h=h, by=by: e.matmul(
                pb[by][:, (h % 8) * 64:(h % 8) * 64 + 64],
                lhsT=A4[:, h * 128:(h + 1) * 128],
                rhs=v_tm[:, h * K:h * K + K],
                start=True, stop=False), deps=[a_ev[h // 4]] + proj_ev["v"])
            intra_mms.append(tok1)
            y_mm[h] = p.op("tensor", lambda e, h=h, by=by: e.matmul(
                pb[by][:, (h % 8) * 64:(h % 8) * 64 + 64],
                lhsT=Ru[:, (h // 2) * 128:(h // 2) * 128 + 128],
                rhs=S_bfz[:, h * K:(h + 1) * K],
                start=False, stop=True), deps=[dep["Sbf_ready"], t_ru])
            bs = 2 + h // 8
            s_mm[h] = bank_mm(bs, lambda e, h=h, bs=bs: e.matmul(
                pb[bs][:, (h % 8) * 64:(h % 8) * 64 + 64],
                lhsT=Khz[:, h * 128:(h + 1) * 128],
                rhs=v_tm[:, h * K:h * K + K],
                start=True, stop=True), deps=[khz_toks[h]] + proj_ev["v"])
        yev = []
        sev = []
        for half in range(2):
            tok = p.op("vector", lambda e, half=half: e.tensor_tensor(
                out=y_tm[:, half * 512:half * 512 + 512],
                in0=scr2[:, half * 512:half * 512 + 512],
                in1=pb[4 + half][:], op=OP.add),
                deps=[y_mm[half * 8 + 7], t_vsu] + dep["y_r"])
            yev.append(tok)
            dep["scr2_r"].append(tok)
            tok2 = p.op("vector", lambda e, half=half, Snxt=Snxt: e.tensor_tensor(
                out=Snxt[:, half * 512:half * 512 + 512],
                in0=Snxt[:, half * 512:half * 512 + 512],
                in1=pb[2 + half][:], op=OP.add),
                deps=[s_mm[half * 8 + 7], t_sdec])
            sev.append(tok2)
        dep["y_r"] = []
        dep["Sz_r"][par] = [t_sdec]
        dep["Sz_r"][1 - par] = []
        dep["Sz_w"][1 - par] = sev[-1]
        bank_free[4] = yev[0]
        bank_free[5] = yev[1]
        bank_free[2] = sev[0]
        bank_free[3] = sev[1]
        t_sbf = p.op("vector", lambda e, Snxt=Snxt: e.tensor_copy(S_bfz[:], Snxt[:]),
                     deps=sev + list(y_mm.values()))
        dep["Sbf_ready"] = t_sbf
        dep["Sz_r"][1 - par].append(t_sbf)

        t_f2 = p.op("scalar", lambda e: e.activation(
            out=scr[:, 0:512], in_=pb[6][:], func=AF.Silu), deps=[mm_f[0]] + dep["scr_r"])
        t_f2b = p.op("scalar", lambda e: e.activation(
            out=scr[:, 512:1024], in_=pb[7][:], func=AF.Silu), deps=[mm_f[1]])
        bank_free[6] = t_f2
        bank_free[7] = t_f2b
        dep["scr_r"] = []

        dep["Ru_r"] = list(y_mm.values())
        dep["Khz_r"] = list(s_mm.values())
        dep["epl_r"] = [t_sdec, t_khf]
        dep["su_r"] = [t_vsu]
        dep["A4_r"] = dep["A4_r"] + intra_mms
        S["yev"] = yev
        S["sev"] = sev
        S["t_vsu"] = t_vsu
        S["t_f2"] = t_f2
        S["t_f2b"] = t_f2b

    def PB2b(t):
        S = TS[t]
        proj_ev = S["proj_ev"]
        yev = S["yev"]
        sev = S["sev"]
        t_vsu = S["t_vsu"]
        par = t % 2
        Snxt = Sz[1 - par]

        yg3 = y_tm[:].rearrange("p (h c) -> p h c", h=H)
        t_gm = p.op("vector", lambda e: e.tensor_reduce(
            out=gn_sc[:, 0:H], in_=yg3, axis=AX.X, op=OP.add), deps=yev + dep["gn_r"])
        dep["gn_r"] = []
        t_gsq = p.op("scalar", lambda e: e.activation(
            out=scr2[:], in_=y_tm[:], func=AF.Square), deps=yev + dep["scr2_r"])
        t_gr2 = p.op("vector", lambda e: e.tensor_reduce(
            out=gn_sc[:, H:2 * H], in_=scr2[:].rearrange("p (h c) -> p h c", h=H),
            axis=AX.X, op=OP.add), deps=[t_gsq])
        t_gmm = p.op("vector", lambda e: e.tensor_scalar_mul(gn_sc[:, 0:H], gn_sc[:, 0:H], 1.0 / K), deps=[t_gm])
        t_gm2 = p.op("vector", lambda e: e.tensor_tensor(
            out=gn_sc[:, 2 * H:3 * H], in0=gn_sc[:, 0:H], in1=gn_sc[:, 0:H], op=OP.mult), deps=[t_gmm])
        t_gvar = p.op("vector", lambda e: e.scalar_tensor_tensor(
            out=gn_sc[:, H:2 * H], in0=gn_sc[:, H:2 * H], scalar=1.0 / K,
            in1=gn_sc[:, 2 * H:3 * H], op0=OP.mult, op1=OP.subtract), deps=[t_gr2, t_gm2])
        t_gl1 = p.op("scalar", lambda e: e.activation(
            out=gn_sc2[:, 0:H], in_=gn_sc[:, H:2 * H], func=AF.Ln, bias=eps_sb[:]),
            deps=[t_gvar, t_eps])
        t_grs = p.op("scalar", lambda e: e.activation(
            out=gn_sc[:, 3 * H:4 * H], in_=gn_sc2[:, 0:H], func=AF.Exp, scale=-0.5), deps=[t_gl1])
        t_yc = p.op("gpsimd", lambda e: e.tensor_tensor(
            out=yg3, in0=yg3,
            in1=gn_sc[:, 0:H].rearrange("p h -> p h ()").broadcast_to((128, H, K)),
            op=OP.subtract), deps=[t_gmm, t_gsq])
        t_yn = p.op("gpsimd", lambda e: e.tensor_tensor(
            out=yg3, in0=yg3,
            in1=gn_sc[:, 3 * H:4 * H].rearrange("p h -> p h ()").broadcast_to((128, H, K)),
            op=OP.mult), deps=[t_grs, t_yc])
        dep["scr2_r"] = [t_gr2]
        dep["y_r"] = [t_yn]
        dep["gn_r"] = [t_yc, t_yn]

        if debug and t == 1:
            dep["y_r"].append(_dt(p.dma("sync", lambda e: e.dma_start(out=dbg["d_y"], in_=y_tm[:]), deps=[t_yn])))
            dep["Sz_r"][1 - par].append(_dt(p.dma("sync", lambda e: e.dma_start(out=dbg["d_S"], in_=Snxt[:]), deps=[sev[-1]])))

        t_gs = p.op("scalar", lambda e: e.activation(
            out=gs_sb[:], in_=g_raw[:], func=AF.Silu),
            deps=proj_ev["g"] + dep["gs_r"])
        dep["gs_r"] = []
        t_gl = p.op("gpsimd", lambda e: e.tensor_tensor(
            out=gl_b, in0=gs_sb[:], in1=lnw_bct[:], op=OP.mult),
            deps=[t_gs, ev[0], ev[1]] + dep["Khf_r"])
        dep["Khf_r"] = []
        t_glb = p.op("gpsimd", lambda e: e.tensor_tensor(
            out=glb_b, in0=gs_sb[:], in1=lnb_bct[:], op=OP.mult),
            deps=[t_gs, ev[2], ev[3]] + dep["yngfm_r"])
        dep["vg_r"] = [t_gl, t_glb, t_vsu]
        t_y1 = p.op("gpsimd", lambda e: e.tensor_tensor(
            out=scr2[:], in0=y_tm[:], in1=gl_b, op=OP.mult),
            deps=[t_gl, t_yn] + dep["scr2_r"])
        t_y2 = p.op("gpsimd", lambda e: e.tensor_tensor(
            out=y_tm[:].bitcast(BF16)[:, 0:C],
            in0=scr2[:], in1=glb_b, op=OP.add),
            deps=[t_glb, t_y1] + dep["y_r"])
        dep["Khf_r"].append(t_y1)
        dep["scr2_r"] = [t_y2]
        dep["y_r"] = [t_y2]

        S["t_y2"] = t_y2

    def PB3(t):
        S = TS[t]
        proj_ev = S["proj_ev"]
        t_y2 = S["t_y2"]
        t_f2 = S["t_f2"]
        t_f2b = S["t_f2b"]
        xt = S["xt"]
        xb = S["xb"]

        mms2 = []
        for q in range(NQ):
            bi = q // 4
            mms2.append(bank_mm(bi, lambda e, q=q, bi=bi: e.matmul(
                pb[bi][:, (q % 4) * 128:(q % 4) * 128 + 128],
                lhsT=y_tm[:].bitcast(BF16)[:, q * 128:(q + 1) * 128],
                rhs=ident[:], start=True, stop=True),
                deps=[t_y2, t_ident]))
        yngfm_ev = []
        for half in range(2):
            tok = p.op("scalar", lambda e, half=half: e.copy(
                yngfm3[:, 4 * half:4 * half + 4, :],
                pb[half][:].rearrange("p (q c) -> p q c", q=4)),
                deps=[mms2[4 * half + 3]] + (dep["yngfm_r"] if half == 0 else []))
            bank_free[half] = tok
            yngfm_ev.append(tok)
        dep["yngfm_r"] = []
        dep["y_r"] = dep["y_r"] + mms2

        if debug and t == 0:
            dep["yngfm_r"].append(_dt(p.dma("gpsimd", lambda e: e.dma_start(out=dbg["d_yng"], in_=yng_fm[:]), deps=yngfm_ev)))

        mm_l = {}
        for half in range(2):
            bi = 4 + half
            for qi in range(NQ):
                tok = bank_mm(bi, lambda e, half=half, qi=qi, bi=bi: e.matmul(
                    pb[bi][:],
                    lhsT=yng_fm[:, qi * TT:(qi + 1) * TT],
                    rhs=W_sb["o"][:, qi * C + half * 512:qi * C + half * 512 + 512],
                    start=(qi == 0), stop=(qi == NQ - 1)),
                    deps=[t_W["o"], yngfm_ev[0], yngfm_ev[1]])
                dep["yngfm_r"].append(tok)
                if qi == NQ - 1:
                    mm_l[half] = tok
        stm_ev = []
        for half in range(2):
            bi = 4 + half
            tok = p.op("scalar", lambda e, half=half, bi=bi: e.copy(
                sx_fm[:, half * 512:(half + 1) * 512], pb[bi][:]),
                deps=[mm_l[half]] + (dep["stm_r"] if half == 0 else []))
            bank_free[bi] = tok
            stm_ev.append(tok)
        dep["stm_r"] = stm_ev[:]

        if debug and t == 0:
            dep["stm_r"].append(_dt(p.dma("sync", lambda e: e.dma_start(out=dbg["d_stm"], in_=sx_fm[:, 0:C]), deps=stm_ev)))

        t_f3 = p.op("vector", lambda e: e.tensor_tensor(
            out=scr2[:], in0=scr[:], in1=sx_fm[:, 0:C], op=OP.mult),
            deps=[t_f2, t_f2b, stm_ev[0], stm_ev[1]] + dep["scr2_r"])
        dep["stm_r"].append(t_f3)
        t_out = p.op("vector", lambda e, xt=xt: e.tensor_tensor(
            out=xt[:], in0=scr2[:], in1=xt[:], op=OP.add), deps=[t_f3])
        dep["scr_r"] = [t_f3]
        dep["scr2_r"] = [t_out]
        t_st = p.dma("scalar", lambda e, t=t, xt=xt: e.dma_start(
            out=out_h[t * TT:(t + 1) * TT, :], in_=xt[:]), deps=[t_out])
        dep["out_dma"][xb] = t_st
        out_tokens.append(t_st)

    PF1(0)
    PF2(0)
    PF3(0)
    for t in range(n_tiles):
        PB1(t)
        if t + 1 < n_tiles:
            PF1(t + 1)
        PB2a(t)
        if t + 1 < n_tiles:
            PF2(t + 1)
        PB2b(t)
        if t + 1 < n_tiles:
            PF3(t + 1)
        PB3(t)

    p.op("sync", lambda e: e.nop(), deps=out_tokens)
    p.emit(nc, block, sems)
    blk.__exit__(None, None, None)
    return nc, sbuf_used


def host_inputs(inp, b):
    f32 = np.float32
    w2 = np.asarray(inp["L0_attn_tm_w2"], f32)
    maa = np.asarray(inp["L0_attn_time_maa_wkvrg"], f32)
    w2aug = np.concatenate([np.concatenate([w2[i], maa[i:i + 1]], axis=0) for i in range(5)], axis=0)
    td2aug = np.concatenate([np.asarray(inp["L0_attn_td_w2"], f32),
                             np.asarray(inp["L0_attn_time_decay"], f32)[None, :]], axis=0)
    ident = np.eye(128, dtype=f32)
    jj = np.arange(128)
    uexc = (jj[:, None] < jj[None, :]).astype(f32)
    uinc = (jj[:, None] <= jj[None, :]).astype(f32)
    pre32 = (jj[:, None] < 32).astype(f32) * np.ones((1, 128), f32)
    pre96 = (jj[:, None] < 96).astype(f32) * np.ones((1, 128), f32)
    utri = np.concatenate([uexc, uinc, uexc - pre32, uexc - pre96,
                           uinc - pre32, uinc - pre96], axis=1)
    mask4 = np.tile(uexc, (1, 4)).astype(np.uint8)
    ind2 = np.zeros((128, 2), f32)
    ind2[0:64, 0] = 1.0
    ind2[64:128, 1] = 1.0
    return {
        "x": np.ascontiguousarray(np.asarray(inp["L0_attn_x"], f32)[b]),
        "Wr": np.ascontiguousarray(np.asarray(inp["L0_attn_Wr"], f32)),
        "Wk": np.ascontiguousarray(np.asarray(inp["L0_attn_Wk"], f32)),
        "Wv": np.ascontiguousarray(np.asarray(inp["L0_attn_Wv"], f32)),
        "Wg": np.ascontiguousarray(np.asarray(inp["L0_attn_Wg"], f32)),
        "Wo": np.ascontiguousarray(np.asarray(inp["L0_attn_Wo"], f32)),
        "WfT": np.ascontiguousarray(np.asarray(inp["L0_ffn_W"], f32).T),
        "tm_w1": np.ascontiguousarray(np.asarray(inp["L0_attn_tm_w1"], f32)),
        "w2aug": np.ascontiguousarray(w2aug),
        "td_w1": np.ascontiguousarray(np.asarray(inp["L0_attn_td_w1"], f32)),
        "td2aug": np.ascontiguousarray(td2aug),
        "tmaa": np.ascontiguousarray(np.asarray(inp["L0_attn_time_maa_x"], f32)[None, :]),
        "ufaaaa": np.ascontiguousarray(np.asarray(inp["L0_attn_time_faaaa"], f32).reshape(1, -1)),
        "lnw": np.ascontiguousarray(np.asarray(inp["L0_attn_ln_w"], f32)[None, :]),
        "lnb": np.ascontiguousarray(np.asarray(inp["L0_attn_ln_b"], f32)[None, :]),
        "fb": np.ascontiguousarray(np.asarray(inp["L0_ffn_b"], f32)[None, :]),
        "lora_up": np.ascontiguousarray(np.asarray(inp["L0_ffn_lora_up"], f32)),
        "lora_down": np.ascontiguousarray(np.asarray(inp["L0_ffn_lora_down"], f32)),
        "down_aux": np.ascontiguousarray(np.asarray(inp["L0_ffn_down_aux"], f32)),
        "up_auxT": np.ascontiguousarray(np.asarray(inp["L0_ffn_up_aux"], f32).T),
        "ident": ident, "utri": utri, "mask4": mask4, "ind2": ind2,
    }


def kernel(**inputs):
    nc, _ = build_nc(n_tiles=T_FULL // TT, debug=False)
    in_maps = [host_inputs(inputs, b) for b in range(8)]
    res = run_bass_kernel_spmd(nc, in_maps, core_ids=list(range(8)))
    out = np.stack([np.asarray(r["out"]) for r in res.results], axis=0)
    return out.astype(np.float32)


# version 41 (final)
# speedup vs baseline: 1.0637x; 1.0023x over previous
import numpy as np
import ml_dtypes
import concourse.bass as bass
import concourse.mybir as mybir
from concourse.bass_utils import run_bass_kernel_spmd
from concourse.alu_op_type import AluOpType as OP

F32 = mybir.dt.float32
BF16 = mybir.dt.bfloat16
AF = mybir.ActivationFunctionType
AX = mybir.AxisListType

T_FULL = 4096
C = 1024
H = 16
K = 64
TT = 128
NQ = 8
EPS = 1e-5

ENGINES = ("sync", "gpsimd", "tensor", "vector", "scalar")


class Prog:
    def __init__(self):
        self.ops = {e: [] for e in ENGINES}
        self.n = 0
        self.eng_of = {}

    def op(self, eng, fn, deps=(), dma=False):
        opid = self.n
        self.n += 1
        deps = [d for d in deps if d is not None]
        self.ops[eng].append((fn, deps, dma, opid))
        self.eng_of[opid] = (eng, dma)
        return opid

    def dma(self, eng, fn, deps=()):
        return self.op(eng, fn, deps, dma=True)

    def emit(self, nc, block, sems):
        val = {}
        for eng in ENGINES:
            cc = dc = 0
            for fn, deps, is_dma, opid in self.ops[eng]:
                if is_dma:
                    dc += 16
                    val[opid] = (sems[(eng, "d")], dc)
                else:
                    cc += 1
                    val[opid] = (sems[(eng, "c")], cc)

        def section(eng):
            def body(e):
                seen = {}
                dseen = 0
                for fn, deps, is_dma, opid in self.ops[eng]:
                    if is_dma:
                        sd, vd = val[opid]
                        if vd - 16 > 0 and dseen < vd - 16:
                            e.wait_ge(sd, vd - 16)
                            dseen = vd - 16
                            seen[sd.name] = max(seen.get(sd.name, -1), vd - 16)
                    for d in deps:
                        de, ddma = self.eng_of[d]
                        if de == eng == "tensor" and not ddma and not is_dma:
                            continue
                        if de == eng and ddma and is_dma:
                            continue
                        s, v = val[d]
                        if seen.get(s.name, -1) < v:
                            e.wait_ge(s, v)
                            seen[s.name] = v
                    inst = fn(e)
                    inst.then_inc(sems[(eng, "d" if is_dma else "c")], 16 if is_dma else 1)
            return body

        for eng in ENGINES:
            getattr(block, eng)(section(eng))


def build_nc(n_tiles=T_FULL // TT, debug=False):
    T = n_tiles * TT
    nc = bass.Bass("TRN2", target_bir_lowering=False)

    def din(name, shape, dt=F32):
        return nc.dram_tensor(name, shape, dt, kind="ExternalInput").ap()

    x_h = din("x", [T, C])
    w_h = {n: din("W" + n, [C, C]) for n in ["r", "k", "v", "g", "o", "fT"]}
    tmw1_h = din("tm_w1", [C, 160])
    w2aug_h = din("w2aug", [5 * 33, C])
    tdw1_h = din("td_w1", [C, 64])
    td2aug_h = din("td2aug", [65, C])
    tmaa_h = din("tmaa", [1, C])
    u_h = din("ufaaaa", [1, C])
    lnw_h = din("lnw", [1, C])
    lnb_h = din("lnb", [1, C])
    fb_h = din("fb", [1, C])
    lu_h = din("lora_up", [50, 4])
    ld_h = din("lora_down", [4, 100])
    da_h = din("down_aux", [100, C])
    uaT_h = din("up_auxT", [50, C])
    ident_h = din("ident", [128, 128])
    utri_h = din("utri", [128, 6 * 128])
    mask4_h = din("mask4", [128, 512], mybir.dt.uint8)
    ind2_h = din("ind2", [128, 2])

    out_h = nc.dram_tensor("out", [T, C], F32, kind="ExternalOutput").ap()
    dbg = {}
    if debug:
        for name, shape, dt in [
                ("d_xnfm", [128, NQ * TT], F32), ("d_sx", [128, NQ * TT], BF16),
                ("d_r", [128, NQ * TT], BF16), ("d_k", [128, NQ * TT], BF16),
                ("d_lw", [128, C], BF16), ("d_Er", [128, C], BF16),
                ("d_A", [128, H * 128], BF16), ("d_y", [128, C], F32),
                ("d_su", [128, H], F32), ("d_S", [128, H * K], F32),
                ("d_v", [128, C], BF16), ("d_yng", [128, NQ * TT], BF16),
                ("d_stm", [128, C], F32), ("d_weff", [128, NQ * C], BF16),
                ("d_g", [128, C], BF16), ("d_xi", [128, NQ * TT], BF16),
                ("d_Ktz", [128, H * 128], BF16), ("d_Khz", [128, H * 128], BF16)]:
            dbg[name] = nc.dram_tensor(name, shape, dt, kind="ExternalOutput").ap()

    p = Prog()
    dbg_toks = []

    def _dt(tok):
        dbg_toks.append(tok)
        return tok

    def sb(name, shape, dt):
        return nc.alloc_sbuf_tensor(name, shape, dt).ap()

    W_sb = {n: sb(f"W_{n}", [128, NQ * C], BF16) for n in ["r", "k", "v", "g", "o", "f"]}
    tmw1_sb = sb("tmw1_s", [128, NQ * 160], BF16)
    w2p = [sb(f"w2p{j}", [128, C], BF16) for j in range(3)]
    tdw1_sb = sb("tdw1_s", [128, NQ * 64], BF16)
    td2aug_sb = sb("td2aug_s", [65, C], BF16)
    tmaa_sb = sb("tmaa_s", [128, NQ], F32)
    u_sb = sb("ufa", [128, NQ], F32)
    lnwrow = sb("lnwrow_s", [1, C], BF16)
    lnbrow = sb("lnbrow_s", [1, C], BF16)
    fbrow = sb("fbrow", [1, C], BF16)
    lnw_bct = sb("lnw_bct", [128, C], BF16)
    lnb_bct = sb("lnb_bct", [128, C], BF16)
    ident = sb("ident_s", [128, 128], BF16)
    utri = sb("utri_s", [128, 6 * 128], BF16)
    mask4 = sb("mask4_s", [128, 512], mybir.dt.uint8)
    ind2 = sb("ind2_s", [128, 2], BF16)
    ind2u = sb("ind2u_s", [128, 2 * NQ], BF16)
    ones1 = sb("ones1", [1, 128], BF16)
    eps_sb = sb("eps_s", [128, 1], F32)

    x_tm = [sb("x_tm0", [128, C], F32), sb("x_tm1", [128, C], F32)]
    xn_fm = sb("xn_fm", [128, NQ * (TT + 1)], F32)
    sx_fm = sb("sx_fm", [128, NQ * TT], F32)
    xi_bf = sb("xi_bf", [128, NQ * TT], BF16)
    m_sb = [sb(f"mp{j}", [128, TT], BF16) for j in range(3)]
    tanhtd = sb("tanhtd", [65, TT], BF16)
    lw_tm = sb("lw_tm", [128, C], BF16)
    r_bf = sb("r_bf", [128, NQ * TT], BF16)
    k_bf = sb("k_bf", [128, NQ * TT], BF16)
    v_tm = sb("v_tm", [128, C], BF16)
    g_raw = sb("g_raw", [128, C], BF16)
    E1 = sb("E1_s", [128, 2 * C], BF16)
    E1f = E1[:].bitcast(F32)
    xi2 = E1[:, 0:NQ * TT]
    E2 = sb("E2_s", [128, C], F32)
    epl = sb("epl", [128, NQ], F32)
    cmf = sb("cmf", [128, NQ], F32)
    zc = sb("zc", [128, 1], F32)
    KtzA = sb("KtzA", [128, H * 128], BF16)
    KtzB = sb("KtzB", [128, H * 128], BF16)
    Ru = sb("Ru", [128, C], BF16)
    Rab = sb("Rab", [128, C], BF16)
    Khf = sb("Khf", [128, C], BF16)
    Khz = sb("Khz", [128, H * 128], BF16)
    A4 = sb("A4", [128, H * 128], BF16)
    su_sb = sb("su_s", [128, H], F32)
    Sz = [sb("SzA", [128, H * K], F32), sb("SzB", [128, H * K], F32)]
    S_bfz = sb("S_bfz", [128, H * K], BF16)
    y_tm = sb("y_tm", [128, C], F32)
    gn_sc = sb("gn_sc", [128, 4 * H], F32)
    gn_sc2 = sb("gn_sc2", [128, 3 * H], F32)
    ln_sc = sb("ln_sc", [128, 16], F32)
    yng_fm = sb("yng_fm", [128, NQ * TT], BF16)
    scr = sb("scr", [128, C], F32)
    scr2 = sb("scr2", [128, C], F32)
    sx_bf = sb("sx_bf", [128, NQ * TT], BF16)
    hi_tm = E1[:, C:2 * C]
    lo_tm = Rab[:]
    gl_b = Khf[:]
    glb_b = yng_fm[:]
    lnE = E2[:]
    rku = sx_bf[:]

    xnb = sb("xnb", [128, NQ * TT], BF16)
    gs_sb = g_raw
    dmix = E2[:].bitcast(BF16)[:, 0:NQ * TT]

    da_v = Sz[0][0:100, :]
    uaT_v = Sz[1][0:50, :]
    c1_v = A4[:].bitcast(F32)[0:4, :]
    c2_v = KtzA[:].bitcast(F32)[0:100, :]
    lu_v = Khz[:].bitcast(F32)[0:50, 0:4]
    ld_v = Khz[:].bitcast(F32)[0:4, 4:104]

    sbuf_used = nc.sbuf_base

    pb = [nc.alloc_psum_tensor(f"pb{i}", [128, 512], F32).ap() for i in range(8)]

    sems = {}
    sem_cms = []
    for e in ENGINES:
        for kind in ("c", "d"):
            cm = nc.semaphore(f"s_{e}_{kind}")
            sems[(e, kind)] = cm.__enter__()
            sem_cms.append(cm)

    blk = nc.Block()
    block = blk.__enter__()

    xnfm3 = xn_fm[:].rearrange("p (q c) -> p q c", q=NQ)
    sx3 = sx_fm[:].rearrange("p (q c) -> p q c", q=NQ)
    sxb3 = sx_bf[:].rearrange("p (q c) -> p q c", q=NQ)
    xnb3 = xnb[:].rearrange("p (q c) -> p q c", q=NQ)
    xi3 = xi_bf[:].rearrange("p (q c) -> p q c", q=NQ)
    xi23 = xi2.rearrange("p (q c) -> p q c", q=NQ)
    rb3 = r_bf[:].rearrange("p (q c) -> p q c", q=NQ)
    kb3 = k_bf[:].rearrange("p (q c) -> p q c", q=NQ)
    E1_3 = E1f.rearrange("p (q c) -> p q c", q=NQ)
    E2_3 = E2[:].rearrange("p (q c) -> p q c", q=NQ)
    KtzA4 = KtzA[:].rearrange("p (h c) -> p h c", h=H)
    KtzB4 = KtzB[:].rearrange("p (h c) -> p h c", h=H)
    Rab3 = Rab[:].rearrange("p (q c) -> p q c", q=NQ)
    rku3 = rku.rearrange("p (q c) -> p q c", q=NQ)
    yngfm3 = yng_fm[:].rearrange("p (q c) -> p q c", q=NQ)
    scr3 = scr[:].rearrange("p (q c) -> p q c", q=NQ)

    def wload(dst, src, eng="gpsimd"):
        return p.dma(eng, lambda e, d=dst, s=src: e.dma_start(out=d, in_=s))

    t_ident = wload(ident[:], ident_h)
    t_ind2 = wload(ind2[:], ind2_h)
    t_zc = p.op("gpsimd", lambda e: e.memset(zc[:], 0.0))
    t_tmaa = p.dma("sync", lambda e: e.dma_start(out=tmaa_sb[:], in_=tmaa_h.rearrange("o (a p) -> p (o a)", p=128), allow_slow_non_contiguous=True))
    t_u = p.dma("sync", lambda e: e.dma_start(out=u_sb[:], in_=u_h.rearrange("o (a p) -> p (o a)", p=128), allow_slow_non_contiguous=True))
    t_iu = p.op("gpsimd", lambda e: e.tensor_tensor(
        out=ind2u[:].rearrange("p (q t) -> p q t", q=NQ),
        in0=u_sb[:].rearrange("p q -> p q ()").broadcast_to((128, NQ, 2)),
        in1=ind2[:].rearrange("p t -> p () t").broadcast_to((128, NQ, 2)),
        op=OP.mult), deps=[t_u, t_ind2])
    t_tmw1 = wload(tmw1_sb[:].rearrange("p (a c) -> p a c", a=NQ), tmw1_h.rearrange("(a p) c -> p a c", p=128))
    t_tdw1 = wload(tdw1_sb[:].rearrange("p (a c) -> p a c", a=NQ), tdw1_h.rearrange("(a p) c -> p a c", p=128))
    t_td2 = wload(td2aug_sb[:], td2aug_h)
    t_lnw = wload(lnwrow[:], lnw_h)
    t_lnb = wload(lnbrow[:], lnb_h)
    t_fb = wload(fbrow[:], fb_h)
    t_w2z = [p.op("gpsimd", lambda e, j=j: e.memset(w2p[j][:], 0.0)) for j in range(3)]
    t_w2 = []
    for i in range(5):
        j, base = i // 2, (i % 2) * 64
        t_w2.append(p.dma("gpsimd", lambda e, i=i, j=j, base=base: e.dma_start(
            out=w2p[j][base:base + 33, :], in_=w2aug_h[33 * i:33 * (i + 1), :]),
            deps=[t_w2z[j]]))
    t_ones = p.op("gpsimd", lambda e: e.memset(ones1[:], 1.0))
    t_eps = p.op("gpsimd", lambda e: e.memset(eps_sb[:], EPS))
    t_mz = [p.op("gpsimd", lambda e, j=j: e.memset(m_sb[j][:], 0.0)) for j in range(3)]
    t_mones = []
    for i in range(5):
        j, base = i // 2, (i % 2) * 64
        t_mones.append(p.op("gpsimd", lambda e, j=j, base=base: e.memset(m_sb[j][base + 32:base + 33, :], 1.0),
                            deps=[t_mz[j]]))
    t_tdones = p.op("gpsimd", lambda e: e.memset(tanhtd[64:65, :], 1.0))
    t_W = {}
    for n in ["fT", "k", "r", "v", "g", "o"]:
        key = "f" if n == "fT" else n
        t_W[key] = wload(W_sb[key][:].rearrange("p (a c) -> p a c", a=NQ),
                         w_h[n].rearrange("(a p) c -> p a c", p=128))
        if n == "fT":
            t_utri = wload(utri[:], utri_h)
            t_mask4 = wload(mask4[:], mask4_h)
    t_lu = p.dma("sync", lambda e: e.dma_start(out=lu_v, in_=lu_h))
    t_ld = p.dma("sync", lambda e: e.dma_start(out=ld_v, in_=ld_h))
    t_da = p.dma("sync", lambda e: e.dma_start(out=da_v, in_=da_h))
    t_uaT = p.dma("sync", lambda e: e.dma_start(out=uaT_v, in_=uaT_h))

    mm = [None] * 4
    mm[0] = p.op("tensor", lambda e: e.matmul(pb[0][:], lhsT=ones1[:], rhs=lnwrow[:, 0:512], start=True, stop=True), deps=[t_ones, t_lnw])
    mm[1] = p.op("tensor", lambda e: e.matmul(pb[1][:], lhsT=ones1[:], rhs=lnwrow[:, 512:1024], start=True, stop=True))
    mm[2] = p.op("tensor", lambda e: e.matmul(pb[2][:], lhsT=ones1[:], rhs=lnbrow[:, 0:512], start=True, stop=True), deps=[t_lnb])
    mm[3] = p.op("tensor", lambda e: e.matmul(pb[3][:], lhsT=ones1[:], rhs=lnbrow[:, 512:1024], start=True, stop=True))
    ev = [None] * 4
    ev[0] = p.op("vector", lambda e: e.tensor_copy(lnw_bct[:, 0:512], pb[0][:]), deps=[mm[0]])
    ev[1] = p.op("vector", lambda e: e.tensor_copy(lnw_bct[:, 512:1024], pb[1][:]), deps=[mm[1]])
    ev[2] = p.op("vector", lambda e: e.tensor_copy(lnb_bct[:, 0:512], pb[2][:]), deps=[mm[2]])
    ev[3] = p.op("vector", lambda e: e.tensor_copy(lnb_bct[:, 512:1024], pb[3][:]), deps=[mm[3]])

    c1a = p.op("tensor", lambda e: e.matmul(pb[4][0:4, :], lhsT=lu_v, rhs=uaT_v[:, 0:512], start=True, stop=True), deps=[t_lu, t_uaT])
    c1b = p.op("tensor", lambda e: e.matmul(pb[5][0:4, :], lhsT=lu_v, rhs=uaT_v[:, 512:1024], start=True, stop=True))
    c1e = p.op("vector", lambda e: e.tensor_copy(c1_v[:, 0:512], pb[4][0:4, :]), deps=[c1a])
    c1f = p.op("vector", lambda e: e.tensor_copy(c1_v[:, 512:1024], pb[5][0:4, :]), deps=[c1b])
    c2a = p.op("tensor", lambda e: e.matmul(pb[6][0:100, :], lhsT=ld_v, rhs=c1_v[:, 0:512], start=True, stop=True), deps=[t_ld, c1e])
    c2b = p.op("tensor", lambda e: e.matmul(pb[7][0:100, :], lhsT=ld_v, rhs=c1_v[:, 512:1024], start=True, stop=True), deps=[c1f])
    c2e = p.op("vector", lambda e: e.tensor_copy(c2_v[:, 0:512], pb[6][0:100, :]), deps=[c2a])
    c2f = p.op("vector", lambda e: e.tensor_copy(c2_v[:, 512:1024], pb[7][0:100, :]), deps=[c2b])
    bank_free = {0: ev[0], 1: ev[1], 2: ev[2], 3: ev[3], 4: c1e, 5: c1f, 6: c2e, 7: c2f}
    weff_toks = []
    for q in range(NQ):
        b0 = (q % 2) * 2
        mm_a = p.op("tensor", lambda e, q=q, b0=b0: e.matmul(
            pb[b0][:], lhsT=da_v[:, q * 128:(q + 1) * 128], rhs=c2_v[:, 0:512], start=True, stop=True),
            deps=[t_da, c2e, c2f, bank_free[b0]])
        mm_b = p.op("tensor", lambda e, q=q, b0=b0: e.matmul(
            pb[b0 + 1][:], lhsT=da_v[:, q * 128:(q + 1) * 128], rhs=c2_v[:, 512:1024], start=True, stop=True),
            deps=[bank_free[b0 + 1]])
        we_a = p.op("vector", lambda e, q=q, b0=b0: e.tensor_tensor(
            out=W_sb["f"][:, q * C:q * C + 512], in0=pb[b0][:], in1=W_sb["f"][:, q * C:q * C + 512], op=OP.add),
            deps=[mm_a, t_W["f"]])
        we_b = p.op("vector", lambda e, q=q, b0=b0: e.tensor_tensor(
            out=W_sb["f"][:, q * C + 512:(q + 1) * C], in0=pb[b0 + 1][:], in1=W_sb["f"][:, q * C + 512:(q + 1) * C], op=OP.add),
            deps=[mm_b])
        bank_free[b0] = we_a
        bank_free[b0 + 1] = we_b
        weff_toks += [we_a, we_b]
    if debug:
        dbg_toks.append(p.dma("gpsimd", lambda e: e.dma_start(out=dbg["d_weff"], in_=W_sb["f"][:]), deps=weff_toks))

    lil_done = weff_toks
    t_zA4 = p.op("gpsimd", lambda e: e.memset(A4[:], 0.0), deps=[c2a, c2b])
    t_zKtzA = p.op("gpsimd", lambda e: e.memset(KtzA[:], 0.0), deps=lil_done)
    t_zKtzB = p.op("gpsimd", lambda e: e.memset(KtzB[:], 0.0), deps=lil_done)
    t_zKhz = p.op("gpsimd", lambda e: e.memset(Khz[:], 0.0), deps=lil_done)
    t_zS0 = p.op("gpsimd", lambda e: e.memset(Sz[0][:], 0.0), deps=lil_done)
    t_zS1 = p.op("gpsimd", lambda e: e.memset(Sz[1][:], 0.0), deps=lil_done)
    t_zSbf = p.op("gpsimd", lambda e: e.memset(S_bfz[:], 0.0))

    dep = {
        "xnfm_r": [], "sx_r": [], "xi_r": [], "lw_r": [], "rk_r": [], "vg_r": [],
        "hi_r": [], "lo_r": [], "scr_r": [], "scr2_r": [], "xnb_r": [], "stm_r": [],
        "y_r": [], "su_r": [], "A4_r": [t_zA4], "KtzA_r": [t_zKtzA], "KtzB_r": [t_zKtzB],
        "Ru_r": [], "Rab_r": [], "Khf_r": [], "Khz_r": [t_zKhz],
        "E1_r": [], "E2_r": [], "cmf_r": [], "epl_r": [], "gs_r": [], "gn_r": [],
        "yngfm_r": [], "m_r": [[] for _ in range(5)], "ttd_r": [],
        "out_dma": [None, None],
        "Sz_w": [t_zS0, t_zS1], "Sz_r": [[], []], "Sbf_ready": t_zSbf,
    }

    MAGIC = 0x5f3759df
    I32 = mybir.dt.int32

    def emit_rsqrt(dst, var_ap, scratch, deps, niter=4):
        n = var_ap.shape[-1]
        ve = scratch[:, 0:n]
        h_ = scratch[:, n:2 * n]
        t_ = scratch[:, 2 * n:3 * n]
        tok = p.op("vector", lambda e: e.tensor_single_scalar(out=ve, in_=var_ap, scalar=EPS, op=OP.add), deps=deps)
        tok = p.op("vector", lambda e: e.tensor_single_scalar(out=h_, in_=ve, scalar=-0.5, op=OP.mult), deps=[tok])
        vi = ve.bitcast(I32)
        di = dst.bitcast(I32)
        tok = p.op("vector", lambda e: e.tensor_single_scalar(out=di, in_=vi, scalar=1, op=OP.arith_shift_right), deps=[tok])
        tok = p.op("vector", lambda e: e.tensor_single_scalar(out=di, in_=di, scalar=-1, op=OP.bitwise_xor), deps=[tok])
        tok = p.op("vector", lambda e: e.tensor_single_scalar(out=di, in_=di, scalar=MAGIC + 1, op=OP.add), deps=[tok])
        for _ in range(niter):
            tok = p.op("vector", lambda e: e.tensor_tensor(out=t_, in0=dst, in1=dst, op=OP.mult), deps=[tok])
            tok = p.op("vector", lambda e: e.tensor_tensor(out=t_, in0=t_, in1=h_, op=OP.mult), deps=[tok])
            tok = p.op("vector", lambda e: e.scalar_tensor_tensor(out=dst, in0=t_, scalar=1.5, in1=dst, op0=OP.add, op1=OP.mult), deps=[tok])
        return tok

    out_tokens = []

    def bank_mm(bi, fn, deps=()):
        return p.op("tensor", fn, deps=list(deps) + [bank_free[bi]])

    GUARD_KEYS = ["xnfm_r", "sx_r", "xi_r", "lw_r", "rk_r", "vg_r", "hi_r", "lo_r",
                  "scr_r", "scr2_r", "xnb_r", "stm_r", "y_r", "su_r", "A4_r", "KtzA_r", "KtzB_r",
                  "Ru_r", "Rab_r", "Khf_r", "Khz_r", "E1_r", "E2_r", "cmf_r",
                  "epl_r", "gs_r", "gn_r", "yngfm_r", "ttd_r"]
    TS = {}

    def PF1(t):
        if debug and dbg_toks:
            for kk_ in GUARD_KEYS:
                dep[kk_] = dep[kk_] + dbg_toks
            dep["Sz_r"][0] = dep["Sz_r"][0] + dbg_toks
            dep["Sz_r"][1] = dep["Sz_r"][1] + dbg_toks
            dbg_toks.clear()
        S = TS.setdefault(t, {})
        xb = t % 2
        xt = x_tm[xb]

        t_x = p.dma("sync", lambda e, t=t, xt=xt: e.dma_start(
            out=xt[:], in_=x_h[t * TT:(t + 1) * TT, :]), deps=[dep["out_dma"][xb]])

        t_r1 = p.op("vector", lambda e, xt=xt: e.tensor_reduce(
            out=ln_sc[:, 0:1], in_=xt[:], axis=AX.X, op=OP.add), deps=[t_x])
        t_sq = p.op("scalar", lambda e, xt=xt: e.activation(
            out=lnE, in_=xt[:], func=AF.Square), deps=[t_x] + dep["E2_r"])
        dep["E2_r"] = []
        t_r2 = p.op("vector", lambda e: e.tensor_reduce(
            out=ln_sc[:, 1:2], in_=lnE, axis=AX.X, op=OP.add), deps=[t_sq])
        t_m = p.op("vector", lambda e: e.tensor_scalar_mul(ln_sc[:, 2:3], ln_sc[:, 0:1], 1.0 / C), deps=[t_r1])
        t_msq = p.op("vector", lambda e: e.tensor_scalar(
            out=ln_sc[:, 3:4], in0=ln_sc[:, 2:3], scalar1=ln_sc[:, 2:3], scalar2=1.0,
            op0=OP.mult, op1=OP.mult), deps=[t_m])
        t_var = p.op("vector", lambda e: e.scalar_tensor_tensor(
            out=ln_sc[:, 4:5], in0=ln_sc[:, 1:2], scalar=1.0 / C, in1=ln_sc[:, 3:4],
            op0=OP.mult, op1=OP.subtract), deps=[t_r2, t_msq])
        t_ln1 = p.op("scalar", lambda e: e.activation(
            out=ln_sc[:, 5:6], in_=ln_sc[:, 4:5], func=AF.Ln, bias=eps_sb[:]),
            deps=[t_var, t_eps])
        t_rstd = p.op("scalar", lambda e: e.activation(
            out=ln_sc[:, 6:7], in_=ln_sc[:, 5:6], func=AF.Exp, scale=-0.5), deps=[t_ln1])
        t_xn = p.op("vector", lambda e, xt=xt: e.tensor_scalar(
            out=lnE, in0=xt[:], scalar1=ln_sc[:, 2:3], scalar2=ln_sc[:, 6:7],
            op0=OP.subtract, op1=OP.mult), deps=[t_rstd, t_r2])
        t_hi = p.op("scalar", lambda e: e.activation(
            out=hi_tm, in_=lnE, func=AF.Copy), deps=[t_xn] + dep["hi_r"] + dep["E1_r"])
        t_lo = p.op("vector", lambda e: e.tensor_tensor(
            out=lo_tm, in0=lnE, in1=hi_tm, op=OP.subtract), deps=[t_hi] + dep["lo_r"] + dep["Rab_r"])
        dep["E2_r"] = [t_hi, t_lo]

        S["t_hi"] = t_hi
        S["t_lo"] = t_lo
        S["xt"] = xt
        S["xb"] = xb

    def PF2(t):
        S = TS[t]
        t_hi = S["t_hi"]
        t_lo = S["t_lo"]

        if t > 0:
            t_bnd = p.op("vector", lambda e: e.tensor_copy(xnfm3[:, :, 0:1], xnfm3[:, :, TT:TT + 1]),
                         deps=dep["xnfm_r"])
        else:
            t_bnd = p.op("vector", lambda e: e.memset(xnfm3[:, :, 0:1], 0.0), deps=dep["xnfm_r"])
        dep["xnfm_r"] = []
        mms = []
        for q in range(NQ):
            bi = q // 4
            bank_mm(bi, lambda e, q=q, bi=bi: e.matmul(
                pb[bi][:, (q % 4) * 128:(q % 4) * 128 + 128],
                lhsT=hi_tm[:, q * 128:(q + 1) * 128], rhs=ident[:], start=True, stop=False),
                deps=[t_hi, t_ident])
            mm2 = p.op("tensor", lambda e, q=q, bi=bi: e.matmul(
                pb[bi][:, (q % 4) * 128:(q % 4) * 128 + 128],
                lhsT=lo_tm[:, q * 128:(q + 1) * 128], rhs=ident[:], start=False, stop=True),
                deps=[t_lo])
            mms.append(mm2)
        dep["hi_r"] = [mms[-1]]
        dep["lo_r"] = [mms[-1]]
        dep["E1_r"].append(mms[-1])
        dep["Rab_r"].append(mms[-1])
        xn_ev = []
        for half in range(2):
            tok = p.op("scalar", lambda e, half=half: e.copy(
                xnfm3[:, 4 * half:4 * half + 4, 1:1 + TT],
                pb[half][:].rearrange("p (q c) -> p q c", q=4)),
                deps=[mms[4 * half + 3], t_bnd])
            bank_free[half] = tok
            xn_ev.append(tok)

        S["xn_ev"] = xn_ev

        t_xnbp = p.op("vector", lambda e: e.tensor_copy(xnb3, xnfm3[:, :, 1:TT + 1]),
                      deps=xn_ev + dep["xnb_r"])
        dep["xnb_r"] = []
        dep["xnfm_r"].append(t_xnbp)
        t_sx = p.op("vector", lambda e: e.tensor_tensor(
            out=sxb3, in0=xnfm3[:, :, 0:TT], in1=xnfm3[:, :, 1:TT + 1], op=OP.subtract),
            deps=xn_ev + dep["sx_r"])
        dep["sx_r"] = []
        dep["xnfm_r"].append(t_sx)
        t_xxx = []
        for q in range(NQ):
            tok = p.op("vector", lambda e, q=q: e.scalar_tensor_tensor(
                out=xi3[:, q, :], in0=sxb3[:, q, :], scalar=tmaa_sb[:, q:q + 1],
                in1=xnb3[:, q, :], op0=OP.mult, op1=OP.add),
                deps=[t_sx, t_tmaa, t_xnbp] + dep["xi_r"])
            t_xxx.append(tok)
            dep["xnb_r"].append(tok)
            dep["sx_r"].append(tok)
        dep["xi_r"] = []

        mm_last = {}
        for part in range(2):
            cols = 128 if part == 0 else 32
            for q in range(NQ):
                tok = bank_mm(4, lambda e, part=part, q=q, cols=cols: e.matmul(
                    pb[4][0:cols, part * 128:part * 128 + TT],
                    lhsT=tmw1_sb[:, q * 160 + part * 128:q * 160 + part * 128 + cols],
                    rhs=xi_bf[:, q * TT:(q + 1) * TT],
                    start=(q == 0), stop=(q == NQ - 1)),
                    deps=[t_tmw1, t_xxx[q]])
                if q == NQ - 1:
                    mm_last[part] = tok
        m_toks = []
        for i in range(5):
            part = 0 if i < 4 else 1
            rs = (i * 32) % 128
            j, base = i // 2, (i % 2) * 64
            tok = p.op("scalar", lambda e, part=part, rs=rs, j=j, base=base: e.activation(
                out=m_sb[j][base:base + 32, :], in_=pb[4][rs:rs + 32, part * 128:part * 128 + TT],
                func=AF.Tanh),
                deps=[mm_last[0], mm_last[1], t_mones[i]] + dep["m_r"][i])
            dep["m_r"][i] = []
            m_toks.append(tok)
        bank_free[4] = m_toks[4]

        S["m_toks"] = m_toks
        S["t_sx"] = t_sx
        S["t_xnbp"] = t_xnbp

    def PF3(t):
        S = TS[t]
        m_toks = S["m_toks"]
        t_sx = S["t_sx"]
        t_xnbp = S["t_xnbp"]
        xn_ev = S["xn_ev"]

        proj_ev = {"r": [], "k": [], "v": [], "g": []}
        lw_toks = []

        def emit_delta(i):
            jm, base = i // 2, (i % 2) * 64
            db0 = (i % 2) * 2
            dmm = []
            for q in range(NQ):
                bi = db0 + q // 4
                tok = bank_mm(bi, lambda e, q=q, bi=bi, jm=jm, base=base: e.matmul(
                    pb[bi][:, (q % 4) * 128:(q % 4) * 128 + 128],
                    lhsT=w2p[jm][base:base + 64, q * 128:(q + 1) * 128],
                    rhs=m_sb[jm][base:base + 64, :], start=True, stop=True),
                    deps=[t_w2[i], m_toks[i]])
                dmm.append(tok)
                dep["m_r"][i].append(tok)
            return dmm

        next_dmm = emit_delta(0)
        prev_xi = None
        for i in range(5):
            db0 = (i % 2) * 2
            dmm = next_dmm
            tmix = []
            for half in range(2):
                bi = db0 + half
                tok = p.op("vector", lambda e, bi=bi, half=half: e.tensor_tensor(
                    out=dmix[:, half * 512:(half + 1) * 512], in0=sx_bf[:, half * 512:(half + 1) * 512],
                    in1=pb[bi][:], op=OP.mult),
                    deps=[dmm[4 * half + 3], t_sx] + (dep["E2_r"] if i == 0 else [prev_xi]))
                bank_free[bi] = tok
                tmix.append(tok)
                dep["sx_r"].append(tok)
            if i == 0:
                dep["E2_r"] = []
            xcur = xi_bf[:] if (i % 2 == 0) else xi2
            xkey = "xi_r" if (i % 2 == 0) else "E1_r"
            xi_tok = p.op("vector", lambda e, xcur=xcur: e.tensor_tensor(
                out=xcur, in0=dmix[:], in1=xnb[:], op=OP.add),
                deps=tmix + [t_xnbp] + dep[xkey])
            dep["xnb_r"].append(xi_tok)
            prev_xi = xi_tok
            dep[xkey] = []
            if i < 4:
                next_dmm = emit_delta(i + 1)
            if i == 4:
                dep["E2_r"] = [xi_tok]
            if debug and t == 0 and i == 3:
                tok_dxi = p.dma("gpsimd", lambda e: e.dma_start(out=dbg["d_xi"], in_=xi_bf[:]), deps=[xi_tok])
                dep["xi_r"].append(_dt(tok_dxi))

            if i == 0:
                last = None
                for q in range(NQ):
                    last = bank_mm(5, lambda e, q=q: e.matmul(
                        pb[5][0:64, 0:TT], lhsT=tdw1_sb[:, q * 64:(q + 1) * 64],
                        rhs=xi_bf[:, q * TT:(q + 1) * TT], start=(q == 0), stop=(q == NQ - 1)),
                        deps=[t_tdw1, xi_tok])
                    dep["xi_r"].append(last)
                t_tanh = p.op("scalar", lambda e: e.activation(
                    out=tanhtd[0:64, :], in_=pb[5][0:64, 0:TT], func=AF.Tanh),
                    deps=[last, t_tdones] + dep["ttd_r"])
                dep["ttd_r"] = []
                bank_free[5] = t_tanh
                wmm = []
                for half in range(2):
                    bi = 6 + half
                    tok = bank_mm(bi, lambda e, half=half, bi=bi: e.matmul(
                        pb[bi][:], lhsT=tanhtd[:],
                        rhs=td2aug_sb[:, half * 512:(half + 1) * 512], start=True, stop=True),
                        deps=[t_td2, t_tanh])
                    wmm.append(tok)
                    dep["ttd_r"].append(tok)
                for half in range(2):
                    tok_lw = p.op("scalar", lambda e, half=half: e.activation(
                        out=lw_tm[:, half * 512:(half + 1) * 512], in_=pb[6 + half][:], func=AF.Exp),
                        deps=[wmm[half]] + dep["lw_r"])
                    bank_free[6 + half] = tok_lw
                    lw_toks.append(tok_lw)
                dep["lw_r"] = []
            elif i in (1, 3):
                name = "k" if i == 1 else "r"
                dst = k_bf if i == 1 else r_bf
                mm_l = {}
                for qo in range(NQ):
                    bi = 4 + qo // 4
                    for qi in range(NQ):
                        tok = bank_mm(bi, lambda e, name=name, qo=qo, qi=qi, bi=bi: e.matmul(
                            pb[bi][:, (qo % 4) * 128:(qo % 4) * 128 + 128],
                            lhsT=W_sb[name][:, qi * C + qo * 128:qi * C + qo * 128 + 128],
                            rhs=xi2[:, qi * TT:(qi + 1) * TT],
                            start=(qi == 0), stop=(qi == NQ - 1)),
                            deps=[t_W[name], xi_tok])
                        dep["E1_r"].append(tok)
                        if qi == NQ - 1:
                            mm_l[qo] = tok
                for half in range(2):
                    bi = 4 + half
                    tok = p.op("scalar", lambda e, half=half, bi=bi, dst=dst: e.copy(
                        dst[:, half * 4 * TT:(half + 1) * 4 * TT], pb[bi][:]),
                        deps=[mm_l[4 * half + 0], mm_l[4 * half + 1], mm_l[4 * half + 2], mm_l[4 * half + 3]]
                        + (dep["rk_r"] if i == 1 and half == 0 else []))
                    bank_free[bi] = tok
                    proj_ev[name].append(tok)
                if i == 3:
                    dep["rk_r"] = []
            else:
                name = "v" if i == 2 else "g"
                dst = v_tm if i == 2 else g_raw
                mm_l = {}
                for half in range(2):
                    bi = 6 + half
                    for qi in range(NQ):
                        tok = bank_mm(bi, lambda e, name=name, half=half, qi=qi, bi=bi: e.matmul(
                            pb[bi][:],
                            lhsT=xi_bf[:, qi * TT:(qi + 1) * TT],
                            rhs=W_sb[name][:, qi * C + half * 512:qi * C + half * 512 + 512],
                            start=(qi == 0), stop=(qi == NQ - 1)),
                            deps=[t_W[name], xi_tok])
                        dep["xi_r"].append(tok)
                        if qi == NQ - 1:
                            mm_l[half] = tok
                for half in range(2):
                    bi = 6 + half
                    tok = p.op("scalar", lambda e, half=half, bi=bi, dst=dst: e.copy(
                        dst[:, half * 512:(half + 1) * 512], pb[bi][:]),
                        deps=[mm_l[half]] + (dep["vg_r"] if i == 2 and half == 0 else []))
                    bank_free[bi] = tok
                    proj_ev[name].append(tok)
                if i == 4:
                    dep["vg_r"] = []

        if debug and t == 0:
            dep["xnfm_r"].append(_dt(p.dma("sync", lambda e: e.dma_start(out=dbg["d_xnfm"], in_=xnfm3[:, :, 1:TT + 1]), deps=xn_ev)))
            dep["sx_r"].append(_dt(p.dma("sync", lambda e: e.dma_start(out=dbg["d_sx"], in_=sx_bf[:]), deps=[t_sx])))
            dep["rk_r"].append(_dt(p.dma("gpsimd", lambda e: e.dma_start(out=dbg["d_r"], in_=r_bf[:]), deps=proj_ev["r"])))
            dep["rk_r"].append(_dt(p.dma("gpsimd", lambda e: e.dma_start(out=dbg["d_k"], in_=k_bf[:]), deps=proj_ev["k"])))
            dep["vg_r"].append(_dt(p.dma("gpsimd", lambda e: e.dma_start(out=dbg["d_v"], in_=v_tm[:]), deps=proj_ev["v"])))
            dep["vg_r"].append(_dt(p.dma("gpsimd", lambda e: e.dma_start(out=dbg["d_g"], in_=g_raw[:]), deps=proj_ev["g"])))
            dep["lw_r"].append(_dt(p.dma("gpsimd", lambda e: e.dma_start(out=dbg["d_lw"], in_=lw_tm[:]), deps=lw_toks)))

        S["proj_ev"] = proj_ev
        S["lw_toks"] = lw_toks

    def PB1(t):
        S = TS[t]
        proj_ev = S["proj_ev"]
        lw_toks = S["lw_toks"]
        t_xnbp = S["t_xnbp"]

        wave1 = {}
        for q in range(NQ):
            m_ex = bank_mm(q // 4, lambda e, q=q: e.matmul(
                pb[q // 4][:, (q % 4) * 128:(q % 4) * 128 + 128],
                lhsT=lw_tm[:, q * 128:(q + 1) * 128],
                rhs=utri[:, 0:128], start=True, stop=True),
                deps=[lw_toks[q // 4], t_utri])
            m_ia = bank_mm(2 + q // 4, lambda e, q=q: e.matmul(
                pb[2 + q // 4][:, (q % 4) * 128:(q % 4) * 128 + 128],
                lhsT=lw_tm[:, q * 128:(q + 1) * 128],
                rhs=utri[:, 4 * 128:5 * 128], start=True, stop=True))
            m_ib = bank_mm(4 + q // 4, lambda e, q=q: e.matmul(
                pb[4 + q // 4][:, (q % 4) * 128:(q % 4) * 128 + 128],
                lhsT=lw_tm[:, q * 128:(q + 1) * 128],
                rhs=utri[:, 5 * 128:6 * 128], start=True, stop=True))
            wave1[q] = (m_ex, m_ia, m_ib)
            dep["lw_r"] += [m_ex, m_ia, m_ib]
        eRu_t = [p.op("scalar", lambda e, hf=hf: e.activation(
            out=E1f[:, hf * 512:(hf + 1) * 512], in_=pb[hf][:], func=AF.Exp, scale=-1.0),
            deps=[wave1[4 * hf + 3][0]] + dep["E1_r"]) for hf in range(2)]
        eA_t = [p.op("scalar", lambda e, hf=hf: e.activation(
            out=E2_3[:, 4 * hf:4 * hf + 4, 0:64],
            in_=pb[2 + hf][:].rearrange("p (qq c) -> p qq c", qq=4)[:, :, 0:64],
            func=AF.Exp, scale=1.0),
            deps=[wave1[4 * hf + 3][1]] + dep["E2_r"]) for hf in range(2)]
        negf = [p.op("scalar", lambda e, q=q: e.activation(
            out=cmf[:, q:q + 1], in_=pb[4 + q // 4][:, (q % 4) * 128 + 127:(q % 4) * 128 + 128],
            func=AF.Copy, scale=-1.0), deps=[wave1[(q // 4) * 4 + 3][2]] + dep["cmf_r"]) for q in range(NQ)]
        Khf3 = Khf[:].rearrange("p (q c) -> p q c", q=NQ)
        ekhf = [p.op("scalar", lambda e, q=q: e.activation(
            out=Khf3[:, q, :], in_=pb[4 + q // 4][:, (q % 4) * 128:(q % 4) * 128 + 128],
            func=AF.Exp, bias=cmf[:, q:q + 1]),
            deps=[negf[q]] + (dep["Khf_r"] if q == 0 else [])) for q in range(NQ)]
        dep["E1_r"] = []
        dep["E2_r"] = []
        dep["cmf_r"] = ekhf[:]
        dep["Khf_r"] = []
        t_ru = p.op("gpsimd", lambda e: e.tensor_tensor(
            out=Ru[:].rearrange("p (q c) -> p q c", q=NQ), in0=rb3, in1=E1_3, op=OP.mult),
            deps=eRu_t + dep["Ru_r"] + proj_ev["r"])
        dep["Ru_r"] = []
        t_khf = p.op("gpsimd", lambda e: e.tensor_tensor(
            out=Khf3, in0=kb3, in1=Khf3, op=OP.mult),
            deps=ekhf + proj_ev["k"])
        ktzA_t = []
        for pr in range(2):
            ktzA_t.append(p.op("gpsimd", lambda e, pr=pr: e.tensor_tensor(
                out=KtzA4[pr * 64:pr * 64 + 64, pr::2, 0:64],
                in0=kb3[pr * 64:pr * 64 + 64, :, 0:64],
                in1=E2_3[pr * 64:pr * 64 + 64, :, 0:64], op=OP.mult),
                deps=eA_t + dep["KtzA_r"] + proj_ev["k"]))
        dep["KtzA_r"] = []
        eB_t = [p.op("scalar", lambda e, hf=hf: e.activation(
            out=E1f[:, hf * 512:(hf + 1) * 512], in_=pb[4 + hf][:], func=AF.Exp, scale=1.0),
            deps=[t_ru, wave1[4 * hf + 3][2]]) for hf in range(2)]
        bank_free[0] = eRu_t[0]
        bank_free[1] = eRu_t[1]
        bank_free[2] = eA_t[0]
        bank_free[3] = eA_t[1]
        bank_free[4] = eB_t[0]
        bank_free[5] = eB_t[1]
        ktzB_t = []
        for pr in range(2):
            ktzB_t.append(p.op("gpsimd", lambda e, pr=pr: e.tensor_tensor(
                out=KtzB4[pr * 64:pr * 64 + 64, pr::2, :],
                in0=kb3[pr * 64:pr * 64 + 64, :, :],
                in1=E1_3[pr * 64:pr * 64 + 64, :, :], op=OP.mult),
                deps=eB_t + dep["KtzB_r"] + proj_ev["k"]))
        dep["KtzB_r"] = []
        wave2 = {}
        for q in range(NQ):
            m_xa = bank_mm(q // 4, lambda e, q=q: e.matmul(
                pb[q // 4][:, (q % 4) * 128:(q % 4) * 128 + 128],
                lhsT=lw_tm[:, q * 128:(q + 1) * 128],
                rhs=utri[:, 2 * 128:3 * 128], start=True, stop=True))
            m_xb = bank_mm(2 + q // 4, lambda e, q=q: e.matmul(
                pb[2 + q // 4][:, (q % 4) * 128:(q % 4) * 128 + 128],
                lhsT=lw_tm[:, q * 128:(q + 1) * 128],
                rhs=utri[:, 3 * 128:4 * 128], start=True, stop=True))
            m_in = bank_mm(4 + q // 4, lambda e, q=q: e.matmul(
                pb[4 + q // 4][:, (q % 4) * 128:(q % 4) * 128 + 128],
                lhsT=lw_tm[:, q * 128:(q + 1) * 128],
                rhs=utri[:, 1 * 128:2 * 128], start=True, stop=True))
            wave2[q] = (m_xa, m_xb, m_in)
            dep["lw_r"] += [m_xa, m_xb, m_in]
        eRA_t = [p.op("scalar", lambda e, hf=hf: e.activation(
            out=E2_3[:, 4 * hf:4 * hf + 4, 0:64],
            in_=pb[hf][:].rearrange("p (qq c) -> p qq c", qq=4)[:, :, 0:64],
            func=AF.Exp, scale=-1.0),
            deps=[wave2[4 * hf + 3][0]] + ktzA_t) for hf in range(2)]
        eRB_t = [p.op("scalar", lambda e, hf=hf: e.activation(
            out=E1_3[:, 4 * hf:4 * hf + 4, 64:128],
            in_=pb[2 + hf][:].rearrange("p (qq c) -> p qq c", qq=4)[:, :, 64:128],
            func=AF.Exp, scale=-1.0),
            deps=[wave2[4 * hf + 3][1], t_khf] + ktzB_t) for hf in range(2)]
        epl_toks = []
        for q in range(NQ):
            epl_toks.append(p.op("scalar", lambda e, q=q: e.activation(
                out=epl[:, q:q + 1], in_=pb[4 + q // 4][:, (q % 4) * 128 + 127:(q % 4) * 128 + 128],
                func=AF.Exp, scale=-1.0),
                deps=[wave2[(q // 4) * 4 + 3][2]] + dep["epl_r"]))
        dep["epl_r"] = []
        bank_free[0] = eRA_t[0]
        bank_free[1] = eRA_t[1]
        bank_free[2] = eRB_t[0]
        bank_free[3] = eRB_t[1]
        bank_free[4] = epl_toks[3]
        bank_free[5] = epl_toks[7]
        t_ra = p.op("gpsimd", lambda e: e.tensor_tensor(
            out=Rab3[:, :, 0:64], in0=rb3[:, :, 0:64], in1=E2_3[:, :, 0:64], op=OP.mult),
            deps=eRA_t + dep["Rab_r"] + proj_ev["r"])
        t_rb = p.op("gpsimd", lambda e: e.tensor_tensor(
            out=Rab3[:, :, 64:128], in0=rb3[:, :, 64:128], in1=E1_3[:, :, 64:128], op=OP.mult),
            deps=eRB_t + dep["Rab_r"] + proj_ev["r"])
        dep["Rab_r"] = []
        t_rku1 = p.op("gpsimd", lambda e: e.tensor_tensor(
            out=rku3, in0=rb3, in1=kb3, op=OP.mult),
            deps=proj_ev["r"] + proj_ev["k"] + dep["sx_r"])


        mm_f = {}
        for half in range(2):
            bi = 6 + half
            for qi in range(NQ):
                tok = bank_mm(bi, lambda e, half=half, qi=qi, bi=bi: e.matmul(
                    pb[bi][:],
                    lhsT=xnb[:, qi * TT:(qi + 1) * TT],
                    rhs=W_sb["f"][:, qi * C + half * 512:qi * C + half * 512 + 512],
                    start=(qi == 0), stop=False),
                    deps=[t_xnbp] + (weff_toks if t == 0 else []))
                dep["xnb_r"].append(tok)
            mm_f[half] = p.op("tensor", lambda e, half=half, bi=bi: e.matmul(
                pb[bi][:], lhsT=ones1[:], rhs=fbrow[:, half * 512:(half + 1) * 512],
                start=False, stop=True), deps=[t_fb, t_ones])
            bank_free[bi] = None

        kht_mm = {}
        for q in range(NQ):
            bi2 = 2 + q // 4
            kht_mm[q] = bank_mm(bi2, lambda e, q=q, bi2=bi2: e.matmul(
                pb[bi2][:, (q % 4) * 128:(q % 4) * 128 + 128],
                lhsT=Khf[:, q * 128:(q + 1) * 128], rhs=ident[:], start=True, stop=True),
                deps=[t_khf])
        khz_toks = {}
        for q in range(NQ):
            bi2 = 2 + q // 4
            for pr in range(2):
                h = 2 * q + pr
                khz_toks[h] = p.op("scalar", lambda e, q=q, bi2=bi2, pr=pr, h=h: e.activation(
                    out=Khz[:, h * 128 + pr * 64:h * 128 + pr * 64 + 64],
                    in_=pb[bi2][:, (q % 4) * 128 + pr * 64:(q % 4) * 128 + pr * 64 + 64],
                    func=AF.Copy),
                    deps=[kht_mm[(q // 4) * 4 + 3]] + dep["Khz_r"])
        dep["Khz_r"] = []
        bank_free[2] = khz_toks[7]
        bank_free[3] = khz_toks[15]

        a_mm = {}
        for h in range(H):
            bi = h // 4
            q = h // 2
            mA = bank_mm(bi, lambda e, h=h, bi=bi, q=q: e.matmul(
                pb[bi][:, (h % 4) * 128:(h % 4) * 128 + 64],
                lhsT=KtzA[:, h * 128:(h + 1) * 128],
                rhs=Rab[:, q * 128:q * 128 + 64],
                start=True, stop=True),
                deps=[ktzA_t[h % 2], t_ra])
            mB = bank_mm(bi, lambda e, h=h, bi=bi, q=q: e.matmul(
                pb[bi][:, (h % 4) * 128 + 64:(h % 4) * 128 + 128],
                lhsT=KtzB[:, h * 128:(h + 1) * 128],
                rhs=Rab[:, q * 128 + 64:q * 128 + 128],
                start=True, stop=True),
                deps=[ktzB_t[h % 2], t_rb])
            a_mm[h] = (mA, mB)
        a_ev = []
        for bi in range(4):
            tok = p.op("vector", lambda e, bi=bi: e.copy_predicated(
                out=A4[:, bi * 512:(bi + 1) * 512], mask=mask4[:], data=pb[bi][:]),
                deps=[a_mm[4 * bi + 3][1], t_mask4] + dep["A4_r"])
            bank_free[bi] = tok
            a_ev.append(tok)
        dep["A4_r"] = []

        su_mm = []
        for q in range(NQ):
            su_mm.append(bank_mm(0, lambda e, q=q: e.matmul(
                pb[0][:, 2 * q:2 * q + 2], lhsT=rku[:, q * 128:(q + 1) * 128],
                rhs=ind2u[:, 2 * q:2 * q + 2], start=True, stop=True),
                deps=[t_rku1, t_iu]))
        for tk in su_mm:
            dep["sx_r"].append(tk)
        t_su = p.op("vector", lambda e: e.tensor_copy(su_sb[:], pb[0][:, 0:H]), deps=su_mm + dep["su_r"])
        dep["su_r"] = []
        bank_free[0] = t_su


        if debug and t == 0:
            tok_der = p.dma("gpsimd", lambda e: e.dma_start(out=dbg["d_Er"], in_=Ru[:]), deps=[t_ru])
            dep["Ru_r"].append(_dt(tok_der))
            dep["A4_r"].append(_dt(p.dma("gpsimd", lambda e: e.dma_start(out=dbg["d_A"], in_=A4[:]), deps=a_ev)))
            dep["su_r"].append(_dt(p.dma("sync", lambda e: e.dma_start(out=dbg["d_su"], in_=su_sb[:]), deps=[t_su])))
            dep["KtzA_r"].append(_dt(p.dma("gpsimd", lambda e: e.dma_start(out=dbg["d_Ktz"], in_=KtzA[:]), deps=ktzA_t)))
            dep["Khz_r"].append(_dt(p.dma("gpsimd", lambda e: e.dma_start(out=dbg["d_Khz"], in_=Khz[:]), deps=list(khz_toks.values()))))

        dep["KtzA_r"] = dep["KtzA_r"] + [a_mm[h][0] for h in range(H)]
        dep["KtzB_r"] = [a_mm[h][1] for h in range(H)]
        dep["Rab_r"] = dep["Rab_r"] + [a_mm[h][0] for h in range(H)] + [a_mm[h][1] for h in range(H)]
        dep["Khf_r"] = list(kht_mm.values())
        dep["E1_r"] = dep["E1_r"] + [t_rb]
        dep["E2_r"] = [t_ra]
        dep["cmf_r"] = [t_khf]
        dep["rk_r"] = [t_ru, t_ra, t_rb, t_khf, t_rku1] + ktzA_t + ktzB_t
        S["t_su"] = t_su
        S["t_ru"] = t_ru
        S["t_khf"] = t_khf
        S["epl_toks"] = epl_toks
        S["khz_toks"] = khz_toks
        S["a_ev"] = a_ev
        S["a_mm"] = a_mm
        S["mm_f"] = mm_f

    def PB2a(t):
        S = TS[t]
        proj_ev = S["proj_ev"]
        t_su = S["t_su"]
        t_ru = S["t_ru"]
        t_khf = S["t_khf"]
        epl_toks = S["epl_toks"]
        khz_toks = S["khz_toks"]
        a_ev = S["a_ev"]
        a_mm = S["a_mm"]
        mm_f = S["mm_f"]
        par = t % 2
        Scur, Snxt = Sz[par], Sz[1 - par]

        t_vsu = p.op("gpsimd", lambda e: e.tensor_tensor(
            out=scr2[:].rearrange("p (h c) -> p h c", h=H),
            in0=v_tm[:].rearrange("p (h c) -> p h c", h=H),
            in1=su_sb[:].rearrange("p h -> p h ()").broadcast_to((128, H, K)),
            op=OP.mult),
            deps=[t_su] + proj_ev["v"] + dep["scr2_r"])
        dep["scr2_r"] = []
        t_sdec = p.op("gpsimd", lambda e, Scur=Scur, Snxt=Snxt: e.tensor_tensor(
            out=Snxt[:].rearrange("p (q c) -> p q c", q=NQ),
            in0=Scur[:].rearrange("p (q c) -> p q c", q=NQ),
            in1=epl[:].rearrange("p q -> p q ()").broadcast_to((128, NQ, 128)),
            op=OP.mult),
            deps=epl_toks + dep["Sz_r"][1 - par] + [dep["Sz_w"][1 - par], dep["Sz_w"][par]])
        dep["Sz_r"][1 - par] = []

        y_mm = {}
        s_mm = {}
        intra_mms = []
        for h in range(H):
            by = 4 + h // 8
            tok1 = bank_mm(by, lambda e, h=h, by=by: e.matmul(
                pb[by][:, (h % 8) * 64:(h % 8) * 64 + 64],
                lhsT=A4[:, h * 128:(h + 1) * 128],
                rhs=v_tm[:, h * K:h * K + K],
                start=True, stop=False), deps=[a_ev[h // 4]] + proj_ev["v"])
            intra_mms.append(tok1)
            y_mm[h] = p.op("tensor", lambda e, h=h, by=by: e.matmul(
                pb[by][:, (h % 8) * 64:(h % 8) * 64 + 64],
                lhsT=Ru[:, (h // 2) * 128:(h // 2) * 128 + 128],
                rhs=S_bfz[:, h * K:(h + 1) * K],
                start=False, stop=True), deps=[dep["Sbf_ready"], t_ru])
            bs = 2 + h // 8
            s_mm[h] = bank_mm(bs, lambda e, h=h, bs=bs: e.matmul(
                pb[bs][:, (h % 8) * 64:(h % 8) * 64 + 64],
                lhsT=Khz[:, h * 128:(h + 1) * 128],
                rhs=v_tm[:, h * K:h * K + K],
                start=True, stop=True), deps=[khz_toks[h]] + proj_ev["v"])
        yev = []
        sev = []
        for half in range(2):
            tok = p.op("vector", lambda e, half=half: e.tensor_tensor(
                out=y_tm[:, half * 512:half * 512 + 512],
                in0=scr2[:, half * 512:half * 512 + 512],
                in1=pb[4 + half][:], op=OP.add),
                deps=[y_mm[half * 8 + 7], t_vsu] + dep["y_r"])
            yev.append(tok)
            dep["scr2_r"].append(tok)
            tok2 = p.op("vector", lambda e, half=half, Snxt=Snxt: e.tensor_tensor(
                out=Snxt[:, half * 512:half * 512 + 512],
                in0=Snxt[:, half * 512:half * 512 + 512],
                in1=pb[2 + half][:], op=OP.add),
                deps=[s_mm[half * 8 + 7], t_sdec])
            sev.append(tok2)
        dep["y_r"] = []
        dep["Sz_r"][par] = [t_sdec]
        dep["Sz_r"][1 - par] = []
        dep["Sz_w"][1 - par] = sev[-1]
        bank_free[4] = yev[0]
        bank_free[5] = yev[1]
        bank_free[2] = sev[0]
        bank_free[3] = sev[1]
        t_sbf = p.op("vector", lambda e, Snxt=Snxt: e.tensor_copy(S_bfz[:], Snxt[:]),
                     deps=sev + list(y_mm.values()))
        dep["Sbf_ready"] = t_sbf
        dep["Sz_r"][1 - par].append(t_sbf)

        t_f2 = p.op("scalar", lambda e: e.activation(
            out=scr[:, 0:512], in_=pb[6][:], func=AF.Silu), deps=[mm_f[0]] + dep["scr_r"])
        t_f2b = p.op("scalar", lambda e: e.activation(
            out=scr[:, 512:1024], in_=pb[7][:], func=AF.Silu), deps=[mm_f[1]])
        bank_free[6] = t_f2
        bank_free[7] = t_f2b
        dep["scr_r"] = []

        dep["Ru_r"] = list(y_mm.values())
        dep["Khz_r"] = list(s_mm.values())
        dep["epl_r"] = [t_sdec, t_khf]
        dep["su_r"] = [t_vsu]
        dep["A4_r"] = dep["A4_r"] + intra_mms
        S["yev"] = yev
        S["sev"] = sev
        S["t_vsu"] = t_vsu
        S["t_f2"] = t_f2
        S["t_f2b"] = t_f2b

    def PB2b(t):
        S = TS[t]
        proj_ev = S["proj_ev"]
        yev = S["yev"]
        sev = S["sev"]
        t_vsu = S["t_vsu"]
        par = t % 2
        Snxt = Sz[1 - par]

        yg3 = y_tm[:].rearrange("p (h c) -> p h c", h=H)
        t_gm = p.op("vector", lambda e: e.tensor_reduce(
            out=gn_sc[:, 0:H], in_=yg3, axis=AX.X, op=OP.add), deps=yev + dep["gn_r"])
        dep["gn_r"] = []
        t_gsq = p.op("scalar", lambda e: e.activation(
            out=scr2[:], in_=y_tm[:], func=AF.Square), deps=yev + dep["scr2_r"])
        t_gr2 = p.op("vector", lambda e: e.tensor_reduce(
            out=gn_sc[:, H:2 * H], in_=scr2[:].rearrange("p (h c) -> p h c", h=H),
            axis=AX.X, op=OP.add), deps=[t_gsq])
        t_gmm = p.op("vector", lambda e: e.tensor_scalar_mul(gn_sc[:, 0:H], gn_sc[:, 0:H], 1.0 / K), deps=[t_gm])
        t_gm2 = p.op("vector", lambda e: e.tensor_tensor(
            out=gn_sc[:, 2 * H:3 * H], in0=gn_sc[:, 0:H], in1=gn_sc[:, 0:H], op=OP.mult), deps=[t_gmm])
        t_gvar = p.op("vector", lambda e: e.scalar_tensor_tensor(
            out=gn_sc[:, H:2 * H], in0=gn_sc[:, H:2 * H], scalar=1.0 / K,
            in1=gn_sc[:, 2 * H:3 * H], op0=OP.mult, op1=OP.subtract), deps=[t_gr2, t_gm2])
        t_gl1 = p.op("scalar", lambda e: e.activation(
            out=gn_sc2[:, 0:H], in_=gn_sc[:, H:2 * H], func=AF.Ln, bias=eps_sb[:]),
            deps=[t_gvar, t_eps])
        t_grs = p.op("scalar", lambda e: e.activation(
            out=gn_sc[:, 3 * H:4 * H], in_=gn_sc2[:, 0:H], func=AF.Exp, scale=-0.5), deps=[t_gl1])
        t_yc = p.op("gpsimd", lambda e: e.tensor_tensor(
            out=yg3, in0=yg3,
            in1=gn_sc[:, 0:H].rearrange("p h -> p h ()").broadcast_to((128, H, K)),
            op=OP.subtract), deps=[t_gmm, t_gsq])
        t_yn = p.op("gpsimd", lambda e: e.tensor_tensor(
            out=yg3, in0=yg3,
            in1=gn_sc[:, 3 * H:4 * H].rearrange("p h -> p h ()").broadcast_to((128, H, K)),
            op=OP.mult), deps=[t_grs, t_yc])
        dep["scr2_r"] = [t_gr2]
        dep["y_r"] = [t_yn]
        dep["gn_r"] = [t_yc, t_yn]

        if debug and t == 1:
            dep["y_r"].append(_dt(p.dma("sync", lambda e: e.dma_start(out=dbg["d_y"], in_=y_tm[:]), deps=[t_yn])))
            dep["Sz_r"][1 - par].append(_dt(p.dma("sync", lambda e: e.dma_start(out=dbg["d_S"], in_=Snxt[:]), deps=[sev[-1]])))

        t_gs = p.op("scalar", lambda e: e.activation(
            out=gs_sb[:], in_=g_raw[:], func=AF.Silu),
            deps=proj_ev["g"] + dep["gs_r"])
        dep["gs_r"] = []
        t_gl = p.op("gpsimd", lambda e: e.tensor_tensor(
            out=gl_b, in0=gs_sb[:], in1=lnw_bct[:], op=OP.mult),
            deps=[t_gs, ev[0], ev[1]] + dep["Khf_r"])
        dep["Khf_r"] = []
        t_glb = p.op("gpsimd", lambda e: e.tensor_tensor(
            out=glb_b, in0=gs_sb[:], in1=lnb_bct[:], op=OP.mult),
            deps=[t_gs, ev[2], ev[3]] + dep["yngfm_r"])
        dep["vg_r"] = [t_gl, t_glb, t_vsu]
        t_y1 = p.op("gpsimd", lambda e: e.tensor_tensor(
            out=scr2[:], in0=y_tm[:], in1=gl_b, op=OP.mult),
            deps=[t_gl, t_yn] + dep["scr2_r"])
        t_y2 = p.op("gpsimd", lambda e: e.tensor_tensor(
            out=y_tm[:].bitcast(BF16)[:, 0:C],
            in0=scr2[:], in1=glb_b, op=OP.add),
            deps=[t_glb, t_y1] + dep["y_r"])
        dep["Khf_r"].append(t_y1)
        dep["scr2_r"] = [t_y2]
        dep["y_r"] = [t_y2]

        S["t_y2"] = t_y2

    def PB3(t):
        S = TS[t]
        proj_ev = S["proj_ev"]
        t_y2 = S["t_y2"]
        t_f2 = S["t_f2"]
        t_f2b = S["t_f2b"]
        xt = S["xt"]
        xb = S["xb"]

        mms2 = []
        for q in range(NQ):
            bi = q // 4
            mms2.append(bank_mm(bi, lambda e, q=q, bi=bi: e.matmul(
                pb[bi][:, (q % 4) * 128:(q % 4) * 128 + 128],
                lhsT=y_tm[:].bitcast(BF16)[:, q * 128:(q + 1) * 128],
                rhs=ident[:], start=True, stop=True),
                deps=[t_y2, t_ident]))
        yngfm_ev = []
        for half in range(2):
            tok = p.op("scalar", lambda e, half=half: e.copy(
                yngfm3[:, 4 * half:4 * half + 4, :],
                pb[half][:].rearrange("p (q c) -> p q c", q=4)),
                deps=[mms2[4 * half + 3]] + (dep["yngfm_r"] if half == 0 else []))
            bank_free[half] = tok
            yngfm_ev.append(tok)
        dep["yngfm_r"] = []
        dep["y_r"] = dep["y_r"] + mms2

        if debug and t == 0:
            dep["yngfm_r"].append(_dt(p.dma("gpsimd", lambda e: e.dma_start(out=dbg["d_yng"], in_=yng_fm[:]), deps=yngfm_ev)))

        mm_l = {}
        for half in range(2):
            bi = 4 + half
            for qi in range(NQ):
                tok = bank_mm(bi, lambda e, half=half, qi=qi, bi=bi: e.matmul(
                    pb[bi][:],
                    lhsT=yng_fm[:, qi * TT:(qi + 1) * TT],
                    rhs=W_sb["o"][:, qi * C + half * 512:qi * C + half * 512 + 512],
                    start=(qi == 0), stop=(qi == NQ - 1)),
                    deps=[t_W["o"], yngfm_ev[0], yngfm_ev[1]])
                dep["yngfm_r"].append(tok)
                if qi == NQ - 1:
                    mm_l[half] = tok
        stm_ev = []
        for half in range(2):
            bi = 4 + half
            tok = p.op("scalar", lambda e, half=half, bi=bi: e.copy(
                sx_fm[:, half * 512:(half + 1) * 512], pb[bi][:]),
                deps=[mm_l[half]] + (dep["stm_r"] if half == 0 else []))
            bank_free[bi] = tok
            stm_ev.append(tok)
        dep["stm_r"] = stm_ev[:]

        if debug and t == 0:
            dep["stm_r"].append(_dt(p.dma("sync", lambda e: e.dma_start(out=dbg["d_stm"], in_=sx_fm[:, 0:C]), deps=stm_ev)))

        t_f3 = p.op("vector", lambda e: e.tensor_tensor(
            out=scr2[:], in0=scr[:], in1=sx_fm[:, 0:C], op=OP.mult),
            deps=[t_f2, t_f2b, stm_ev[0], stm_ev[1]] + dep["scr2_r"])
        dep["stm_r"].append(t_f3)
        t_out = p.op("vector", lambda e, xt=xt: e.tensor_tensor(
            out=xt[:], in0=scr2[:], in1=xt[:], op=OP.add), deps=[t_f3])
        dep["scr_r"] = [t_f3]
        dep["scr2_r"] = [t_out]
        t_st = p.dma("scalar", lambda e, t=t, xt=xt: e.dma_start(
            out=out_h[t * TT:(t + 1) * TT, :], in_=xt[:]), deps=[t_out])
        dep["out_dma"][xb] = t_st
        out_tokens.append(t_st)

    PF1(0)
    PF2(0)
    PF3(0)
    for t in range(n_tiles):
        PB1(t)
        if t + 1 < n_tiles:
            PF1(t + 1)
        PB2a(t)
        if t + 1 < n_tiles:
            PF2(t + 1)
        PB2b(t)
        if t + 1 < n_tiles:
            PF3(t + 1)
        PB3(t)

    p.op("sync", lambda e: e.nop(), deps=out_tokens)
    p.emit(nc, block, sems)
    blk.__exit__(None, None, None)
    return nc, sbuf_used


def host_inputs(inp, b):
    f32 = np.float32
    w2 = np.asarray(inp["L0_attn_tm_w2"], f32)
    maa = np.asarray(inp["L0_attn_time_maa_wkvrg"], f32)
    w2aug = np.concatenate([np.concatenate([w2[i], maa[i:i + 1]], axis=0) for i in range(5)], axis=0)
    td2aug = np.concatenate([np.asarray(inp["L0_attn_td_w2"], f32),
                             np.asarray(inp["L0_attn_time_decay"], f32)[None, :]], axis=0)
    ident = np.eye(128, dtype=f32)
    jj = np.arange(128)
    uexc = (jj[:, None] < jj[None, :]).astype(f32)
    uinc = (jj[:, None] <= jj[None, :]).astype(f32)
    pre32 = (jj[:, None] < 32).astype(f32) * np.ones((1, 128), f32)
    pre96 = (jj[:, None] < 96).astype(f32) * np.ones((1, 128), f32)
    utri = np.concatenate([uexc, uinc, uexc - pre32, uexc - pre96,
                           uinc - pre32, uinc - pre96], axis=1)
    mask4 = np.tile(uexc, (1, 4)).astype(np.uint8)
    ind2 = np.zeros((128, 2), f32)
    ind2[0:64, 0] = 1.0
    ind2[64:128, 1] = 1.0
    return {
        "x": np.ascontiguousarray(np.asarray(inp["L0_attn_x"], f32)[b]),
        "Wr": np.ascontiguousarray(np.asarray(inp["L0_attn_Wr"], f32)),
        "Wk": np.ascontiguousarray(np.asarray(inp["L0_attn_Wk"], f32)),
        "Wv": np.ascontiguousarray(np.asarray(inp["L0_attn_Wv"], f32)),
        "Wg": np.ascontiguousarray(np.asarray(inp["L0_attn_Wg"], f32)),
        "Wo": np.ascontiguousarray(np.asarray(inp["L0_attn_Wo"], f32)),
        "WfT": np.ascontiguousarray(np.asarray(inp["L0_ffn_W"], f32).T),
        "tm_w1": np.ascontiguousarray(np.asarray(inp["L0_attn_tm_w1"], f32)),
        "w2aug": np.ascontiguousarray(w2aug),
        "td_w1": np.ascontiguousarray(np.asarray(inp["L0_attn_td_w1"], f32)),
        "td2aug": np.ascontiguousarray(td2aug),
        "tmaa": np.ascontiguousarray(np.asarray(inp["L0_attn_time_maa_x"], f32)[None, :]),
        "ufaaaa": np.ascontiguousarray(np.asarray(inp["L0_attn_time_faaaa"], f32).reshape(1, -1)),
        "lnw": np.ascontiguousarray(np.asarray(inp["L0_attn_ln_w"], f32)[None, :]),
        "lnb": np.ascontiguousarray(np.asarray(inp["L0_attn_ln_b"], f32)[None, :]),
        "fb": np.ascontiguousarray(np.asarray(inp["L0_ffn_b"], f32)[None, :]),
        "lora_up": np.ascontiguousarray(np.asarray(inp["L0_ffn_lora_up"], f32)),
        "lora_down": np.ascontiguousarray(np.asarray(inp["L0_ffn_lora_down"], f32)),
        "down_aux": np.ascontiguousarray(np.asarray(inp["L0_ffn_down_aux"], f32)),
        "up_auxT": np.ascontiguousarray(np.asarray(inp["L0_ffn_up_aux"], f32).T),
        "ident": ident, "utri": utri, "mask4": mask4, "ind2": ind2,
    }


def kernel(**inputs):
    nc, _ = build_nc(n_tiles=T_FULL // TT, debug=False)
    in_maps = [host_inputs(inputs, b) for b in range(8)]
    res = run_bass_kernel_spmd(nc, in_maps, core_ids=list(range(8)))
    out = np.stack([np.asarray(r["out"]) for r in res.results], axis=0)
    return out.astype(np.float32)
